# version 13
# baseline (speedup 1.0000x reference)
import numpy as np
import ml_dtypes
from contextlib import ExitStack
import concourse.bass as bass
import concourse.mybir as mybir
from concourse.bass_utils import run_bass_kernel_spmd

F32 = mybir.dt.float32
BF16 = mybir.dt.bfloat16
I32 = mybir.dt.int32
AF = mybir.ActivationFunctionType
ALU = mybir.AluOpType
AXX = mybir.AxisListType.X
f32 = np.float32

D = 2048
S = 2048
H = 16
NCD = D // 128
NTS = S // 128
MLP0 = 16384
INTER1 = 8192
EPS_LN = 1e-5
EPS_L1 = 1e-6
ISQ_HD = float(1.0 / np.sqrt(128.0))
BF = np.dtype(ml_dtypes.bfloat16)


class PB:

    def __init__(self):
        self.nc = bass.Bass("TRN2", target_bir_lowering=False)
        self.stack = ExitStack()
        self.cur = self.stack
        self.sems = {}
        self._consts = False
        self._pctr = 0

    def begin_phase(self):
        self._pctr += 1
        self.cur = ExitStack()
        return self._pctr

    def end_phase(self):
        self.cur.close()
        self.cur = self.stack

    def sem(self, name):
        if name not in self.sems:
            h = self.stack.enter_context(self.nc.semaphore(name))
            self.sems[name] = [h, 0]
        return self.sems[name]

    def sbuf(self, name, shape, dt, td=False, global_=False):
        st = self.stack if global_ else self.cur
        return st.enter_context(
            self.nc.sbuf_tensor(f"{name}_{self._pctr}", list(shape), dt,
                                is_transpose_dest=td))

    def psum(self, name, shape, dt=F32):
        return self.cur.enter_context(
            self.nc.psum_tensor(f"{name}_{self._pctr}", list(shape), dt))

    def consts(self):
        if self._consts:
            return
        self._consts = True
        nc = self.nc
        self.ident = self.sbuf("c_ident", [128, 128], BF16, global_=True)
        self.trimask = self.sbuf("c_tri", [128, 128], BF16, global_=True)
        self.utri = self.sbuf("c_utri", [128, 128], I32, global_=True)
        self.neg30 = self.sbuf("c_n30", [128, 128], F32, global_=True)
        self.ones1 = self.sbuf("c_ones1", [1, 512], BF16, global_=True)
        w1 = self.sbuf("c_iw1", [128, 128], I32)
        w1b = self.sbuf("c_iw1b", [128, 128], I32)
        w2 = self.sbuf("c_iw2", [128, 128], I32)
        with nc.Block("consts") as block:
            @block.gpsimd
            def _(g):
                g.iota(w1[:, :], [[1, 128]], channel_multiplier=0)
                g.iota(w1b[:, :], [[1, 128]], channel_multiplier=128)
            @block.vector
            def _(v):
                v.memset(self.ones1[:, :], 1.0)
        with nc.Block("consts2") as block:
            @block.vector
            def _(v):
                v.tensor_scalar(w2[:, :], w1[:, :], 129, None, ALU.mult)
                v.tensor_tensor(w2[:, :], w2[:, :], w1b[:, :], ALU.is_equal)
                v.tensor_copy(self.ident[:, :], w2[:, :])
                v.tensor_tensor(w2[:, :], w1b[:, :], w1[:, :], ALU.subtract)
                v.tensor_scalar(w2[:, :], w2[:, :], 7, None, ALU.logical_shift_right)
                v.tensor_tensor(w2[:, :], w1[:, :], w2[:, :], ALU.is_le)
                v.tensor_copy(self.trimask[:, :], w2[:, :])
                v.tensor_scalar(self.utri[:, :], w2[:, :], -1, 1, ALU.mult, ALU.add)
                v.memset(self.neg30[:, :], -1e30)


def emit_ln(pb, x_dram, xn_fm_dram, s_in=None, g_in=None, x_out=None):
    nc = pb.nc
    pb.begin_phase()
    qS = pb.sem("qS")
    qG = pb.sem("qG")
    eV = pb.sem("eV")
    eS = pb.sem("eS")
    eT = pb.sem("eT")
    fused = s_in is not None
    nper = 3 if fused else 1
    nv = 3 if fused else 2
    xt = pb.sbuf("ln_x", [128, 2 * D], F32)
    st_t = pb.sbuf("ln_s", [128, 2 * D], F32) if fused else None
    gt = pb.sbuf("ln_g", [128, 2 * D], F32) if fused else None
    xsq = pb.sbuf("ln_xsq", [128, D], F32)
    stat = pb.sbuf("ln_stat", [128, 8], F32)
    xn_tm = pb.sbuf("ln_xntm", [128, NTS * D], BF16)
    qS0, qG0, eV0, eS0, eT0 = qS[1], qG[1], eV[1], eS[1], eT[1]

    with nc.Block(f"ln_a_{pb._pctr}") as block:
        @block.sync
        def _(s):
            for t in range(NTS):
                b = t % 2
                if t >= 2:
                    s.wait_ge(eV[0], eV0 + nv * (t - 1))
                    if fused:
                        s.wait_ge(qG[0], qG0 + 16 * (t - 1))
                s.dma_start(out=xt[:, b * D:(b + 1) * D],
                            in_=x_dram[t * 128:(t + 1) * 128, :]).then_inc(qS[0], 16)
                if fused:
                    s.dma_start(out=st_t[:, b * D:(b + 1) * D],
                                in_=s_in[t * 128:(t + 1) * 128, :]).then_inc(qS[0], 16)
                    s.dma_start(out=gt[:, b * D:(b + 1) * D],
                                in_=g_in[t * 128:(t + 1) * 128, :]).then_inc(qS[0], 16)

        @block.vector
        def _(v):
            for t in range(NTS):
                b = t % 2
                sl = slice(b * D, (b + 1) * D)
                v.wait_ge(qS[0], qS0 + 16 * nper * (t + 1))
                if fused:
                    v.tensor_tensor(st_t[:, sl], st_t[:, sl], gt[:, sl], ALU.mult)
                    v.tensor_tensor(xt[:, sl], xt[:, sl], st_t[:, sl], ALU.add)
                    v.drain().then_inc(eV[0], 1)
                v.tensor_reduce(stat[:, 0:1], xt[:, sl], axis=AXX, op=ALU.add)
                v.wait_ge(eS[0], eS0 + 2 * t + 1)
                v.tensor_reduce(stat[:, 1:2], xsq[:, :], axis=AXX, op=ALU.add)
                v.drain()
                v.tensor_scalar(stat[:, 2:3], stat[:, 0:1], float(1.0 / D), None, ALU.mult)
                v.drain()
                v.tensor_tensor(stat[:, 3:4], stat[:, 2:3], stat[:, 2:3], ALU.mult)
                v.drain()
                v.scalar_tensor_tensor(stat[:, 3:4], stat[:, 1:2], float(1.0 / D),
                                       stat[:, 3:4], ALU.mult, ALU.subtract)
                v.drain()
                v.tensor_scalar(stat[:, 3:4], stat[:, 3:4], float(EPS_LN), None, ALU.add)
                v.drain().then_inc(eV[0], 1)
                v.wait_ge(eS[0], eS0 + 2 * t + 2)
                v.reciprocal(stat[:, 5:6], stat[:, 4:5])
                v.drain()
                v.scalar_tensor_tensor(stat[:, 6:7], stat[:, 2:3], float(-1.0),
                                       stat[:, 5:6], ALU.mult, ALU.mult)
                v.drain()
                v.tensor_scalar(xn_tm[:, t * D:(t + 1) * D], xt[:, sl],
                                stat[:, 5:6], stat[:, 6:7], ALU.mult, ALU.add)
                v.drain().then_inc(eV[0], 1)

        @block.scalar
        def _(sc):
            for t in range(NTS):
                b = t % 2
                sc.wait_ge(qS[0], qS0 + 16 * nper * (t + 1))
                if fused:
                    sc.wait_ge(eV[0], eV0 + nv * t + 1)
                if t >= 1:
                    sc.wait_ge(eV[0], eV0 + nv * (t - 1) + nv - 1)
                sc.activation(xsq[:, :], xt[:, b * D:(b + 1) * D], AF.Square)
                sc.drain().then_inc(eS[0], 1)
                sc.wait_ge(eV[0], eV0 + nv * t + (nv - 1))
                sc.activation(stat[:, 4:5], stat[:, 3:4], AF.Sqrt)
                sc.drain().then_inc(eS[0], 1)

        if fused:
            @block.gpsimd
            def _(g):
                for t in range(NTS):
                    b = t % 2
                    g.wait_ge(eV[0], eV0 + nv * t + 1)
                    g.dma_start(out=x_out[t * 128:(t + 1) * 128, :],
                                in_=xt[:, b * D:(b + 1) * D]).then_inc(qG[0], 16)
                g.wait_ge(qG[0], qG0 + 16 * NTS)
    qS[1] += 16 * nper * NTS
    eV[1] += nv * NTS
    eS[1] += 2 * NTS
    if fused:
        qG[1] += 16 * NTS

    ptb = [pb.psum(f"ln_pt{i}", [128, 1024], BF16) for i in range(4)]
    stage = pb.sbuf("ln_stage", [128, 2 * D], BF16)
    eT0, eV0, qG0 = eT[1], eV[1], qG[1]
    with nc.Block(f"ln_b_{pb._pctr}") as block:
        @block.tensor
        def _(te):
            for t in range(NTS):
                for c in range(NCD):
                    k = t * NCD + c
                    if k >= 4:
                        te.wait_ge(eV[0], eV0 + k - 3)
                    te.transpose(ptb[k % 4][:, 0:128],
                                 xn_tm[:, t * D + c * 128: t * D + (c + 1) * 128],
                                 pb.ident[:, :]).then_inc(eT[0], 1)

        @block.vector
        def _(v):
            for t in range(NTS):
                b = t % 2
                if t >= 2:
                    v.wait_ge(qG[0], qG0 + 16 * NCD * (t - 1))
                for c in range(NCD):
                    k = t * NCD + c
                    v.wait_ge(eT[0], eT0 + k + 1)
                    v.tensor_copy(stage[:, b * D + c * 128: b * D + (c + 1) * 128],
                                  ptb[k % 4][:, 0:128])
                    v.drain().then_inc(eV[0], 1)

        @block.gpsimd
        def _(g):
            for t in range(NTS):
                b = t % 2
                g.wait_ge(eV[0], eV0 + NCD * (t + 1))
                for c in range(NCD):
                    g.dma_start(out=xn_fm_dram[c, :, t * 128:(t + 1) * 128],
                                in_=stage[:, b * D + c * 128: b * D + (c + 1) * 128]
                                ).then_inc(qG[0], 16)
            g.wait_ge(qG[0], qG0 + 16 * NCD * NTS)
    eT[1] += NTS * NCD
    eV[1] += NTS * NCD
    qG[1] += 16 * NCD * NTS
    pb.end_phase()


def emit_qkv(pb, xn_fm_dram, wq, bq, wk, bk, wv, bv, q_fm_dram, k_fm_dram, v_tm_dram):
    nc = pb.nc
    pb.begin_phase()
    qS = pb.sem("qS")
    qG = pb.sem("qG")
    qC = pb.sem("qC")
    eT = pb.sem("eT")
    eV = pb.sem("eV")
    eWq = pb.sem("qk_eWq")
    xn = pb.sbuf("qk_xn", [128, NCD * S], BF16)
    wt = pb.sbuf("qk_w", [128, NCD * D], BF16)
    brow = pb.sbuf("qk_b", [1, 3 * D], BF16)
    outst = pb.sbuf("qk_o", [128, 2 * D], BF16)
    ps = [pb.psum(f"qk_ps{i}", [128, 512], F32) for i in range(8)]
    qS0, qG0, qC0, eT0, eV0 = qS[1], qG[1], qC[1], eT[1], eV[1]
    eWq0 = eWq[1]

    with nc.Block(f"qkv_{pb._pctr}") as block:
        @block.sync
        def _(s):
            for c in range(NCD):
                s.dma_start(out=xn[:, c * S:(c + 1) * S], in_=xn_fm_dram[c, :, :]
                            ).then_inc(qS[0], 16)

        @block.gpsimd
        def _(g):
            for i, b in enumerate((bq, bk, bv)):
                g.dma_start(out=brow[0:1, i * D:(i + 1) * D], in_=b[None, :]).then_inc(qG[0], 16)
            for wi, w in enumerate((wq, wk, wv)):
                if wi > 0:
                    g.wait_ge(eT[0], eT0 + 16 * wi)
                for c in range(NCD):
                    g.dma_start(out=wt[:, c * D:(c + 1) * D],
                                in_=w[c * 128:(c + 1) * 128, :]).then_inc(qG[0], 16)
                g.wait_ge(qG[0], qG0 + 16 * (3 + NCD * (wi + 1)))
                g.sem_inc(eWq[0], 1)

        @block.tensor
        def _(te):
            te.wait_ge(qS[0], qS0 + 16 * NCD)
            for wi in range(2):
                te.wait_ge(eWq[0], eWq0 + wi + 1)
                for n in range(NCD):
                    grp = wi * 16 + n
                    if grp >= 2:
                        te.wait_ge(eV[0], eV0 + 4 * (grp - 1))
                    bank = (grp % 2) * 4
                    for d in range(NCD):
                        for t in range(4):
                            te.matmul(ps[bank + t][:, :],
                                      wt[:, d * D + n * 128: d * D + (n + 1) * 128],
                                      xn[:, d * S + t * 512: d * S + (t + 1) * 512],
                                      start=(d == 0), stop=False)
                    for t in range(4):
                        ins = te.matmul(ps[bank + t][:, :],
                                        brow[0:1, wi * D + n * 128: wi * D + (n + 1) * 128],
                                        pb.ones1[0:1, 0:512], start=False, stop=True)
                    ins.then_inc(eT[0], 1)
            te.wait_ge(eWq[0], eWq0 + 3)
            for t in range(NTS):
                grp = 32 + t
                if grp >= 2:
                    te.wait_ge(eV[0], eV0 + 4 * (grp - 1))
                bank = (grp % 2) * 4
                for d in range(NCD):
                    for nb in range(4):
                        te.matmul(ps[bank + nb][:, :],
                                  xn[:, d * S + t * 128: d * S + (t + 1) * 128],
                                  wt[:, d * D + nb * 512: d * D + (nb + 1) * 512],
                                  start=(d == 0), stop=False)
                for nb in range(4):
                    ins = te.matmul(ps[bank + nb][:, :],
                                    pb.ones1[0:1, 0:128],
                                    brow[0:1, 2 * D + nb * 512: 2 * D + (nb + 1) * 512],
                                    start=False, stop=True)
                ins.then_inc(eT[0], 1)

        @block.vector
        def _(v):
            for grp in range(48):
                v.wait_ge(eT[0], eT0 + grp + 1)
                if grp >= 2:
                    v.wait_ge(qC[0], qC0 + 16 * (grp - 1))
                b = (grp % 2) * D
                bank = (grp % 2) * 4
                for t in range(4):
                    v.tensor_copy(outst[:, b + t * 512: b + (t + 1) * 512],
                                  ps[bank + t][:, :]).then_inc(eV[0], 1)

        @block.scalar
        def _(sc):
            for grp in range(48):
                sc.wait_ge(eV[0], eV0 + 4 * (grp + 1))
                b = (grp % 2) * D
                if grp < 16:
                    dst = q_fm_dram[grp, :, :]
                elif grp < 32:
                    dst = k_fm_dram[grp - 16, :, :]
                else:
                    dst = v_tm_dram[grp - 32, :, :]
                sc.dma_start(out=dst, in_=outst[:, b:b + D]).then_inc(qC[0], 16)
            sc.wait_ge(qC[0], qC0 + 16 * 48)

    qS[1] += 16 * NCD
    qG[1] += 16 * (3 + 3 * NCD)
    eWq[1] += 3
    eT[1] += 48
    eV[1] += 192
    qC[1] += 16 * 48
    pb.end_phase()


def emit_att0(pb, q_fm_dram, k_fm_dram, v_tm_dram, ctx_fm_dram):
    nc = pb.nc
    pb.begin_phase()
    qS = pb.sem("qS")
    qG = pb.sem("qG")
    eHd = pb.sem("at_eHd")
    eT = pb.sem("at_eT")
    eTt = pb.sem("at_eTt")
    eTc = pb.sem("at_eTc")
    eVm = pb.sem("at_eVm")
    eS = pb.sem("at_eS")
    eVp = pb.sem("at_eVp")
    eVe = pb.sem("at_eVe")
    eSe = pb.sem("at_eSe")
    eVz = pb.sem("at_eVz")
    eVc = pb.sem("at_eVc")
    qh = pb.sbuf("at_q", [128, 2 * S], BF16)
    kh = pb.sbuf("at_k", [128, 2 * S], BF16)
    vh = pb.sbuf("at_v", [128, 2 * S], BF16)
    E = pb.sbuf("at_E", [128, S], BF16)
    P = pb.sbuf("at_P", [128, S], BF16)
    PT = pb.sbuf("at_PT", [128, 16 * 512], BF16)
    cst = pb.sbuf("at_cst", [128, 2 * 512], BF16)
    mst = pb.sbuf("at_mst", [128, 16], F32)
    psc = [pb.psum(f"at_sc{i}", [128, 512]) for i in range(4)]
    pst = [pb.psum(f"at_tp{i}", [128, 1024], BF16) for i in range(2)]
    pctx = pb.psum("at_ctx", [128, 512])
    qS0, qG0 = qS[1], qG[1]
    (eHd0, eT0, eTt0, eTc0, eVm0, eS0, eVp0, eVe0, eSe0, eVz0, eVc0) = (
        eHd[1], eT[1], eTt[1], eTc[1], eVm[1], eS[1], eVp[1], eVe[1], eSe[1],
        eVz[1], eVc[1])

    def tr_bf(h, i):
        return h * 136 + i * (i + 1) // 2

    def grp_bf(h, i):
        return h * 40 + sum((ii + 4) // 4 for ii in range(i))

    def z_bf(h, i):
        perh = sum(4 * (ii // 4 + 1) - (ii + 1) for ii in range(16))
        return h * perh + sum(4 * (ii // 4 + 1) - (ii + 1) for ii in range(i))

    ZPH = sum(4 * (ii // 4 + 1) - (ii + 1) for ii in range(16))

    with nc.Block(f"att0_{pb._pctr}") as block:
        @block.sync
        def _(s):
            for h in range(H):
                b = h % 2
                if h >= 2:
                    s.wait_ge(eT[0], eT0 + (h - 1) * 16)
                    s.wait_ge(eTc[0], eTc0 + (h - 1) * 4)
                s.dma_start(out=qh[:, b * S:(b + 1) * S], in_=q_fm_dram[h, :, :]
                            ).then_inc(qS[0], 16)
                s.dma_start(out=kh[:, b * S:(b + 1) * S], in_=k_fm_dram[h, :, :]
                            ).then_inc(qS[0], 16)
                for t in range(NTS):
                    s.dma_start(out=vh[:, b * S + t * 128: b * S + (t + 1) * 128],
                                in_=v_tm_dram[t, :, h * 128:(h + 1) * 128]
                                ).then_inc(qS[0], 16)
                s.wait_ge(qS[0], qS0 + 16 * 18 * (h + 1))
                s.sem_inc(eHd[0], 1)

        @block.tensor
        def _(te):
            for h in range(H):
                b = h % 2
                te.wait_ge(eHd[0], eHd0 + h + 1)
                for it in range(17):
                    if it < 16:
                        i = it
                        k = h * 16 + i
                        jbn = i // 4 + 1
                        if k >= 1:
                            te.wait_ge(eS[0], eS0 + k)
                        for jb in range(jbn):
                            ins = te.matmul(psc[jb][:, :],
                                            qh[:, b * S + i * 128: b * S + (i + 1) * 128],
                                            kh[:, b * S + jb * 512: b * S + (jb + 1) * 512],
                                            start=True, stop=True)
                        ins.then_inc(eT[0], 1)
                    if it >= 1:
                        i = it - 1
                        k = h * 16 + i
                        te.wait_ge(eVp[0], eVp0 + k + 1)
                        for gi2 in range((i + 4) // 4):
                            j0 = gi2 * 4
                            j1 = min(i + 1, j0 + 4)
                            gidx = grp_bf(h, i) + gi2
                            if gidx >= 2:
                                te.wait_ge(eVe[0], eVe0 + gidx - 1)
                            for jt in range(j0, j1):
                                te.transpose(pst[gidx % 2][:, (jt - j0) * 128:
                                                           (jt - j0 + 1) * 128],
                                             P[:, jt * 128:(jt + 1) * 128],
                                             pb.ident[:, :]).then_inc(eTt[0], 1)
                        if i % 4 == 3:
                            g = h * 4 + i // 4
                            if g >= 1:
                                te.wait_ge(eVc[0], eVc0 + g)
                            te.wait_ge(eVe[0], eVe0 + grp_bf(h, i + 1))
                            te.wait_ge(eVz[0], eVz0 + z_bf(h, i + 1))
                            njt = 4 * (i // 4 + 1)
                            for jt in range(njt):
                                ins = te.matmul(pctx[:, :],
                                                vh[:, b * S + jt * 128: b * S + (jt + 1) * 128],
                                                PT[:, jt * 512:(jt + 1) * 512],
                                                start=(jt == 0), stop=(jt == njt - 1))
                            ins.then_inc(eTc[0], 1)

        @block.vector
        def _(v):
            for h in range(H):
                for it in range(17):
                    if it >= 1:
                        i = it - 1
                        ii = i % 4
                        for gi2 in range((i + 4) // 4):
                            j0 = gi2 * 4
                            j1 = min(i + 1, j0 + 4)
                            gidx = grp_bf(h, i) + gi2
                            v.wait_ge(eTt[0], eTt0 + tr_bf(h, i) + j1)
                            dst = bass.AP(PT.tensor if hasattr(PT, "tensor") else PT,
                                          0, [[0, 0]])
                            ptv = PT[:, :].rearrange("p (jt c) -> p jt c", c=512)
                            v.tensor_copy(ptv[:, j0:j1, ii * 128:(ii + 1) * 128],
                                          pst[gidx % 2][:, 0:(j1 - j0) * 128]
                                          .rearrange("p (jt c) -> p jt c", c=128)
                                          ).then_inc(eVe[0], 1)
                        for jt in range(i + 1, 4 * (i // 4 + 1)):
                            v.memset(PT[:, jt * 512 + ii * 128:
                                        jt * 512 + (ii + 1) * 128], 0.0)
                            v.drain().then_inc(eVz[0], 1)
                        if i % 4 == 3:
                            g = h * 4 + i // 4
                            v.wait_ge(eTc[0], eTc0 + g + 1)
                            if g >= 2:
                                v.wait_ge(qG[0], qG0 + 16 * (g - 1))
                            v.tensor_copy(cst[:, (g % 2) * 512:(g % 2 + 1) * 512],
                                          pctx[:, :]).then_inc(eVc[0], 1)
                    if it < 16:
                        i = it
                        k = h * 16 + i
                        jbn = i // 4 + 1
                        v.wait_ge(eT[0], eT0 + k + 1)
                        v.copy_predicated(psc[i // 4][:, (i % 4) * 128:(i % 4 + 1) * 128],
                                          pb.utri[:, :], pb.neg30[:, :])
                        lo = (i + 1) * 128 - (jbn - 1) * 512
                        if lo < 512:
                            v.memset(psc[jbn - 1][:, lo:512], -1e30)
                        v.drain()
                        for jb in range(jbn):
                            v.reduce_max(mst[:, jb:jb + 1], psc[jb][:, :], axis=AXX)
                        v.drain()
                        v.reduce_max(mst[:, 4:5], mst[:, 0:jbn], axis=AXX)
                        v.drain()
                        v.tensor_scalar(mst[:, 5:6], mst[:, 4:5], float(-ISQ_HD), None,
                                        ALU.mult)
                        v.drain().then_inc(eVm[0], 1)
                        v.wait_ge(eS[0], eS0 + k + 1)
                        v.tensor_reduce(mst[:, 6:7], mst[:, 8:8 + jbn], axis=AXX,
                                        op=ALU.add)
                        v.drain()
                        v.reciprocal(mst[:, 7:8], mst[:, 6:7])
                        v.drain()
                        v.wait_ge(eTt[0], eTt0 + tr_bf(h, i))
                        v.tensor_scalar(P[:, 0:jbn * 512], E[:, 0:jbn * 512],
                                        mst[:, 7:8], None, ALU.mult)
                        v.drain().then_inc(eVp[0], 1)
        @block.scalar
        def _(sc):
            for h in range(H):
                for it in range(17):

                    if it < 16:
                        i = it
                        k = h * 16 + i
                        jbn = i // 4 + 1
                        sc.wait_ge(eVm[0], eVm0 + k + 1)
                        if k >= 1:
                            sc.wait_ge(eVp[0], eVp0 + k)
                        for jb in range(jbn):
                            sc.activation(E[:, jb * 512:(jb + 1) * 512], psc[jb][:, :],
                                          AF.Exp, bias=mst[:, 5:6], scale=f32(ISQ_HD),
                                          accum_out=mst[:, 8 + jb:9 + jb])
                        sc.drain().then_inc(eS[0], 1)
        @block.gpsimd
        def _(g_):
            for h in range(H):
                for ig in range(4):
                    g = h * 4 + ig
                    g_.wait_ge(eVc[0], eVc0 + g + 1)
                    g_.dma_start(out=ctx_fm_dram[h, :, ig * 512:(ig + 1) * 512],
                                 in_=cst[:, (g % 2) * 512:(g % 2 + 1) * 512]
                                 ).then_inc(qG[0], 16)
            g_.wait_ge(qG[0], qG0 + 16 * 64)

    qS[1] += 16 * 18 * H
    eHd[1] += H
    eT[1] += 16 * H
    eTt[1] += 136 * H
    eTc[1] += 4 * H
    eVm[1] += 16 * H
    eS[1] += 16 * H
    eVp[1] += 16 * H
    eVe[1] += 40 * H
    eVz[1] += ZPH * H
    eVc[1] += 4 * H
    qG[1] += 16 * 64
    pb.end_phase()


def emit_wo0(pb, ctx_fm_dram, wo, bo, s0_dram):
    nc = pb.nc
    pb.begin_phase()
    qS = pb.sem("qS")
    qG = pb.sem("qG")
    qC = pb.sem("qC")
    eT = pb.sem("wo_eT")
    eV = pb.sem("wo_eV")
    ctx = pb.sbuf("wo_ctx", [128, NCD * S], BF16)
    wt = pb.sbuf("wo_w", [128, NCD * D], BF16)
    brow = pb.sbuf("wo_b", [1, D], BF16)
    stage = pb.sbuf("wo_st", [128, 2 * D], F32)
    ps = [pb.psum(f"wo_ps{i}", [128, 512]) for i in range(8)]
    qS0, qG0, qC0, eT0, eV0 = qS[1], qG[1], qC[1], eT[1], eV[1]

    with nc.Block(f"wo0_{pb._pctr}") as block:
        @block.sync
        def _(s):
            for c in range(NCD):
                s.dma_start(out=ctx[:, c * S:(c + 1) * S], in_=ctx_fm_dram[c, :, :]
                            ).then_inc(qS[0], 16)

        @block.gpsimd
        def _(g):
            g.dma_start(out=brow[0:1, :], in_=bo[None, :]).then_inc(qG[0], 16)
            for c in range(NCD):
                g.dma_start(out=wt[:, c * D:(c + 1) * D],
                            in_=wo[c * 128:(c + 1) * 128, :]).then_inc(qG[0], 16)

        @block.tensor
        def _(te):
            te.wait_ge(qS[0], qS0 + 16 * NCD)
            te.wait_ge(qG[0], qG0 + 16 * (NCD + 1))
            for t in range(NTS):
                if t >= 2:
                    te.wait_ge(eV[0], eV0 + t - 1)
                bank = (t % 2) * 4
                for nb in range(4):
                    for d in range(NCD):
                        te.matmul(ps[bank + nb][:, :],
                                  ctx[:, d * S + t * 128: d * S + (t + 1) * 128],
                                  wt[:, d * D + nb * 512: d * D + (nb + 1) * 512],
                                  start=(d == 0), stop=False)
                    ins = te.matmul(ps[bank + nb][:, :], pb.ones1[0:1, 0:128],
                                    brow[0:1, nb * 512:(nb + 1) * 512],
                                    start=False, stop=True)
                ins.then_inc(eT[0], 1)

        @block.vector
        def _(v):
            for t in range(NTS):
                v.wait_ge(eT[0], eT0 + t + 1)
                if t >= 2:
                    v.wait_ge(qC[0], qC0 + 16 * (t - 1))
                b = (t % 2) * D
                bank = (t % 2) * 4
                for nb in range(4):
                    v.tensor_copy(stage[:, b + nb * 512: b + (nb + 1) * 512],
                                  ps[bank + nb][:, :])
                v.drain().then_inc(eV[0], 1)

        @block.scalar
        def _(sc):
            for t in range(NTS):
                sc.wait_ge(eV[0], eV0 + t + 1)
                sc.dma_start(out=s0_dram[t * 128:(t + 1) * 128, :],
                             in_=stage[:, (t % 2) * D:(t % 2 + 1) * D]).then_inc(qC[0], 16)
            sc.wait_ge(qC[0], qC0 + 16 * NTS)

    qS[1] += 16 * NCD
    qG[1] += 16 * (NCD + 1)
    eT[1] += NTS
    eV[1] += NTS
    qC[1] += 16 * NTS
    pb.end_phase()


def emit_dq(pb, packed_dram, scales_dram, out_dram, NR, KPT):
    nc = pb.nc
    pb.begin_phase()
    qS = pb.sem("qS")
    qG = pb.sem("qG")
    eP = pb.sem("dq_eP")
    eV0 = pb.sem("dq_eV0")
    eS1 = pb.sem("dq_eS1")
    eS2 = pb.sem("dq_eS2")
    eV1 = pb.sem("dq_eV1")
    eV2 = pb.sem("dq_eV2")
    KP = 1024
    KO = 2 * KP
    NB = KO // 32
    nrt = NR // 128
    nkc = KPT // KP
    ntile = nrt * nkc
    wt = pb.sbuf("dq_w", [128, 2 * KP], I32)
    sct = pb.sbuf("dq_sc", [128, 2 * NB], F32)
    nib = pb.sbuf("dq_nib", [128, 2 * KO], I32)
    A = pb.sbuf("dq_A", [128, 2 * KO], F32)
    B = pb.sbuf("dq_B", [128, 2 * KO], F32)
    C = pb.sbuf("dq_C", [128, 2 * KO], F32)
    Dd = pb.sbuf("dq_D", [128, 2 * KO], F32)
    ob = pb.sbuf("dq_o", [128, 2 * KO], BF16)
    qS0, qG0 = qS[1], qG[1]
    eP0, eS10, eS20, eV10, eV20 = eP[1], eS1[1], eS2[1], eV1[1], eV2[1]
    eV00 = eV0[1]

    def tiles():
        for r in range(nrt):
            for kc in range(nkc):
                yield r, kc

    with nc.Block(f"dq_{pb._pctr}") as block:
        @block.sync
        def _(s):
            for k, (r, kc) in enumerate(tiles()):
                b = k % 2
                if k >= 2:
                    s.wait_ge(eV0[0], eV00 + k - 1)
                    s.wait_ge(eV2[0], eV20 + k - 1)
                s.dma_start(out=wt[:, b * KP:(b + 1) * KP],
                            in_=packed_dram[r * 128:(r + 1) * 128, kc * KP:(kc + 1) * KP]
                            ).then_inc(qS[0], 16)
                s.dma_start(out=sct[:, b * NB:(b + 1) * NB],
                            in_=scales_dram[r * 128:(r + 1) * 128, kc * NB:(kc + 1) * NB]
                            ).then_inc(qS[0], 16)

        @block.gpsimd
        def _(g):
            for k in range(ntile):
                b = k % 2
                sl = slice(b * KO, (b + 1) * KO)
                g.wait_ge(eS2[0], eS20 + k + 1)
                g.tensor_tensor(A[:, sl], A[:, sl], Dd[:, sl], ALU.add)
                g.drain().then_inc(eP[0], 1)
                if k >= 1:
                    g.wait_ge(eV2[0], eV20 + k)
                    bb = (k - 1) % 2
                    rr, kk = (k - 1) // nkc, (k - 1) % nkc
                    g.dma_start(out=out_dram[rr * 128:(rr + 1) * 128,
                                             kk * KO:(kk + 1) * KO],
                                in_=ob[:, bb * KO:(bb + 1) * KO]).then_inc(qG[0], 16)
            g.wait_ge(eV2[0], eV20 + ntile)
            bb = (ntile - 1) % 2
            rr, kk = (ntile - 1) // nkc, (ntile - 1) % nkc
            g.dma_start(out=out_dram[rr * 128:(rr + 1) * 128, kk * KO:(kk + 1) * KO],
                        in_=ob[:, bb * KO:(bb + 1) * KO]).then_inc(qG[0], 16)
            g.wait_ge(qG[0], qG0 + 16 * ntile)

        @block.scalar
        def _(sc):
            for k in range(ntile):
                b = k % 2
                sl = slice(b * KO, (b + 1) * KO)
                sc.wait_ge(eV0[0], eV00 + k + 1)
                if k >= 2:
                    sc.wait_ge(eP[0], eP0 + k - 1)
                    sc.wait_ge(eV2[0], eV20 + k - 1)
                sc.activation(A[:, sl], nib[:, sl], AF.Copy, bias=0.0)
                sc.activation(B[:, sl], A[:, sl], AF.Sign, bias=f32(7.5), scale=f32(-1.0))
                sc.drain().then_inc(eS1[0], 1)
                sc.wait_ge(eV1[0], eV10 + k + 1)
                sc.activation(A[:, sl], C[:, sl], AF.Relu, bias=f32(-4.0), scale=f32(0.5))
                sc.activation(Dd[:, sl], C[:, sl], AF.Relu, bias=f32(-10.0))
                sc.drain().then_inc(eS2[0], 1)

        @block.vector
        def _(v):
            for k in range(ntile):
                b = k % 2
                sl = slice(b * KO, (b + 1) * KO)
                v.wait_ge(qS[0], qS0 + 32 * k + 16)
                if k >= 2:
                    v.wait_ge(eS1[0], eS10 + k - 1)
                nv = nib[:, sl].rearrange("p (k two) -> p k two", two=2)
                wv = wt[:, b * KP:(b + 1) * KP].unsqueeze(-1)
                v.tensor_scalar(nv[:, :, 0:1], wv, 15, None, ALU.bitwise_and)
                v.tensor_scalar(nv[:, :, 1:2], wv, 4, 15,
                                ALU.logical_shift_right, ALU.bitwise_and)
                v.drain().then_inc(eV0[0], 1)
                v.wait_ge(eS1[0], eS10 + k + 1)
                v.scalar_tensor_tensor(C[:, sl], B[:, sl], 4.0, A[:, sl],
                                       ALU.mult, ALU.add)
                v.drain().then_inc(eV1[0], 1)
                v.wait_ge(eP[0], eP0 + k + 1)
                v.scalar_tensor_tensor(C[:, sl], C[:, sl], 0.5, A[:, sl],
                                       ALU.mult, ALU.add)
                v.scalar_tensor_tensor(C[:, sl], C[:, sl], -2.0, B[:, sl],
                                       ALU.add, ALU.mult)
                if k >= 2:
                    v.wait_ge(qG[0], qG0 + 16 * (k - 1))
                v.wait_ge(qS[0], qS0 + 32 * (k + 1))
                cv = C[:, sl].rearrange("p (bk j) -> p bk j", j=32)
                ov = ob[:, sl].rearrange("p (bk j) -> p bk j", j=32)
                for hb in range(NB // 32):
                    scv = sct[:, b * NB + hb * 32: b * NB + (hb + 1) * 32]
                    scb = scv.unsqueeze(-1).broadcast_to((128, 32, 32))
                    v.tensor_tensor(ov[:, hb * 32:(hb + 1) * 32, :],
                                    cv[:, hb * 32:(hb + 1) * 32, :], scb, ALU.mult)
                v.drain().then_inc(eV2[0], 1)

    qS[1] += 32 * ntile
    qG[1] += 16 * ntile
    eP[1] += ntile
    eV0[1] += ntile
    eS1[1] += ntile
    eS2[1] += ntile
    eV1[1] += ntile
    eV2[1] += ntile
    pb.end_phase()


def emit_ffn0a(pb, xn_fm_dram, wg_bf, wu_bf, inter_dram):
    nc = pb.nc
    pb.begin_phase()
    qS = pb.sem("qS")
    qG = pb.sem("qG")
    qC = pb.sem("qC")
    eT = pb.sem("fa_eT")
    eSg = pb.sem("fa_eSg")
    eV = pb.sem("fa_eV")
    eVn = pb.sem("fa_eVn")
    xn = pb.sbuf("fa_xn", [128, NCD * S], BF16)
    wg = pb.sbuf("fa_wg", [128, 2 * NCD * 512], BF16, td=True)
    wu = pb.sbuf("fa_wu", [128, 2 * NCD * 512], BF16, td=True)
    gsb = pb.sbuf("fa_g", [128, 512], F32)
    ibuf = pb.sbuf("fa_i", [128, 2 * S], BF16)
    ps = [pb.psum(f"fa_ps{i}", [128, 512]) for i in range(4)]
    qS0, qG0, qC0 = qS[1], qG[1], qC[1]
    eT0, eSg0, eV0, eVn0 = eT[1], eSg[1], eV[1], eVn[1]
    NG = 32

    with nc.Block(f"ffn0a_{pb._pctr}") as block:
        @block.gpsimd
        def _(g):
            for c in range(NCD):
                g.dma_start(out=xn[:, c * S:(c + 1) * S], in_=xn_fm_dram[c, :, :]
                            ).then_inc(qG[0], 16)

        @block.sync
        def _(s):
            for g in range(NG):
                b = g % 2
                if g >= 2:
                    s.wait_ge(eT[0], eT0 + 16 * (g - 1))
                for d in range(NCD):
                    s.dma_start(out=wg[:, (b * NCD + d) * 512:(b * NCD + d + 1) * 512],
                                in_=wg_bf[g * 512:(g + 1) * 512, d * 128:(d + 1) * 128],
                                transpose=True).then_inc(qS[0], 16)
                    s.dma_start(out=wu[:, (b * NCD + d) * 512:(b * NCD + d + 1) * 512],
                                in_=wu_bf[g * 512:(g + 1) * 512, d * 128:(d + 1) * 128],
                                transpose=True).then_inc(qS[0], 16)

        @block.tensor
        def _(te):
            te.wait_ge(qG[0], qG0 + 16 * NCD)
            for g in range(NG):
                b = g % 2
                te.wait_ge(eWf[0], eWf0 + g + 1)
                for ncg in range(4):
                    for st in range(4):
                        idx = (g * 4 + ncg) * 4 + st
                        if idx >= 2:
                            te.wait_ge(eV[0], eV0 + idx - 1)
                        bg = (st % 2) * 2
                        for d in range(NCD):
                            te.matmul(ps[bg][:, :],
                                      wg[:, (b * NCD + d) * 512 + ncg * 128:
                                         (b * NCD + d) * 512 + (ncg + 1) * 128],
                                      xn[:, d * S + st * 512: d * S + (st + 1) * 512],
                                      start=(d == 0), stop=(d == NCD - 1))
                        for d in range(NCD):
                            ins = te.matmul(ps[bg + 1][:, :],
                                            wu[:, (b * NCD + d) * 512 + ncg * 128:
                                               (b * NCD + d) * 512 + (ncg + 1) * 128],
                                            xn[:, d * S + st * 512: d * S + (st + 1) * 512],
                                            start=(d == 0), stop=(d == NCD - 1))
                        ins.then_inc(eT[0], 1)

        @block.scalar
        def _(sc):
            for nc_i in range(128):
                for st in range(4):
                    idx = nc_i * 4 + st
                    sc.wait_ge(eT[0], eT0 + idx + 1)
                    if idx >= 1:
                        sc.wait_ge(eV[0], eV0 + idx)
                    bg = (st % 2) * 2
                    sc.activation(gsb[:, :], ps[bg][:, :], AF.Gelu_apprx_tanh)
                    sc.drain().then_inc(eSg[0], 1)
                if nc_i >= 1:
                    sc.wait_ge(eVn[0], eVn0 + nc_i)
                    pb_ = (nc_i - 1) % 2
                    sc.dma_start(out=inter_dram[nc_i - 1, :, :],
                                 in_=ibuf[:, pb_ * S:(pb_ + 1) * S]).then_inc(qC[0], 16)
            sc.wait_ge(eVn[0], eVn0 + 128)
            sc.dma_start(out=inter_dram[127, :, :], in_=ibuf[:, S:2 * S]
                         ).then_inc(qC[0], 16)
            sc.wait_ge(qC[0], qC0 + 16 * 128)

        @block.vector
        def _(v):
            for nc_i in range(128):
                nb = nc_i % 2
                for st in range(4):
                    idx = nc_i * 4 + st
                    v.wait_ge(eSg[0], eSg0 + idx + 1)
                    if nc_i >= 2 and st == 0:
                        v.wait_ge(qC[0], qC0 + 16 * (nc_i - 1))
                    bg = (st % 2) * 2
                    v.tensor_tensor(ibuf[:, nb * S + st * 512: nb * S + (st + 1) * 512],
                                    gsb[:, :], ps[bg + 1][:, :],
                                    ALU.mult).then_inc(eV[0], 1)
                v.drain().then_inc(eVn[0], 1)

    qG[1] += 16 * NCD
    qS[1] += 16 * 32 * NG
    eT[1] += 512
    eSg[1] += 512
    eV[1] += 512
    eVn[1] += 128
    qC[1] += 16 * 128
    pb.end_phase()


def emit_ffn0b(pb, inter_dram, wd_bf, g0_dram, nchunks=128, act="silu"):
    nc = pb.nc
    pb.begin_phase()
    qS = pb.sem("qS")
    qG = pb.sem("qG")
    qGs2 = pb.sem("fb_qGs2")
    qC = pb.sem("qC")
    eTk = pb.sem("fb_eTk")
    eS = pb.sem("fb_eS")
    eWdS = pb.sem("fb_eWdS")
    eWdC = pb.sem("fb_eWdC")
    RING = 8
    ibig = pb.sbuf("fb_i", [128, nchunks * 512], BF16)
    wd = pb.sbuf("fb_wd", [128, RING * 1024], BF16, td=True)
    stage = pb.sbuf("fb_st", [128, 2 * 4096], F32)
    ps = [pb.psum(f"fb_ps{i}", [128, 512]) for i in range(8)]
    qS0, qG0, qGs20, qC0 = qS[1], qG[1], qGs2[1], qC[1]
    eTk0, eS0, eWdS0, eWdC0 = eTk[1], eS[1], eWdS[1], eWdC[1]
    afunc = AF.Silu if act == "silu" else AF.Copy
    NGI = 8

    with nc.Block(f"ffn0b_{pb._pctr}") as block:
        @block.gpsimd
        def _(g):
            for sg in range(4):
                if sg > 0:
                    g.wait_ge(eTk[0], eTk0 + sg * 2 * nchunks)
                for nc_i in range(nchunks):
                    g.dma_start(out=ibig[:, nc_i * 512:(nc_i + 1) * 512],
                                in_=inter_dram[nc_i, :, sg * 512:(sg + 1) * 512]
                                ).then_inc(qG[0], 16)
                if sg > 0:
                    for mbp in range(2):
                        gi = (sg - 1) * 2 + mbp
                        g.wait_ge(eS[0], eS0 + gi + 1)
                        for mb2 in range(2):
                            for ss in range(4):
                                g.dma_start(
                                    out=g0_dram[(sg - 1) * 512 + ss * 128:
                                                (sg - 1) * 512 + (ss + 1) * 128,
                                                mbp * 1024 + mb2 * 512:
                                                mbp * 1024 + (mb2 + 1) * 512],
                                    in_=stage[:, (gi % 2) * 4096 + mb2 * 2048 + ss * 512:
                                              (gi % 2) * 4096 + mb2 * 2048 + (ss + 1) * 512]
                                    ).then_inc(qGs2[0], 16)
            for mbp in range(2):
                gi = 6 + mbp
                g.wait_ge(eS[0], eS0 + gi + 1)
                for mb2 in range(2):
                    for ss in range(4):
                        g.dma_start(
                            out=g0_dram[3 * 512 + ss * 128: 3 * 512 + (ss + 1) * 128,
                                        mbp * 1024 + mb2 * 512:
                                        mbp * 1024 + (mb2 + 1) * 512],
                            in_=stage[:, (gi % 2) * 4096 + mb2 * 2048 + ss * 512:
                                      (gi % 2) * 4096 + mb2 * 2048 + (ss + 1) * 512]
                            ).then_inc(qGs2[0], 16)
            g.wait_ge(qGs2[0], qGs20 + 16 * 64)

        @block.sync
        def _(s):
            nS = 0
            for k in range(NGI * nchunks):
                gi, nc_i = divmod(k, nchunks)
                mbp = gi % 2
                sg = gi // 2
                if k >= RING:
                    s.wait_ge(eTk[0], eTk0 + k - RING + 1)
                s.dma_start(out=wd[:, (k % RING) * 1024:(k % RING + 1) * 1024],
                            in_=wd_bf[mbp * 1024:(mbp + 1) * 1024,
                                      nc_i * 128:(nc_i + 1) * 128],
                            transpose=True).then_inc(qS[0], 16)
                nS += 1
                if k % 4 == 3:
                    s.wait_ge(qS[0], qS0 + 16 * nS)
                    s.sem_inc(eWdS[0], 4)

        @block.scalar
        def _(sc):
            nC = 0
            for gi in range(NGI):
                sg, mbp2 = divmod(gi, 2)
                if gi >= 1:
                    gp = gi - 1
                    sc.wait_ge(eTk[0], eTk0 + gi * nchunks)
                    if gp >= 2:
                        sc.wait_ge(qGs2[0], qGs20 + 16 * 8 * (gp - 1))
                    bs = (gp % 2) * 4096
                    for mb2 in range(2):
                        for ss in range(4):
                            sc.activation(
                                stage[:, bs + mb2 * 2048 + ss * 512:
                                      bs + mb2 * 2048 + (ss + 1) * 512],
                                ps[mb2 * 4 + ss][:, :], afunc)
                    sc.drain().then_inc(eS[0], 1)

            gp = NGI - 1
            sc.wait_ge(eTk[0], eTk0 + NGI * nchunks)
            sc.wait_ge(qGs2[0], qGs20 + 16 * 8 * (gp - 1))
            bs = (gp % 2) * 4096
            for mb2 in range(2):
                for ss in range(4):
                    sc.activation(stage[:, bs + mb2 * 2048 + ss * 512:
                                        bs + mb2 * 2048 + (ss + 1) * 512],
                                  ps[mb2 * 4 + ss][:, :], afunc)
            sc.drain().then_inc(eS[0], 1)

        @block.tensor
        def _(te):
            for gi in range(NGI):
                sg = gi // 2
                te.wait_ge(qG[0], qG0 + 16 * nchunks * (sg + 1))
                if gi >= 1:
                    te.wait_ge(eS[0], eS0 + gi)
                for nc_i in range(nchunks):
                    k = gi * nchunks + nc_i
                    te.wait_ge(eWdS[0], eWdS0 + k + 1)
                    for mb2 in range(2):
                        for ss in range(4):
                            ins = te.matmul(
                                ps[mb2 * 4 + ss][:, :],
                                ibig[:, nc_i * 512 + ss * 128:
                                     nc_i * 512 + (ss + 1) * 128],
                                wd[:, (k % RING) * 1024 + mb2 * 512:
                                   (k % RING) * 1024 + (mb2 + 1) * 512],
                                start=(nc_i == 0), stop=(nc_i == nchunks - 1))
                    ins.then_inc(eTk[0], 1)

    qG[1] += 16 * nchunks * 4
    qGs2[1] += 16 * 64
    qS[1] += 16 * NGI * nchunks
    eWdS[1] += NGI * nchunks
    eTk[1] += NGI * nchunks
    eS[1] += NGI
    pb.end_phase()


def emit_casts(pb, jobs):
    nc = pb.nc
    pb.begin_phase()
    qG = pb.sem("qG")
    qG0 = qG[1]
    n = 0
    with nc.Block(f"casts_{pb._pctr}") as block:
        @block.gpsimd
        def _(g):
            nonlocal n
            for src, dst, nr, ncol in jobs:
                step = max(1, (4 << 20) // (ncol * 4))
                for r0 in range(0, nr, step):
                    r1 = min(nr, r0 + step)
                    g.dma_start(out=dst[r0:r1, :], in_=src[r0:r1, :]).then_inc(qG[0], 16)
                    n += 1
            g.wait_ge(qG[0], qG0 + 16 * n)
    qG[1] += 16 * n
    pb.end_phase()


def emit_scaled_cast(pb, w_dram, scale_vec_dram, out_dram, NRr, NCc, imm):
    nc = pb.nc
    pb.begin_phase()
    qS = pb.sem("qS")
    qG = pb.sem("qG")
    qC = pb.sem("qC")
    eT = pb.sem("sc_eT")
    eV = pb.sem("sc_eV")
    nrt = NRr // 128
    wt = pb.sbuf("scst_w", [128, 2 * NCc], F32)
    ob = pb.sbuf("scst_o", [128, 2 * NCc], BF16)
    svrow = pb.sbuf("scst_sv", [nrt, 128], BF16)
    scol = pb.sbuf("scst_c", [128, nrt], F32)
    pst = pb.psum("scst_p", [128, 1024], BF16)
    qS0, qG0, qC0, eT0, eV0 = qS[1], qG[1], qC[1], eT[1], eV[1]

    with nc.Block(f"scast_{pb._pctr}") as block:
        @block.gpsimd
        def _(g):
            g.dma_start(out=svrow[:, :], in_=scale_vec_dram.rearrange("(r p) -> r p", p=128)
                        ).then_inc(qG[0], 16)
        @block.tensor
        def _(te):
            te.wait_ge(qG[0], qG0 + 16)
            te.transpose(pst[:, 0:nrt], svrow[0:nrt, :], pb.ident[0:nrt, 0:nrt]).then_inc(eT[0], 1)
        @block.vector
        def _(v):
            v.wait_ge(eT[0], eT0 + 1)
            v.tensor_copy(scol[:, :], pst[:, 0:nrt])
            v.drain().then_inc(eV[0], 1)
            for r in range(nrt):
                b = r % 2
                v.wait_ge(qS[0], qS0 + 16 * (r + 1))
                if r >= 2:
                    v.wait_ge(qC[0], qC0 + 16 * (r - 1))
                v.tensor_scalar(ob[:, b * NCc:(b + 1) * NCc], wt[:, b * NCc:(b + 1) * NCc],
                                scol[:, r:r + 1], float(imm), ALU.mult, ALU.mult)
                v.drain().then_inc(eV[0], 1)
        @block.sync
        def _(s):
            for r in range(nrt):
                b = r % 2
                if r >= 2:
                    s.wait_ge(eV[0], eV0 + r)
                s.dma_start(out=wt[:, b * NCc:(b + 1) * NCc],
                            in_=w_dram[r * 128:(r + 1) * 128, :]).then_inc(qS[0], 16)
        @block.scalar
        def _(sc):
            for r in range(nrt):
                b = r % 2
                sc.wait_ge(eV[0], eV0 + r + 2)
                sc.dma_start(out=out_dram[r * 128:(r + 1) * 128, :],
                             in_=ob[:, b * NCc:(b + 1) * NCc]).then_inc(qC[0], 16)
            sc.wait_ge(qC[0], qC0 + 16 * nrt)
    qG[1] += 16
    eT[1] += 1
    eV[1] += 1 + nrt
    qS[1] += 16 * nrt
    qC[1] += 16 * nrt
    pb.end_phase()


def _emit_h128(pb, v_eng, h128, w1, w1b, tmp1, tmp2):
    ALUv = ALU
    v = v_eng
    v.tensor_scalar(tmp1[:, :], w1b[:, :], 7, None, ALUv.logical_shift_right)
    v.drain()
    v.tensor_tensor(tmp1[:, :], tmp1[:, :], w1[:, :], ALUv.bitwise_and)
    v.drain()
    v.tensor_scalar(tmp2[:, :], tmp1[:, :], 1, None, ALUv.bitwise_and)
    v.drain()
    for b in range(1, 7):
        v.tensor_scalar(tmp1[:, :], tmp1[:, :], 1, None, ALUv.logical_shift_right)
        v.drain()
        v.tensor_tensor(tmp2[:, :], tmp2[:, :],
                        tmp1[:, :], ALUv.bitwise_xor)
        v.drain()
    v.tensor_scalar(tmp2[:, :], tmp2[:, :], 1, None, ALUv.bitwise_and)
    v.drain()
    v.tensor_scalar(tmp2[:, :], tmp2[:, :], -2, 1, ALUv.mult, ALUv.add)
    v.drain()
    v.tensor_copy(h128[:, :], tmp2[:, :])
    v.drain()


def emit_fwht1(pb, xn_fm_dram, su_dram, xr_fm_dram):
    nc = pb.nc
    pb.begin_phase()
    qS = pb.sem("qS")
    qG = pb.sem("qG")
    qC = pb.sem("qC")
    eT = pb.sem("fw_eT")
    eV = pb.sem("fw_eV")
    eVq = pb.sem("fw_eVq")
    eP = pb.sem("fw_eP")
    xn = pb.sbuf("fw_xn", [128, NCD * S], BF16)
    hsu = pb.sbuf("fw_h", [128, NCD * 128], BF16)
    surow = pb.sbuf("fw_su", [NCD, 128], BF16)
    sucol = pb.sbuf("fw_suc", [128, NCD], F32)
    w1 = pb.sbuf("fw_w1", [128, 128], I32)
    w1b = pb.sbuf("fw_w1b", [128, 128], I32)
    tmp1 = pb.sbuf("fw_t1", [128, 128], I32)
    tmp2 = pb.sbuf("fw_t2", [128, 128], I32)
    h128 = pb.sbuf("fw_h128", [128, 128], BF16)
    ya = pb.sbuf("fw_ya", [128, NCD * 512], BF16)
    yb = pb.sbuf("fw_yb", [128, NCD * 512], BF16)
    yst = pb.sbuf("fw_yst", [128, NCD * 512], BF16)
    pst = pb.psum("fw_pt", [128, 1024], BF16)
    psb = [pb.psum(f"fw_ps{i}", [128, 512]) for i in range(2)]
    qS0, qG0, qC0, eT0, eV0, eVq0 = qS[1], qG[1], qC[1], eT[1], eV[1], eVq[1]

    with nc.Block(f"fwht1_{pb._pctr}") as block:
        @block.gpsimd
        def _(g):
            g.iota(w1[:, :], [[1, 128]], channel_multiplier=0)
            g.iota(w1b[:, :], [[1, 128]], channel_multiplier=128)
            g.drain().then_inc(eP[0], 1)
            g.dma_start(out=surow[:, :], in_=su_dram.rearrange("(r p) -> r p", p=128)
                        ).then_inc(qG[0], 16)
            for c in range(NCD):
                g.dma_start(out=xn[:, c * S:(c + 1) * S], in_=xn_fm_dram[c, :, :]
                            ).then_inc(qG[0], 16)

        @block.tensor
        def _(te):
            te.wait_ge(qG[0], qG0 + 16)
            te.transpose(pst[:, 0:NCD], surow[0:NCD, :], pb.ident[0:NCD, 0:NCD]).then_inc(eT[0], 1)
            te.wait_ge(eV[0], eV0 + 2)
            te.wait_ge(qG[0], qG0 + 16 * (1 + NCD))
            for st in range(4):
                for c in range(NCD):
                    k = st * NCD + c
                    if k >= 2:
                        te.wait_ge(eVq[0], eVq0 + k - 1)
                    te.matmul(psb[k % 2][:, :], hsu[:, c * 128:(c + 1) * 128],
                              xn[:, c * S + st * 512: c * S + (st + 1) * 512],
                              start=True, stop=True).then_inc(eT[0], 1)

        @block.vector
        def _(v):
            v.wait_ge(eT[0], eT0 + 1)
            v.tensor_copy(sucol[:, :], pst[:, 0:NCD])
            v.drain().then_inc(eV[0], 1)
            v.wait_ge(eP[0], eP[1] + 1)
            _emit_h128(pb, v, h128, w1, w1b, tmp1, tmp2)
            for c in range(NCD):
                v.tensor_scalar(hsu[:, c * 128:(c + 1) * 128], h128[:, :],
                                sucol[:, c:c + 1], None, ALU.mult)
            v.drain().then_inc(eV[0], 1)
            for st in range(4):
                for c in range(NCD):
                    k = st * NCD + c
                    v.wait_ge(eT[0], eT0 + 1 + k + 1)
                    v.tensor_copy(ya[:, c * 512:(c + 1) * 512],
                                  psb[k % 2][:, :]).then_inc(eVq[0], 1)
                src, dst = ya, yb
                for stage in range(4):
                    hstep = 1 << stage
                    for pair in range(8):
                        blk = (pair // hstep) * 2 * hstep
                        off = pair % hstep
                        ca, cb = blk + off, blk + off + hstep
                        v.tensor_tensor(dst[:, ca * 512:(ca + 1) * 512],
                                        src[:, ca * 512:(ca + 1) * 512],
                                        src[:, cb * 512:(cb + 1) * 512], ALU.add)
                        v.tensor_tensor(dst[:, cb * 512:(cb + 1) * 512],
                                        src[:, ca * 512:(ca + 1) * 512],
                                        src[:, cb * 512:(cb + 1) * 512], ALU.subtract)
                    src, dst = dst, src
                if st >= 1:
                    v.wait_ge(qC[0], qC0 + 16 * NCD * st)
                v.tensor_copy(yst[:, :], src[:, :])
                v.drain().then_inc(eV[0], 1)

        @block.scalar
        def _(sc):
            for st in range(4):
                sc.wait_ge(eV[0], eV0 + 2 + (st + 1))
                for c in range(NCD):
                    sc.dma_start(out=xr_fm_dram[c, :, st * 512:(st + 1) * 512],
                                 in_=yst[:, c * 512:(c + 1) * 512]).then_inc(qC[0], 16)
            sc.wait_ge(qC[0], qC0 + 16 * NCD * 4)

    qG[1] += 16 * (1 + NCD)
    eP[1] += 1
    eT[1] += 1 + 64
    eV[1] += 2 + 4
    eVq[1] += 64
    qC[1] += 16 * NCD * 4
    pb.end_phase()


def emit_ffn1a(pb, xr_fm_dram, wup_bf, wgate_bf, wsc_dram, sudp_dram, y_dram):
    nc = pb.nc
    pb.begin_phase()
    qS = pb.sem("qS")
    qG = pb.sem("qG")
    qC = pb.sem("qC")
    eT = pb.sem("f1_eT")
    eTv = pb.sem("f1_eTv")
    eSg = pb.sem("f1_eSg")
    eV = pb.sem("f1_eV")
    eVn = pb.sem("f1_eVn")
    eVc = pb.sem("f1_eVc")
    eWf = pb.sem("f1_eWf")
    NCH = INTER1 // 128
    xr = pb.sbuf("f1_xr", [128, NCD * S], BF16)
    wg = pb.sbuf("f1_wg", [128, 2 * NCD * 512], BF16, td=True)
    wu = pb.sbuf("f1_wu", [128, 2 * NCD * 512], BF16, td=True)
    wscrow = pb.sbuf("f1_wr", [128, 128], BF16)
    sdrow = pb.sbuf("f1_sr", [64, 128], BF16)
    ucol = pb.sbuf("f1_uc", [128, NCH], F32)
    gcol = pb.sbuf("f1_gc", [128, NCH], F32)
    gsb = pb.sbuf("f1_g", [128, 512], F32)
    usb = pb.sbuf("f1_u", [128, 512], F32)
    ibuf = pb.sbuf("f1_i", [128, 2 * S], BF16)
    pst = pb.psum("f1_pt", [128, 1024], BF16)
    ps = [pb.psum(f"f1_ps{i}", [128, 512]) for i in range(4)]
    qS0, qG0, qC0 = qS[1], qG[1], qC[1]
    eT0, eTv0, eSg0, eV0, eVn0, eVc0 = eT[1], eTv[1], eSg[1], eV[1], eVn[1], eVc[1]
    eWf0 = eWf[1]
    NG = NCH // 4

    with nc.Block(f"ffn1a_{pb._pctr}") as block:
        @block.gpsimd
        def _(g):
            g.dma_start(out=wscrow[:, :], in_=wsc_dram.rearrange("(r p) -> r p", p=128)
                        ).then_inc(qG[0], 16)
            g.dma_start(out=sdrow[:, :], in_=sudp_dram.rearrange("(r p) -> r p", p=128)
                        ).then_inc(qG[0], 16)
            for c in range(NCD):
                g.dma_start(out=xr[:, c * S:(c + 1) * S], in_=xr_fm_dram[c, :, :]
                            ).then_inc(qG[0], 16)

        @block.tensor
        def _(te):
            te.wait_ge(qG[0], qG0 + 32)
            te.transpose(pst[:, 0:128], wscrow[:, :], pb.ident[:, :]).then_inc(eTv[0], 1)
            te.wait_ge(eVc[0], eVc0 + 1)
            te.transpose(pst[:, 128:192], sdrow[0:64, :], pb.ident[0:64, 0:64]).then_inc(eTv[0], 1)
            te.wait_ge(qG[0], qG0 + 16 * (2 + NCD))
            te.wait_ge(eVc[0], eVc0 + 2)
            for g in range(NG):
                b = g % 2
                te.wait_ge(eWf[0], eWf0 + g + 1)
                for ncg in range(4):
                    for st in range(4):
                        idx = (g * 4 + ncg) * 4 + st
                        if idx >= 2:
                            te.wait_ge(eV[0], eV0 + idx - 1)
                        bg = (st % 2) * 2
                        for d in range(NCD):
                            te.matmul(ps[bg][:, :],
                                      wg[:, (b * NCD + d) * 512 + ncg * 128:
                                         (b * NCD + d) * 512 + (ncg + 1) * 128],
                                      xr[:, d * S + st * 512: d * S + (st + 1) * 512],
                                      start=(d == 0), stop=(d == NCD - 1))
                        for d in range(NCD):
                            ins = te.matmul(ps[bg + 1][:, :],
                                            wu[:, (b * NCD + d) * 512 + ncg * 128:
                                               (b * NCD + d) * 512 + (ncg + 1) * 128],
                                            xr[:, d * S + st * 512: d * S + (st + 1) * 512],
                                            start=(d == 0), stop=(d == NCD - 1))
                        ins.then_inc(eT[0], 1)

        @block.sync
        def _(s):
            for g in range(NG):
                b = g % 2
                if g >= 2:
                    s.wait_ge(eT[0], eT0 + 16 * (g - 1))
                for d in range(NCD):
                    s.dma_start(out=wg[:, (b * NCD + d) * 512:(b * NCD + d + 1) * 512],
                                in_=wgate_bf[g * 512:(g + 1) * 512, d * 128:(d + 1) * 128],
                                transpose=True).then_inc(qS[0], 16)
                    s.dma_start(out=wu[:, (b * NCD + d) * 512:(b * NCD + d + 1) * 512],
                                in_=wup_bf[g * 512:(g + 1) * 512, d * 128:(d + 1) * 128],
                                transpose=True).then_inc(qS[0], 16)
                s.wait_ge(qS[0], qS0 + 16 * 32 * (g + 1))
                s.sem_inc(eWf[0], 1)

        @block.vector
        def _(v):
            isq = float(1.0 / np.sqrt(2048.0))
            v.wait_ge(eTv[0], eTv0 + 1)
            v.tensor_scalar(ucol[:, :], pst[:, 0:64], float(isq), None, ALU.mult)
            v.tensor_scalar(gcol[:, :], pst[:, 64:128], float(isq), None, ALU.mult)
            v.drain().then_inc(eVc[0], 1)
            v.wait_ge(eTv[0], eTv0 + 2)
            v.tensor_tensor(ucol[:, :], ucol[:, :], pst[:, 128:192], ALU.mult)
            v.drain().then_inc(eVc[0], 1)
            for nc_i in range(NCH):
                nb = nc_i % 2
                for st in range(4):
                    idx = nc_i * 4 + st
                    v.wait_ge(eSg[0], eSg0 + idx + 1)
                    if nc_i >= 2 and st == 0:
                        v.wait_ge(qC[0], qC0 + 16 * (nc_i - 1))
                    bg = (st % 2) * 2
                    v.tensor_scalar(usb[:, :], ps[bg + 1][:, :], ucol[:, nc_i:nc_i + 1],
                                    None, ALU.mult)
                    v.tensor_tensor(ibuf[:, nb * S + st * 512: nb * S + (st + 1) * 512],
                                    gsb[:, :], usb[:, :], ALU.mult).then_inc(eV[0], 1)
                v.drain().then_inc(eVn[0], 1)

        @block.scalar
        def _(sc):
            for nc_i in range(NCH):
                for st in range(4):
                    idx = nc_i * 4 + st
                    sc.wait_ge(eT[0], eT0 + idx + 1)
                    if idx >= 1:
                        sc.wait_ge(eV[0], eV0 + idx)
                    bg = (st % 2) * 2
                    sc.activation(gsb[:, :], ps[bg][:, :], AF.Silu,
                                  scale=mst_scale(gcol, nc_i))
                    sc.drain().then_inc(eSg[0], 1)
                if nc_i >= 1:
                    sc.wait_ge(eVn[0], eVn0 + nc_i)
                    pb_ = (nc_i - 1) % 2
                    sc.dma_start(out=y_dram[nc_i - 1, :, :],
                                 in_=ibuf[:, pb_ * S:(pb_ + 1) * S]).then_inc(qC[0], 16)
            sc.wait_ge(eVn[0], eVn0 + NCH)
            sc.dma_start(out=y_dram[NCH - 1, :, :], in_=ibuf[:, S:2 * S]
                         ).then_inc(qC[0], 16)
            sc.wait_ge(qC[0], qC0 + 16 * NCH)

    qG[1] += 16 * (2 + NCD)
    qS[1] += 16 * 32 * NG
    eWf[1] += NG
    eTv[1] += 2
    eVc[1] += 2
    eT[1] += NCH * 4
    eSg[1] += NCH * 4
    eV[1] += NCH * 4
    eVn[1] += NCH
    qC[1] += 16 * NCH
    pb.end_phase()


def mst_scale(col, i):
    return col[:, i:i + 1]


def emit_ffn1b(pb, y_dram, wd_bf, g1_dram):
    nc = pb.nc
    pb.begin_phase()
    qS = pb.sem("qS")
    qG = pb.sem("qG")
    qC = pb.sem("qC")
    eTh = pb.sem("g1_eTh")
    eTk = pb.sem("g1_eTk")
    eVh = pb.sem("g1_eVh")
    eVz = pb.sem("g1_eVz")
    eS = pb.sem("g1_eS")
    eP = pb.sem("g1_eP")
    eWd1 = pb.sem("g1_eWd")
    NCH = INTER1 // 128
    w1 = pb.sbuf("g1_w1", [128, 128], I32)
    w1b = pb.sbuf("g1_w1b", [128, 128], I32)
    tmp1 = pb.sbuf("g1_t1", [128, 128], I32)
    tmp2 = pb.sbuf("g1_t2", [128, 128], I32)
    h128 = pb.sbuf("g1_h", [128, 128], BF16)
    ya = pb.sbuf("g1_ya", [128, NCH * 512], BF16)
    yb = pb.sbuf("g1_yb", [128, NCH * 512], BF16)
    RING = 8
    wd = pb.sbuf("g1_wd", [128, RING * 1024], BF16, td=True)
    stage = pb.sbuf("g1_st", [128, 2 * 4096], F32)
    psd = [pb.psum(f"g1_pd{i}", [128, 512]) for i in range(8)]
    psb = [psd[0], psd[1]]
    qS0, qG0, qC0 = qS[1], qG[1], qC[1]
    eTh0, eTk0, eVh0, eVz0, eS0 = eTh[1], eTk[1], eVh[1], eVz[1], eS[1]
    eWd10 = eWd1[1]

    with nc.Block(f"ffn1b_{pb._pctr}") as block:
        @block.gpsimd
        def _(g):
            g.iota(w1[:, :], [[1, 128]], channel_multiplier=0)
            g.iota(w1b[:, :], [[1, 128]], channel_multiplier=128)
            g.drain().then_inc(eP[0], 1)
            for sg in range(4):
                if sg >= 1:
                    g.wait_ge(eVz[0], eVz0 + sg)
                for c in range(NCH):
                    g.dma_start(out=ya[:, c * 512:(c + 1) * 512],
                                in_=y_dram[c, :, sg * 512:(sg + 1) * 512]
                                ).then_inc(qG[0], 16)

        @block.tensor
        def _(te):
            te.wait_ge(eVh[0], eVh0 + 1)
            for sg in range(4):
                te.wait_ge(qG[0], qG0 + 16 * NCH * (sg + 1))
                if sg >= 1:
                    te.wait_ge(eS[0], eS0 + 2 * sg)
                for c in range(NCH):
                    k = sg * NCH + c
                    if k >= 2:
                        te.wait_ge(eVh[0], eVh0 + 1 + k - 1)
                    te.matmul(psb[k % 2][:, :], h128[:, :],
                              ya[:, c * 512:(c + 1) * 512],
                              start=True, stop=True).then_inc(eTh[0], 1)
                te.wait_ge(eVz[0], eVz0 + sg + 1)
                for mbp in range(2):
                    gi = sg * 2 + mbp
                    if gi >= 1:
                        te.wait_ge(eS[0], eS0 + gi)
                    for nc_i in range(NCH):
                        k2 = gi * NCH + nc_i
                        te.wait_ge(eWd1[0], eWd10 + k2 + 1)
                        for mb2 in range(2):
                            for ss in range(4):
                                ins = te.matmul(psd[mb2 * 4 + ss][:, :],
                                                yb[:, nc_i * 512 + ss * 128:
                                                   nc_i * 512 + (ss + 1) * 128],
                                                wd[:, (k2 % RING) * 1024 + mb2 * 512:
                                                   (k2 % RING) * 1024 + (mb2 + 1) * 512],
                                                start=(nc_i == 0), stop=(nc_i == NCH - 1))
                        ins.then_inc(eTk[0], 1)

        @block.vector
        def _(v):
            v.wait_ge(eP[0], eP[1] + 1)
            _emit_h128(pb, v, h128, w1, w1b, tmp1, tmp2)
            v.drain().then_inc(eVh[0], 1)
            for sg in range(4):
                for c in range(NCH):
                    k = sg * NCH + c
                    v.wait_ge(eTh[0], eTh0 + k + 1)
                    v.tensor_copy(ya[:, c * 512:(c + 1) * 512],
                                  psb[k % 2][:, :]).then_inc(eVh[0], 1)
                if sg >= 1:
                    v.wait_ge(eTk[0], eTk0 + sg * 2 * NCH)
                src, dst = ya, yb
                for stage_i in range(6):
                    hstep = 1 << stage_i
                    for pair in range(32):
                        blk = (pair // hstep) * 2 * hstep
                        off = pair % hstep
                        ca, cb = blk + off, blk + off + hstep
                        v.tensor_tensor(dst[:, ca * 512:(ca + 1) * 512],
                                        src[:, ca * 512:(ca + 1) * 512],
                                        src[:, cb * 512:(cb + 1) * 512], ALU.add)
                        v.tensor_tensor(dst[:, cb * 512:(cb + 1) * 512],
                                        src[:, ca * 512:(ca + 1) * 512],
                                        src[:, cb * 512:(cb + 1) * 512], ALU.subtract)
                    src, dst = dst, src
                if src is not yb:
                    v.tensor_copy(yb[:, :], src[:, :])
                v.drain().then_inc(eVz[0], 1)

        @block.sync
        def _(s):
            for sg in range(4):
                for mbp in range(2):
                    for nc_i in range(NCH):
                        k2 = (sg * 2 + mbp) * NCH + nc_i
                        if k2 >= RING:
                            s.wait_ge(eTk[0], eTk0 + k2 - RING + 1)
                        s.dma_start(out=wd[:, (k2 % RING) * 1024:(k2 % RING + 1) * 1024],
                                    in_=wd_bf[mbp * 1024:(mbp + 1) * 1024,
                                              nc_i * 128:(nc_i + 1) * 128],
                                    transpose=True).then_inc(qS[0], 16)
                        if k2 % 4 == 3:
                            s.wait_ge(qS[0], qS0 + 16 * (k2 + 1))
                            s.sem_inc(eWd1[0], 4)

        @block.scalar
        def _(sc):
            for sg in range(4):
                for mbp in range(2):
                    gi = sg * 2 + mbp
                    sc.wait_ge(eTk[0], eTk0 + (gi + 1) * NCH)
                    if gi >= 2:
                        sc.wait_ge(qC[0], qC0 + 16 * 8 * (gi - 1))
                    bs = (gi % 2) * 4096
                    for mb2 in range(2):
                        for ss in range(4):
                            sc.activation(stage[:, bs + mb2 * 2048 + ss * 512:
                                                bs + mb2 * 2048 + (ss + 1) * 512],
                                          psd[mb2 * 4 + ss][:, :], AF.Silu)
                    sc.drain().then_inc(eS[0], 1)
                    for mb2 in range(2):
                        for ss in range(4):
                            sc.dma_start(out=g1_dram[sg * 512 + ss * 128:
                                                     sg * 512 + (ss + 1) * 128,
                                                     mbp * 1024 + mb2 * 512:
                                                     mbp * 1024 + (mb2 + 1) * 512],
                                         in_=stage[:, bs + mb2 * 2048 + ss * 512:
                                                   bs + mb2 * 2048 + (ss + 1) * 512]
                                         ).then_inc(qC[0], 16)
            sc.wait_ge(qC[0], qC0 + 16 * 64)

    qG[1] += 16 * NCH * 4
    qS[1] += 16 * 8 * NCH
    eWd1[1] += 8 * NCH
    eP[1] += 1
    eTh[1] += 4 * NCH
    eTk[1] += 8 * NCH
    eVh[1] += 1 + 4 * NCH
    eVz[1] += 4
    eS[1] += 16
    qC[1] += 16 * 64
    pb.end_phase()


def emit_l1a(pb, xn_fm_dram, wq1_bf, sq_fm_dram):
    nc = pb.nc
    pb.begin_phase()
    qS = pb.sem("qS")
    qG = pb.sem("qG")
    qC = pb.sem("qC")
    eT = pb.sem("la_eT")
    eV1 = pb.sem("la_eV1")
    eS = pb.sem("la_eS")
    eV2 = pb.sem("la_eV2")
    xn = pb.sbuf("la_xn", [128, NCD * S], BF16)
    wq = pb.sbuf("la_wq", [128, NCD * 4 * 512], BF16, td=True)
    t1 = pb.sbuf("la_t1", [128, 512], F32)
    esb = pb.sbuf("la_e", [128, 512], F32)
    ost = pb.sbuf("la_o", [128, 2 * S], BF16)
    ps = [pb.psum(f"la_ps{i}", [128, 512]) for i in range(2)]
    qS0, qG0, qC0, eT0, eV10, eS0, eV20 = qS[1], qG[1], qC[1], eT[1], eV1[1], eS[1], eV2[1]

    with nc.Block(f"l1a_{pb._pctr}") as block:
        @block.gpsimd
        def _(g):
            for c in range(NCD):
                g.dma_start(out=xn[:, c * S:(c + 1) * S], in_=xn_fm_dram[c, :, :]
                            ).then_inc(qG[0], 16)

        @block.sync
        def _(s):
            for ng in range(4):
                for d in range(NCD):
                    s.dma_start(out=wq[:, (ng * NCD + d) * 512:(ng * NCD + d + 1) * 512],
                                in_=wq1_bf[ng * 512:(ng + 1) * 512, d * 128:(d + 1) * 128],
                                transpose=True).then_inc(qS[0], 16)

        @block.tensor
        def _(te):
            te.wait_ge(qG[0], qG0 + 16 * NCD)
            te.wait_ge(qS[0], qS0 + 16 * 64)
            for n in range(NCD):
                ng, ns = n // 4, n % 4
                for st in range(4):
                    idx = n * 4 + st
                    if idx >= 2:
                        te.wait_ge(eV2[0], eV20 + idx - 1)
                    for d in range(NCD):
                        ins = te.matmul(ps[idx % 2][:, :],
                                        wq[:, (ng * NCD + d) * 512 + ns * 128:
                                           (ng * NCD + d) * 512 + (ns + 1) * 128],
                                        xn[:, d * S + st * 512: d * S + (st + 1) * 512],
                                        start=(d == 0), stop=(d == NCD - 1))
                    ins.then_inc(eT[0], 1)

        @block.vector
        def _(v):
            for n in range(NCD):
                nb = n % 2
                for st in range(4):
                    idx = n * 4 + st
                    v.wait_ge(eT[0], eT0 + idx + 1)
                    if st == 0 and n >= 2:
                        v.wait_ge(qC[0], qC0 + 16 * (n - 1))
                    v.tensor_scalar(t1[:, :], ps[idx % 2][:, :], 0.0, None, ALU.min)
                    v.drain().then_inc(eV1[0], 1)
                    v.wait_ge(eS[0], eS0 + idx + 1)
                    v.scalar_tensor_tensor(ost[:, nb * S + st * 512: nb * S + (st + 1) * 512],
                                           ps[idx % 2][:, :], 0.0, esb[:, :],
                                           ALU.max, ALU.add)
                    v.drain().then_inc(eV2[0], 1)

        @block.scalar
        def _(sc):
            for n in range(NCD):
                for st in range(4):
                    idx = n * 4 + st
                    sc.wait_ge(eV1[0], eV10 + idx + 1)
                    sc.activation(esb[:, :], t1[:, :], AF.Exp)
                    sc.drain().then_inc(eS[0], 1)
                if n >= 1:
                    sc.wait_ge(eV2[0], eV20 + n * 4)
                    sc.wait_ge(qC[0], qC0 + 16 * max(0, n - 1))
                    pb_ = (n - 1) % 2
                    sc.dma_start(out=sq_fm_dram[n - 1, :, :],
                                 in_=ost[:, pb_ * S:(pb_ + 1) * S]).then_inc(qC[0], 16)
            sc.wait_ge(eV2[0], eV20 + 64)
            sc.dma_start(out=sq_fm_dram[NCD - 1, :, :], in_=ost[:, S:2 * S]
                         ).then_inc(qC[0], 16)
            sc.wait_ge(qC[0], qC0 + 16 * NCD)

    qG[1] += 16 * NCD
    qS[1] += 16 * 64
    eT[1] += 64
    eV1[1] += 64
    eS[1] += 64
    eV2[1] += 64
    qC[1] += 16 * NCD
    pb.end_phase()


def emit_l1b(pb, sq_fm_dram, mem_dram, mn_dram, bw_dram, comb_fm_dram):
    nc = pb.nc
    pb.begin_phase()
    qS = pb.sem("qS")
    qG = pb.sem("qG")
    qC = pb.sem("qC")
    eT = pb.sem("lb_eT")
    eTt = pb.sem("lb_eTt")
    eTp = pb.sem("lb_eTp")
    eV = pb.sem("lb_eV")
    eVe = pb.sem("lb_eVe")
    eVp = pb.sem("lb_eVp")
    eSp = pb.sem("lb_eSp")
    sq = pb.sbuf("lb_sq", [128, NCD * S], BF16)
    memn = pb.sbuf("lb_m", [128, 64 * 132], BF16)
    mnrow = pb.sbuf("lb_mnr", [64, 128], BF16)
    mncol = pb.sbuf("lb_mnc", [128, 64], BF16)
    bwrow = pb.sbuf("lb_bw", [1, 64], F32)
    mnfrow = pb.sbuf("lb_mnf", [1, 8192], F32)
    prep = pb.sbuf("lb_pr", [1, 64], F32)
    prep2 = pb.sbuf("lb_pr2", [1, 16], F32)
    wrow = pb.sbuf("lb_wr", [1, 64], BF16)
    wrep = pb.sbuf("lb_wrep", [128, 64], F32)
    rcol = pb.sbuf("lb_rc", [128, 4], F32)
    acc = pb.sbuf("lb_acc", [128, 128], F32)
    accb = pb.sbuf("lb_accb", [128, 128], BF16)
    cst = pb.sbuf("lb_cst", [128, 2 * 128], BF16)
    psA = [pb.psum(f"lb_ps{i}", [128, 512]) for i in range(4)]
    pst = pb.psum("lb_pt", [128, 1024], BF16)
    pstT = [pb.psum(f"lb_pt2{i}", [128, 1024], BF16) for i in range(2)]
    psw = pb.psum("lb_pw", [128, 64])
    qS0, qG0, qC0 = qS[1], qG[1], qC[1]
    eT0, eTt0, eTp0, eV0, eVe0, eVp0, eSp0 = (
        eT[1], eTt[1], eTp[1], eV[1], eVe[1], eVp[1], eSp[1])

    with nc.Block(f"l1b_{pb._pctr}") as block:
        @block.gpsimd
        def _(g):
            g.dma_start(out=mnrow[:, :], in_=mn_dram.rearrange("n h d -> (n h) d")
                        ).then_inc(qG[0], 16)
            for n in range(4):
                for h in range(H):
                    g.dma_start(out=memn[:, (n * 16 + h) * 132:(n * 16 + h) * 132 + 128],
                                in_=mem_dram[n, h, :, :]).then_inc(qG[0], 16)
            for c in range(NCD):
                g.dma_start(out=sq[:, c * S:(c + 1) * S], in_=sq_fm_dram[c, :, :]
                            ).then_inc(qG[0], 16)

        @block.sync
        def _(s):
            s.dma_start(out=bwrow[0:1, :], in_=bw_dram.rearrange("h n -> (h n)")[None, :]
                        ).then_inc(qS[0], 16)
            s.dma_start(out=mnfrow[0:1, :], in_=mn_dram.rearrange("n h d -> (n h d)")[None, :]
                        ).then_inc(qS[0], 16)

        @block.vector
        def _(v):
            v.wait_ge(qS[0], qS0 + 32)
            bw3 = bwrow[0:1, :].rearrange("o (h n) -> o h n", n=4)
            v.tensor_reduce(prep2[0:1, 0:16].unsqueeze(-1),
                            bw3, axis=AXX, op=ALU.max)
            v.drain()
            mx3 = prep2[0:1, 0:16].unsqueeze(-1).broadcast_to((1, 16, 4))
            v.tensor_tensor(prep[0:1, :].rearrange("o (h n) -> o h n", n=4), bw3, mx3,
                            ALU.subtract)
            v.drain().then_inc(eVp[0], 1)
            v.wait_ge(eSp[0], eSp0 + 1)
            pr3 = bwrow[0:1, :].rearrange("o (h n) -> o h n", n=4)
            v.tensor_reduce(prep2[0:1, 0:16].unsqueeze(-1),
                            pr3, axis=AXX, op=ALU.add)
            v.drain()
            v.reciprocal(prep2[0:1, 0:16], prep2[0:1, 0:16])
            v.drain()
            rc3 = prep2[0:1, 0:16].unsqueeze(-1).broadcast_to((1, 16, 4))
            v.tensor_tensor(pr3, pr3, rc3, ALU.mult)
            v.drain()
            mn3 = mnfrow[0:1, :].rearrange("o (n r) -> o n r", n=4)
            v.tensor_reduce(prep2[0:1, 0:4].unsqueeze(-1),
                            mn3, axis=AXX, op=ALU.add)
            v.drain()
            v.tensor_scalar(prep2[0:1, 0:4], prep2[0:1, 0:4], float(EPS_L1), None,
                            ALU.is_ge)
            v.drain()
            act3 = prep2[0:1, 0:4].unsqueeze(1).broadcast_to((1, 16, 4))
            v.tensor_tensor(pr3, pr3, act3, ALU.mult)
            v.drain()
            v.tensor_copy(wrow[0:1, :], bwrow[0:1, :])
            v.drain().then_inc(eVp[0], 1)
            v.wait_ge(eTp[0], eTp0 + 2)
            v.tensor_copy(wrep[:, :], psw[:, :])
            v.tensor_copy(mncol[:, :], pst[:, 0:64])
            v.drain().then_inc(eVp[0], 1)
            for n in range(4):
                for h in range(H):
                    v.tensor_copy(memn[:, (n * 16 + h) * 132 + 128:(n * 16 + h) * 132 + 129],
                                  mncol[:, n * 16 + h:n * 16 + h + 1])
            v.drain().then_inc(eVp[0], 1)
            for h in range(H):
                for st in range(NTS):
                    k = h * NTS + st
                    v.wait_ge(eT[0], eT0 + k + 1)
                    for n in range(4):
                        v.tensor_scalar(rcol[:, n:n + 1], psA[n][:, 128:129],
                                        float(EPS_L1), None, ALU.max)
                    v.drain()
                    v.reciprocal(rcol[:, 0:1], rcol[:, 0:1])
                    v.reciprocal(rcol[:, 1:2], rcol[:, 1:2])
                    v.reciprocal(rcol[:, 2:3], rcol[:, 2:3])
                    v.reciprocal(rcol[:, 3:4], rcol[:, 3:4])
                    v.drain()
                    for n in range(4):
                        v.tensor_tensor(rcol[:, n:n + 1], rcol[:, n:n + 1],
                                        wrep[:, h * 4 + n:h * 4 + n + 1], ALU.mult)
                    v.drain()
                    for n in range(4):
                        if n == 0:
                            v.tensor_scalar(acc[:, :], psA[0][:, 0:128],
                                            rcol[:, 0:1], None, ALU.mult)
                        else:
                            v.scalar_tensor_tensor(acc[:, :], psA[n][:, 0:128],
                                                   rcol[:, n:n + 1], acc[:, :],
                                                   ALU.mult, ALU.add)
                        v.drain()
                    v.tensor_copy(accb[:, :], acc[:, :])
                    v.drain().then_inc(eV[0], 1)
                    v.wait_ge(eTt[0], eTt0 + k + 1)
                    if k >= 2:
                        v.wait_ge(qC[0], qC0 + 16 * (k - 1))
                    v.tensor_copy(cst[:, (k % 2) * 128:(k % 2 + 1) * 128],
                                  pstT[k % 2][:, 0:128]).then_inc(eVe[0], 1)

        @block.scalar
        def _(sc):
            sc.wait_ge(eVp[0], eVp0 + 1)
            sc.activation(bwrow[0:1, :], prep[0:1, :], AF.Exp)
            sc.drain().then_inc(eSp[0], 1)
            for h in range(H):
                for st in range(NTS):
                    k = h * NTS + st
                    sc.wait_ge(eVe[0], eVe0 + k + 1)
                    sc.dma_start(out=comb_fm_dram[h, :, st * 128:(st + 1) * 128],
                                 in_=cst[:, (k % 2) * 128:(k % 2 + 1) * 128]
                                 ).then_inc(qC[0], 16)
            sc.wait_ge(qC[0], qC0 + 16 * 256)

        @block.tensor
        def _(te):
            te.wait_ge(eVp[0], eVp0 + 2)
            te.matmul(psw[:, :], pb.ones1[0:1, 0:128], wrow[0:1, :],
                      start=True, stop=True).then_inc(eTp[0], 1)
            te.wait_ge(qG[0], qG0 + 16)
            te.transpose(pst[:, 0:64], mnrow[0:64, :], pb.ident[0:64, 0:64]).then_inc(eTp[0], 1)
            te.wait_ge(eVp[0], eVp0 + 4)
            te.wait_ge(qG[0], qG0 + 16 * (1 + 64 + NCD))
            for h in range(H):
                for st in range(NTS):
                    k = h * NTS + st
                    if k >= 1:
                        te.wait_ge(eV[0], eV0 + k)
                    for n in range(4):
                        ins = te.matmul(psA[n][:, 0:129],
                                        sq[:, h * S + st * 128:h * S + (st + 1) * 128],
                                        memn[:, (n * 16 + h) * 132:(n * 16 + h) * 132 + 129],
                                        start=True, stop=True)
                    ins.then_inc(eT[0], 1)
                    te.wait_ge(eV[0], eV0 + k + 1)
                    if k >= 2:
                        te.wait_ge(eVe[0], eVe0 + k - 1)
                    te.transpose(pstT[k % 2][:, 0:128],
                                 accb[:, :], pb.ident[:, :]).then_inc(eTt[0], 1)

    qG[1] += 16 * (1 + 64 + NCD)
    qS[1] += 32
    eVp[1] += 4
    eSp[1] += 1
    eTp[1] += 2
    eT[1] += 256
    eTt[1] += 256
    eV[1] += 256
    eVe[1] += 256
    qC[1] += 16 * 256
    pb.end_phase()


def emit_l1o(pb, comb_fm_dram, wo1_bf, s1_dram):
    nc = pb.nc
    pb.begin_phase()
    qS = pb.sem("qS")
    qG = pb.sem("qG")
    qC = pb.sem("qC")
    eT = pb.sem("lo_eT")
    eV = pb.sem("lo_eV")
    comb = pb.sbuf("lo_c", [128, NCD * S], BF16)
    wt = pb.sbuf("lo_w", [128, NCD * 4 * 512], BF16, td=True)
    stage = pb.sbuf("lo_st", [128, 2 * D], F32)
    ps = [pb.psum(f"lo_ps{i}", [128, 512]) for i in range(8)]
    qS0, qG0, qC0, eT0, eV0 = qS[1], qG[1], qC[1], eT[1], eV[1]

    with nc.Block(f"l1o_{pb._pctr}") as block:
        @block.gpsimd
        def _(g):
            for c in range(NCD):
                g.dma_start(out=comb[:, c * S:(c + 1) * S], in_=comb_fm_dram[c, :, :]
                            ).then_inc(qG[0], 16)

        @block.sync
        def _(s):
            for mb in range(4):
                for d in range(NCD):
                    s.dma_start(out=wt[:, (mb * NCD + d) * 512:(mb * NCD + d + 1) * 512],
                                in_=wo1_bf[mb * 512:(mb + 1) * 512, d * 128:(d + 1) * 128],
                                transpose=True).then_inc(qS[0], 16)

        @block.tensor
        def _(te):
            te.wait_ge(qG[0], qG0 + 16 * NCD)
            te.wait_ge(qS[0], qS0 + 16 * 64)
            for t in range(NTS):
                if t >= 2:
                    te.wait_ge(eV[0], eV0 + t - 1)
                bank = (t % 2) * 4
                for mb in range(4):
                    for d in range(NCD):
                        ins = te.matmul(ps[bank + mb][:, :],
                                        comb[:, d * S + t * 128: d * S + (t + 1) * 128],
                                        wt[:, (mb * NCD + d) * 512:(mb * NCD + d + 1) * 512],
                                        start=(d == 0), stop=(d == NCD - 1))
                ins.then_inc(eT[0], 1)

        @block.vector
        def _(v):
            for t in range(NTS):
                v.wait_ge(eT[0], eT0 + t + 1)
                if t >= 2:
                    v.wait_ge(qC[0], qC0 + 16 * (t - 1))
                b = (t % 2) * D
                bank = (t % 2) * 4
                for mb in range(4):
                    v.tensor_copy(stage[:, b + mb * 512: b + (mb + 1) * 512],
                                  ps[bank + mb][:, :])
                v.drain().then_inc(eV[0], 1)

        @block.scalar
        def _(sc):
            for t in range(NTS):
                sc.wait_ge(eV[0], eV0 + t + 1)
                sc.dma_start(out=s1_dram[t * 128:(t + 1) * 128, :],
                             in_=stage[:, (t % 2) * D:(t % 2 + 1) * D]).then_inc(qC[0], 16)
            sc.wait_ge(qC[0], qC0 + 16 * NTS)

    qG[1] += 16 * NCD
    qS[1] += 16 * 64
    eT[1] += NTS
    eV[1] += NTS
    qC[1] += 16 * NTS
    pb.end_phase()


def emit_res(pb, x_dram, s_dram, g_dram, out_dram):
    nc = pb.nc
    pb.begin_phase()
    qS = pb.sem("qS")
    qG = pb.sem("qG")
    eV = pb.sem("rs_eV")
    xt = pb.sbuf("rs_x", [128, 2 * D], F32)
    st_t = pb.sbuf("rs_s", [128, 2 * D], F32)
    gt = pb.sbuf("rs_g", [128, 2 * D], F32)
    qS0, qG0, eV0 = qS[1], qG[1], eV[1]

    with nc.Block(f"res_{pb._pctr}") as block:
        @block.sync
        def _(s):
            for t in range(NTS):
                b = t % 2
                if t >= 2:
                    s.wait_ge(qG[0], qG0 + 16 * (t - 1))
                s.dma_start(out=xt[:, b * D:(b + 1) * D],
                            in_=x_dram[t * 128:(t + 1) * 128, :]).then_inc(qS[0], 16)
                s.dma_start(out=st_t[:, b * D:(b + 1) * D],
                            in_=s_dram[t * 128:(t + 1) * 128, :]).then_inc(qS[0], 16)
                s.dma_start(out=gt[:, b * D:(b + 1) * D],
                            in_=g_dram[t * 128:(t + 1) * 128, :]).then_inc(qS[0], 16)

        @block.vector
        def _(v):
            for t in range(NTS):
                b = t % 2
                sl = slice(b * D, (b + 1) * D)
                v.wait_ge(qS[0], qS0 + 48 * (t + 1))
                v.tensor_tensor(st_t[:, sl], st_t[:, sl], gt[:, sl], ALU.mult)
                v.tensor_tensor(xt[:, sl], xt[:, sl], st_t[:, sl], ALU.add)
                v.drain().then_inc(eV[0], 1)

        @block.gpsimd
        def _(g):
            for t in range(NTS):
                b = t % 2
                g.wait_ge(eV[0], eV0 + t + 1)
                g.dma_start(out=out_dram[t * 128:(t + 1) * 128, :],
                            in_=xt[:, b * D:(b + 1) * D]).then_inc(qG[0], 16)
            g.wait_ge(qG[0], qG0 + 16 * NTS)

    qS[1] += 48 * NTS
    eV[1] += NTS
    qG[1] += 16 * NTS
    pb.end_phase()


def build_program():
    pb = PB()
    nc = pb.nc
    ein = lambda n, sh, dt=F32: nc.dram_tensor(n, sh, dt, kind="ExternalInput")
    x0 = ein("x0", [S, D])
    wq0 = ein("L0_Wq", [D, D]); bq0 = ein("L0_bq", [D])
    wk0 = ein("L0_Wk", [D, D]); bk0 = ein("L0_bk", [D])
    wv0 = ein("L0_Wv", [D, D]); bv0 = ein("L0_bv", [D])
    wo0 = ein("L0_Wo", [D, D]); bo0 = ein("L0_bo", [D])
    gWp = ein("gWp", [MLP0, 1024], I32); gSc = ein("gSc", [MLP0, 64])
    uWp = ein("uWp", [MLP0, 1024], I32); uSc = ein("uSc", [MLP0, 64])
    dWp = ein("dWp", [D, 8192], I32); dSc = ein("dSc", [D, 512])
    wq1 = ein("L1_Wq", [D, D]); wo1 = ein("L1_Wo", [D, D])
    bw1 = ein("L1_bw", [16, 4])
    mem1 = ein("L1_mem", [4, 16, 128, 128]); mn1 = ein("L1_mn", [4, 16, 128])
    su_ug = ein("su_ug", [D]); su_dp = ein("su_dp", [INTER1])
    wsc_ug = ein("wsc_ug", [2 * INTER1]); wsc_dp = ein("wsc_dp", [D])
    w_up = ein("w_up", [INTER1, D]); w_gate = ein("w_gate", [INTER1, D])
    w_down = ein("w_down", [D, INTER1])
    out = nc.dram_tensor("out", [S, D], F32, kind="ExternalOutput")
    dt = lambda n, sh, d=BF16: nc.dram_tensor(n, sh, d)
    xn0 = dt("xn0", [NCD, 128, S])
    q_fm = dt("q_fm", [H, 128, S]); k_fm = dt("k_fm", [H, 128, S])
    v_tm = dt("v_tm", [NTS, 128, D]); ctx_fm = dt("ctx_fm", [H, 128, S])
    s0 = dt("s0", [S, D], F32)
    wg_bf = dt("wg_bf", [MLP0, D]); wu_bf = dt("wu_bf", [MLP0, D])
    wd0_bf = dt("wd0_bf", [D, MLP0])
    inter = dt("inter", [128, 128, S])
    g0 = dt("g0", [S, D], F32)
    x1 = dt("x1", [S, D], F32)
    xn1 = dt("xn1", [NCD, 128, S])
    wq1_bf = dt("wq1_bf", [D, D]); wo1_bf = dt("wo1_bf", [D, D])
    wup_bf = dt("wup_bf", [INTER1, D]); wg1_bf = dt("wg1_bf", [INTER1, D])
    wd1_bf = dt("wd1_bf", [D, INTER1])
    sq_fm = dt("sq_fm", [NCD, 128, S]); comb_fm = dt("comb_fm", [NCD, 128, S])
    s1 = dt("s1", [S, D], F32)
    xr_fm = dt("xr_fm", [NCD, 128, S])
    y1 = dt("y1", [64, 128, S])
    g1 = dt("g1", [S, D], F32)

    pb.consts()
    emit_ln(pb, x0, xn0)
    emit_qkv(pb, xn0, wq0, bq0, wk0, bk0, wv0, bv0, q_fm, k_fm, v_tm)
    emit_att0(pb, q_fm, k_fm, v_tm, ctx_fm)
    emit_wo0(pb, ctx_fm, wo0, bo0, s0)
    emit_ffn0_fused(pb, xn0, gWp, gSc, uWp, uSc, dWp, dSc,
                    wg_bf, wu_bf, wd0_bf, inter)
    emit_ffn0b(pb, inter, wd0_bf, g0, nchunks=128, act="silu")
    emit_ln(pb, x0, xn1, s_in=s0, g_in=g0, x_out=x1)
    emit_casts(pb, [(wq1, wq1_bf, D, D), (wo1, wo1_bf, D, D),
                    (w_up, wup_bf, INTER1, D), (w_gate, wg1_bf, INTER1, D)])
    emit_scaled_cast(pb, w_down, wsc_dp, wd1_bf, D, INTER1, 1.0 / np.sqrt(float(INTER1)))
    emit_l1a(pb, xn1, wq1_bf, sq_fm)
    emit_l1b(pb, sq_fm, mem1, mn1, bw1, comb_fm)
    emit_l1o(pb, comb_fm, wo1_bf, s1)
    emit_fwht1(pb, xn1, su_ug, xr_fm)
    emit_ffn1a(pb, xr_fm, wup_bf, wg1_bf, wsc_ug, su_dp, y1)
    emit_ffn1b(pb, y1, wd1_bf, g1)
    emit_res(pb, x1, s1, g1, out)
    return pb


_PROG_CACHE = {}


def kernel(**inputs):
    if "prog" not in _PROG_CACHE:
        _PROG_CACHE["prog"] = build_program()
    pb = _PROG_CACHE["prog"]
    x = np.ascontiguousarray(inputs["L0_attn_input_sequences"])
    _ = inputs["L1_attn_Wk"], inputs["L1_attn_Wv"]
    common = {
        "L0_Wq": inputs["L0_attn_Wq"], "L0_bq": inputs["L0_attn_bq"],
        "L0_Wk": inputs["L0_attn_Wk"], "L0_bk": inputs["L0_attn_bk"],
        "L0_Wv": inputs["L0_attn_Wv"], "L0_bv": inputs["L0_attn_bv"],
        "L0_Wo": inputs["L0_attn_Wo"], "L0_bo": inputs["L0_attn_bo"],
        "gWp": inputs["L0_ffn_gate_W_packed"], "gSc": inputs["L0_ffn_gate_scales"],
        "uWp": inputs["L0_ffn_up_W_packed"], "uSc": inputs["L0_ffn_up_scales"],
        "dWp": inputs["L0_ffn_down_W_packed"], "dSc": inputs["L0_ffn_down_scales"],
        "L1_Wq": inputs["L1_attn_Wq"], "L1_Wo": inputs["L1_attn_Wo"],
        "L1_bw": inputs["L1_attn_bank_weights"],
        "L1_mem": inputs["L1_attn_memories"], "L1_mn": inputs["L1_attn_memory_norms"],
        "su_ug": inputs["L1_ffn_SU_ug"], "su_dp": inputs["L1_ffn_SU_dp"],
        "wsc_ug": inputs["L1_ffn_Wscale_ug"], "wsc_dp": inputs["L1_ffn_Wscale_dp"],
        "w_up": inputs["L1_ffn_W_up"], "w_gate": inputs["L1_ffn_W_gate"],
        "w_down": inputs["L1_ffn_W_down"],
    }
    common = {k: np.ascontiguousarray(v) for k, v in common.items()}
    in_maps = [dict(common, x0=x[i]) for i in range(8)]
    res = run_bass_kernel_spmd(pb.nc, in_maps, core_ids=list(range(8)))
    return np.stack([np.asarray(res.results[i]["out"]) for i in range(8)], axis=0)


def emit_ffn0_fused(pb, xn_fm_dram, gWp, gSc, uWp, uSc, dWp, dSc,
                    wg_bf, wu_bf, wd0_bf, inter_dram):
    nc = pb.nc
    pb.begin_phase()
    KP, KO, NB, R = 512, 1024, 32, 5
    qGp = pb.sem("fz_qGp")
    qGc = pb.sem("fz_qGc")
    qGs = pb.sem("fz_qGs")
    eLd = pb.sem("fz_eLd")
    eSt = pb.sem("fz_eSt")
    eWg = pb.sem("fz_eWg")
    qS = pb.sem("qS")
    qC = pb.sem("qC")
    dV0 = pb.sem("fz_dV0")
    dV1 = pb.sem("fz_dV1")
    dV2 = pb.sem("fz_dV2")
    dS1 = pb.sem("fz_dS1")
    dS2 = pb.sem("fz_dS2")
    dP = pb.sem("fz_dP")
    fT = pb.sem("fz_fT")
    fSg = pb.sem("fz_fSg")
    fV = pb.sem("fz_fV")
    fVn = pb.sem("fz_fVn")
    wt = pb.sbuf("fz_w", [128, R * KP], I32)
    sct = pb.sbuf("fz_sc", [128, R * NB], F32)
    nibD = pb.sbuf("fz_nD", [128, R * KO], I32)
    Df = nibD[:, :].bitcast(F32)
    A = pb.sbuf("fz_A", [128, R * KO], F32)
    B = pb.sbuf("fz_B", [128, R * KO], F32)
    C = pb.sbuf("fz_C", [128, R * KO], F32)
    obv = None
    ob = B[:, :].bitcast(BF16)
    xn = pb.sbuf("fz_xn", [128, NCD * S], BF16)
    wg = pb.sbuf("fz_wg", [128, NCD * 512], BF16, td=True)
    wu = pb.sbuf("fz_wu", [128, NCD * 512], BF16, td=True)
    gsb = pb.sbuf("fz_g", [128, 512], F32)
    ibuf = pb.sbuf("fz_i", [128, 2 * S], BF16)
    ps = [pb.psum(f"fz_ps{i}", [128, 512]) for i in range(4)]

    TILES = []
    for g in range(32):
        for r in range(4 * g, 4 * g + 4):
            for kc in range(2):
                TILES.append((gWp, gSc, wg_bf, r, kc))
        for r in range(4 * g, 4 * g + 4):
            for kc in range(2):
                TILES.append((uWp, uSc, wu_bf, r, kc))
    for r in range(16):
        for kc in range(16):
            TILES.append((dWp, dSc, wd0_bf, r, kc))
    T = len(TILES)
    NCHUNK = T // 16

    qGp0, qGc0, qGs0, qS0, qC0 = qGp[1], qGc[1], qGs[1], qS[1], qC[1]
    eLd0, eSt0, eWg0 = eLd[1], eSt[1], eWg[1]
    dV00, dV10, dV20, dS10, dS20, dP0 = dV0[1], dV1[1], dV2[1], dS1[1], dS2[1], dP[1]
    fT0, fSg0, fV0, fVn0 = fT[1], fSg[1], fV[1], fVn[1]

    def dq_src(t):
        pk, sc_, dst, r, kc = TILES[t]
        return (pk[r * 128:(r + 1) * 128, kc * KP:(kc + 1) * KP],
                sc_[r * 128:(r + 1) * 128, kc * NB:(kc + 1) * NB],
                dst[r * 128:(r + 1) * 128, kc * KO:(kc + 1) * KO])

    def sl(t, n):
        s_ = t % R
        return slice(s_ * n, (s_ + 1) * n)

    LEAD = 24
    def events():
        out = []
        tnext = 0
        for j in range(512):
            target = min(T, LEAD + ((j + 1) * (T - LEAD)) // 512)
            while tnext < target:
                out.append(("dq", tnext))
                tnext += 1
            out.append(("ev", j))
        while tnext < T:
            out.append(("dq", tnext))
            tnext += 1
        for t in range(T, T + 6):
            out.append(("dq", t))
        return out

    with nc.Block(f"ffz_{pb._pctr}") as block:
        @block.gpsimd
        def _(g_):
            for t in range(T + 6):
                if t < T:
                    pk, sc_, _d = dq_src(t)
                    if t >= R:
                        g_.wait_ge(dV0[0], dV00 + t - R + 1)
                        g_.wait_ge(dV2[0], dV20 + t - R + 1)
                    g_.dma_start(out=wt[:, sl(t, KP)], in_=pk).then_inc(qGp[0], 16)
                    g_.dma_start(out=sct[:, sl(t, NB)], in_=sc_).then_inc(qGc[0], 16)
                    if t % 2 == 1:
                        g_.wait_ge(qGp[0], qGp0 + 16 * (t + 1))
                        g_.wait_ge(qGc[0], qGc0 + 16 * (t + 1))
                        g_.sem_inc(eLd[0], 2)
                if t >= R and t - R < T:
                    t3 = t - R
                    g_.wait_ge(dV2[0], dV20 + t3 + 1)
                    _pk, _sc, dst = dq_src(t3)
                    s3 = (t3 % R) * KO * 2
                    g_.dma_start(out=dst, in_=ob[:, s3:s3 + KO]).then_inc(qGs[0], 16)
                    if t3 % 16 == 15:
                        g_.wait_ge(qGs[0], qGs0 + 16 * (t3 + 1))
                        g_.sem_inc(eSt[0], 1)
            g_.wait_ge(qGs[0], qGs0 + 16 * T)

        @block.vector
        def _(v):
            for kind, t in events():
                if kind == "dq":
                    if t >= 4 and t - 4 < T:
                        t2 = t - 4
                        v.wait_ge(dS2[0], dS20 + t2 + 1)
                        v.tensor_tensor(A[:, sl(t2, KO)], A[:, sl(t2, KO)],
                                        Df[:, sl(t2, KO)], ALU.add)
                        v.scalar_tensor_tensor(C[:, sl(t2, KO)], C[:, sl(t2, KO)], 0.5,
                                               A[:, sl(t2, KO)], ALU.mult, ALU.add)
                        v.scalar_tensor_tensor(C[:, sl(t2, KO)], C[:, sl(t2, KO)], -2.0,
                                               B[:, sl(t2, KO)], ALU.add, ALU.mult)
                        s2o = (t2 % R) * KO * 2
                        cv = C[:, sl(t2, KO)].rearrange("p (bk j) -> p bk j", j=32)
                        ov = ob[:, s2o:s2o + KO].rearrange("p (bk j) -> p bk j", j=32)
                        scb = sct[:, sl(t2, NB)].unsqueeze(-1).broadcast_to((128, NB, 32))
                        v.tensor_tensor(ov, cv, scb, ALU.mult)
                        v.drain().then_inc(dV2[0], 1)
                    if t >= 2 and t - 2 < T:
                        t1 = t - 2
                        v.wait_ge(dS1[0], dS10 + t1 + 1)
                        v.scalar_tensor_tensor(C[:, sl(t1, KO)], B[:, sl(t1, KO)], 4.0,
                                               A[:, sl(t1, KO)], ALU.mult, ALU.add)
                        v.drain().then_inc(dV1[0], 1)
                    if t < T:
                        v.wait_ge(eLd[0], eLd0 + t + 1)
                        if t >= R:
                            v.wait_ge(dS1[0], dS10 + t - R + 1)
                            v.wait_ge(dS2[0], dS20 + t - R + 1)
                        nv = nibD[:, sl(t, KO)].rearrange("p (k two) -> p k two", two=2)
                        wv = wt[:, sl(t, KP)].unsqueeze(-1)
                        v.tensor_scalar(nv[:, :, 0:1], wv, 15, None, ALU.bitwise_and)
                        v.tensor_scalar(nv[:, :, 1:2], wv, 4, 15,
                                        ALU.logical_shift_right, ALU.bitwise_and)
                        v.drain().then_inc(dV0[0], 1)
                else:
                    j = t
                    nci, st = j // 4, j % 4
                    v.wait_ge(fSg[0], fSg0 + j + 1)
                    if st == 0 and nci >= 2:
                        v.wait_ge(qC[0], qC0 + 16 * (nci - 1))
                    bg = st % 2
                    ins = v.tensor_tensor(
                        ibuf[:, (nci % 2) * S + st * 512:(nci % 2) * S + (st + 1) * 512],
                        gsb[:, :], ps[bg * 2 + 1][:, :], ALU.mult)
                    ins.then_inc(fV[0], 1)
                    if st == 3:
                        v.drain().then_inc(fVn[0], 1)

        @block.scalar
        def _(sc_):
            for kind, t in events():
                if kind == "dq":
                    if t >= 1 and t - 1 < T:
                        t0 = t - 1
                        sc_.wait_ge(dV0[0], dV00 + t0 + 1)
                        if t0 >= R:
                            sc_.wait_ge(dV2[0], dV20 + t0 - R + 1)
                            sc_.wait_ge(qGs[0], qGs0 + 16 * (t0 - R + 1))
                        sc_.activation(A[:, sl(t0, KO)], nibD[:, sl(t0, KO)], AF.Copy,
                                       bias=0.0)
                        sc_.activation(B[:, sl(t0, KO)], A[:, sl(t0, KO)], AF.Sign,
                                       bias=f32(7.5), scale=f32(-1.0))
                        sc_.drain().then_inc(dS1[0], 1)
                    if t >= 3 and t - 3 < T:
                        t1 = t - 3
                        sc_.wait_ge(dV1[0], dV10 + t1 + 1)
                        sc_.activation(A[:, sl(t1, KO)], C[:, sl(t1, KO)], AF.Relu,
                                       bias=f32(-4.0), scale=f32(0.5))
                        sc_.activation(Df[:, sl(t1, KO)], C[:, sl(t1, KO)], AF.Relu,
                                       bias=f32(-10.0))
                        sc_.drain().then_inc(dS2[0], 1)
                else:
                    j = t
                    nci, st = j // 4, j % 4
                    sc_.wait_ge(fT[0], fT0 + j + 1)
                    if j >= 1:
                        sc_.wait_ge(fV[0], fV0 + j)
                    bg = st % 2
                    sc_.activation(gsb[:, :], ps[bg * 2][:, :], AF.Gelu_apprx_tanh)
                    sc_.drain().then_inc(fSg[0], 1)
                    if st == 3 and nci >= 1:
                        sc_.wait_ge(fVn[0], fVn0 + nci)
                        sc_.dma_start(out=inter_dram[nci - 1, :, :],
                                      in_=ibuf[:, ((nci - 1) % 2) * S:
                                               ((nci - 1) % 2 + 1) * S]
                                      ).then_inc(qC[0], 16)
            sc_.wait_ge(fVn[0], fVn0 + 128)
            sc_.dma_start(out=inter_dram[127, :, :], in_=ibuf[:, S:2 * S]
                          ).then_inc(qC[0], 16)
            sc_.wait_ge(qC[0], qC0 + 16 * 128)

        @block.sync
        def _(s):
            for cc in range(NCD):
                s.dma_start(out=xn[:, cc * S:(cc + 1) * S], in_=xn_fm_dram[cc, :, :]
                            ).then_inc(qS[0], 16)
            for g in range(32):
                s.wait_ge(eSt[0], eSt0 + g + 1)
                if g >= 1:
                    s.wait_ge(fT[0], fT0 + 16 * g)
                for d in range(NCD):
                    s.dma_start(out=wg[:, d * 512:(d + 1) * 512],
                                in_=wg_bf[g * 512:(g + 1) * 512, d * 128:(d + 1) * 128],
                                transpose=True).then_inc(qS[0], 16)
                for d in range(NCD):
                    s.dma_start(out=wu[:, d * 512:(d + 1) * 512],
                                in_=wu_bf[g * 512:(g + 1) * 512, d * 128:(d + 1) * 128],
                                transpose=True).then_inc(qS[0], 16)
                s.wait_ge(qS[0], qS0 + 16 * (NCD + 32 * (g + 1)))
                s.sem_inc(eWg[0], 1)

        @block.tensor
        def _(te):
            te.wait_ge(qS[0], qS0 + 16 * NCD)
            for g in range(32):
                te.wait_ge(eWg[0], eWg0 + g + 1)
                for ncg in range(4):
                    for st in range(4):
                        idx = (g * 4 + ncg) * 4 + st
                        if idx >= 2:
                            te.wait_ge(fV[0], fV0 + idx - 1)
                        bg = st % 2
                        for d in range(NCD):
                            te.matmul(ps[bg * 2][:, :],
                                      wg[:, d * 512 + ncg * 128: d * 512 + (ncg + 1) * 128],
                                      xn[:, d * S + st * 512: d * S + (st + 1) * 512],
                                      start=(d == 0), stop=(d == NCD - 1))
                        for d in range(NCD):
                            ins = te.matmul(ps[bg * 2 + 1][:, :],
                                            wu[:, d * 512 + ncg * 128: d * 512 + (ncg + 1) * 128],
                                            xn[:, d * S + st * 512: d * S + (st + 1) * 512],
                                            start=(d == 0), stop=(d == NCD - 1))
                        ins.then_inc(fT[0], 1)

    qGp[1] += 16 * T
    qGc[1] += 16 * T
    qGs[1] += 16 * T
    qS[1] += 16 * (NCD + 32 * 32)
    qC[1] += 16 * 128
    eLd[1] += T
    eSt[1] += 48
    eWg[1] += 32
    dV0[1] += T
    dV1[1] += T
    dV2[1] += T
    dS1[1] += T
    dS2[1] += T
    fT[1] += 512
    fSg[1] += 512
    fV[1] += 512
    fVn[1] += 128
    pb.end_phase()


# version 14
# speedup vs baseline: 1.0171x; 1.0171x over previous
import numpy as np
import ml_dtypes
from contextlib import ExitStack
import concourse.bass as bass
import concourse.mybir as mybir
from concourse.bass_utils import run_bass_kernel_spmd

F32 = mybir.dt.float32
BF16 = mybir.dt.bfloat16
I32 = mybir.dt.int32
AF = mybir.ActivationFunctionType
ALU = mybir.AluOpType
AXX = mybir.AxisListType.X
f32 = np.float32

D = 2048
S = 2048
H = 16
NCD = D // 128
NTS = S // 128
MLP0 = 16384
INTER1 = 8192
EPS_LN = 1e-5
EPS_L1 = 1e-6
ISQ_HD = float(1.0 / np.sqrt(128.0))
BF = np.dtype(ml_dtypes.bfloat16)


class PB:

    def __init__(self):
        self.nc = bass.Bass("TRN2", target_bir_lowering=False)
        self.stack = ExitStack()
        self.cur = self.stack
        self.sems = {}
        self._consts = False
        self._pctr = 0

    def begin_phase(self):
        self._pctr += 1
        self.cur = ExitStack()
        return self._pctr

    def end_phase(self):
        self.cur.close()
        self.cur = self.stack

    def sem(self, name):
        if name not in self.sems:
            h = self.stack.enter_context(self.nc.semaphore(name))
            self.sems[name] = [h, 0]
        return self.sems[name]

    def sbuf(self, name, shape, dt, td=False, global_=False):
        st = self.stack if global_ else self.cur
        return st.enter_context(
            self.nc.sbuf_tensor(f"{name}_{self._pctr}", list(shape), dt,
                                is_transpose_dest=td))

    def psum(self, name, shape, dt=F32):
        return self.cur.enter_context(
            self.nc.psum_tensor(f"{name}_{self._pctr}", list(shape), dt))

    def consts(self):
        if self._consts:
            return
        self._consts = True
        nc = self.nc
        self.ident = self.sbuf("c_ident", [128, 128], BF16, global_=True)
        self.trimask = self.sbuf("c_tri", [128, 128], BF16, global_=True)
        self.utri = self.sbuf("c_utri", [128, 128], I32, global_=True)
        self.neg30 = self.sbuf("c_n30", [128, 128], F32, global_=True)
        self.ones1 = self.sbuf("c_ones1", [1, 512], BF16, global_=True)
        w1 = self.sbuf("c_iw1", [128, 128], I32)
        w1b = self.sbuf("c_iw1b", [128, 128], I32)
        w2 = self.sbuf("c_iw2", [128, 128], I32)
        with nc.Block("consts") as block:
            @block.gpsimd
            def _(g):
                g.iota(w1[:, :], [[1, 128]], channel_multiplier=0)
                g.iota(w1b[:, :], [[1, 128]], channel_multiplier=128)
            @block.vector
            def _(v):
                v.memset(self.ones1[:, :], 1.0)
        with nc.Block("consts2") as block:
            @block.vector
            def _(v):
                v.tensor_scalar(w2[:, :], w1[:, :], 129, None, ALU.mult)
                v.tensor_tensor(w2[:, :], w2[:, :], w1b[:, :], ALU.is_equal)
                v.tensor_copy(self.ident[:, :], w2[:, :])
                v.tensor_tensor(w2[:, :], w1b[:, :], w1[:, :], ALU.subtract)
                v.tensor_scalar(w2[:, :], w2[:, :], 7, None, ALU.logical_shift_right)
                v.tensor_tensor(w2[:, :], w1[:, :], w2[:, :], ALU.is_le)
                v.tensor_copy(self.trimask[:, :], w2[:, :])
                v.tensor_scalar(self.utri[:, :], w2[:, :], -1, 1, ALU.mult, ALU.add)
                v.memset(self.neg30[:, :], -1e30)


def emit_ln(pb, x_dram, xn_fm_dram, s_in=None, g_in=None, x_out=None):
    nc = pb.nc
    pb.begin_phase()
    qS = pb.sem("qS")
    qG = pb.sem("qG")
    eV = pb.sem("eV")
    eS = pb.sem("eS")
    eT = pb.sem("eT")
    fused = s_in is not None
    nper = 3 if fused else 1
    nv = 3 if fused else 2
    xt = pb.sbuf("ln_x", [128, 2 * D], F32)
    st_t = pb.sbuf("ln_s", [128, 2 * D], F32) if fused else None
    gt = pb.sbuf("ln_g", [128, 2 * D], F32) if fused else None
    xsq = pb.sbuf("ln_xsq", [128, D], F32)
    stat = pb.sbuf("ln_stat", [128, 8], F32)
    xn_tm = pb.sbuf("ln_xntm", [128, NTS * D], BF16)
    qS0, qG0, eV0, eS0, eT0 = qS[1], qG[1], eV[1], eS[1], eT[1]

    with nc.Block(f"ln_a_{pb._pctr}") as block:
        @block.sync
        def _(s):
            for t in range(NTS):
                b = t % 2
                if t >= 2:
                    s.wait_ge(eV[0], eV0 + nv * (t - 1))
                    if fused:
                        s.wait_ge(qG[0], qG0 + 16 * (t - 1))
                s.dma_start(out=xt[:, b * D:(b + 1) * D],
                            in_=x_dram[t * 128:(t + 1) * 128, :]).then_inc(qS[0], 16)
                if fused:
                    s.dma_start(out=st_t[:, b * D:(b + 1) * D],
                                in_=s_in[t * 128:(t + 1) * 128, :]).then_inc(qS[0], 16)
                    s.dma_start(out=gt[:, b * D:(b + 1) * D],
                                in_=g_in[t * 128:(t + 1) * 128, :]).then_inc(qS[0], 16)

        @block.vector
        def _(v):
            for t in range(NTS):
                b = t % 2
                sl = slice(b * D, (b + 1) * D)
                v.wait_ge(qS[0], qS0 + 16 * nper * (t + 1))
                if fused:
                    v.tensor_tensor(st_t[:, sl], st_t[:, sl], gt[:, sl], ALU.mult)
                    v.tensor_tensor(xt[:, sl], xt[:, sl], st_t[:, sl], ALU.add)
                    v.drain().then_inc(eV[0], 1)
                v.tensor_reduce(stat[:, 0:1], xt[:, sl], axis=AXX, op=ALU.add)
                v.wait_ge(eS[0], eS0 + 2 * t + 1)
                v.tensor_reduce(stat[:, 1:2], xsq[:, :], axis=AXX, op=ALU.add)
                v.drain()
                v.tensor_scalar(stat[:, 2:3], stat[:, 0:1], float(1.0 / D), None, ALU.mult)
                v.drain()
                v.tensor_tensor(stat[:, 3:4], stat[:, 2:3], stat[:, 2:3], ALU.mult)
                v.drain()
                v.scalar_tensor_tensor(stat[:, 3:4], stat[:, 1:2], float(1.0 / D),
                                       stat[:, 3:4], ALU.mult, ALU.subtract)
                v.drain()
                v.tensor_scalar(stat[:, 3:4], stat[:, 3:4], float(EPS_LN), None, ALU.add)
                v.drain().then_inc(eV[0], 1)
                v.wait_ge(eS[0], eS0 + 2 * t + 2)
                v.reciprocal(stat[:, 5:6], stat[:, 4:5])
                v.drain()
                v.scalar_tensor_tensor(stat[:, 6:7], stat[:, 2:3], float(-1.0),
                                       stat[:, 5:6], ALU.mult, ALU.mult)
                v.drain()
                v.tensor_scalar(xn_tm[:, t * D:(t + 1) * D], xt[:, sl],
                                stat[:, 5:6], stat[:, 6:7], ALU.mult, ALU.add)
                v.drain().then_inc(eV[0], 1)

        @block.scalar
        def _(sc):
            for t in range(NTS):
                b = t % 2
                sc.wait_ge(qS[0], qS0 + 16 * nper * (t + 1))
                if fused:
                    sc.wait_ge(eV[0], eV0 + nv * t + 1)
                if t >= 1:
                    sc.wait_ge(eV[0], eV0 + nv * (t - 1) + nv - 1)
                sc.activation(xsq[:, :], xt[:, b * D:(b + 1) * D], AF.Square)
                sc.drain().then_inc(eS[0], 1)
                sc.wait_ge(eV[0], eV0 + nv * t + (nv - 1))
                sc.activation(stat[:, 4:5], stat[:, 3:4], AF.Sqrt)
                sc.drain().then_inc(eS[0], 1)

        if fused:
            @block.gpsimd
            def _(g):
                for t in range(NTS):
                    b = t % 2
                    g.wait_ge(eV[0], eV0 + nv * t + 1)
                    g.dma_start(out=x_out[t * 128:(t + 1) * 128, :],
                                in_=xt[:, b * D:(b + 1) * D]).then_inc(qG[0], 16)
                g.wait_ge(qG[0], qG0 + 16 * NTS)
    qS[1] += 16 * nper * NTS
    eV[1] += nv * NTS
    eS[1] += 2 * NTS
    if fused:
        qG[1] += 16 * NTS

    ptb = [pb.psum(f"ln_pt{i}", [128, 1024], BF16) for i in range(4)]
    stage = pb.sbuf("ln_stage", [128, 2 * D], BF16)
    eT0, eV0, qG0 = eT[1], eV[1], qG[1]
    with nc.Block(f"ln_b_{pb._pctr}") as block:
        @block.tensor
        def _(te):
            for t in range(NTS):
                for c in range(NCD):
                    k = t * NCD + c
                    if k >= 4:
                        te.wait_ge(eV[0], eV0 + k - 3)
                    te.transpose(ptb[k % 4][:, 0:128],
                                 xn_tm[:, t * D + c * 128: t * D + (c + 1) * 128],
                                 pb.ident[:, :]).then_inc(eT[0], 1)

        @block.vector
        def _(v):
            for t in range(NTS):
                b = t % 2
                if t >= 2:
                    v.wait_ge(qG[0], qG0 + 16 * NCD * (t - 1))
                for c in range(NCD):
                    k = t * NCD + c
                    v.wait_ge(eT[0], eT0 + k + 1)
                    v.tensor_copy(stage[:, b * D + c * 128: b * D + (c + 1) * 128],
                                  ptb[k % 4][:, 0:128])
                    v.drain().then_inc(eV[0], 1)

        @block.gpsimd
        def _(g):
            for t in range(NTS):
                b = t % 2
                g.wait_ge(eV[0], eV0 + NCD * (t + 1))
                for c in range(NCD):
                    g.dma_start(out=xn_fm_dram[c, :, t * 128:(t + 1) * 128],
                                in_=stage[:, b * D + c * 128: b * D + (c + 1) * 128]
                                ).then_inc(qG[0], 16)
            g.wait_ge(qG[0], qG0 + 16 * NCD * NTS)
    eT[1] += NTS * NCD
    eV[1] += NTS * NCD
    qG[1] += 16 * NCD * NTS
    pb.end_phase()


def emit_qkv(pb, xn_fm_dram, wq, bq, wk, bk, wv, bv, q_fm_dram, k_fm_dram, v_tm_dram):
    nc = pb.nc
    pb.begin_phase()
    qS = pb.sem("qS")
    qG = pb.sem("qG")
    qC = pb.sem("qC")
    eT = pb.sem("eT")
    eV = pb.sem("eV")
    eWq = pb.sem("qk_eWq")
    xn = pb.sbuf("qk_xn", [128, NCD * S], BF16)
    wt = pb.sbuf("qk_w", [128, NCD * D], BF16)
    brow = pb.sbuf("qk_b", [1, 3 * D], BF16)
    outst = pb.sbuf("qk_o", [128, 2 * D], BF16)
    ps = [pb.psum(f"qk_ps{i}", [128, 512], F32) for i in range(8)]
    qS0, qG0, qC0, eT0, eV0 = qS[1], qG[1], qC[1], eT[1], eV[1]
    eWq0 = eWq[1]

    with nc.Block(f"qkv_{pb._pctr}") as block:
        @block.sync
        def _(s):
            for c in range(NCD):
                s.dma_start(out=xn[:, c * S:(c + 1) * S], in_=xn_fm_dram[c, :, :]
                            ).then_inc(qS[0], 16)

        @block.gpsimd
        def _(g):
            for i, b in enumerate((bq, bk, bv)):
                g.dma_start(out=brow[0:1, i * D:(i + 1) * D], in_=b[None, :]).then_inc(qG[0], 16)
            for wi, w in enumerate((wq, wk, wv)):
                if wi > 0:
                    g.wait_ge(eT[0], eT0 + 16 * wi)
                for c in range(NCD):
                    g.dma_start(out=wt[:, c * D:(c + 1) * D],
                                in_=w[c * 128:(c + 1) * 128, :]).then_inc(qG[0], 16)
                g.wait_ge(qG[0], qG0 + 16 * (3 + NCD * (wi + 1)))
                g.sem_inc(eWq[0], 1)

        @block.tensor
        def _(te):
            te.wait_ge(qS[0], qS0 + 16 * NCD)
            for wi in range(2):
                te.wait_ge(eWq[0], eWq0 + wi + 1)
                for n in range(NCD):
                    grp = wi * 16 + n
                    if grp >= 2:
                        te.wait_ge(eV[0], eV0 + 4 * (grp - 1))
                    bank = (grp % 2) * 4
                    for d in range(NCD):
                        for t in range(4):
                            te.matmul(ps[bank + t][:, :],
                                      wt[:, d * D + n * 128: d * D + (n + 1) * 128],
                                      xn[:, d * S + t * 512: d * S + (t + 1) * 512],
                                      start=(d == 0), stop=False)
                    for t in range(4):
                        ins = te.matmul(ps[bank + t][:, :],
                                        brow[0:1, wi * D + n * 128: wi * D + (n + 1) * 128],
                                        pb.ones1[0:1, 0:512], start=False, stop=True)
                    ins.then_inc(eT[0], 1)
            te.wait_ge(eWq[0], eWq0 + 3)
            for t in range(NTS):
                grp = 32 + t
                if grp >= 2:
                    te.wait_ge(eV[0], eV0 + 4 * (grp - 1))
                bank = (grp % 2) * 4
                for d in range(NCD):
                    for nb in range(4):
                        te.matmul(ps[bank + nb][:, :],
                                  xn[:, d * S + t * 128: d * S + (t + 1) * 128],
                                  wt[:, d * D + nb * 512: d * D + (nb + 1) * 512],
                                  start=(d == 0), stop=False)
                for nb in range(4):
                    ins = te.matmul(ps[bank + nb][:, :],
                                    pb.ones1[0:1, 0:128],
                                    brow[0:1, 2 * D + nb * 512: 2 * D + (nb + 1) * 512],
                                    start=False, stop=True)
                ins.then_inc(eT[0], 1)

        @block.vector
        def _(v):
            for grp in range(48):
                v.wait_ge(eT[0], eT0 + grp + 1)
                if grp >= 2:
                    v.wait_ge(qC[0], qC0 + 16 * (grp - 1))
                b = (grp % 2) * D
                bank = (grp % 2) * 4
                for t in range(4):
                    v.tensor_copy(outst[:, b + t * 512: b + (t + 1) * 512],
                                  ps[bank + t][:, :]).then_inc(eV[0], 1)

        @block.scalar
        def _(sc):
            for grp in range(48):
                sc.wait_ge(eV[0], eV0 + 4 * (grp + 1))
                b = (grp % 2) * D
                if grp < 16:
                    dst = q_fm_dram[grp, :, :]
                elif grp < 32:
                    dst = k_fm_dram[grp - 16, :, :]
                else:
                    dst = v_tm_dram[grp - 32, :, :]
                sc.dma_start(out=dst, in_=outst[:, b:b + D]).then_inc(qC[0], 16)
            sc.wait_ge(qC[0], qC0 + 16 * 48)

    qS[1] += 16 * NCD
    qG[1] += 16 * (3 + 3 * NCD)
    eWq[1] += 3
    eT[1] += 48
    eV[1] += 192
    qC[1] += 16 * 48
    pb.end_phase()


def emit_att0(pb, q_fm_dram, k_fm_dram, v_tm_dram, ctx_fm_dram):
    nc = pb.nc
    pb.begin_phase()
    qS = pb.sem("qS")
    qG = pb.sem("qG")
    eHd = pb.sem("at_eHd")
    eT = pb.sem("at_eT")
    eTt = pb.sem("at_eTt")
    eTc = pb.sem("at_eTc")
    eVm = pb.sem("at_eVm")
    eS = pb.sem("at_eS")
    eVp = pb.sem("at_eVp")
    eVe = pb.sem("at_eVe")
    eSe = pb.sem("at_eSe")
    eVz = pb.sem("at_eVz")
    eVc = pb.sem("at_eVc")
    qh = pb.sbuf("at_q", [128, 2 * S], BF16)
    kh = pb.sbuf("at_k", [128, 2 * S], BF16)
    vh = pb.sbuf("at_v", [128, 2 * S], BF16)
    E = pb.sbuf("at_E", [128, S], BF16)
    P = pb.sbuf("at_P", [128, S], BF16)
    PT = pb.sbuf("at_PT", [128, 16 * 512], BF16)
    cst = pb.sbuf("at_cst", [128, 2 * 512], BF16)
    mst = pb.sbuf("at_mst", [128, 16], F32)
    psc = [pb.psum(f"at_sc{i}", [128, 512]) for i in range(4)]
    pst = [pb.psum(f"at_tp{i}", [128, 1024], BF16) for i in range(2)]
    pctx = pb.psum("at_ctx", [128, 512])
    qS0, qG0 = qS[1], qG[1]
    (eHd0, eT0, eTt0, eTc0, eVm0, eS0, eVp0, eVe0, eSe0, eVz0, eVc0) = (
        eHd[1], eT[1], eTt[1], eTc[1], eVm[1], eS[1], eVp[1], eVe[1], eSe[1],
        eVz[1], eVc[1])

    def tr_bf(h, i):
        return h * 136 + i * (i + 1) // 2

    def grp_bf(h, i):
        return h * 40 + sum((ii + 4) // 4 for ii in range(i))

    def z_bf(h, i):
        perh = sum(4 * (ii // 4 + 1) - (ii + 1) for ii in range(16))
        return h * perh + sum(4 * (ii // 4 + 1) - (ii + 1) for ii in range(i))

    ZPH = sum(4 * (ii // 4 + 1) - (ii + 1) for ii in range(16))

    with nc.Block(f"att0_{pb._pctr}") as block:
        @block.sync
        def _(s):
            for h in range(H):
                b = h % 2
                if h >= 2:
                    s.wait_ge(eT[0], eT0 + (h - 1) * 16)
                    s.wait_ge(eTc[0], eTc0 + (h - 1) * 4)
                s.dma_start(out=qh[:, b * S:(b + 1) * S], in_=q_fm_dram[h, :, :]
                            ).then_inc(qS[0], 16)
                s.dma_start(out=kh[:, b * S:(b + 1) * S], in_=k_fm_dram[h, :, :]
                            ).then_inc(qS[0], 16)
                for t in range(NTS):
                    s.dma_start(out=vh[:, b * S + t * 128: b * S + (t + 1) * 128],
                                in_=v_tm_dram[t, :, h * 128:(h + 1) * 128]
                                ).then_inc(qS[0], 16)
                s.wait_ge(qS[0], qS0 + 16 * 18 * (h + 1))
                s.sem_inc(eHd[0], 1)

        @block.tensor
        def _(te):
            for h in range(H):
                b = h % 2
                te.wait_ge(eHd[0], eHd0 + h + 1)
                for it in range(17):
                    if it < 16:
                        i = it
                        k = h * 16 + i
                        jbn = i // 4 + 1
                        if k >= 1:
                            te.wait_ge(eS[0], eS0 + k)
                        for jb in range(jbn):
                            ins = te.matmul(psc[jb][:, :],
                                            qh[:, b * S + i * 128: b * S + (i + 1) * 128],
                                            kh[:, b * S + jb * 512: b * S + (jb + 1) * 512],
                                            start=True, stop=True)
                        ins.then_inc(eT[0], 1)
                    if it >= 1:
                        i = it - 1
                        k = h * 16 + i
                        te.wait_ge(eVp[0], eVp0 + k + 1)
                        for gi2 in range((i + 4) // 4):
                            j0 = gi2 * 4
                            j1 = min(i + 1, j0 + 4)
                            gidx = grp_bf(h, i) + gi2
                            if gidx >= 2:
                                te.wait_ge(eVe[0], eVe0 + gidx - 1)
                            for jt in range(j0, j1):
                                te.transpose(pst[gidx % 2][:, (jt - j0) * 128:
                                                           (jt - j0 + 1) * 128],
                                             P[:, jt * 128:(jt + 1) * 128],
                                             pb.ident[:, :]).then_inc(eTt[0], 1)
                        if i % 4 == 3:
                            g = h * 4 + i // 4
                            if g >= 1:
                                te.wait_ge(eVc[0], eVc0 + g)
                            te.wait_ge(eVe[0], eVe0 + grp_bf(h, i + 1))
                            te.wait_ge(eVz[0], eVz0 + z_bf(h, i + 1))
                            njt = 4 * (i // 4 + 1)
                            for jt in range(njt):
                                ins = te.matmul(pctx[:, :],
                                                vh[:, b * S + jt * 128: b * S + (jt + 1) * 128],
                                                PT[:, jt * 512:(jt + 1) * 512],
                                                start=(jt == 0), stop=(jt == njt - 1))
                            ins.then_inc(eTc[0], 1)

        @block.vector
        def _(v):
            for h in range(H):
                for it in range(17):
                    if it >= 1:
                        i = it - 1
                        ii = i % 4
                        for gi2 in range((i + 4) // 4):
                            j0 = gi2 * 4
                            j1 = min(i + 1, j0 + 4)
                            gidx = grp_bf(h, i) + gi2
                            v.wait_ge(eTt[0], eTt0 + tr_bf(h, i) + j1)
                            dst = bass.AP(PT.tensor if hasattr(PT, "tensor") else PT,
                                          0, [[0, 0]])
                            ptv = PT[:, :].rearrange("p (jt c) -> p jt c", c=512)
                            v.tensor_copy(ptv[:, j0:j1, ii * 128:(ii + 1) * 128],
                                          pst[gidx % 2][:, 0:(j1 - j0) * 128]
                                          .rearrange("p (jt c) -> p jt c", c=128)
                                          ).then_inc(eVe[0], 1)
                        for jt in range(i + 1, 4 * (i // 4 + 1)):
                            v.memset(PT[:, jt * 512 + ii * 128:
                                        jt * 512 + (ii + 1) * 128], 0.0)
                            v.drain().then_inc(eVz[0], 1)
                        if i % 4 == 3:
                            g = h * 4 + i // 4
                            v.wait_ge(eTc[0], eTc0 + g + 1)
                            if g >= 2:
                                v.wait_ge(qG[0], qG0 + 16 * (g - 1))
                            v.tensor_copy(cst[:, (g % 2) * 512:(g % 2 + 1) * 512],
                                          pctx[:, :]).then_inc(eVc[0], 1)
                    if it < 16:
                        i = it
                        k = h * 16 + i
                        jbn = i // 4 + 1
                        v.wait_ge(eT[0], eT0 + k + 1)
                        v.copy_predicated(psc[i // 4][:, (i % 4) * 128:(i % 4 + 1) * 128],
                                          pb.utri[:, :], pb.neg30[:, :])
                        lo = (i + 1) * 128 - (jbn - 1) * 512
                        if lo < 512:
                            v.memset(psc[jbn - 1][:, lo:512], -1e30)
                        v.drain()
                        for jb in range(jbn):
                            v.reduce_max(mst[:, jb:jb + 1], psc[jb][:, :], axis=AXX)
                        v.drain()
                        v.reduce_max(mst[:, 4:5], mst[:, 0:jbn], axis=AXX)
                        v.drain()
                        v.tensor_scalar(mst[:, 5:6], mst[:, 4:5], float(-ISQ_HD), None,
                                        ALU.mult)
                        v.drain().then_inc(eVm[0], 1)
                        v.wait_ge(eS[0], eS0 + k + 1)
                        v.tensor_reduce(mst[:, 6:7], mst[:, 8:8 + jbn], axis=AXX,
                                        op=ALU.add)
                        v.drain()
                        v.reciprocal(mst[:, 7:8], mst[:, 6:7])
                        v.drain()
                        v.wait_ge(eTt[0], eTt0 + tr_bf(h, i))
                        v.tensor_scalar(P[:, 0:jbn * 512], E[:, 0:jbn * 512],
                                        mst[:, 7:8], None, ALU.mult)
                        v.drain().then_inc(eVp[0], 1)
        @block.scalar
        def _(sc):
            for h in range(H):
                for it in range(17):

                    if it < 16:
                        i = it
                        k = h * 16 + i
                        jbn = i // 4 + 1
                        sc.wait_ge(eVm[0], eVm0 + k + 1)
                        if k >= 1:
                            sc.wait_ge(eVp[0], eVp0 + k)
                        for jb in range(jbn):
                            sc.activation(E[:, jb * 512:(jb + 1) * 512], psc[jb][:, :],
                                          AF.Exp, bias=mst[:, 5:6], scale=f32(ISQ_HD),
                                          accum_out=mst[:, 8 + jb:9 + jb])
                        sc.drain().then_inc(eS[0], 1)
        @block.gpsimd
        def _(g_):
            for h in range(H):
                for ig in range(4):
                    g = h * 4 + ig
                    g_.wait_ge(eVc[0], eVc0 + g + 1)
                    g_.dma_start(out=ctx_fm_dram[h, :, ig * 512:(ig + 1) * 512],
                                 in_=cst[:, (g % 2) * 512:(g % 2 + 1) * 512]
                                 ).then_inc(qG[0], 16)
            g_.wait_ge(qG[0], qG0 + 16 * 64)

    qS[1] += 16 * 18 * H
    eHd[1] += H
    eT[1] += 16 * H
    eTt[1] += 136 * H
    eTc[1] += 4 * H
    eVm[1] += 16 * H
    eS[1] += 16 * H
    eVp[1] += 16 * H
    eVe[1] += 40 * H
    eVz[1] += ZPH * H
    eVc[1] += 4 * H
    qG[1] += 16 * 64
    pb.end_phase()


def emit_wo0(pb, ctx_fm_dram, wo, bo, s0_dram):
    nc = pb.nc
    pb.begin_phase()
    qS = pb.sem("qS")
    qG = pb.sem("qG")
    qC = pb.sem("qC")
    eT = pb.sem("wo_eT")
    eV = pb.sem("wo_eV")
    ctx = pb.sbuf("wo_ctx", [128, NCD * S], BF16)
    wt = pb.sbuf("wo_w", [128, NCD * D], BF16)
    brow = pb.sbuf("wo_b", [1, D], BF16)
    stage = pb.sbuf("wo_st", [128, 2 * D], F32)
    ps = [pb.psum(f"wo_ps{i}", [128, 512]) for i in range(8)]
    qS0, qG0, qC0, eT0, eV0 = qS[1], qG[1], qC[1], eT[1], eV[1]

    with nc.Block(f"wo0_{pb._pctr}") as block:
        @block.sync
        def _(s):
            for c in range(NCD):
                s.dma_start(out=ctx[:, c * S:(c + 1) * S], in_=ctx_fm_dram[c, :, :]
                            ).then_inc(qS[0], 16)

        @block.gpsimd
        def _(g):
            g.dma_start(out=brow[0:1, :], in_=bo[None, :]).then_inc(qG[0], 16)
            for c in range(NCD):
                g.dma_start(out=wt[:, c * D:(c + 1) * D],
                            in_=wo[c * 128:(c + 1) * 128, :]).then_inc(qG[0], 16)

        @block.tensor
        def _(te):
            te.wait_ge(qS[0], qS0 + 16 * NCD)
            te.wait_ge(qG[0], qG0 + 16 * (NCD + 1))
            for t in range(NTS):
                if t >= 2:
                    te.wait_ge(eV[0], eV0 + t - 1)
                bank = (t % 2) * 4
                for nb in range(4):
                    for d in range(NCD):
                        te.matmul(ps[bank + nb][:, :],
                                  ctx[:, d * S + t * 128: d * S + (t + 1) * 128],
                                  wt[:, d * D + nb * 512: d * D + (nb + 1) * 512],
                                  start=(d == 0), stop=False)
                    ins = te.matmul(ps[bank + nb][:, :], pb.ones1[0:1, 0:128],
                                    brow[0:1, nb * 512:(nb + 1) * 512],
                                    start=False, stop=True)
                ins.then_inc(eT[0], 1)

        @block.vector
        def _(v):
            for t in range(NTS):
                v.wait_ge(eT[0], eT0 + t + 1)
                if t >= 2:
                    v.wait_ge(qC[0], qC0 + 16 * (t - 1))
                b = (t % 2) * D
                bank = (t % 2) * 4
                for nb in range(4):
                    v.tensor_copy(stage[:, b + nb * 512: b + (nb + 1) * 512],
                                  ps[bank + nb][:, :])
                v.drain().then_inc(eV[0], 1)

        @block.scalar
        def _(sc):
            for t in range(NTS):
                sc.wait_ge(eV[0], eV0 + t + 1)
                sc.dma_start(out=s0_dram[t * 128:(t + 1) * 128, :],
                             in_=stage[:, (t % 2) * D:(t % 2 + 1) * D]).then_inc(qC[0], 16)
            sc.wait_ge(qC[0], qC0 + 16 * NTS)

    qS[1] += 16 * NCD
    qG[1] += 16 * (NCD + 1)
    eT[1] += NTS
    eV[1] += NTS
    qC[1] += 16 * NTS
    pb.end_phase()


def emit_dq(pb, packed_dram, scales_dram, out_dram, NR, KPT):
    nc = pb.nc
    pb.begin_phase()
    qS = pb.sem("qS")
    qG = pb.sem("qG")
    eP = pb.sem("dq_eP")
    eV0 = pb.sem("dq_eV0")
    eS1 = pb.sem("dq_eS1")
    eS2 = pb.sem("dq_eS2")
    eV1 = pb.sem("dq_eV1")
    eV2 = pb.sem("dq_eV2")
    KP = 1024
    KO = 2 * KP
    NB = KO // 32
    nrt = NR // 128
    nkc = KPT // KP
    ntile = nrt * nkc
    wt = pb.sbuf("dq_w", [128, 2 * KP], I32)
    sct = pb.sbuf("dq_sc", [128, 2 * NB], F32)
    nib = pb.sbuf("dq_nib", [128, 2 * KO], I32)
    A = pb.sbuf("dq_A", [128, 2 * KO], F32)
    B = pb.sbuf("dq_B", [128, 2 * KO], F32)
    C = pb.sbuf("dq_C", [128, 2 * KO], F32)
    Dd = pb.sbuf("dq_D", [128, 2 * KO], F32)
    ob = pb.sbuf("dq_o", [128, 2 * KO], BF16)
    qS0, qG0 = qS[1], qG[1]
    eP0, eS10, eS20, eV10, eV20 = eP[1], eS1[1], eS2[1], eV1[1], eV2[1]
    eV00 = eV0[1]

    def tiles():
        for r in range(nrt):
            for kc in range(nkc):
                yield r, kc

    with nc.Block(f"dq_{pb._pctr}") as block:
        @block.sync
        def _(s):
            for k, (r, kc) in enumerate(tiles()):
                b = k % 2
                if k >= 2:
                    s.wait_ge(eV0[0], eV00 + k - 1)
                    s.wait_ge(eV2[0], eV20 + k - 1)
                s.dma_start(out=wt[:, b * KP:(b + 1) * KP],
                            in_=packed_dram[r * 128:(r + 1) * 128, kc * KP:(kc + 1) * KP]
                            ).then_inc(qS[0], 16)
                s.dma_start(out=sct[:, b * NB:(b + 1) * NB],
                            in_=scales_dram[r * 128:(r + 1) * 128, kc * NB:(kc + 1) * NB]
                            ).then_inc(qS[0], 16)

        @block.gpsimd
        def _(g):
            for k in range(ntile):
                b = k % 2
                sl = slice(b * KO, (b + 1) * KO)
                g.wait_ge(eS2[0], eS20 + k + 1)
                g.tensor_tensor(A[:, sl], A[:, sl], Dd[:, sl], ALU.add)
                g.drain().then_inc(eP[0], 1)
                if k >= 1:
                    g.wait_ge(eV2[0], eV20 + k)
                    bb = (k - 1) % 2
                    rr, kk = (k - 1) // nkc, (k - 1) % nkc
                    g.dma_start(out=out_dram[rr * 128:(rr + 1) * 128,
                                             kk * KO:(kk + 1) * KO],
                                in_=ob[:, bb * KO:(bb + 1) * KO]).then_inc(qG[0], 16)
            g.wait_ge(eV2[0], eV20 + ntile)
            bb = (ntile - 1) % 2
            rr, kk = (ntile - 1) // nkc, (ntile - 1) % nkc
            g.dma_start(out=out_dram[rr * 128:(rr + 1) * 128, kk * KO:(kk + 1) * KO],
                        in_=ob[:, bb * KO:(bb + 1) * KO]).then_inc(qG[0], 16)
            g.wait_ge(qG[0], qG0 + 16 * ntile)

        @block.scalar
        def _(sc):
            for k in range(ntile):
                b = k % 2
                sl = slice(b * KO, (b + 1) * KO)
                sc.wait_ge(eV0[0], eV00 + k + 1)
                if k >= 2:
                    sc.wait_ge(eP[0], eP0 + k - 1)
                    sc.wait_ge(eV2[0], eV20 + k - 1)
                sc.activation(A[:, sl], nib[:, sl], AF.Copy, bias=0.0)
                sc.activation(B[:, sl], A[:, sl], AF.Sign, bias=f32(7.5), scale=f32(-1.0))
                sc.drain().then_inc(eS1[0], 1)
                sc.wait_ge(eV1[0], eV10 + k + 1)
                sc.activation(A[:, sl], C[:, sl], AF.Relu, bias=f32(-4.0), scale=f32(0.5))
                sc.activation(Dd[:, sl], C[:, sl], AF.Relu, bias=f32(-10.0))
                sc.drain().then_inc(eS2[0], 1)

        @block.vector
        def _(v):
            for k in range(ntile):
                b = k % 2
                sl = slice(b * KO, (b + 1) * KO)
                v.wait_ge(qS[0], qS0 + 32 * k + 16)
                if k >= 2:
                    v.wait_ge(eS1[0], eS10 + k - 1)
                nv = nib[:, sl].rearrange("p (k two) -> p k two", two=2)
                wv = wt[:, b * KP:(b + 1) * KP].unsqueeze(-1)
                v.tensor_scalar(nv[:, :, 0:1], wv, 15, None, ALU.bitwise_and)
                v.tensor_scalar(nv[:, :, 1:2], wv, 4, 15,
                                ALU.logical_shift_right, ALU.bitwise_and)
                v.drain().then_inc(eV0[0], 1)
                v.wait_ge(eS1[0], eS10 + k + 1)
                v.scalar_tensor_tensor(C[:, sl], B[:, sl], 4.0, A[:, sl],
                                       ALU.mult, ALU.add)
                v.drain().then_inc(eV1[0], 1)
                v.wait_ge(eP[0], eP0 + k + 1)
                v.scalar_tensor_tensor(C[:, sl], C[:, sl], 0.5, A[:, sl],
                                       ALU.mult, ALU.add)
                v.scalar_tensor_tensor(C[:, sl], C[:, sl], -2.0, B[:, sl],
                                       ALU.add, ALU.mult)
                if k >= 2:
                    v.wait_ge(qG[0], qG0 + 16 * (k - 1))
                v.wait_ge(qS[0], qS0 + 32 * (k + 1))
                cv = C[:, sl].rearrange("p (bk j) -> p bk j", j=32)
                ov = ob[:, sl].rearrange("p (bk j) -> p bk j", j=32)
                for hb in range(NB // 32):
                    scv = sct[:, b * NB + hb * 32: b * NB + (hb + 1) * 32]
                    scb = scv.unsqueeze(-1).broadcast_to((128, 32, 32))
                    v.tensor_tensor(ov[:, hb * 32:(hb + 1) * 32, :],
                                    cv[:, hb * 32:(hb + 1) * 32, :], scb, ALU.mult)
                v.drain().then_inc(eV2[0], 1)

    qS[1] += 32 * ntile
    qG[1] += 16 * ntile
    eP[1] += ntile
    eV0[1] += ntile
    eS1[1] += ntile
    eS2[1] += ntile
    eV1[1] += ntile
    eV2[1] += ntile
    pb.end_phase()


def emit_ffn0a(pb, xn_fm_dram, wg_bf, wu_bf, inter_dram):
    nc = pb.nc
    pb.begin_phase()
    qS = pb.sem("qS")
    qG = pb.sem("qG")
    qC = pb.sem("qC")
    eT = pb.sem("fa_eT")
    eSg = pb.sem("fa_eSg")
    eV = pb.sem("fa_eV")
    eVn = pb.sem("fa_eVn")
    xn = pb.sbuf("fa_xn", [128, NCD * S], BF16)
    wg = pb.sbuf("fa_wg", [128, 2 * NCD * 512], BF16, td=True)
    wu = pb.sbuf("fa_wu", [128, 2 * NCD * 512], BF16, td=True)
    gsb = pb.sbuf("fa_g", [128, 512], F32)
    ibuf = pb.sbuf("fa_i", [128, 2 * S], BF16)
    ps = [pb.psum(f"fa_ps{i}", [128, 512]) for i in range(4)]
    qS0, qG0, qC0 = qS[1], qG[1], qC[1]
    eT0, eSg0, eV0, eVn0 = eT[1], eSg[1], eV[1], eVn[1]
    NG = 32

    with nc.Block(f"ffn0a_{pb._pctr}") as block:
        @block.gpsimd
        def _(g):
            for c in range(NCD):
                g.dma_start(out=xn[:, c * S:(c + 1) * S], in_=xn_fm_dram[c, :, :]
                            ).then_inc(qG[0], 16)

        @block.sync
        def _(s):
            for g in range(NG):
                b = g % 2
                if g >= 2:
                    s.wait_ge(eT[0], eT0 + 16 * (g - 1))
                for d in range(NCD):
                    s.dma_start(out=wg[:, (b * NCD + d) * 512:(b * NCD + d + 1) * 512],
                                in_=wg_bf[g * 512:(g + 1) * 512, d * 128:(d + 1) * 128],
                                transpose=True).then_inc(qS[0], 16)
                    s.dma_start(out=wu[:, (b * NCD + d) * 512:(b * NCD + d + 1) * 512],
                                in_=wu_bf[g * 512:(g + 1) * 512, d * 128:(d + 1) * 128],
                                transpose=True).then_inc(qS[0], 16)

        @block.tensor
        def _(te):
            te.wait_ge(qG[0], qG0 + 16 * NCD)
            for g in range(NG):
                b = g % 2
                te.wait_ge(eWf[0], eWf0 + g + 1)
                for ncg in range(4):
                    for st in range(4):
                        idx = (g * 4 + ncg) * 4 + st
                        if idx >= 2:
                            te.wait_ge(eV[0], eV0 + idx - 1)
                        bg = (st % 2) * 2
                        for d in range(NCD):
                            te.matmul(ps[bg][:, :],
                                      wg[:, (b * NCD + d) * 512 + ncg * 128:
                                         (b * NCD + d) * 512 + (ncg + 1) * 128],
                                      xn[:, d * S + st * 512: d * S + (st + 1) * 512],
                                      start=(d == 0), stop=(d == NCD - 1))
                        for d in range(NCD):
                            ins = te.matmul(ps[bg + 1][:, :],
                                            wu[:, (b * NCD + d) * 512 + ncg * 128:
                                               (b * NCD + d) * 512 + (ncg + 1) * 128],
                                            xn[:, d * S + st * 512: d * S + (st + 1) * 512],
                                            start=(d == 0), stop=(d == NCD - 1))
                        ins.then_inc(eT[0], 1)

        @block.scalar
        def _(sc):
            for nc_i in range(128):
                for st in range(4):
                    idx = nc_i * 4 + st
                    sc.wait_ge(eT[0], eT0 + idx + 1)
                    if idx >= 1:
                        sc.wait_ge(eV[0], eV0 + idx)
                    bg = (st % 2) * 2
                    sc.activation(gsb[:, :], ps[bg][:, :], AF.Gelu_apprx_tanh)
                    sc.drain().then_inc(eSg[0], 1)
                if nc_i >= 1:
                    sc.wait_ge(eVn[0], eVn0 + nc_i)
                    pb_ = (nc_i - 1) % 2
                    sc.dma_start(out=inter_dram[nc_i - 1, :, :],
                                 in_=ibuf[:, pb_ * S:(pb_ + 1) * S]).then_inc(qC[0], 16)
            sc.wait_ge(eVn[0], eVn0 + 128)
            sc.dma_start(out=inter_dram[127, :, :], in_=ibuf[:, S:2 * S]
                         ).then_inc(qC[0], 16)
            sc.wait_ge(qC[0], qC0 + 16 * 128)

        @block.vector
        def _(v):
            for nc_i in range(128):
                nb = nc_i % 2
                for st in range(4):
                    idx = nc_i * 4 + st
                    v.wait_ge(eSg[0], eSg0 + idx + 1)
                    if nc_i >= 2 and st == 0:
                        v.wait_ge(qC[0], qC0 + 16 * (nc_i - 1))
                    bg = (st % 2) * 2
                    v.tensor_tensor(ibuf[:, nb * S + st * 512: nb * S + (st + 1) * 512],
                                    gsb[:, :], ps[bg + 1][:, :],
                                    ALU.mult).then_inc(eV[0], 1)
                v.drain().then_inc(eVn[0], 1)

    qG[1] += 16 * NCD
    qS[1] += 16 * 32 * NG
    eT[1] += 512
    eSg[1] += 512
    eV[1] += 512
    eVn[1] += 128
    qC[1] += 16 * 128
    pb.end_phase()


def emit_ffn0b(pb, inter_dram, wd_bf, g0_dram, nchunks=128, act="silu"):
    nc = pb.nc
    pb.begin_phase()
    qS = pb.sem("qS")
    qG = pb.sem("qG")
    qGs2 = pb.sem("fb_qGs2")
    qC = pb.sem("qC")
    eTk = pb.sem("fb_eTk")
    eS = pb.sem("fb_eS")
    eWdS = pb.sem("fb_eWdS")
    eWdC = pb.sem("fb_eWdC")
    RING = 8
    ibig = pb.sbuf("fb_i", [128, nchunks * 512], BF16)
    wd = pb.sbuf("fb_wd", [128, RING * 1024], BF16, td=True)
    stage = pb.sbuf("fb_st", [128, 2 * 4096], F32)
    ps = [pb.psum(f"fb_ps{i}", [128, 512]) for i in range(8)]
    qS0, qG0, qGs20, qC0 = qS[1], qG[1], qGs2[1], qC[1]
    eTk0, eS0, eWdS0, eWdC0 = eTk[1], eS[1], eWdS[1], eWdC[1]
    afunc = AF.Silu if act == "silu" else AF.Copy
    NGI = 8

    with nc.Block(f"ffn0b_{pb._pctr}") as block:
        @block.gpsimd
        def _(g):
            for sg in range(4):
                if sg > 0:
                    g.wait_ge(eTk[0], eTk0 + sg * 2 * nchunks)
                for nc_i in range(nchunks):
                    g.dma_start(out=ibig[:, nc_i * 512:(nc_i + 1) * 512],
                                in_=inter_dram[nc_i, :, sg * 512:(sg + 1) * 512]
                                ).then_inc(qG[0], 16)
                if sg > 0:
                    for mbp in range(2):
                        gi = (sg - 1) * 2 + mbp
                        g.wait_ge(eS[0], eS0 + gi + 1)
                        for mb2 in range(2):
                            for ss in range(4):
                                g.dma_start(
                                    out=g0_dram[(sg - 1) * 512 + ss * 128:
                                                (sg - 1) * 512 + (ss + 1) * 128,
                                                mbp * 1024 + mb2 * 512:
                                                mbp * 1024 + (mb2 + 1) * 512],
                                    in_=stage[:, (gi % 2) * 4096 + mb2 * 2048 + ss * 512:
                                              (gi % 2) * 4096 + mb2 * 2048 + (ss + 1) * 512]
                                    ).then_inc(qGs2[0], 16)
            for mbp in range(2):
                gi = 6 + mbp
                g.wait_ge(eS[0], eS0 + gi + 1)
                for mb2 in range(2):
                    for ss in range(4):
                        g.dma_start(
                            out=g0_dram[3 * 512 + ss * 128: 3 * 512 + (ss + 1) * 128,
                                        mbp * 1024 + mb2 * 512:
                                        mbp * 1024 + (mb2 + 1) * 512],
                            in_=stage[:, (gi % 2) * 4096 + mb2 * 2048 + ss * 512:
                                      (gi % 2) * 4096 + mb2 * 2048 + (ss + 1) * 512]
                            ).then_inc(qGs2[0], 16)
            g.wait_ge(qGs2[0], qGs20 + 16 * 64)

        @block.sync
        def _(s):
            nS = 0
            for k in range(NGI * nchunks):
                gi, nc_i = divmod(k, nchunks)
                mbp = gi % 2
                sg = gi // 2
                if k >= RING:
                    s.wait_ge(eTk[0], eTk0 + k - RING + 1)
                s.dma_start(out=wd[:, (k % RING) * 1024:(k % RING + 1) * 1024],
                            in_=wd_bf[mbp * 1024:(mbp + 1) * 1024,
                                      nc_i * 128:(nc_i + 1) * 128],
                            transpose=True).then_inc(qS[0], 16)
                nS += 1
                if k % 4 == 3:
                    s.wait_ge(qS[0], qS0 + 16 * nS)
                    s.sem_inc(eWdS[0], 4)

        @block.scalar
        def _(sc):
            nC = 0
            for gi in range(NGI):
                sg, mbp2 = divmod(gi, 2)
                if gi >= 1:
                    gp = gi - 1
                    sc.wait_ge(eTk[0], eTk0 + gi * nchunks)
                    if gp >= 2:
                        sc.wait_ge(qGs2[0], qGs20 + 16 * 8 * (gp - 1))
                    bs = (gp % 2) * 4096
                    for mb2 in range(2):
                        for ss in range(4):
                            sc.activation(
                                stage[:, bs + mb2 * 2048 + ss * 512:
                                      bs + mb2 * 2048 + (ss + 1) * 512],
                                ps[mb2 * 4 + ss][:, :], afunc)
                    sc.drain().then_inc(eS[0], 1)

            gp = NGI - 1
            sc.wait_ge(eTk[0], eTk0 + NGI * nchunks)
            sc.wait_ge(qGs2[0], qGs20 + 16 * 8 * (gp - 1))
            bs = (gp % 2) * 4096
            for mb2 in range(2):
                for ss in range(4):
                    sc.activation(stage[:, bs + mb2 * 2048 + ss * 512:
                                        bs + mb2 * 2048 + (ss + 1) * 512],
                                  ps[mb2 * 4 + ss][:, :], afunc)
            sc.drain().then_inc(eS[0], 1)

        @block.tensor
        def _(te):
            for gi in range(NGI):
                sg = gi // 2
                te.wait_ge(qG[0], qG0 + 16 * nchunks * (sg + 1))
                if gi >= 1:
                    te.wait_ge(eS[0], eS0 + gi)
                for nc_i in range(nchunks):
                    k = gi * nchunks + nc_i
                    te.wait_ge(eWdS[0], eWdS0 + k + 1)
                    for mb2 in range(2):
                        for ss in range(4):
                            ins = te.matmul(
                                ps[mb2 * 4 + ss][:, :],
                                ibig[:, nc_i * 512 + ss * 128:
                                     nc_i * 512 + (ss + 1) * 128],
                                wd[:, (k % RING) * 1024 + mb2 * 512:
                                   (k % RING) * 1024 + (mb2 + 1) * 512],
                                start=(nc_i == 0), stop=(nc_i == nchunks - 1))
                    ins.then_inc(eTk[0], 1)

    qG[1] += 16 * nchunks * 4
    qGs2[1] += 16 * 64
    qS[1] += 16 * NGI * nchunks
    eWdS[1] += NGI * nchunks
    eTk[1] += NGI * nchunks
    eS[1] += NGI
    pb.end_phase()


def emit_casts(pb, jobs):
    nc = pb.nc
    pb.begin_phase()
    qG = pb.sem("qG")
    qG0 = qG[1]
    n = 0
    with nc.Block(f"casts_{pb._pctr}") as block:
        @block.gpsimd
        def _(g):
            nonlocal n
            for src, dst, nr, ncol in jobs:
                step = max(1, (4 << 20) // (ncol * 4))
                for r0 in range(0, nr, step):
                    r1 = min(nr, r0 + step)
                    g.dma_start(out=dst[r0:r1, :], in_=src[r0:r1, :]).then_inc(qG[0], 16)
                    n += 1
            g.wait_ge(qG[0], qG0 + 16 * n)
    qG[1] += 16 * n
    pb.end_phase()


def emit_scaled_cast(pb, w_dram, scale_vec_dram, out_dram, NRr, NCc, imm):
    nc = pb.nc
    pb.begin_phase()
    qS = pb.sem("qS")
    qG = pb.sem("qG")
    qC = pb.sem("qC")
    eT = pb.sem("sc_eT")
    eV = pb.sem("sc_eV")
    nrt = NRr // 128
    wt = pb.sbuf("scst_w", [128, 2 * NCc], F32)
    ob = pb.sbuf("scst_o", [128, 2 * NCc], BF16)
    svrow = pb.sbuf("scst_sv", [nrt, 128], BF16)
    scol = pb.sbuf("scst_c", [128, nrt], F32)
    pst = pb.psum("scst_p", [128, 1024], BF16)
    qS0, qG0, qC0, eT0, eV0 = qS[1], qG[1], qC[1], eT[1], eV[1]

    with nc.Block(f"scast_{pb._pctr}") as block:
        @block.gpsimd
        def _(g):
            g.dma_start(out=svrow[:, :], in_=scale_vec_dram.rearrange("(r p) -> r p", p=128)
                        ).then_inc(qG[0], 16)
        @block.tensor
        def _(te):
            te.wait_ge(qG[0], qG0 + 16)
            te.transpose(pst[:, 0:nrt], svrow[0:nrt, :], pb.ident[0:nrt, 0:nrt]).then_inc(eT[0], 1)
        @block.vector
        def _(v):
            v.wait_ge(eT[0], eT0 + 1)
            v.tensor_copy(scol[:, :], pst[:, 0:nrt])
            v.drain().then_inc(eV[0], 1)
            for r in range(nrt):
                b = r % 2
                v.wait_ge(qS[0], qS0 + 16 * (r + 1))
                if r >= 2:
                    v.wait_ge(qC[0], qC0 + 16 * (r - 1))
                v.tensor_scalar(ob[:, b * NCc:(b + 1) * NCc], wt[:, b * NCc:(b + 1) * NCc],
                                scol[:, r:r + 1], float(imm), ALU.mult, ALU.mult)
                v.drain().then_inc(eV[0], 1)
        @block.sync
        def _(s):
            for r in range(nrt):
                b = r % 2
                if r >= 2:
                    s.wait_ge(eV[0], eV0 + r)
                s.dma_start(out=wt[:, b * NCc:(b + 1) * NCc],
                            in_=w_dram[r * 128:(r + 1) * 128, :]).then_inc(qS[0], 16)
        @block.scalar
        def _(sc):
            for r in range(nrt):
                b = r % 2
                sc.wait_ge(eV[0], eV0 + r + 2)
                sc.dma_start(out=out_dram[r * 128:(r + 1) * 128, :],
                             in_=ob[:, b * NCc:(b + 1) * NCc]).then_inc(qC[0], 16)
            sc.wait_ge(qC[0], qC0 + 16 * nrt)
    qG[1] += 16
    eT[1] += 1
    eV[1] += 1 + nrt
    qS[1] += 16 * nrt
    qC[1] += 16 * nrt
    pb.end_phase()


def _emit_h128(pb, v_eng, h128, w1, w1b, tmp1, tmp2):
    ALUv = ALU
    v = v_eng
    v.tensor_scalar(tmp1[:, :], w1b[:, :], 7, None, ALUv.logical_shift_right)
    v.drain()
    v.tensor_tensor(tmp1[:, :], tmp1[:, :], w1[:, :], ALUv.bitwise_and)
    v.drain()
    v.tensor_scalar(tmp2[:, :], tmp1[:, :], 1, None, ALUv.bitwise_and)
    v.drain()
    for b in range(1, 7):
        v.tensor_scalar(tmp1[:, :], tmp1[:, :], 1, None, ALUv.logical_shift_right)
        v.drain()
        v.tensor_tensor(tmp2[:, :], tmp2[:, :],
                        tmp1[:, :], ALUv.bitwise_xor)
        v.drain()
    v.tensor_scalar(tmp2[:, :], tmp2[:, :], 1, None, ALUv.bitwise_and)
    v.drain()
    v.tensor_scalar(tmp2[:, :], tmp2[:, :], -2, 1, ALUv.mult, ALUv.add)
    v.drain()
    v.tensor_copy(h128[:, :], tmp2[:, :])
    v.drain()


def emit_fwht1(pb, xn_fm_dram, su_dram, xr_fm_dram):
    nc = pb.nc
    pb.begin_phase()
    qS = pb.sem("qS")
    qG = pb.sem("qG")
    qC = pb.sem("qC")
    eT = pb.sem("fw_eT")
    eV = pb.sem("fw_eV")
    eVq = pb.sem("fw_eVq")
    eP = pb.sem("fw_eP")
    xn = pb.sbuf("fw_xn", [128, NCD * S], BF16)
    hsu = pb.sbuf("fw_h", [128, NCD * 128], BF16)
    surow = pb.sbuf("fw_su", [NCD, 128], BF16)
    sucol = pb.sbuf("fw_suc", [128, NCD], F32)
    w1 = pb.sbuf("fw_w1", [128, 128], I32)
    w1b = pb.sbuf("fw_w1b", [128, 128], I32)
    tmp1 = pb.sbuf("fw_t1", [128, 128], I32)
    tmp2 = pb.sbuf("fw_t2", [128, 128], I32)
    h128 = pb.sbuf("fw_h128", [128, 128], BF16)
    ya = pb.sbuf("fw_ya", [128, NCD * 512], BF16)
    yb = pb.sbuf("fw_yb", [128, NCD * 512], BF16)
    yst = pb.sbuf("fw_yst", [128, NCD * 512], BF16)
    pst = pb.psum("fw_pt", [128, 1024], BF16)
    psb = [pb.psum(f"fw_ps{i}", [128, 512]) for i in range(2)]
    qS0, qG0, qC0, eT0, eV0, eVq0 = qS[1], qG[1], qC[1], eT[1], eV[1], eVq[1]

    with nc.Block(f"fwht1_{pb._pctr}") as block:
        @block.gpsimd
        def _(g):
            g.iota(w1[:, :], [[1, 128]], channel_multiplier=0)
            g.iota(w1b[:, :], [[1, 128]], channel_multiplier=128)
            g.drain().then_inc(eP[0], 1)
            g.dma_start(out=surow[:, :], in_=su_dram.rearrange("(r p) -> r p", p=128)
                        ).then_inc(qG[0], 16)
            for c in range(NCD):
                g.dma_start(out=xn[:, c * S:(c + 1) * S], in_=xn_fm_dram[c, :, :]
                            ).then_inc(qG[0], 16)

        @block.tensor
        def _(te):
            te.wait_ge(qG[0], qG0 + 16)
            te.transpose(pst[:, 0:NCD], surow[0:NCD, :], pb.ident[0:NCD, 0:NCD]).then_inc(eT[0], 1)
            te.wait_ge(eV[0], eV0 + 2)
            te.wait_ge(qG[0], qG0 + 16 * (1 + NCD))
            for st in range(4):
                for c in range(NCD):
                    k = st * NCD + c
                    if k >= 2:
                        te.wait_ge(eVq[0], eVq0 + k - 1)
                    te.matmul(psb[k % 2][:, :], hsu[:, c * 128:(c + 1) * 128],
                              xn[:, c * S + st * 512: c * S + (st + 1) * 512],
                              start=True, stop=True).then_inc(eT[0], 1)

        @block.vector
        def _(v):
            v.wait_ge(eT[0], eT0 + 1)
            v.tensor_copy(sucol[:, :], pst[:, 0:NCD])
            v.drain().then_inc(eV[0], 1)
            v.wait_ge(eP[0], eP[1] + 1)
            _emit_h128(pb, v, h128, w1, w1b, tmp1, tmp2)
            for c in range(NCD):
                v.tensor_scalar(hsu[:, c * 128:(c + 1) * 128], h128[:, :],
                                sucol[:, c:c + 1], None, ALU.mult)
            v.drain().then_inc(eV[0], 1)
            for st in range(4):
                for c in range(NCD):
                    k = st * NCD + c
                    v.wait_ge(eT[0], eT0 + 1 + k + 1)
                    v.tensor_copy(ya[:, c * 512:(c + 1) * 512],
                                  psb[k % 2][:, :]).then_inc(eVq[0], 1)
                src, dst = ya, yb
                for stage in range(4):
                    hstep = 1 << stage
                    for pair in range(8):
                        blk = (pair // hstep) * 2 * hstep
                        off = pair % hstep
                        ca, cb = blk + off, blk + off + hstep
                        v.tensor_tensor(dst[:, ca * 512:(ca + 1) * 512],
                                        src[:, ca * 512:(ca + 1) * 512],
                                        src[:, cb * 512:(cb + 1) * 512], ALU.add)
                        v.tensor_tensor(dst[:, cb * 512:(cb + 1) * 512],
                                        src[:, ca * 512:(ca + 1) * 512],
                                        src[:, cb * 512:(cb + 1) * 512], ALU.subtract)
                    src, dst = dst, src
                if st >= 1:
                    v.wait_ge(qC[0], qC0 + 16 * NCD * st)
                v.tensor_copy(yst[:, :], src[:, :])
                v.drain().then_inc(eV[0], 1)

        @block.scalar
        def _(sc):
            for st in range(4):
                sc.wait_ge(eV[0], eV0 + 2 + (st + 1))
                for c in range(NCD):
                    sc.dma_start(out=xr_fm_dram[c, :, st * 512:(st + 1) * 512],
                                 in_=yst[:, c * 512:(c + 1) * 512]).then_inc(qC[0], 16)
            sc.wait_ge(qC[0], qC0 + 16 * NCD * 4)

    qG[1] += 16 * (1 + NCD)
    eP[1] += 1
    eT[1] += 1 + 64
    eV[1] += 2 + 4
    eVq[1] += 64
    qC[1] += 16 * NCD * 4
    pb.end_phase()


def emit_ffn1a(pb, xr_fm_dram, wup_bf, wgate_bf, wsc_dram, sudp_dram, y_dram):
    nc = pb.nc
    pb.begin_phase()
    qS = pb.sem("qS")
    qG = pb.sem("qG")
    qC = pb.sem("qC")
    eT = pb.sem("f1_eT")
    eTv = pb.sem("f1_eTv")
    eSg = pb.sem("f1_eSg")
    eV = pb.sem("f1_eV")
    eVn = pb.sem("f1_eVn")
    eVc = pb.sem("f1_eVc")
    eWf = pb.sem("f1_eWf")
    NCH = INTER1 // 128
    xr = pb.sbuf("f1_xr", [128, NCD * S], BF16)
    wg = pb.sbuf("f1_wg", [128, 2 * NCD * 512], BF16, td=True)
    wu = pb.sbuf("f1_wu", [128, 2 * NCD * 512], BF16, td=True)
    wscrow = pb.sbuf("f1_wr", [128, 128], BF16)
    sdrow = pb.sbuf("f1_sr", [64, 128], BF16)
    ucol = pb.sbuf("f1_uc", [128, NCH], F32)
    gcol = pb.sbuf("f1_gc", [128, NCH], F32)
    gsb = pb.sbuf("f1_g", [128, 512], F32)
    usb = pb.sbuf("f1_u", [128, 512], F32)
    ibuf = pb.sbuf("f1_i", [128, 2 * S], BF16)
    pst = pb.psum("f1_pt", [128, 1024], BF16)
    ps = [pb.psum(f"f1_ps{i}", [128, 512]) for i in range(4)]
    qS0, qG0, qC0 = qS[1], qG[1], qC[1]
    eT0, eTv0, eSg0, eV0, eVn0, eVc0 = eT[1], eTv[1], eSg[1], eV[1], eVn[1], eVc[1]
    eWf0 = eWf[1]
    NG = NCH // 4

    with nc.Block(f"ffn1a_{pb._pctr}") as block:
        @block.gpsimd
        def _(g):
            g.dma_start(out=wscrow[:, :], in_=wsc_dram.rearrange("(r p) -> r p", p=128)
                        ).then_inc(qG[0], 16)
            g.dma_start(out=sdrow[:, :], in_=sudp_dram.rearrange("(r p) -> r p", p=128)
                        ).then_inc(qG[0], 16)
            for c in range(NCD):
                g.dma_start(out=xr[:, c * S:(c + 1) * S], in_=xr_fm_dram[c, :, :]
                            ).then_inc(qG[0], 16)

        @block.tensor
        def _(te):
            te.wait_ge(qG[0], qG0 + 32)
            te.transpose(pst[:, 0:128], wscrow[:, :], pb.ident[:, :]).then_inc(eTv[0], 1)
            te.wait_ge(eVc[0], eVc0 + 1)
            te.transpose(pst[:, 128:192], sdrow[0:64, :], pb.ident[0:64, 0:64]).then_inc(eTv[0], 1)
            te.wait_ge(qG[0], qG0 + 16 * (2 + NCD))
            te.wait_ge(eVc[0], eVc0 + 2)
            for g in range(NG):
                b = g % 2
                te.wait_ge(eWf[0], eWf0 + g + 1)
                for ncg in range(4):
                    for st in range(4):
                        idx = (g * 4 + ncg) * 4 + st
                        if idx >= 2:
                            te.wait_ge(eV[0], eV0 + idx - 1)
                        bg = (st % 2) * 2
                        for d in range(NCD):
                            te.matmul(ps[bg][:, :],
                                      wg[:, (b * NCD + d) * 512 + ncg * 128:
                                         (b * NCD + d) * 512 + (ncg + 1) * 128],
                                      xr[:, d * S + st * 512: d * S + (st + 1) * 512],
                                      start=(d == 0), stop=(d == NCD - 1))
                        for d in range(NCD):
                            ins = te.matmul(ps[bg + 1][:, :],
                                            wu[:, (b * NCD + d) * 512 + ncg * 128:
                                               (b * NCD + d) * 512 + (ncg + 1) * 128],
                                            xr[:, d * S + st * 512: d * S + (st + 1) * 512],
                                            start=(d == 0), stop=(d == NCD - 1))
                        ins.then_inc(eT[0], 1)

        @block.sync
        def _(s):
            for g in range(NG):
                b = g % 2
                if g >= 2:
                    s.wait_ge(eT[0], eT0 + 16 * (g - 1))
                for d in range(NCD):
                    s.dma_start(out=wg[:, (b * NCD + d) * 512:(b * NCD + d + 1) * 512],
                                in_=wgate_bf[g * 512:(g + 1) * 512, d * 128:(d + 1) * 128],
                                transpose=True).then_inc(qS[0], 16)
                    s.dma_start(out=wu[:, (b * NCD + d) * 512:(b * NCD + d + 1) * 512],
                                in_=wup_bf[g * 512:(g + 1) * 512, d * 128:(d + 1) * 128],
                                transpose=True).then_inc(qS[0], 16)
                s.wait_ge(qS[0], qS0 + 16 * 32 * (g + 1))
                s.sem_inc(eWf[0], 1)

        @block.vector
        def _(v):
            isq = float(1.0 / np.sqrt(2048.0))
            v.wait_ge(eTv[0], eTv0 + 1)
            v.tensor_scalar(ucol[:, :], pst[:, 0:64], float(isq), None, ALU.mult)
            v.tensor_scalar(gcol[:, :], pst[:, 64:128], float(isq), None, ALU.mult)
            v.drain().then_inc(eVc[0], 1)
            v.wait_ge(eTv[0], eTv0 + 2)
            v.tensor_tensor(ucol[:, :], ucol[:, :], pst[:, 128:192], ALU.mult)
            v.drain().then_inc(eVc[0], 1)
            for nc_i in range(NCH):
                nb = nc_i % 2
                for st in range(4):
                    idx = nc_i * 4 + st
                    v.wait_ge(eSg[0], eSg0 + idx + 1)
                    if nc_i >= 2 and st == 0:
                        v.wait_ge(qC[0], qC0 + 16 * (nc_i - 1))
                    bg = (st % 2) * 2
                    v.tensor_scalar(usb[:, :], ps[bg + 1][:, :], ucol[:, nc_i:nc_i + 1],
                                    None, ALU.mult)
                    v.tensor_tensor(ibuf[:, nb * S + st * 512: nb * S + (st + 1) * 512],
                                    gsb[:, :], usb[:, :], ALU.mult).then_inc(eV[0], 1)
                v.drain().then_inc(eVn[0], 1)

        @block.scalar
        def _(sc):
            for nc_i in range(NCH):
                for st in range(4):
                    idx = nc_i * 4 + st
                    sc.wait_ge(eT[0], eT0 + idx + 1)
                    if idx >= 1:
                        sc.wait_ge(eV[0], eV0 + idx)
                    bg = (st % 2) * 2
                    sc.activation(gsb[:, :], ps[bg][:, :], AF.Silu,
                                  scale=mst_scale(gcol, nc_i))
                    sc.drain().then_inc(eSg[0], 1)
                if nc_i >= 1:
                    sc.wait_ge(eVn[0], eVn0 + nc_i)
                    pb_ = (nc_i - 1) % 2
                    sc.dma_start(out=y_dram[nc_i - 1, :, :],
                                 in_=ibuf[:, pb_ * S:(pb_ + 1) * S]).then_inc(qC[0], 16)
            sc.wait_ge(eVn[0], eVn0 + NCH)
            sc.dma_start(out=y_dram[NCH - 1, :, :], in_=ibuf[:, S:2 * S]
                         ).then_inc(qC[0], 16)
            sc.wait_ge(qC[0], qC0 + 16 * NCH)

    qG[1] += 16 * (2 + NCD)
    qS[1] += 16 * 32 * NG
    eWf[1] += NG
    eTv[1] += 2
    eVc[1] += 2
    eT[1] += NCH * 4
    eSg[1] += NCH * 4
    eV[1] += NCH * 4
    eVn[1] += NCH
    qC[1] += 16 * NCH
    pb.end_phase()


def mst_scale(col, i):
    return col[:, i:i + 1]


def emit_ffn1b(pb, y_dram, wd_bf, g1_dram):
    nc = pb.nc
    pb.begin_phase()
    qS = pb.sem("qS")
    qG = pb.sem("qG")
    qC = pb.sem("qC")
    eTh = pb.sem("g1_eTh")
    eTk = pb.sem("g1_eTk")
    eVh = pb.sem("g1_eVh")
    eVz = pb.sem("g1_eVz")
    eS = pb.sem("g1_eS")
    eP = pb.sem("g1_eP")
    eWd1 = pb.sem("g1_eWd")
    NCH = INTER1 // 128
    w1 = pb.sbuf("g1_w1", [128, 128], I32)
    w1b = pb.sbuf("g1_w1b", [128, 128], I32)
    tmp1 = pb.sbuf("g1_t1", [128, 128], I32)
    tmp2 = pb.sbuf("g1_t2", [128, 128], I32)
    h128 = pb.sbuf("g1_h", [128, 128], BF16)
    ya = pb.sbuf("g1_ya", [128, NCH * 512], BF16)
    yb = pb.sbuf("g1_yb", [128, NCH * 512], BF16)
    RING = 8
    wd = pb.sbuf("g1_wd", [128, RING * 1024], BF16, td=True)
    stage = pb.sbuf("g1_st", [128, 2 * 4096], F32)
    psd = [pb.psum(f"g1_pd{i}", [128, 512]) for i in range(8)]
    psb = [psd[0], psd[1]]
    qS0, qG0, qC0 = qS[1], qG[1], qC[1]
    eTh0, eTk0, eVh0, eVz0, eS0 = eTh[1], eTk[1], eVh[1], eVz[1], eS[1]
    eWd10 = eWd1[1]

    with nc.Block(f"ffn1b_{pb._pctr}") as block:
        @block.gpsimd
        def _(g):
            g.iota(w1[:, :], [[1, 128]], channel_multiplier=0)
            g.iota(w1b[:, :], [[1, 128]], channel_multiplier=128)
            g.drain().then_inc(eP[0], 1)
            for sg in range(4):
                if sg >= 1:
                    g.wait_ge(eVz[0], eVz0 + sg)
                for c in range(NCH):
                    g.dma_start(out=ya[:, c * 512:(c + 1) * 512],
                                in_=y_dram[c, :, sg * 512:(sg + 1) * 512]
                                ).then_inc(qG[0], 16)

        @block.tensor
        def _(te):
            te.wait_ge(eVh[0], eVh0 + 1)
            for sg in range(4):
                te.wait_ge(qG[0], qG0 + 16 * NCH * (sg + 1))
                if sg >= 1:
                    te.wait_ge(eS[0], eS0 + 2 * sg)
                for c in range(NCH):
                    k = sg * NCH + c
                    if k >= 2:
                        te.wait_ge(eVh[0], eVh0 + 1 + k - 1)
                    te.matmul(psb[k % 2][:, :], h128[:, :],
                              ya[:, c * 512:(c + 1) * 512],
                              start=True, stop=True).then_inc(eTh[0], 1)
                te.wait_ge(eVz[0], eVz0 + sg + 1)
                for mbp in range(2):
                    gi = sg * 2 + mbp
                    if gi >= 1:
                        te.wait_ge(eS[0], eS0 + gi)
                    for nc_i in range(NCH):
                        k2 = gi * NCH + nc_i
                        te.wait_ge(eWd1[0], eWd10 + k2 + 1)
                        for mb2 in range(2):
                            for ss in range(4):
                                ins = te.matmul(psd[mb2 * 4 + ss][:, :],
                                                yb[:, nc_i * 512 + ss * 128:
                                                   nc_i * 512 + (ss + 1) * 128],
                                                wd[:, (k2 % RING) * 1024 + mb2 * 512:
                                                   (k2 % RING) * 1024 + (mb2 + 1) * 512],
                                                start=(nc_i == 0), stop=(nc_i == NCH - 1))
                        ins.then_inc(eTk[0], 1)

        @block.vector
        def _(v):
            v.wait_ge(eP[0], eP[1] + 1)
            _emit_h128(pb, v, h128, w1, w1b, tmp1, tmp2)
            v.drain().then_inc(eVh[0], 1)
            for sg in range(4):
                for c in range(NCH):
                    k = sg * NCH + c
                    v.wait_ge(eTh[0], eTh0 + k + 1)
                    v.tensor_copy(ya[:, c * 512:(c + 1) * 512],
                                  psb[k % 2][:, :]).then_inc(eVh[0], 1)
                if sg >= 1:
                    v.wait_ge(eTk[0], eTk0 + sg * 2 * NCH)
                src, dst = ya, yb
                for stage_i in range(6):
                    hstep = 1 << stage_i
                    for pair in range(32):
                        blk = (pair // hstep) * 2 * hstep
                        off = pair % hstep
                        ca, cb = blk + off, blk + off + hstep
                        v.tensor_tensor(dst[:, ca * 512:(ca + 1) * 512],
                                        src[:, ca * 512:(ca + 1) * 512],
                                        src[:, cb * 512:(cb + 1) * 512], ALU.add)
                        v.tensor_tensor(dst[:, cb * 512:(cb + 1) * 512],
                                        src[:, ca * 512:(ca + 1) * 512],
                                        src[:, cb * 512:(cb + 1) * 512], ALU.subtract)
                    src, dst = dst, src
                if src is not yb:
                    v.tensor_copy(yb[:, :], src[:, :])
                v.drain().then_inc(eVz[0], 1)

        @block.sync
        def _(s):
            for sg in range(4):
                for mbp in range(2):
                    for nc_i in range(NCH):
                        k2 = (sg * 2 + mbp) * NCH + nc_i
                        if k2 >= RING:
                            s.wait_ge(eTk[0], eTk0 + k2 - RING + 1)
                        s.dma_start(out=wd[:, (k2 % RING) * 1024:(k2 % RING + 1) * 1024],
                                    in_=wd_bf[mbp * 1024:(mbp + 1) * 1024,
                                              nc_i * 128:(nc_i + 1) * 128],
                                    transpose=True).then_inc(qS[0], 16)
                        if k2 % 4 == 3:
                            s.wait_ge(qS[0], qS0 + 16 * (k2 + 1))
                            s.sem_inc(eWd1[0], 4)

        @block.scalar
        def _(sc):
            for sg in range(4):
                for mbp in range(2):
                    gi = sg * 2 + mbp
                    sc.wait_ge(eTk[0], eTk0 + (gi + 1) * NCH)
                    if gi >= 2:
                        sc.wait_ge(qC[0], qC0 + 16 * 8 * (gi - 1))
                    bs = (gi % 2) * 4096
                    for mb2 in range(2):
                        for ss in range(4):
                            sc.activation(stage[:, bs + mb2 * 2048 + ss * 512:
                                                bs + mb2 * 2048 + (ss + 1) * 512],
                                          psd[mb2 * 4 + ss][:, :], AF.Silu)
                    sc.drain().then_inc(eS[0], 1)
                    for mb2 in range(2):
                        for ss in range(4):
                            sc.dma_start(out=g1_dram[sg * 512 + ss * 128:
                                                     sg * 512 + (ss + 1) * 128,
                                                     mbp * 1024 + mb2 * 512:
                                                     mbp * 1024 + (mb2 + 1) * 512],
                                         in_=stage[:, bs + mb2 * 2048 + ss * 512:
                                                   bs + mb2 * 2048 + (ss + 1) * 512]
                                         ).then_inc(qC[0], 16)
            sc.wait_ge(qC[0], qC0 + 16 * 64)

    qG[1] += 16 * NCH * 4
    qS[1] += 16 * 8 * NCH
    eWd1[1] += 8 * NCH
    eP[1] += 1
    eTh[1] += 4 * NCH
    eTk[1] += 8 * NCH
    eVh[1] += 1 + 4 * NCH
    eVz[1] += 4
    eS[1] += 16
    qC[1] += 16 * 64
    pb.end_phase()


def emit_l1a(pb, xn_fm_dram, wq1_bf, sq_fm_dram):
    nc = pb.nc
    pb.begin_phase()
    qS = pb.sem("qS")
    qG = pb.sem("qG")
    qC = pb.sem("qC")
    eT = pb.sem("la_eT")
    eV1 = pb.sem("la_eV1")
    eS = pb.sem("la_eS")
    eV2 = pb.sem("la_eV2")
    xn = pb.sbuf("la_xn", [128, NCD * S], BF16)
    wq = pb.sbuf("la_wq", [128, NCD * 4 * 512], BF16, td=True)
    t1 = pb.sbuf("la_t1", [128, 512], F32)
    esb = pb.sbuf("la_e", [128, 512], F32)
    ost = pb.sbuf("la_o", [128, 2 * S], BF16)
    ps = [pb.psum(f"la_ps{i}", [128, 512]) for i in range(2)]
    qS0, qG0, qC0, eT0, eV10, eS0, eV20 = qS[1], qG[1], qC[1], eT[1], eV1[1], eS[1], eV2[1]

    with nc.Block(f"l1a_{pb._pctr}") as block:
        @block.gpsimd
        def _(g):
            for c in range(NCD):
                g.dma_start(out=xn[:, c * S:(c + 1) * S], in_=xn_fm_dram[c, :, :]
                            ).then_inc(qG[0], 16)

        @block.sync
        def _(s):
            for ng in range(4):
                for d in range(NCD):
                    s.dma_start(out=wq[:, (ng * NCD + d) * 512:(ng * NCD + d + 1) * 512],
                                in_=wq1_bf[ng * 512:(ng + 1) * 512, d * 128:(d + 1) * 128],
                                transpose=True).then_inc(qS[0], 16)

        @block.tensor
        def _(te):
            te.wait_ge(qG[0], qG0 + 16 * NCD)
            te.wait_ge(qS[0], qS0 + 16 * 64)
            for n in range(NCD):
                ng, ns = n // 4, n % 4
                for st in range(4):
                    idx = n * 4 + st
                    if idx >= 2:
                        te.wait_ge(eV2[0], eV20 + idx - 1)
                    for d in range(NCD):
                        ins = te.matmul(ps[idx % 2][:, :],
                                        wq[:, (ng * NCD + d) * 512 + ns * 128:
                                           (ng * NCD + d) * 512 + (ns + 1) * 128],
                                        xn[:, d * S + st * 512: d * S + (st + 1) * 512],
                                        start=(d == 0), stop=(d == NCD - 1))
                    ins.then_inc(eT[0], 1)

        @block.vector
        def _(v):
            for n in range(NCD):
                nb = n % 2
                for st in range(4):
                    idx = n * 4 + st
                    v.wait_ge(eT[0], eT0 + idx + 1)
                    if st == 0 and n >= 2:
                        v.wait_ge(qC[0], qC0 + 16 * (n - 1))
                    v.tensor_scalar(t1[:, :], ps[idx % 2][:, :], 0.0, None, ALU.min)
                    v.drain().then_inc(eV1[0], 1)
                    v.wait_ge(eS[0], eS0 + idx + 1)
                    v.scalar_tensor_tensor(ost[:, nb * S + st * 512: nb * S + (st + 1) * 512],
                                           ps[idx % 2][:, :], 0.0, esb[:, :],
                                           ALU.max, ALU.add)
                    v.drain().then_inc(eV2[0], 1)

        @block.scalar
        def _(sc):
            for n in range(NCD):
                for st in range(4):
                    idx = n * 4 + st
                    sc.wait_ge(eV1[0], eV10 + idx + 1)
                    sc.activation(esb[:, :], t1[:, :], AF.Exp)
                    sc.drain().then_inc(eS[0], 1)
                if n >= 1:
                    sc.wait_ge(eV2[0], eV20 + n * 4)
                    sc.wait_ge(qC[0], qC0 + 16 * max(0, n - 1))
                    pb_ = (n - 1) % 2
                    sc.dma_start(out=sq_fm_dram[n - 1, :, :],
                                 in_=ost[:, pb_ * S:(pb_ + 1) * S]).then_inc(qC[0], 16)
            sc.wait_ge(eV2[0], eV20 + 64)
            sc.dma_start(out=sq_fm_dram[NCD - 1, :, :], in_=ost[:, S:2 * S]
                         ).then_inc(qC[0], 16)
            sc.wait_ge(qC[0], qC0 + 16 * NCD)

    qG[1] += 16 * NCD
    qS[1] += 16 * 64
    eT[1] += 64
    eV1[1] += 64
    eS[1] += 64
    eV2[1] += 64
    qC[1] += 16 * NCD
    pb.end_phase()


def emit_l1b(pb, sq_fm_dram, mem_dram, mn_dram, bw_dram, comb_fm_dram):
    nc = pb.nc
    pb.begin_phase()
    qS = pb.sem("qS")
    qG = pb.sem("qG")
    qC = pb.sem("qC")
    eT = pb.sem("lb_eT")
    eTt = pb.sem("lb_eTt")
    eTp = pb.sem("lb_eTp")
    eV = pb.sem("lb_eV")
    eVe = pb.sem("lb_eVe")
    eVp = pb.sem("lb_eVp")
    eSp = pb.sem("lb_eSp")
    sq = pb.sbuf("lb_sq", [128, NCD * S], BF16)
    memn = pb.sbuf("lb_m", [128, 64 * 132], BF16)
    mnrow = pb.sbuf("lb_mnr", [64, 128], BF16)
    mncol = pb.sbuf("lb_mnc", [128, 64], BF16)
    bwrow = pb.sbuf("lb_bw", [1, 64], F32)
    mnfrow = pb.sbuf("lb_mnf", [1, 8192], F32)
    prep = pb.sbuf("lb_pr", [1, 64], F32)
    prep2 = pb.sbuf("lb_pr2", [1, 16], F32)
    wrow = pb.sbuf("lb_wr", [1, 64], BF16)
    wrep = pb.sbuf("lb_wrep", [128, 64], F32)
    rcol = pb.sbuf("lb_rc", [128, 4], F32)
    acc = pb.sbuf("lb_acc", [128, 128], F32)
    accb = pb.sbuf("lb_accb", [128, 128], BF16)
    cst = pb.sbuf("lb_cst", [128, 2 * 128], BF16)
    psA = [pb.psum(f"lb_ps{i}", [128, 512]) for i in range(4)]
    pst = pb.psum("lb_pt", [128, 1024], BF16)
    pstT = [pb.psum(f"lb_pt2{i}", [128, 1024], BF16) for i in range(2)]
    psw = pb.psum("lb_pw", [128, 64])
    qS0, qG0, qC0 = qS[1], qG[1], qC[1]
    eT0, eTt0, eTp0, eV0, eVe0, eVp0, eSp0 = (
        eT[1], eTt[1], eTp[1], eV[1], eVe[1], eVp[1], eSp[1])

    with nc.Block(f"l1b_{pb._pctr}") as block:
        @block.gpsimd
        def _(g):
            g.dma_start(out=mnrow[:, :], in_=mn_dram.rearrange("n h d -> (n h) d")
                        ).then_inc(qG[0], 16)
            for n in range(4):
                for h in range(H):
                    g.dma_start(out=memn[:, (n * 16 + h) * 132:(n * 16 + h) * 132 + 128],
                                in_=mem_dram[n, h, :, :]).then_inc(qG[0], 16)
            for c in range(NCD):
                g.dma_start(out=sq[:, c * S:(c + 1) * S], in_=sq_fm_dram[c, :, :]
                            ).then_inc(qG[0], 16)

        @block.sync
        def _(s):
            s.dma_start(out=bwrow[0:1, :], in_=bw_dram.rearrange("h n -> (h n)")[None, :]
                        ).then_inc(qS[0], 16)
            s.dma_start(out=mnfrow[0:1, :], in_=mn_dram.rearrange("n h d -> (n h d)")[None, :]
                        ).then_inc(qS[0], 16)

        @block.vector
        def _(v):
            v.wait_ge(qS[0], qS0 + 32)
            bw3 = bwrow[0:1, :].rearrange("o (h n) -> o h n", n=4)
            v.tensor_reduce(prep2[0:1, 0:16].unsqueeze(-1),
                            bw3, axis=AXX, op=ALU.max)
            v.drain()
            mx3 = prep2[0:1, 0:16].unsqueeze(-1).broadcast_to((1, 16, 4))
            v.tensor_tensor(prep[0:1, :].rearrange("o (h n) -> o h n", n=4), bw3, mx3,
                            ALU.subtract)
            v.drain().then_inc(eVp[0], 1)
            v.wait_ge(eSp[0], eSp0 + 1)
            pr3 = bwrow[0:1, :].rearrange("o (h n) -> o h n", n=4)
            v.tensor_reduce(prep2[0:1, 0:16].unsqueeze(-1),
                            pr3, axis=AXX, op=ALU.add)
            v.drain()
            v.reciprocal(prep2[0:1, 0:16], prep2[0:1, 0:16])
            v.drain()
            rc3 = prep2[0:1, 0:16].unsqueeze(-1).broadcast_to((1, 16, 4))
            v.tensor_tensor(pr3, pr3, rc3, ALU.mult)
            v.drain()
            mn3 = mnfrow[0:1, :].rearrange("o (n r) -> o n r", n=4)
            v.tensor_reduce(prep2[0:1, 0:4].unsqueeze(-1),
                            mn3, axis=AXX, op=ALU.add)
            v.drain()
            v.tensor_scalar(prep2[0:1, 0:4], prep2[0:1, 0:4], float(EPS_L1), None,
                            ALU.is_ge)
            v.drain()
            act3 = prep2[0:1, 0:4].unsqueeze(1).broadcast_to((1, 16, 4))
            v.tensor_tensor(pr3, pr3, act3, ALU.mult)
            v.drain()
            v.tensor_copy(wrow[0:1, :], bwrow[0:1, :])
            v.drain().then_inc(eVp[0], 1)
            v.wait_ge(eTp[0], eTp0 + 2)
            v.tensor_copy(wrep[:, :], psw[:, :])
            v.tensor_copy(mncol[:, :], pst[:, 0:64])
            v.drain().then_inc(eVp[0], 1)
            for n in range(4):
                for h in range(H):
                    v.tensor_copy(memn[:, (n * 16 + h) * 132 + 128:(n * 16 + h) * 132 + 129],
                                  mncol[:, n * 16 + h:n * 16 + h + 1])
            v.drain().then_inc(eVp[0], 1)
            for h in range(H):
                for st in range(NTS):
                    k = h * NTS + st
                    v.wait_ge(eT[0], eT0 + k + 1)
                    for n in range(4):
                        v.tensor_scalar(rcol[:, n:n + 1], psA[n][:, 128:129],
                                        float(EPS_L1), None, ALU.max)
                    v.drain()
                    v.reciprocal(rcol[:, 0:1], rcol[:, 0:1])
                    v.reciprocal(rcol[:, 1:2], rcol[:, 1:2])
                    v.reciprocal(rcol[:, 2:3], rcol[:, 2:3])
                    v.reciprocal(rcol[:, 3:4], rcol[:, 3:4])
                    v.drain()
                    for n in range(4):
                        v.tensor_tensor(rcol[:, n:n + 1], rcol[:, n:n + 1],
                                        wrep[:, h * 4 + n:h * 4 + n + 1], ALU.mult)
                    v.drain()
                    for n in range(4):
                        if n == 0:
                            v.tensor_scalar(acc[:, :], psA[0][:, 0:128],
                                            rcol[:, 0:1], None, ALU.mult)
                        else:
                            v.scalar_tensor_tensor(acc[:, :], psA[n][:, 0:128],
                                                   rcol[:, n:n + 1], acc[:, :],
                                                   ALU.mult, ALU.add)
                        v.drain()
                    v.tensor_copy(accb[:, :], acc[:, :])
                    v.drain().then_inc(eV[0], 1)
                    v.wait_ge(eTt[0], eTt0 + k + 1)
                    if k >= 2:
                        v.wait_ge(qC[0], qC0 + 16 * (k - 1))
                    v.tensor_copy(cst[:, (k % 2) * 128:(k % 2 + 1) * 128],
                                  pstT[k % 2][:, 0:128]).then_inc(eVe[0], 1)

        @block.scalar
        def _(sc):
            sc.wait_ge(eVp[0], eVp0 + 1)
            sc.activation(bwrow[0:1, :], prep[0:1, :], AF.Exp)
            sc.drain().then_inc(eSp[0], 1)
            for h in range(H):
                for st in range(NTS):
                    k = h * NTS + st
                    sc.wait_ge(eVe[0], eVe0 + k + 1)
                    sc.dma_start(out=comb_fm_dram[h, :, st * 128:(st + 1) * 128],
                                 in_=cst[:, (k % 2) * 128:(k % 2 + 1) * 128]
                                 ).then_inc(qC[0], 16)
            sc.wait_ge(qC[0], qC0 + 16 * 256)

        @block.tensor
        def _(te):
            te.wait_ge(eVp[0], eVp0 + 2)
            te.matmul(psw[:, :], pb.ones1[0:1, 0:128], wrow[0:1, :],
                      start=True, stop=True).then_inc(eTp[0], 1)
            te.wait_ge(qG[0], qG0 + 16)
            te.transpose(pst[:, 0:64], mnrow[0:64, :], pb.ident[0:64, 0:64]).then_inc(eTp[0], 1)
            te.wait_ge(eVp[0], eVp0 + 4)
            te.wait_ge(qG[0], qG0 + 16 * (1 + 64 + NCD))
            for h in range(H):
                for st in range(NTS):
                    k = h * NTS + st
                    if k >= 1:
                        te.wait_ge(eV[0], eV0 + k)
                    for n in range(4):
                        ins = te.matmul(psA[n][:, 0:129],
                                        sq[:, h * S + st * 128:h * S + (st + 1) * 128],
                                        memn[:, (n * 16 + h) * 132:(n * 16 + h) * 132 + 129],
                                        start=True, stop=True)
                    ins.then_inc(eT[0], 1)
                    te.wait_ge(eV[0], eV0 + k + 1)
                    if k >= 2:
                        te.wait_ge(eVe[0], eVe0 + k - 1)
                    te.transpose(pstT[k % 2][:, 0:128],
                                 accb[:, :], pb.ident[:, :]).then_inc(eTt[0], 1)

    qG[1] += 16 * (1 + 64 + NCD)
    qS[1] += 32
    eVp[1] += 4
    eSp[1] += 1
    eTp[1] += 2
    eT[1] += 256
    eTt[1] += 256
    eV[1] += 256
    eVe[1] += 256
    qC[1] += 16 * 256
    pb.end_phase()


def emit_l1o(pb, comb_fm_dram, wo1_bf, s1_dram):
    nc = pb.nc
    pb.begin_phase()
    qS = pb.sem("qS")
    qG = pb.sem("qG")
    qC = pb.sem("qC")
    eT = pb.sem("lo_eT")
    eV = pb.sem("lo_eV")
    comb = pb.sbuf("lo_c", [128, NCD * S], BF16)
    wt = pb.sbuf("lo_w", [128, NCD * 4 * 512], BF16, td=True)
    stage = pb.sbuf("lo_st", [128, 2 * D], F32)
    ps = [pb.psum(f"lo_ps{i}", [128, 512]) for i in range(8)]
    qS0, qG0, qC0, eT0, eV0 = qS[1], qG[1], qC[1], eT[1], eV[1]

    with nc.Block(f"l1o_{pb._pctr}") as block:
        @block.gpsimd
        def _(g):
            for c in range(NCD):
                g.dma_start(out=comb[:, c * S:(c + 1) * S], in_=comb_fm_dram[c, :, :]
                            ).then_inc(qG[0], 16)

        @block.sync
        def _(s):
            for mb in range(4):
                for d in range(NCD):
                    s.dma_start(out=wt[:, (mb * NCD + d) * 512:(mb * NCD + d + 1) * 512],
                                in_=wo1_bf[mb * 512:(mb + 1) * 512, d * 128:(d + 1) * 128],
                                transpose=True).then_inc(qS[0], 16)

        @block.tensor
        def _(te):
            te.wait_ge(qG[0], qG0 + 16 * NCD)
            te.wait_ge(qS[0], qS0 + 16 * 64)
            for t in range(NTS):
                if t >= 2:
                    te.wait_ge(eV[0], eV0 + t - 1)
                bank = (t % 2) * 4
                for mb in range(4):
                    for d in range(NCD):
                        ins = te.matmul(ps[bank + mb][:, :],
                                        comb[:, d * S + t * 128: d * S + (t + 1) * 128],
                                        wt[:, (mb * NCD + d) * 512:(mb * NCD + d + 1) * 512],
                                        start=(d == 0), stop=(d == NCD - 1))
                ins.then_inc(eT[0], 1)

        @block.vector
        def _(v):
            for t in range(NTS):
                v.wait_ge(eT[0], eT0 + t + 1)
                if t >= 2:
                    v.wait_ge(qC[0], qC0 + 16 * (t - 1))
                b = (t % 2) * D
                bank = (t % 2) * 4
                for mb in range(4):
                    v.tensor_copy(stage[:, b + mb * 512: b + (mb + 1) * 512],
                                  ps[bank + mb][:, :])
                v.drain().then_inc(eV[0], 1)

        @block.scalar
        def _(sc):
            for t in range(NTS):
                sc.wait_ge(eV[0], eV0 + t + 1)
                sc.dma_start(out=s1_dram[t * 128:(t + 1) * 128, :],
                             in_=stage[:, (t % 2) * D:(t % 2 + 1) * D]).then_inc(qC[0], 16)
            sc.wait_ge(qC[0], qC0 + 16 * NTS)

    qG[1] += 16 * NCD
    qS[1] += 16 * 64
    eT[1] += NTS
    eV[1] += NTS
    qC[1] += 16 * NTS
    pb.end_phase()


def emit_res(pb, x_dram, s_dram, g_dram, out_dram):
    nc = pb.nc
    pb.begin_phase()
    qS = pb.sem("qS")
    qG = pb.sem("qG")
    eV = pb.sem("rs_eV")
    xt = pb.sbuf("rs_x", [128, 2 * D], F32)
    st_t = pb.sbuf("rs_s", [128, 2 * D], F32)
    gt = pb.sbuf("rs_g", [128, 2 * D], F32)
    qS0, qG0, eV0 = qS[1], qG[1], eV[1]

    with nc.Block(f"res_{pb._pctr}") as block:
        @block.sync
        def _(s):
            for t in range(NTS):
                b = t % 2
                if t >= 2:
                    s.wait_ge(qG[0], qG0 + 16 * (t - 1))
                s.dma_start(out=xt[:, b * D:(b + 1) * D],
                            in_=x_dram[t * 128:(t + 1) * 128, :]).then_inc(qS[0], 16)
                s.dma_start(out=st_t[:, b * D:(b + 1) * D],
                            in_=s_dram[t * 128:(t + 1) * 128, :]).then_inc(qS[0], 16)
                s.dma_start(out=gt[:, b * D:(b + 1) * D],
                            in_=g_dram[t * 128:(t + 1) * 128, :]).then_inc(qS[0], 16)

        @block.vector
        def _(v):
            for t in range(NTS):
                b = t % 2
                sl = slice(b * D, (b + 1) * D)
                v.wait_ge(qS[0], qS0 + 48 * (t + 1))
                v.tensor_tensor(st_t[:, sl], st_t[:, sl], gt[:, sl], ALU.mult)
                v.tensor_tensor(xt[:, sl], xt[:, sl], st_t[:, sl], ALU.add)
                v.drain().then_inc(eV[0], 1)

        @block.gpsimd
        def _(g):
            for t in range(NTS):
                b = t % 2
                g.wait_ge(eV[0], eV0 + t + 1)
                g.dma_start(out=out_dram[t * 128:(t + 1) * 128, :],
                            in_=xt[:, b * D:(b + 1) * D]).then_inc(qG[0], 16)
            g.wait_ge(qG[0], qG0 + 16 * NTS)

    qS[1] += 48 * NTS
    eV[1] += NTS
    qG[1] += 16 * NTS
    pb.end_phase()


def build_program():
    pb = PB()
    nc = pb.nc
    ein = lambda n, sh, dt=F32: nc.dram_tensor(n, sh, dt, kind="ExternalInput")
    x0 = ein("x0", [S, D])
    wq0 = ein("L0_Wq", [D, D]); bq0 = ein("L0_bq", [D])
    wk0 = ein("L0_Wk", [D, D]); bk0 = ein("L0_bk", [D])
    wv0 = ein("L0_Wv", [D, D]); bv0 = ein("L0_bv", [D])
    wo0 = ein("L0_Wo", [D, D]); bo0 = ein("L0_bo", [D])
    gWp = ein("gWp", [MLP0, 1024], I32); gSc = ein("gSc", [MLP0, 64])
    uWp = ein("uWp", [MLP0, 1024], I32); uSc = ein("uSc", [MLP0, 64])
    dWp = ein("dWp", [D, 8192], I32); dSc = ein("dSc", [D, 512])
    wq1 = ein("L1_Wq", [D, D]); wo1 = ein("L1_Wo", [D, D])
    bw1 = ein("L1_bw", [16, 4])
    mem1 = ein("L1_mem", [4, 16, 128, 128]); mn1 = ein("L1_mn", [4, 16, 128])
    su_ug = ein("su_ug", [D]); su_dp = ein("su_dp", [INTER1])
    wsc_ug = ein("wsc_ug", [2 * INTER1]); wsc_dp = ein("wsc_dp", [D])
    w_up = ein("w_up", [INTER1, D]); w_gate = ein("w_gate", [INTER1, D])
    w_down = ein("w_down", [D, INTER1])
    out = nc.dram_tensor("out", [S, D], F32, kind="ExternalOutput")
    dt = lambda n, sh, d=BF16: nc.dram_tensor(n, sh, d)
    xn0 = dt("xn0", [NCD, 128, S])
    q_fm = dt("q_fm", [H, 128, S]); k_fm = dt("k_fm", [H, 128, S])
    v_tm = dt("v_tm", [NTS, 128, D]); ctx_fm = dt("ctx_fm", [H, 128, S])
    s0 = dt("s0", [S, D], F32)
    wg_bf = dt("wg_bf", [MLP0, D]); wu_bf = dt("wu_bf", [MLP0, D])
    wd0_bf = dt("wd0_bf", [D, MLP0])
    inter = dt("inter", [128, 128, S])
    g0 = dt("g0", [S, D], F32)
    x1 = dt("x1", [S, D], F32)
    xn1 = dt("xn1", [NCD, 128, S])
    wq1_bf = dt("wq1_bf", [D, D]); wo1_bf = dt("wo1_bf", [D, D])
    wup_bf = dt("wup_bf", [INTER1, D]); wg1_bf = dt("wg1_bf", [INTER1, D])
    wd1_bf = dt("wd1_bf", [D, INTER1])
    sq_fm = dt("sq_fm", [NCD, 128, S]); comb_fm = dt("comb_fm", [NCD, 128, S])
    s1 = dt("s1", [S, D], F32)
    xr_fm = dt("xr_fm", [NCD, 128, S])
    y1 = dt("y1", [64, 128, S])
    g1 = dt("g1", [S, D], F32)

    pb.consts()
    emit_ln(pb, x0, xn0)
    emit_qkv(pb, xn0, wq0, bq0, wk0, bk0, wv0, bv0, q_fm, k_fm, v_tm)
    emit_att0(pb, q_fm, k_fm, v_tm, ctx_fm)
    emit_wo0(pb, ctx_fm, wo0, bo0, s0)
    emit_ffn0_fused(pb, xn0, gWp, gSc, uWp, uSc, dWp, dSc,
                    wg_bf, wu_bf, wd0_bf, inter)
    emit_ffn0b(pb, inter, wd0_bf, g0, nchunks=128, act="silu")
    emit_ln(pb, x0, xn1, s_in=s0, g_in=g0, x_out=x1)
    emit_casts(pb, [(wq1, wq1_bf, D, D), (wo1, wo1_bf, D, D),
                    (w_up, wup_bf, INTER1, D), (w_gate, wg1_bf, INTER1, D)])
    emit_scaled_cast(pb, w_down, wsc_dp, wd1_bf, D, INTER1, 1.0 / np.sqrt(float(INTER1)))
    emit_l1a(pb, xn1, wq1_bf, sq_fm)
    emit_l1b(pb, sq_fm, mem1, mn1, bw1, comb_fm)
    emit_l1o(pb, comb_fm, wo1_bf, s1)
    emit_fwht1(pb, xn1, su_ug, xr_fm)
    emit_ffn1a(pb, xr_fm, wup_bf, wg1_bf, wsc_ug, su_dp, y1)
    emit_ffn1b(pb, y1, wd1_bf, g1)
    emit_res(pb, x1, s1, g1, out)
    return pb


_PROG_CACHE = {}


def kernel(**inputs):
    if "prog" not in _PROG_CACHE:
        _PROG_CACHE["prog"] = build_program()
    pb = _PROG_CACHE["prog"]
    x = np.ascontiguousarray(inputs["L0_attn_input_sequences"])
    _ = inputs["L1_attn_Wk"], inputs["L1_attn_Wv"]
    common = {
        "L0_Wq": inputs["L0_attn_Wq"], "L0_bq": inputs["L0_attn_bq"],
        "L0_Wk": inputs["L0_attn_Wk"], "L0_bk": inputs["L0_attn_bk"],
        "L0_Wv": inputs["L0_attn_Wv"], "L0_bv": inputs["L0_attn_bv"],
        "L0_Wo": inputs["L0_attn_Wo"], "L0_bo": inputs["L0_attn_bo"],
        "gWp": inputs["L0_ffn_gate_W_packed"], "gSc": inputs["L0_ffn_gate_scales"],
        "uWp": inputs["L0_ffn_up_W_packed"], "uSc": inputs["L0_ffn_up_scales"],
        "dWp": inputs["L0_ffn_down_W_packed"], "dSc": inputs["L0_ffn_down_scales"],
        "L1_Wq": inputs["L1_attn_Wq"], "L1_Wo": inputs["L1_attn_Wo"],
        "L1_bw": inputs["L1_attn_bank_weights"],
        "L1_mem": inputs["L1_attn_memories"], "L1_mn": inputs["L1_attn_memory_norms"],
        "su_ug": inputs["L1_ffn_SU_ug"], "su_dp": inputs["L1_ffn_SU_dp"],
        "wsc_ug": inputs["L1_ffn_Wscale_ug"], "wsc_dp": inputs["L1_ffn_Wscale_dp"],
        "w_up": inputs["L1_ffn_W_up"], "w_gate": inputs["L1_ffn_W_gate"],
        "w_down": inputs["L1_ffn_W_down"],
    }
    common = {k: np.ascontiguousarray(v) for k, v in common.items()}
    in_maps = [dict(common, x0=x[i]) for i in range(8)]
    res = run_bass_kernel_spmd(pb.nc, in_maps, core_ids=list(range(8)))
    return np.stack([np.asarray(res.results[i]["out"]) for i in range(8)], axis=0)


def emit_ffn0_fused(pb, xn_fm_dram, gWp, gSc, uWp, uSc, dWp, dSc,
                    wg_bf, wu_bf, wd0_bf, inter_dram):
    nc = pb.nc
    pb.begin_phase()
    KP, KO, NB, R = 512, 1024, 32, 5
    qGp = pb.sem("fz_qGp")
    qGc = pb.sem("fz_qGc")
    qGs = pb.sem("fz_qGs")
    eLd = pb.sem("fz_eLd")
    eSt = pb.sem("fz_eSt")
    eWg = pb.sem("fz_eWg")
    qS = pb.sem("qS")
    qC = pb.sem("qC")
    dV0 = pb.sem("fz_dV0")
    dV1 = pb.sem("fz_dV1")
    dV2 = pb.sem("fz_dV2")
    dS1 = pb.sem("fz_dS1")
    dS2 = pb.sem("fz_dS2")
    dP = pb.sem("fz_dP")
    fT = pb.sem("fz_fT")
    fSg = pb.sem("fz_fSg")
    fV = pb.sem("fz_fV")
    fVn = pb.sem("fz_fVn")
    wt = pb.sbuf("fz_w", [128, R * KP], I32)
    sct = pb.sbuf("fz_sc", [128, R * NB], F32)
    nibD = pb.sbuf("fz_nD", [128, R * KO], I32)
    Df = nibD[:, :].bitcast(F32)
    A = pb.sbuf("fz_A", [128, R * KO], F32)
    B = pb.sbuf("fz_B", [128, R * KO], F32)
    C = pb.sbuf("fz_C", [128, R * KO], F32)
    obv = None
    ob = B[:, :].bitcast(BF16)
    xn = pb.sbuf("fz_xn", [128, NCD * S], BF16)
    wg = pb.sbuf("fz_wg", [128, NCD * 512], BF16, td=True)
    wu = pb.sbuf("fz_wu", [128, NCD * 512], BF16, td=True)
    gsb = pb.sbuf("fz_g", [128, 512], F32)
    ibuf = pb.sbuf("fz_i", [128, 2 * S], BF16)
    ps = [pb.psum(f"fz_ps{i}", [128, 512]) for i in range(4)]

    TILES = []
    for g in range(32):
        for r in range(4 * g, 4 * g + 4):
            for kc in range(2):
                TILES.append((gWp, gSc, wg_bf, r, kc))
        for r in range(4 * g, 4 * g + 4):
            for kc in range(2):
                TILES.append((uWp, uSc, wu_bf, r, kc))
    for r in range(16):
        for kc in range(16):
            TILES.append((dWp, dSc, wd0_bf, r, kc))
    T = len(TILES)
    NCHUNK = T // 16

    qGp0, qGc0, qGs0, qS0, qC0 = qGp[1], qGc[1], qGs[1], qS[1], qC[1]
    eLd0, eSt0, eWg0 = eLd[1], eSt[1], eWg[1]
    dV00, dV10, dV20, dS10, dS20, dP0 = dV0[1], dV1[1], dV2[1], dS1[1], dS2[1], dP[1]
    fT0, fSg0, fV0, fVn0 = fT[1], fSg[1], fV[1], fVn[1]

    def dq_src(t):
        pk, sc_, dst, r, kc = TILES[t]
        return (pk[r * 128:(r + 1) * 128, kc * KP:(kc + 1) * KP],
                sc_[r * 128:(r + 1) * 128, kc * NB:(kc + 1) * NB],
                dst[r * 128:(r + 1) * 128, kc * KO:(kc + 1) * KO])

    def sl(t, n):
        s_ = t % R
        return slice(s_ * n, (s_ + 1) * n)

    LEAD = 24
    def events():
        out = []
        tnext = 0
        for j in range(512):
            target = min(T, LEAD + ((j + 1) * (T - LEAD)) // 512)
            while tnext < target:
                out.append(("dq", tnext))
                tnext += 1
            out.append(("ev", j))
        while tnext < T:
            out.append(("dq", tnext))
            tnext += 1
        for t in range(T, T + 6):
            out.append(("dq", t))
        return out

    with nc.Block(f"ffz_{pb._pctr}") as block:
        @block.gpsimd
        def _(g_):
            for t in range(T + 6):
                if t < T:
                    pk, sc_, _d = dq_src(t)
                    if t >= R:
                        g_.wait_ge(dV0[0], dV00 + t - R + 1)
                        g_.wait_ge(dV2[0], dV20 + t - R + 1)
                    g_.dma_start(out=wt[:, sl(t, KP)], in_=pk).then_inc(qGp[0], 16)
                    g_.dma_start(out=sct[:, sl(t, NB)], in_=sc_).then_inc(qGc[0], 16)
                    if t % 2 == 1:
                        g_.wait_ge(qGp[0], qGp0 + 16 * (t + 1))
                        g_.wait_ge(qGc[0], qGc0 + 16 * (t + 1))
                        g_.sem_inc(eLd[0], 2)
                if t >= R and t - R < T:
                    t3 = t - R
                    g_.wait_ge(dV2[0], dV20 + t3 + 1)
                    _pk, _sc, dst = dq_src(t3)
                    s3 = (t3 % R) * KO * 2
                    g_.dma_start(out=dst, in_=ob[:, s3:s3 + KO]).then_inc(qGs[0], 16)
                    if t3 % 16 == 15:
                        g_.wait_ge(qGs[0], qGs0 + 16 * (t3 + 1))
                        g_.sem_inc(eSt[0], 1)
            g_.wait_ge(qGs[0], qGs0 + 16 * T)

        @block.vector
        def _(v):
            for kind, t in events():
                if kind == "dq":
                    if t >= 4 and t - 4 < T:
                        t2 = t - 4
                        v.wait_ge(dS2[0], dS20 + t2 + 1)
                        v.tensor_tensor(A[:, sl(t2, KO)], A[:, sl(t2, KO)],
                                        Df[:, sl(t2, KO)], ALU.add)
                        v.scalar_tensor_tensor(C[:, sl(t2, KO)], C[:, sl(t2, KO)], 0.5,
                                               A[:, sl(t2, KO)], ALU.mult, ALU.add)
                        v.scalar_tensor_tensor(C[:, sl(t2, KO)], C[:, sl(t2, KO)], -2.0,
                                               B[:, sl(t2, KO)], ALU.add, ALU.mult)
                        s2o = (t2 % R) * KO * 2
                        cv = C[:, sl(t2, KO)].rearrange("p (bk j) -> p bk j", j=32)
                        ov = ob[:, s2o:s2o + KO].rearrange("p (bk j) -> p bk j", j=32)
                        scb = sct[:, sl(t2, NB)].unsqueeze(-1).broadcast_to((128, NB, 32))
                        v.tensor_tensor(ov, cv, scb, ALU.mult).then_inc(dV2[0], 1)
                    if t >= 2 and t - 2 < T:
                        t1 = t - 2
                        v.wait_ge(dS1[0], dS10 + t1 + 1)
                        v.scalar_tensor_tensor(C[:, sl(t1, KO)], B[:, sl(t1, KO)], 4.0,
                                               A[:, sl(t1, KO)], ALU.mult,
                                               ALU.add).then_inc(dV1[0], 1)
                    if t < T:
                        v.wait_ge(eLd[0], eLd0 + t + 1)
                        if t >= R:
                            v.wait_ge(dS1[0], dS10 + t - R + 1)
                            v.wait_ge(dS2[0], dS20 + t - R + 1)
                        nv = nibD[:, sl(t, KO)].rearrange("p (k two) -> p k two", two=2)
                        wv = wt[:, sl(t, KP)].unsqueeze(-1)
                        v.tensor_scalar(nv[:, :, 0:1], wv, 15, None, ALU.bitwise_and)
                        v.tensor_scalar(nv[:, :, 1:2], wv, 4, 15,
                                        ALU.logical_shift_right,
                                        ALU.bitwise_and).then_inc(dV0[0], 1)
                else:
                    j = t
                    nci, st = j // 4, j % 4
                    v.wait_ge(fSg[0], fSg0 + j + 1)
                    if st == 0 and nci >= 2:
                        v.wait_ge(qC[0], qC0 + 16 * (nci - 1))
                    bg = st % 2
                    ins = v.tensor_tensor(
                        ibuf[:, (nci % 2) * S + st * 512:(nci % 2) * S + (st + 1) * 512],
                        gsb[:, :], ps[bg * 2 + 1][:, :], ALU.mult)
                    ins.then_inc(fV[0], 1)
                    if st == 3:
                        v.drain().then_inc(fVn[0], 1)

        @block.scalar
        def _(sc_):
            for kind, t in events():
                if kind == "dq":
                    if t >= 1 and t - 1 < T:
                        t0 = t - 1
                        sc_.wait_ge(dV0[0], dV00 + t0 + 1)
                        if t0 >= R:
                            sc_.wait_ge(dV2[0], dV20 + t0 - R + 1)
                            sc_.wait_ge(qGs[0], qGs0 + 16 * (t0 - R + 1))
                        sc_.activation(A[:, sl(t0, KO)], nibD[:, sl(t0, KO)], AF.Copy,
                                       bias=0.0)
                        sc_.activation(B[:, sl(t0, KO)], A[:, sl(t0, KO)], AF.Sign,
                                       bias=f32(7.5), scale=f32(-1.0)).then_inc(dS1[0], 1)
                    if t >= 3 and t - 3 < T:
                        t1 = t - 3
                        sc_.wait_ge(dV1[0], dV10 + t1 + 1)
                        sc_.activation(A[:, sl(t1, KO)], C[:, sl(t1, KO)], AF.Relu,
                                       bias=f32(-4.0), scale=f32(0.5))
                        sc_.activation(Df[:, sl(t1, KO)], C[:, sl(t1, KO)], AF.Relu,
                                       bias=f32(-10.0)).then_inc(dS2[0], 1)
                else:
                    j = t
                    nci, st = j // 4, j % 4
                    sc_.wait_ge(fT[0], fT0 + j + 1)
                    if j >= 1:
                        sc_.wait_ge(fV[0], fV0 + j)
                    bg = st % 2
                    sc_.activation(gsb[:, :], ps[bg * 2][:, :],
                                   AF.Gelu_apprx_tanh).then_inc(fSg[0], 1)
                    if st == 3 and nci >= 1:
                        sc_.wait_ge(fVn[0], fVn0 + nci)
                        sc_.dma_start(out=inter_dram[nci - 1, :, :],
                                      in_=ibuf[:, ((nci - 1) % 2) * S:
                                               ((nci - 1) % 2 + 1) * S]
                                      ).then_inc(qC[0], 16)
            sc_.wait_ge(fVn[0], fVn0 + 128)
            sc_.dma_start(out=inter_dram[127, :, :], in_=ibuf[:, S:2 * S]
                          ).then_inc(qC[0], 16)
            sc_.wait_ge(qC[0], qC0 + 16 * 128)

        @block.sync
        def _(s):
            for cc in range(NCD):
                s.dma_start(out=xn[:, cc * S:(cc + 1) * S], in_=xn_fm_dram[cc, :, :]
                            ).then_inc(qS[0], 16)
            for g in range(32):
                s.wait_ge(eSt[0], eSt0 + g + 1)
                if g >= 1:
                    s.wait_ge(fT[0], fT0 + 16 * g)
                for d in range(NCD):
                    s.dma_start(out=wg[:, d * 512:(d + 1) * 512],
                                in_=wg_bf[g * 512:(g + 1) * 512, d * 128:(d + 1) * 128],
                                transpose=True).then_inc(qS[0], 16)
                for d in range(NCD):
                    s.dma_start(out=wu[:, d * 512:(d + 1) * 512],
                                in_=wu_bf[g * 512:(g + 1) * 512, d * 128:(d + 1) * 128],
                                transpose=True).then_inc(qS[0], 16)
                s.wait_ge(qS[0], qS0 + 16 * (NCD + 32 * (g + 1)))
                s.sem_inc(eWg[0], 1)

        @block.tensor
        def _(te):
            te.wait_ge(qS[0], qS0 + 16 * NCD)
            for g in range(32):
                te.wait_ge(eWg[0], eWg0 + g + 1)
                for ncg in range(4):
                    for st in range(4):
                        idx = (g * 4 + ncg) * 4 + st
                        if idx >= 2:
                            te.wait_ge(fV[0], fV0 + idx - 1)
                        bg = st % 2
                        for d in range(NCD):
                            te.matmul(ps[bg * 2][:, :],
                                      wg[:, d * 512 + ncg * 128: d * 512 + (ncg + 1) * 128],
                                      xn[:, d * S + st * 512: d * S + (st + 1) * 512],
                                      start=(d == 0), stop=(d == NCD - 1))
                        for d in range(NCD):
                            ins = te.matmul(ps[bg * 2 + 1][:, :],
                                            wu[:, d * 512 + ncg * 128: d * 512 + (ncg + 1) * 128],
                                            xn[:, d * S + st * 512: d * S + (st + 1) * 512],
                                            start=(d == 0), stop=(d == NCD - 1))
                        ins.then_inc(fT[0], 1)

    qGp[1] += 16 * T
    qGc[1] += 16 * T
    qGs[1] += 16 * T
    qS[1] += 16 * (NCD + 32 * 32)
    qC[1] += 16 * 128
    eLd[1] += T
    eSt[1] += 48
    eWg[1] += 32
    dV0[1] += T
    dV1[1] += T
    dV2[1] += T
    dS1[1] += T
    dS2[1] += T
    fT[1] += 512
    fSg[1] += 512
    fV[1] += 512
    fVn[1] += 128
    pb.end_phase()


# version 15
# speedup vs baseline: 1.0252x; 1.0080x over previous
import numpy as np
import ml_dtypes
from contextlib import ExitStack
import concourse.bass as bass
import concourse.mybir as mybir
from concourse.bass_utils import run_bass_kernel_spmd

F32 = mybir.dt.float32
BF16 = mybir.dt.bfloat16
I32 = mybir.dt.int32
AF = mybir.ActivationFunctionType
ALU = mybir.AluOpType
AXX = mybir.AxisListType.X
f32 = np.float32

D = 2048
S = 2048
H = 16
NCD = D // 128
NTS = S // 128
MLP0 = 16384
INTER1 = 8192
EPS_LN = 1e-5
EPS_L1 = 1e-6
ISQ_HD = float(1.0 / np.sqrt(128.0))
BF = np.dtype(ml_dtypes.bfloat16)


class PB:

    def __init__(self):
        self.nc = bass.Bass("TRN2", target_bir_lowering=False)
        self.stack = ExitStack()
        self.cur = self.stack
        self.sems = {}
        self._consts = False
        self._pctr = 0

    def begin_phase(self):
        self._pctr += 1
        self.cur = ExitStack()
        return self._pctr

    def end_phase(self):
        self.cur.close()
        self.cur = self.stack

    def sem(self, name):
        if name not in self.sems:
            h = self.stack.enter_context(self.nc.semaphore(name))
            self.sems[name] = [h, 0]
        return self.sems[name]

    def sbuf(self, name, shape, dt, td=False, global_=False):
        st = self.stack if global_ else self.cur
        return st.enter_context(
            self.nc.sbuf_tensor(f"{name}_{self._pctr}", list(shape), dt,
                                is_transpose_dest=td))

    def psum(self, name, shape, dt=F32):
        return self.cur.enter_context(
            self.nc.psum_tensor(f"{name}_{self._pctr}", list(shape), dt))

    def consts(self):
        if self._consts:
            return
        self._consts = True
        nc = self.nc
        self.ident = self.sbuf("c_ident", [128, 128], BF16, global_=True)
        self.trimask = self.sbuf("c_tri", [128, 128], BF16, global_=True)
        self.utri = self.sbuf("c_utri", [128, 128], I32, global_=True)
        self.neg30 = self.sbuf("c_n30", [128, 128], F32, global_=True)
        self.ones1 = self.sbuf("c_ones1", [1, 512], BF16, global_=True)
        w1 = self.sbuf("c_iw1", [128, 128], I32)
        w1b = self.sbuf("c_iw1b", [128, 128], I32)
        w2 = self.sbuf("c_iw2", [128, 128], I32)
        with nc.Block("consts") as block:
            @block.gpsimd
            def _(g):
                g.iota(w1[:, :], [[1, 128]], channel_multiplier=0)
                g.iota(w1b[:, :], [[1, 128]], channel_multiplier=128)
            @block.vector
            def _(v):
                v.memset(self.ones1[:, :], 1.0)
        with nc.Block("consts2") as block:
            @block.vector
            def _(v):
                v.tensor_scalar(w2[:, :], w1[:, :], 129, None, ALU.mult)
                v.tensor_tensor(w2[:, :], w2[:, :], w1b[:, :], ALU.is_equal)
                v.tensor_copy(self.ident[:, :], w2[:, :])
                v.tensor_tensor(w2[:, :], w1b[:, :], w1[:, :], ALU.subtract)
                v.tensor_scalar(w2[:, :], w2[:, :], 7, None, ALU.logical_shift_right)
                v.tensor_tensor(w2[:, :], w1[:, :], w2[:, :], ALU.is_le)
                v.tensor_copy(self.trimask[:, :], w2[:, :])
                v.tensor_scalar(self.utri[:, :], w2[:, :], -1, 1, ALU.mult, ALU.add)
                v.memset(self.neg30[:, :], -1e30)


def emit_ln(pb, x_dram, xn_fm_dram, s_in=None, g_in=None, x_out=None):
    nc = pb.nc
    pb.begin_phase()
    qS = pb.sem("qS")
    qG = pb.sem("qG")
    eV = pb.sem("eV")
    eS = pb.sem("eS")
    eT = pb.sem("eT")
    fused = s_in is not None
    nper = 3 if fused else 1
    nv = 3 if fused else 2
    xt = pb.sbuf("ln_x", [128, 2 * D], F32)
    st_t = pb.sbuf("ln_s", [128, 2 * D], F32) if fused else None
    gt = pb.sbuf("ln_g", [128, 2 * D], F32) if fused else None
    xsq = pb.sbuf("ln_xsq", [128, D], F32)
    stat = pb.sbuf("ln_stat", [128, 8], F32)
    xn_tm = pb.sbuf("ln_xntm", [128, NTS * D], BF16)
    qS0, qG0, eV0, eS0, eT0 = qS[1], qG[1], eV[1], eS[1], eT[1]

    with nc.Block(f"ln_a_{pb._pctr}") as block:
        @block.sync
        def _(s):
            for t in range(NTS):
                b = t % 2
                if t >= 2:
                    s.wait_ge(eV[0], eV0 + nv * (t - 1))
                    if fused:
                        s.wait_ge(qG[0], qG0 + 16 * (t - 1))
                s.dma_start(out=xt[:, b * D:(b + 1) * D],
                            in_=x_dram[t * 128:(t + 1) * 128, :]).then_inc(qS[0], 16)
                if fused:
                    s.dma_start(out=st_t[:, b * D:(b + 1) * D],
                                in_=s_in[t * 128:(t + 1) * 128, :]).then_inc(qS[0], 16)
                    s.dma_start(out=gt[:, b * D:(b + 1) * D],
                                in_=g_in[t * 128:(t + 1) * 128, :]).then_inc(qS[0], 16)

        @block.vector
        def _(v):
            for t in range(NTS):
                b = t % 2
                sl = slice(b * D, (b + 1) * D)
                v.wait_ge(qS[0], qS0 + 16 * nper * (t + 1))
                if fused:
                    v.tensor_tensor(st_t[:, sl], st_t[:, sl], gt[:, sl], ALU.mult)
                    v.tensor_tensor(xt[:, sl], xt[:, sl], st_t[:, sl], ALU.add)
                    v.drain().then_inc(eV[0], 1)
                v.tensor_reduce(stat[:, 0:1], xt[:, sl], axis=AXX, op=ALU.add)
                v.wait_ge(eS[0], eS0 + 2 * t + 1)
                v.tensor_reduce(stat[:, 1:2], xsq[:, :], axis=AXX, op=ALU.add)
                v.drain()
                v.tensor_scalar(stat[:, 2:3], stat[:, 0:1], float(1.0 / D), None, ALU.mult)
                v.drain()
                v.tensor_tensor(stat[:, 3:4], stat[:, 2:3], stat[:, 2:3], ALU.mult)
                v.drain()
                v.scalar_tensor_tensor(stat[:, 3:4], stat[:, 1:2], float(1.0 / D),
                                       stat[:, 3:4], ALU.mult, ALU.subtract)
                v.drain()
                v.tensor_scalar(stat[:, 3:4], stat[:, 3:4], float(EPS_LN), None, ALU.add)
                v.drain().then_inc(eV[0], 1)
                v.wait_ge(eS[0], eS0 + 2 * t + 2)
                v.reciprocal(stat[:, 5:6], stat[:, 4:5])
                v.drain()
                v.scalar_tensor_tensor(stat[:, 6:7], stat[:, 2:3], float(-1.0),
                                       stat[:, 5:6], ALU.mult, ALU.mult)
                v.drain()
                v.tensor_scalar(xn_tm[:, t * D:(t + 1) * D], xt[:, sl],
                                stat[:, 5:6], stat[:, 6:7], ALU.mult, ALU.add)
                v.drain().then_inc(eV[0], 1)

        @block.scalar
        def _(sc):
            for t in range(NTS):
                b = t % 2
                sc.wait_ge(qS[0], qS0 + 16 * nper * (t + 1))
                if fused:
                    sc.wait_ge(eV[0], eV0 + nv * t + 1)
                if t >= 1:
                    sc.wait_ge(eV[0], eV0 + nv * (t - 1) + nv - 1)
                sc.activation(xsq[:, :], xt[:, b * D:(b + 1) * D], AF.Square)
                sc.drain().then_inc(eS[0], 1)
                sc.wait_ge(eV[0], eV0 + nv * t + (nv - 1))
                sc.activation(stat[:, 4:5], stat[:, 3:4], AF.Sqrt)
                sc.drain().then_inc(eS[0], 1)

        if fused:
            @block.gpsimd
            def _(g):
                for t in range(NTS):
                    b = t % 2
                    g.wait_ge(eV[0], eV0 + nv * t + 1)
                    g.dma_start(out=x_out[t * 128:(t + 1) * 128, :],
                                in_=xt[:, b * D:(b + 1) * D]).then_inc(qG[0], 16)
                g.wait_ge(qG[0], qG0 + 16 * NTS)
    qS[1] += 16 * nper * NTS
    eV[1] += nv * NTS
    eS[1] += 2 * NTS
    if fused:
        qG[1] += 16 * NTS

    ptb = [pb.psum(f"ln_pt{i}", [128, 1024], BF16) for i in range(4)]
    stage = pb.sbuf("ln_stage", [128, 2 * D], BF16)
    eT0, eV0, qG0 = eT[1], eV[1], qG[1]
    with nc.Block(f"ln_b_{pb._pctr}") as block:
        @block.tensor
        def _(te):
            for t in range(NTS):
                for c in range(NCD):
                    k = t * NCD + c
                    if k >= 4:
                        te.wait_ge(eV[0], eV0 + k - 3)
                    te.transpose(ptb[k % 4][:, 0:128],
                                 xn_tm[:, t * D + c * 128: t * D + (c + 1) * 128],
                                 pb.ident[:, :]).then_inc(eT[0], 1)

        @block.vector
        def _(v):
            for t in range(NTS):
                b = t % 2
                if t >= 2:
                    v.wait_ge(qG[0], qG0 + 16 * NCD * (t - 1))
                for c in range(NCD):
                    k = t * NCD + c
                    v.wait_ge(eT[0], eT0 + k + 1)
                    v.tensor_copy(stage[:, b * D + c * 128: b * D + (c + 1) * 128],
                                  ptb[k % 4][:, 0:128])
                    v.drain().then_inc(eV[0], 1)

        @block.gpsimd
        def _(g):
            for t in range(NTS):
                b = t % 2
                g.wait_ge(eV[0], eV0 + NCD * (t + 1))
                for c in range(NCD):
                    g.dma_start(out=xn_fm_dram[c, :, t * 128:(t + 1) * 128],
                                in_=stage[:, b * D + c * 128: b * D + (c + 1) * 128]
                                ).then_inc(qG[0], 16)
            g.wait_ge(qG[0], qG0 + 16 * NCD * NTS)
    eT[1] += NTS * NCD
    eV[1] += NTS * NCD
    qG[1] += 16 * NCD * NTS
    pb.end_phase()


def emit_qkv(pb, xn_fm_dram, wq, bq, wk, bk, wv, bv, q_fm_dram, k_fm_dram, v_tm_dram):
    nc = pb.nc
    pb.begin_phase()
    qS = pb.sem("qS")
    qG = pb.sem("qG")
    qC = pb.sem("qC")
    eT = pb.sem("eT")
    eV = pb.sem("eV")
    eWq = pb.sem("qk_eWq")
    xn = pb.sbuf("qk_xn", [128, NCD * S], BF16)
    wt = pb.sbuf("qk_w", [128, NCD * D], BF16)
    brow = pb.sbuf("qk_b", [1, 3 * D], BF16)
    outst = pb.sbuf("qk_o", [128, 2 * D], BF16)
    ps = [pb.psum(f"qk_ps{i}", [128, 512], F32) for i in range(8)]
    qS0, qG0, qC0, eT0, eV0 = qS[1], qG[1], qC[1], eT[1], eV[1]
    eWq0 = eWq[1]

    with nc.Block(f"qkv_{pb._pctr}") as block:
        @block.sync
        def _(s):
            for c in range(NCD):
                s.dma_start(out=xn[:, c * S:(c + 1) * S], in_=xn_fm_dram[c, :, :]
                            ).then_inc(qS[0], 16)

        @block.gpsimd
        def _(g):
            for i, b in enumerate((bq, bk, bv)):
                g.dma_start(out=brow[0:1, i * D:(i + 1) * D], in_=b[None, :]).then_inc(qG[0], 16)
            for wi, w in enumerate((wq, wk, wv)):
                if wi > 0:
                    g.wait_ge(eT[0], eT0 + 16 * wi)
                for c in range(NCD):
                    g.dma_start(out=wt[:, c * D:(c + 1) * D],
                                in_=w[c * 128:(c + 1) * 128, :]).then_inc(qG[0], 16)
                g.wait_ge(qG[0], qG0 + 16 * (3 + NCD * (wi + 1)))
                g.sem_inc(eWq[0], 1)

        @block.tensor
        def _(te):
            te.wait_ge(qS[0], qS0 + 16 * NCD)
            for wi in range(2):
                te.wait_ge(eWq[0], eWq0 + wi + 1)
                for n in range(NCD):
                    grp = wi * 16 + n
                    if grp >= 2:
                        te.wait_ge(eV[0], eV0 + 4 * (grp - 1))
                    bank = (grp % 2) * 4
                    for d in range(NCD):
                        for t in range(4):
                            te.matmul(ps[bank + t][:, :],
                                      wt[:, d * D + n * 128: d * D + (n + 1) * 128],
                                      xn[:, d * S + t * 512: d * S + (t + 1) * 512],
                                      start=(d == 0), stop=False)
                    for t in range(4):
                        ins = te.matmul(ps[bank + t][:, :],
                                        brow[0:1, wi * D + n * 128: wi * D + (n + 1) * 128],
                                        pb.ones1[0:1, 0:512], start=False, stop=True)
                    ins.then_inc(eT[0], 1)
            te.wait_ge(eWq[0], eWq0 + 3)
            for t in range(NTS):
                grp = 32 + t
                if grp >= 2:
                    te.wait_ge(eV[0], eV0 + 4 * (grp - 1))
                bank = (grp % 2) * 4
                for d in range(NCD):
                    for nb in range(4):
                        te.matmul(ps[bank + nb][:, :],
                                  xn[:, d * S + t * 128: d * S + (t + 1) * 128],
                                  wt[:, d * D + nb * 512: d * D + (nb + 1) * 512],
                                  start=(d == 0), stop=False)
                for nb in range(4):
                    ins = te.matmul(ps[bank + nb][:, :],
                                    pb.ones1[0:1, 0:128],
                                    brow[0:1, 2 * D + nb * 512: 2 * D + (nb + 1) * 512],
                                    start=False, stop=True)
                ins.then_inc(eT[0], 1)

        @block.vector
        def _(v):
            for grp in range(48):
                v.wait_ge(eT[0], eT0 + grp + 1)
                if grp >= 2:
                    v.wait_ge(qC[0], qC0 + 16 * (grp - 1))
                b = (grp % 2) * D
                bank = (grp % 2) * 4
                for t in range(4):
                    v.tensor_copy(outst[:, b + t * 512: b + (t + 1) * 512],
                                  ps[bank + t][:, :]).then_inc(eV[0], 1)

        @block.scalar
        def _(sc):
            for grp in range(48):
                sc.wait_ge(eV[0], eV0 + 4 * (grp + 1))
                b = (grp % 2) * D
                if grp < 16:
                    dst = q_fm_dram[grp, :, :]
                elif grp < 32:
                    dst = k_fm_dram[grp - 16, :, :]
                else:
                    dst = v_tm_dram[grp - 32, :, :]
                sc.dma_start(out=dst, in_=outst[:, b:b + D]).then_inc(qC[0], 16)
            sc.wait_ge(qC[0], qC0 + 16 * 48)

    qS[1] += 16 * NCD
    qG[1] += 16 * (3 + 3 * NCD)
    eWq[1] += 3
    eT[1] += 48
    eV[1] += 192
    qC[1] += 16 * 48
    pb.end_phase()


def emit_att0(pb, q_fm_dram, k_fm_dram, v_tm_dram, ctx_fm_dram):
    nc = pb.nc
    pb.begin_phase()
    qS = pb.sem("qS")
    qG = pb.sem("qG")
    eHd = pb.sem("at_eHd")
    eT = pb.sem("at_eT")
    eTt = pb.sem("at_eTt")
    eTc = pb.sem("at_eTc")
    eVm = pb.sem("at_eVm")
    eS = pb.sem("at_eS")
    eVp = pb.sem("at_eVp")
    eVe = pb.sem("at_eVe")
    eSe = pb.sem("at_eSe")
    eVz = pb.sem("at_eVz")
    eVc = pb.sem("at_eVc")
    qh = pb.sbuf("at_q", [128, 2 * S], BF16)
    kh = pb.sbuf("at_k", [128, 2 * S], BF16)
    vh = pb.sbuf("at_v", [128, 2 * S], BF16)
    E = pb.sbuf("at_E", [128, S], BF16)
    P = pb.sbuf("at_P", [128, S], BF16)
    PT = pb.sbuf("at_PT", [128, 16 * 512], BF16)
    cst = pb.sbuf("at_cst", [128, 2 * 512], BF16)
    mst = pb.sbuf("at_mst", [128, 16], F32)
    psc = [pb.psum(f"at_sc{i}", [128, 512]) for i in range(4)]
    pst = [pb.psum(f"at_tp{i}", [128, 1024], BF16) for i in range(2)]
    pctx = pb.psum("at_ctx", [128, 512])
    qS0, qG0 = qS[1], qG[1]
    (eHd0, eT0, eTt0, eTc0, eVm0, eS0, eVp0, eVe0, eSe0, eVz0, eVc0) = (
        eHd[1], eT[1], eTt[1], eTc[1], eVm[1], eS[1], eVp[1], eVe[1], eSe[1],
        eVz[1], eVc[1])

    def tr_bf(h, i):
        return h * 136 + i * (i + 1) // 2

    def grp_bf(h, i):
        return h * 40 + sum((ii + 4) // 4 for ii in range(i))

    def z_bf(h, i):
        perh = sum(4 * (ii // 4 + 1) - (ii + 1) for ii in range(16))
        return h * perh + sum(4 * (ii // 4 + 1) - (ii + 1) for ii in range(i))

    ZPH = sum(4 * (ii // 4 + 1) - (ii + 1) for ii in range(16))

    with nc.Block(f"att0_{pb._pctr}") as block:
        @block.sync
        def _(s):
            for h in range(H):
                b = h % 2
                if h >= 2:
                    s.wait_ge(eT[0], eT0 + (h - 1) * 16)
                    s.wait_ge(eTc[0], eTc0 + (h - 1) * 4)
                s.dma_start(out=qh[:, b * S:(b + 1) * S], in_=q_fm_dram[h, :, :]
                            ).then_inc(qS[0], 16)
                s.dma_start(out=kh[:, b * S:(b + 1) * S], in_=k_fm_dram[h, :, :]
                            ).then_inc(qS[0], 16)
                for t in range(NTS):
                    s.dma_start(out=vh[:, b * S + t * 128: b * S + (t + 1) * 128],
                                in_=v_tm_dram[t, :, h * 128:(h + 1) * 128]
                                ).then_inc(qS[0], 16)
                s.wait_ge(qS[0], qS0 + 16 * 18 * (h + 1))
                s.sem_inc(eHd[0], 1)

        @block.tensor
        def _(te):
            for h in range(H):
                b = h % 2
                te.wait_ge(eHd[0], eHd0 + h + 1)
                for it in range(17):
                    if it < 16:
                        i = it
                        k = h * 16 + i
                        jbn = i // 4 + 1
                        if k >= 1:
                            te.wait_ge(eS[0], eS0 + k)
                        for jb in range(jbn):
                            ins = te.matmul(psc[jb][:, :],
                                            qh[:, b * S + i * 128: b * S + (i + 1) * 128],
                                            kh[:, b * S + jb * 512: b * S + (jb + 1) * 512],
                                            start=True, stop=True)
                        ins.then_inc(eT[0], 1)
                    if it >= 1:
                        i = it - 1
                        k = h * 16 + i
                        te.wait_ge(eVp[0], eVp0 + k + 1)
                        for gi2 in range((i + 4) // 4):
                            j0 = gi2 * 4
                            j1 = min(i + 1, j0 + 4)
                            gidx = grp_bf(h, i) + gi2
                            if gidx >= 2:
                                te.wait_ge(eVe[0], eVe0 + gidx - 1)
                            for jt in range(j0, j1):
                                te.transpose(pst[gidx % 2][:, (jt - j0) * 128:
                                                           (jt - j0 + 1) * 128],
                                             P[:, jt * 128:(jt + 1) * 128],
                                             pb.ident[:, :]).then_inc(eTt[0], 1)
                        if i % 4 == 3:
                            g = h * 4 + i // 4
                            if g >= 1:
                                te.wait_ge(eVc[0], eVc0 + g)
                            te.wait_ge(eVe[0], eVe0 + grp_bf(h, i + 1))
                            te.wait_ge(eVz[0], eVz0 + z_bf(h, i + 1))
                            njt = 4 * (i // 4 + 1)
                            for jt in range(njt):
                                ins = te.matmul(pctx[:, :],
                                                vh[:, b * S + jt * 128: b * S + (jt + 1) * 128],
                                                PT[:, jt * 512:(jt + 1) * 512],
                                                start=(jt == 0), stop=(jt == njt - 1))
                            ins.then_inc(eTc[0], 1)

        @block.vector
        def _(v):
            for h in range(H):
                for it in range(17):
                    if it >= 1:
                        i = it - 1
                        ii = i % 4
                        for gi2 in range((i + 4) // 4):
                            j0 = gi2 * 4
                            j1 = min(i + 1, j0 + 4)
                            gidx = grp_bf(h, i) + gi2
                            v.wait_ge(eTt[0], eTt0 + tr_bf(h, i) + j1)
                            dst = bass.AP(PT.tensor if hasattr(PT, "tensor") else PT,
                                          0, [[0, 0]])
                            ptv = PT[:, :].rearrange("p (jt c) -> p jt c", c=512)
                            v.tensor_copy(ptv[:, j0:j1, ii * 128:(ii + 1) * 128],
                                          pst[gidx % 2][:, 0:(j1 - j0) * 128]
                                          .rearrange("p (jt c) -> p jt c", c=128)
                                          ).then_inc(eVe[0], 1)
                        for jt in range(i + 1, 4 * (i // 4 + 1)):
                            v.memset(PT[:, jt * 512 + ii * 128:
                                        jt * 512 + (ii + 1) * 128], 0.0)
                            v.drain().then_inc(eVz[0], 1)
                        if i % 4 == 3:
                            g = h * 4 + i // 4
                            v.wait_ge(eTc[0], eTc0 + g + 1)
                            if g >= 2:
                                v.wait_ge(qG[0], qG0 + 16 * (g - 1))
                            v.tensor_copy(cst[:, (g % 2) * 512:(g % 2 + 1) * 512],
                                          pctx[:, :]).then_inc(eVc[0], 1)
                    if it < 16:
                        i = it
                        k = h * 16 + i
                        jbn = i // 4 + 1
                        v.wait_ge(eT[0], eT0 + k + 1)
                        v.copy_predicated(psc[i // 4][:, (i % 4) * 128:(i % 4 + 1) * 128],
                                          pb.utri[:, :], pb.neg30[:, :])
                        lo = (i + 1) * 128 - (jbn - 1) * 512
                        if lo < 512:
                            v.memset(psc[jbn - 1][:, lo:512], -1e30)
                        v.drain()
                        for jb in range(jbn):
                            v.reduce_max(mst[:, jb:jb + 1], psc[jb][:, :], axis=AXX)
                        v.drain()
                        v.reduce_max(mst[:, 4:5], mst[:, 0:jbn], axis=AXX)
                        v.drain()
                        v.tensor_scalar(mst[:, 5:6], mst[:, 4:5], float(-ISQ_HD), None,
                                        ALU.mult)
                        v.drain().then_inc(eVm[0], 1)
                        v.wait_ge(eS[0], eS0 + k + 1)
                        v.tensor_reduce(mst[:, 6:7], mst[:, 8:8 + jbn], axis=AXX,
                                        op=ALU.add)
                        v.drain()
                        v.reciprocal(mst[:, 7:8], mst[:, 6:7])
                        v.drain()
                        v.wait_ge(eTt[0], eTt0 + tr_bf(h, i))
                        v.tensor_scalar(P[:, 0:jbn * 512], E[:, 0:jbn * 512],
                                        mst[:, 7:8], None, ALU.mult)
                        v.drain().then_inc(eVp[0], 1)
        @block.scalar
        def _(sc):
            for h in range(H):
                for it in range(17):

                    if it < 16:
                        i = it
                        k = h * 16 + i
                        jbn = i // 4 + 1
                        sc.wait_ge(eVm[0], eVm0 + k + 1)
                        if k >= 1:
                            sc.wait_ge(eVp[0], eVp0 + k)
                        for jb in range(jbn):
                            sc.activation(E[:, jb * 512:(jb + 1) * 512], psc[jb][:, :],
                                          AF.Exp, bias=mst[:, 5:6], scale=f32(ISQ_HD),
                                          accum_out=mst[:, 8 + jb:9 + jb])
                        sc.drain().then_inc(eS[0], 1)
        @block.gpsimd
        def _(g_):
            for h in range(H):
                for ig in range(4):
                    g = h * 4 + ig
                    g_.wait_ge(eVc[0], eVc0 + g + 1)
                    g_.dma_start(out=ctx_fm_dram[h, :, ig * 512:(ig + 1) * 512],
                                 in_=cst[:, (g % 2) * 512:(g % 2 + 1) * 512]
                                 ).then_inc(qG[0], 16)
            g_.wait_ge(qG[0], qG0 + 16 * 64)

    qS[1] += 16 * 18 * H
    eHd[1] += H
    eT[1] += 16 * H
    eTt[1] += 136 * H
    eTc[1] += 4 * H
    eVm[1] += 16 * H
    eS[1] += 16 * H
    eVp[1] += 16 * H
    eVe[1] += 40 * H
    eVz[1] += ZPH * H
    eVc[1] += 4 * H
    qG[1] += 16 * 64
    pb.end_phase()


def emit_wo0(pb, ctx_fm_dram, wo, bo, s0_dram):
    nc = pb.nc
    pb.begin_phase()
    qS = pb.sem("qS")
    qG = pb.sem("qG")
    qC = pb.sem("qC")
    eT = pb.sem("wo_eT")
    eV = pb.sem("wo_eV")
    ctx = pb.sbuf("wo_ctx", [128, NCD * S], BF16)
    wt = pb.sbuf("wo_w", [128, NCD * D], BF16)
    brow = pb.sbuf("wo_b", [1, D], BF16)
    stage = pb.sbuf("wo_st", [128, 2 * D], F32)
    ps = [pb.psum(f"wo_ps{i}", [128, 512]) for i in range(8)]
    qS0, qG0, qC0, eT0, eV0 = qS[1], qG[1], qC[1], eT[1], eV[1]

    with nc.Block(f"wo0_{pb._pctr}") as block:
        @block.sync
        def _(s):
            for c in range(NCD):
                s.dma_start(out=ctx[:, c * S:(c + 1) * S], in_=ctx_fm_dram[c, :, :]
                            ).then_inc(qS[0], 16)

        @block.gpsimd
        def _(g):
            g.dma_start(out=brow[0:1, :], in_=bo[None, :]).then_inc(qG[0], 16)
            for c in range(NCD):
                g.dma_start(out=wt[:, c * D:(c + 1) * D],
                            in_=wo[c * 128:(c + 1) * 128, :]).then_inc(qG[0], 16)

        @block.tensor
        def _(te):
            te.wait_ge(qS[0], qS0 + 16 * NCD)
            te.wait_ge(qG[0], qG0 + 16 * (NCD + 1))
            for t in range(NTS):
                if t >= 2:
                    te.wait_ge(eV[0], eV0 + t - 1)
                bank = (t % 2) * 4
                for nb in range(4):
                    for d in range(NCD):
                        te.matmul(ps[bank + nb][:, :],
                                  ctx[:, d * S + t * 128: d * S + (t + 1) * 128],
                                  wt[:, d * D + nb * 512: d * D + (nb + 1) * 512],
                                  start=(d == 0), stop=False)
                    ins = te.matmul(ps[bank + nb][:, :], pb.ones1[0:1, 0:128],
                                    brow[0:1, nb * 512:(nb + 1) * 512],
                                    start=False, stop=True)
                ins.then_inc(eT[0], 1)

        @block.vector
        def _(v):
            for t in range(NTS):
                v.wait_ge(eT[0], eT0 + t + 1)
                if t >= 2:
                    v.wait_ge(qC[0], qC0 + 16 * (t - 1))
                b = (t % 2) * D
                bank = (t % 2) * 4
                for nb in range(4):
                    v.tensor_copy(stage[:, b + nb * 512: b + (nb + 1) * 512],
                                  ps[bank + nb][:, :])
                v.drain().then_inc(eV[0], 1)

        @block.scalar
        def _(sc):
            for t in range(NTS):
                sc.wait_ge(eV[0], eV0 + t + 1)
                sc.dma_start(out=s0_dram[t * 128:(t + 1) * 128, :],
                             in_=stage[:, (t % 2) * D:(t % 2 + 1) * D]).then_inc(qC[0], 16)
            sc.wait_ge(qC[0], qC0 + 16 * NTS)

    qS[1] += 16 * NCD
    qG[1] += 16 * (NCD + 1)
    eT[1] += NTS
    eV[1] += NTS
    qC[1] += 16 * NTS
    pb.end_phase()


def emit_dq(pb, packed_dram, scales_dram, out_dram, NR, KPT):
    nc = pb.nc
    pb.begin_phase()
    qS = pb.sem("qS")
    qG = pb.sem("qG")
    eP = pb.sem("dq_eP")
    eV0 = pb.sem("dq_eV0")
    eS1 = pb.sem("dq_eS1")
    eS2 = pb.sem("dq_eS2")
    eV1 = pb.sem("dq_eV1")
    eV2 = pb.sem("dq_eV2")
    KP = 1024
    KO = 2 * KP
    NB = KO // 32
    nrt = NR // 128
    nkc = KPT // KP
    ntile = nrt * nkc
    wt = pb.sbuf("dq_w", [128, 2 * KP], I32)
    sct = pb.sbuf("dq_sc", [128, 2 * NB], F32)
    nib = pb.sbuf("dq_nib", [128, 2 * KO], I32)
    A = pb.sbuf("dq_A", [128, 2 * KO], F32)
    B = pb.sbuf("dq_B", [128, 2 * KO], F32)
    C = pb.sbuf("dq_C", [128, 2 * KO], F32)
    Dd = pb.sbuf("dq_D", [128, 2 * KO], F32)
    ob = pb.sbuf("dq_o", [128, 2 * KO], BF16)
    qS0, qG0 = qS[1], qG[1]
    eP0, eS10, eS20, eV10, eV20 = eP[1], eS1[1], eS2[1], eV1[1], eV2[1]
    eV00 = eV0[1]

    def tiles():
        for r in range(nrt):
            for kc in range(nkc):
                yield r, kc

    with nc.Block(f"dq_{pb._pctr}") as block:
        @block.sync
        def _(s):
            for k, (r, kc) in enumerate(tiles()):
                b = k % 2
                if k >= 2:
                    s.wait_ge(eV0[0], eV00 + k - 1)
                    s.wait_ge(eV2[0], eV20 + k - 1)
                s.dma_start(out=wt[:, b * KP:(b + 1) * KP],
                            in_=packed_dram[r * 128:(r + 1) * 128, kc * KP:(kc + 1) * KP]
                            ).then_inc(qS[0], 16)
                s.dma_start(out=sct[:, b * NB:(b + 1) * NB],
                            in_=scales_dram[r * 128:(r + 1) * 128, kc * NB:(kc + 1) * NB]
                            ).then_inc(qS[0], 16)

        @block.gpsimd
        def _(g):
            for k in range(ntile):
                b = k % 2
                sl = slice(b * KO, (b + 1) * KO)
                g.wait_ge(eS2[0], eS20 + k + 1)
                g.tensor_tensor(A[:, sl], A[:, sl], Dd[:, sl], ALU.add)
                g.drain().then_inc(eP[0], 1)
                if k >= 1:
                    g.wait_ge(eV2[0], eV20 + k)
                    bb = (k - 1) % 2
                    rr, kk = (k - 1) // nkc, (k - 1) % nkc
                    g.dma_start(out=out_dram[rr * 128:(rr + 1) * 128,
                                             kk * KO:(kk + 1) * KO],
                                in_=ob[:, bb * KO:(bb + 1) * KO]).then_inc(qG[0], 16)
            g.wait_ge(eV2[0], eV20 + ntile)
            bb = (ntile - 1) % 2
            rr, kk = (ntile - 1) // nkc, (ntile - 1) % nkc
            g.dma_start(out=out_dram[rr * 128:(rr + 1) * 128, kk * KO:(kk + 1) * KO],
                        in_=ob[:, bb * KO:(bb + 1) * KO]).then_inc(qG[0], 16)
            g.wait_ge(qG[0], qG0 + 16 * ntile)

        @block.scalar
        def _(sc):
            for k in range(ntile):
                b = k % 2
                sl = slice(b * KO, (b + 1) * KO)
                sc.wait_ge(eV0[0], eV00 + k + 1)
                if k >= 2:
                    sc.wait_ge(eP[0], eP0 + k - 1)
                    sc.wait_ge(eV2[0], eV20 + k - 1)
                sc.activation(A[:, sl], nib[:, sl], AF.Copy, bias=0.0)
                sc.activation(B[:, sl], A[:, sl], AF.Sign, bias=f32(7.5), scale=f32(-1.0))
                sc.drain().then_inc(eS1[0], 1)
                sc.wait_ge(eV1[0], eV10 + k + 1)
                sc.activation(A[:, sl], C[:, sl], AF.Relu, bias=f32(-4.0), scale=f32(0.5))
                sc.activation(Dd[:, sl], C[:, sl], AF.Relu, bias=f32(-10.0))
                sc.drain().then_inc(eS2[0], 1)

        @block.vector
        def _(v):
            for k in range(ntile):
                b = k % 2
                sl = slice(b * KO, (b + 1) * KO)
                v.wait_ge(qS[0], qS0 + 32 * k + 16)
                if k >= 2:
                    v.wait_ge(eS1[0], eS10 + k - 1)
                nv = nib[:, sl].rearrange("p (k two) -> p k two", two=2)
                wv = wt[:, b * KP:(b + 1) * KP].unsqueeze(-1)
                v.tensor_scalar(nv[:, :, 0:1], wv, 15, None, ALU.bitwise_and)
                v.tensor_scalar(nv[:, :, 1:2], wv, 4, 15,
                                ALU.logical_shift_right, ALU.bitwise_and)
                v.drain().then_inc(eV0[0], 1)
                v.wait_ge(eS1[0], eS10 + k + 1)
                v.scalar_tensor_tensor(C[:, sl], B[:, sl], 4.0, A[:, sl],
                                       ALU.mult, ALU.add)
                v.drain().then_inc(eV1[0], 1)
                v.wait_ge(eP[0], eP0 + k + 1)
                v.scalar_tensor_tensor(C[:, sl], C[:, sl], 0.5, A[:, sl],
                                       ALU.mult, ALU.add)
                v.scalar_tensor_tensor(C[:, sl], C[:, sl], -2.0, B[:, sl],
                                       ALU.add, ALU.mult)
                if k >= 2:
                    v.wait_ge(qG[0], qG0 + 16 * (k - 1))
                v.wait_ge(qS[0], qS0 + 32 * (k + 1))
                cv = C[:, sl].rearrange("p (bk j) -> p bk j", j=32)
                ov = ob[:, sl].rearrange("p (bk j) -> p bk j", j=32)
                for hb in range(NB // 32):
                    scv = sct[:, b * NB + hb * 32: b * NB + (hb + 1) * 32]
                    scb = scv.unsqueeze(-1).broadcast_to((128, 32, 32))
                    v.tensor_tensor(ov[:, hb * 32:(hb + 1) * 32, :],
                                    cv[:, hb * 32:(hb + 1) * 32, :], scb, ALU.mult)
                v.drain().then_inc(eV2[0], 1)

    qS[1] += 32 * ntile
    qG[1] += 16 * ntile
    eP[1] += ntile
    eV0[1] += ntile
    eS1[1] += ntile
    eS2[1] += ntile
    eV1[1] += ntile
    eV2[1] += ntile
    pb.end_phase()


def emit_ffn0a(pb, xn_fm_dram, wg_bf, wu_bf, inter_dram):
    nc = pb.nc
    pb.begin_phase()
    qS = pb.sem("qS")
    qG = pb.sem("qG")
    qC = pb.sem("qC")
    eT = pb.sem("fa_eT")
    eSg = pb.sem("fa_eSg")
    eV = pb.sem("fa_eV")
    eVn = pb.sem("fa_eVn")
    xn = pb.sbuf("fa_xn", [128, NCD * S], BF16)
    wg = pb.sbuf("fa_wg", [128, 2 * NCD * 512], BF16, td=True)
    wu = pb.sbuf("fa_wu", [128, 2 * NCD * 512], BF16, td=True)
    gsb = pb.sbuf("fa_g", [128, 512], F32)
    ibuf = pb.sbuf("fa_i", [128, 2 * S], BF16)
    ps = [pb.psum(f"fa_ps{i}", [128, 512]) for i in range(4)]
    qS0, qG0, qC0 = qS[1], qG[1], qC[1]
    eT0, eSg0, eV0, eVn0 = eT[1], eSg[1], eV[1], eVn[1]
    NG = 32

    with nc.Block(f"ffn0a_{pb._pctr}") as block:
        @block.gpsimd
        def _(g):
            for c in range(NCD):
                g.dma_start(out=xn[:, c * S:(c + 1) * S], in_=xn_fm_dram[c, :, :]
                            ).then_inc(qG[0], 16)

        @block.sync
        def _(s):
            for g in range(NG):
                b = g % 2
                if g >= 2:
                    s.wait_ge(eT[0], eT0 + 16 * (g - 1))
                for d in range(NCD):
                    s.dma_start(out=wg[:, (b * NCD + d) * 512:(b * NCD + d + 1) * 512],
                                in_=wg_bf[g * 512:(g + 1) * 512, d * 128:(d + 1) * 128],
                                transpose=True).then_inc(qS[0], 16)
                    s.dma_start(out=wu[:, (b * NCD + d) * 512:(b * NCD + d + 1) * 512],
                                in_=wu_bf[g * 512:(g + 1) * 512, d * 128:(d + 1) * 128],
                                transpose=True).then_inc(qS[0], 16)

        @block.tensor
        def _(te):
            te.wait_ge(qG[0], qG0 + 16 * NCD)
            for g in range(NG):
                b = g % 2
                te.wait_ge(eWf[0], eWf0 + g + 1)
                for ncg in range(4):
                    for st in range(4):
                        idx = (g * 4 + ncg) * 4 + st
                        if idx >= 2:
                            te.wait_ge(eV[0], eV0 + idx - 1)
                        bg = (st % 2) * 2
                        for d in range(NCD):
                            te.matmul(ps[bg][:, :],
                                      wg[:, (b * NCD + d) * 512 + ncg * 128:
                                         (b * NCD + d) * 512 + (ncg + 1) * 128],
                                      xn[:, d * S + st * 512: d * S + (st + 1) * 512],
                                      start=(d == 0), stop=(d == NCD - 1))
                        for d in range(NCD):
                            ins = te.matmul(ps[bg + 1][:, :],
                                            wu[:, (b * NCD + d) * 512 + ncg * 128:
                                               (b * NCD + d) * 512 + (ncg + 1) * 128],
                                            xn[:, d * S + st * 512: d * S + (st + 1) * 512],
                                            start=(d == 0), stop=(d == NCD - 1))
                        ins.then_inc(eT[0], 1)

        @block.scalar
        def _(sc):
            for nc_i in range(128):
                for st in range(4):
                    idx = nc_i * 4 + st
                    sc.wait_ge(eT[0], eT0 + idx + 1)
                    if idx >= 1:
                        sc.wait_ge(eV[0], eV0 + idx)
                    bg = (st % 2) * 2
                    sc.activation(gsb[:, :], ps[bg][:, :], AF.Gelu_apprx_tanh)
                    sc.drain().then_inc(eSg[0], 1)
                if nc_i >= 1:
                    sc.wait_ge(eVn[0], eVn0 + nc_i)
                    pb_ = (nc_i - 1) % 2
                    sc.dma_start(out=inter_dram[nc_i - 1, :, :],
                                 in_=ibuf[:, pb_ * S:(pb_ + 1) * S]).then_inc(qC[0], 16)
            sc.wait_ge(eVn[0], eVn0 + 128)
            sc.dma_start(out=inter_dram[127, :, :], in_=ibuf[:, S:2 * S]
                         ).then_inc(qC[0], 16)
            sc.wait_ge(qC[0], qC0 + 16 * 128)

        @block.vector
        def _(v):
            for nc_i in range(128):
                nb = nc_i % 2
                for st in range(4):
                    idx = nc_i * 4 + st
                    v.wait_ge(eSg[0], eSg0 + idx + 1)
                    if nc_i >= 2 and st == 0:
                        v.wait_ge(qC[0], qC0 + 16 * (nc_i - 1))
                    bg = (st % 2) * 2
                    v.tensor_tensor(ibuf[:, nb * S + st * 512: nb * S + (st + 1) * 512],
                                    gsb[:, :], ps[bg + 1][:, :],
                                    ALU.mult).then_inc(eV[0], 1)
                v.drain().then_inc(eVn[0], 1)

    qG[1] += 16 * NCD
    qS[1] += 16 * 32 * NG
    eT[1] += 512
    eSg[1] += 512
    eV[1] += 512
    eVn[1] += 128
    qC[1] += 16 * 128
    pb.end_phase()


def emit_ffn0b(pb, inter_dram, wd_bf, g0_dram, nchunks=128, act="silu"):
    nc = pb.nc
    pb.begin_phase()
    qS = pb.sem("qS")
    qG = pb.sem("qG")
    qGs2 = pb.sem("fb_qGs2")
    qC = pb.sem("qC")
    eTk = pb.sem("fb_eTk")
    eS = pb.sem("fb_eS")
    eWdS = pb.sem("fb_eWdS")
    eWdC = pb.sem("fb_eWdC")
    RING = 8
    ibig = pb.sbuf("fb_i", [128, nchunks * 512], BF16)
    wd = pb.sbuf("fb_wd", [128, RING * 1024], BF16, td=True)
    stage = pb.sbuf("fb_st", [128, 2 * 4096], F32)
    ps = [pb.psum(f"fb_ps{i}", [128, 512]) for i in range(8)]
    qS0, qG0, qGs20, qC0 = qS[1], qG[1], qGs2[1], qC[1]
    eTk0, eS0, eWdS0, eWdC0 = eTk[1], eS[1], eWdS[1], eWdC[1]
    afunc = AF.Silu if act == "silu" else AF.Copy
    NGI = 8

    with nc.Block(f"ffn0b_{pb._pctr}") as block:
        @block.gpsimd
        def _(g):
            for sg in range(4):
                if sg > 0:
                    g.wait_ge(eTk[0], eTk0 + sg * 2 * nchunks)
                for nc_i in range(nchunks):
                    g.dma_start(out=ibig[:, nc_i * 512:(nc_i + 1) * 512],
                                in_=inter_dram[nc_i, :, sg * 512:(sg + 1) * 512]
                                ).then_inc(qG[0], 16)
                if sg > 0:
                    for mbp in range(2):
                        gi = (sg - 1) * 2 + mbp
                        g.wait_ge(eS[0], eS0 + gi + 1)
                        for mb2 in range(2):
                            for ss in range(4):
                                g.dma_start(
                                    out=g0_dram[(sg - 1) * 512 + ss * 128:
                                                (sg - 1) * 512 + (ss + 1) * 128,
                                                mbp * 1024 + mb2 * 512:
                                                mbp * 1024 + (mb2 + 1) * 512],
                                    in_=stage[:, (gi % 2) * 4096 + mb2 * 2048 + ss * 512:
                                              (gi % 2) * 4096 + mb2 * 2048 + (ss + 1) * 512]
                                    ).then_inc(qGs2[0], 16)
            for mbp in range(2):
                gi = 6 + mbp
                g.wait_ge(eS[0], eS0 + gi + 1)
                for mb2 in range(2):
                    for ss in range(4):
                        g.dma_start(
                            out=g0_dram[3 * 512 + ss * 128: 3 * 512 + (ss + 1) * 128,
                                        mbp * 1024 + mb2 * 512:
                                        mbp * 1024 + (mb2 + 1) * 512],
                            in_=stage[:, (gi % 2) * 4096 + mb2 * 2048 + ss * 512:
                                      (gi % 2) * 4096 + mb2 * 2048 + (ss + 1) * 512]
                            ).then_inc(qGs2[0], 16)
            g.wait_ge(qGs2[0], qGs20 + 16 * 64)

        @block.sync
        def _(s):
            nS = 0
            for k in range(NGI * nchunks):
                gi, nc_i = divmod(k, nchunks)
                mbp = gi % 2
                sg = gi // 2
                if k >= RING:
                    s.wait_ge(eTk[0], eTk0 + k - RING + 1)
                s.dma_start(out=wd[:, (k % RING) * 1024:(k % RING + 1) * 1024],
                            in_=wd_bf[mbp * 1024:(mbp + 1) * 1024,
                                      nc_i * 128:(nc_i + 1) * 128],
                            transpose=True).then_inc(qS[0], 16)
                nS += 1
                if k % 4 == 3:
                    s.wait_ge(qS[0], qS0 + 16 * nS)
                    s.sem_inc(eWdS[0], 4)

        @block.scalar
        def _(sc):
            nC = 0
            for gi in range(NGI):
                sg, mbp2 = divmod(gi, 2)
                if gi >= 1:
                    gp = gi - 1
                    sc.wait_ge(eTk[0], eTk0 + gi * nchunks)
                    if gp >= 2:
                        sc.wait_ge(qGs2[0], qGs20 + 16 * 8 * (gp - 1))
                    bs = (gp % 2) * 4096
                    for mb2 in range(2):
                        for ss in range(4):
                            sc.activation(
                                stage[:, bs + mb2 * 2048 + ss * 512:
                                      bs + mb2 * 2048 + (ss + 1) * 512],
                                ps[mb2 * 4 + ss][:, :], afunc)
                    sc.drain().then_inc(eS[0], 1)

            gp = NGI - 1
            sc.wait_ge(eTk[0], eTk0 + NGI * nchunks)
            sc.wait_ge(qGs2[0], qGs20 + 16 * 8 * (gp - 1))
            bs = (gp % 2) * 4096
            for mb2 in range(2):
                for ss in range(4):
                    sc.activation(stage[:, bs + mb2 * 2048 + ss * 512:
                                        bs + mb2 * 2048 + (ss + 1) * 512],
                                  ps[mb2 * 4 + ss][:, :], afunc)
            sc.drain().then_inc(eS[0], 1)

        @block.tensor
        def _(te):
            for gi in range(NGI):
                sg = gi // 2
                te.wait_ge(qG[0], qG0 + 16 * nchunks * (sg + 1))
                if gi >= 1:
                    te.wait_ge(eS[0], eS0 + gi)
                for nc_i in range(nchunks):
                    k = gi * nchunks + nc_i
                    te.wait_ge(eWdS[0], eWdS0 + k + 1)
                    for mb2 in range(2):
                        for ss in range(4):
                            ins = te.matmul(
                                ps[mb2 * 4 + ss][:, :],
                                ibig[:, nc_i * 512 + ss * 128:
                                     nc_i * 512 + (ss + 1) * 128],
                                wd[:, (k % RING) * 1024 + mb2 * 512:
                                   (k % RING) * 1024 + (mb2 + 1) * 512],
                                start=(nc_i == 0), stop=(nc_i == nchunks - 1))
                    ins.then_inc(eTk[0], 1)

    qG[1] += 16 * nchunks * 4
    qGs2[1] += 16 * 64
    qS[1] += 16 * NGI * nchunks
    eWdS[1] += NGI * nchunks
    eTk[1] += NGI * nchunks
    eS[1] += NGI
    pb.end_phase()


def emit_casts(pb, jobs):
    nc = pb.nc
    pb.begin_phase()
    qG = pb.sem("qG")
    qG0 = qG[1]
    n = 0
    with nc.Block(f"casts_{pb._pctr}") as block:
        @block.gpsimd
        def _(g):
            nonlocal n
            for src, dst, nr, ncol in jobs:
                step = max(1, (4 << 20) // (ncol * 4))
                for r0 in range(0, nr, step):
                    r1 = min(nr, r0 + step)
                    g.dma_start(out=dst[r0:r1, :], in_=src[r0:r1, :]).then_inc(qG[0], 16)
                    n += 1
            g.wait_ge(qG[0], qG0 + 16 * n)
    qG[1] += 16 * n
    pb.end_phase()


def emit_scaled_cast(pb, w_dram, scale_vec_dram, out_dram, NRr, NCc, imm):
    nc = pb.nc
    pb.begin_phase()
    qS = pb.sem("qS")
    qG = pb.sem("qG")
    qC = pb.sem("qC")
    eT = pb.sem("sc_eT")
    eV = pb.sem("sc_eV")
    nrt = NRr // 128
    wt = pb.sbuf("scst_w", [128, 2 * NCc], F32)
    ob = pb.sbuf("scst_o", [128, 2 * NCc], BF16)
    svrow = pb.sbuf("scst_sv", [nrt, 128], BF16)
    scol = pb.sbuf("scst_c", [128, nrt], F32)
    pst = pb.psum("scst_p", [128, 1024], BF16)
    qS0, qG0, qC0, eT0, eV0 = qS[1], qG[1], qC[1], eT[1], eV[1]

    with nc.Block(f"scast_{pb._pctr}") as block:
        @block.gpsimd
        def _(g):
            g.dma_start(out=svrow[:, :], in_=scale_vec_dram.rearrange("(r p) -> r p", p=128)
                        ).then_inc(qG[0], 16)
        @block.tensor
        def _(te):
            te.wait_ge(qG[0], qG0 + 16)
            te.transpose(pst[:, 0:nrt], svrow[0:nrt, :], pb.ident[0:nrt, 0:nrt]).then_inc(eT[0], 1)
        @block.vector
        def _(v):
            v.wait_ge(eT[0], eT0 + 1)
            v.tensor_copy(scol[:, :], pst[:, 0:nrt])
            v.drain().then_inc(eV[0], 1)
            for r in range(nrt):
                b = r % 2
                v.wait_ge(qS[0], qS0 + 16 * (r + 1))
                if r >= 2:
                    v.wait_ge(qC[0], qC0 + 16 * (r - 1))
                v.tensor_scalar(ob[:, b * NCc:(b + 1) * NCc], wt[:, b * NCc:(b + 1) * NCc],
                                scol[:, r:r + 1], float(imm), ALU.mult, ALU.mult)
                v.drain().then_inc(eV[0], 1)
        @block.sync
        def _(s):
            for r in range(nrt):
                b = r % 2
                if r >= 2:
                    s.wait_ge(eV[0], eV0 + r)
                s.dma_start(out=wt[:, b * NCc:(b + 1) * NCc],
                            in_=w_dram[r * 128:(r + 1) * 128, :]).then_inc(qS[0], 16)
        @block.scalar
        def _(sc):
            for r in range(nrt):
                b = r % 2
                sc.wait_ge(eV[0], eV0 + r + 2)
                sc.dma_start(out=out_dram[r * 128:(r + 1) * 128, :],
                             in_=ob[:, b * NCc:(b + 1) * NCc]).then_inc(qC[0], 16)
            sc.wait_ge(qC[0], qC0 + 16 * nrt)
    qG[1] += 16
    eT[1] += 1
    eV[1] += 1 + nrt
    qS[1] += 16 * nrt
    qC[1] += 16 * nrt
    pb.end_phase()


def _emit_h128(pb, v_eng, h128, w1, w1b, tmp1, tmp2):
    ALUv = ALU
    v = v_eng
    v.tensor_scalar(tmp1[:, :], w1b[:, :], 7, None, ALUv.logical_shift_right)
    v.drain()
    v.tensor_tensor(tmp1[:, :], tmp1[:, :], w1[:, :], ALUv.bitwise_and)
    v.drain()
    v.tensor_scalar(tmp2[:, :], tmp1[:, :], 1, None, ALUv.bitwise_and)
    v.drain()
    for b in range(1, 7):
        v.tensor_scalar(tmp1[:, :], tmp1[:, :], 1, None, ALUv.logical_shift_right)
        v.drain()
        v.tensor_tensor(tmp2[:, :], tmp2[:, :],
                        tmp1[:, :], ALUv.bitwise_xor)
        v.drain()
    v.tensor_scalar(tmp2[:, :], tmp2[:, :], 1, None, ALUv.bitwise_and)
    v.drain()
    v.tensor_scalar(tmp2[:, :], tmp2[:, :], -2, 1, ALUv.mult, ALUv.add)
    v.drain()
    v.tensor_copy(h128[:, :], tmp2[:, :])
    v.drain()


def emit_fwht1(pb, xn_fm_dram, su_dram, xr_fm_dram):
    nc = pb.nc
    pb.begin_phase()
    qS = pb.sem("qS")
    qG = pb.sem("qG")
    qC = pb.sem("qC")
    eT = pb.sem("fw_eT")
    eV = pb.sem("fw_eV")
    eVq = pb.sem("fw_eVq")
    eP = pb.sem("fw_eP")
    xn = pb.sbuf("fw_xn", [128, NCD * S], BF16)
    hsu = pb.sbuf("fw_h", [128, NCD * 128], BF16)
    surow = pb.sbuf("fw_su", [NCD, 128], BF16)
    sucol = pb.sbuf("fw_suc", [128, NCD], F32)
    w1 = pb.sbuf("fw_w1", [128, 128], I32)
    w1b = pb.sbuf("fw_w1b", [128, 128], I32)
    tmp1 = pb.sbuf("fw_t1", [128, 128], I32)
    tmp2 = pb.sbuf("fw_t2", [128, 128], I32)
    h128 = pb.sbuf("fw_h128", [128, 128], BF16)
    ya = pb.sbuf("fw_ya", [128, NCD * 512], BF16)
    yb = pb.sbuf("fw_yb", [128, NCD * 512], BF16)
    yst = pb.sbuf("fw_yst", [128, NCD * 512], BF16)
    pst = pb.psum("fw_pt", [128, 1024], BF16)
    psb = [pb.psum(f"fw_ps{i}", [128, 512]) for i in range(2)]
    qS0, qG0, qC0, eT0, eV0, eVq0 = qS[1], qG[1], qC[1], eT[1], eV[1], eVq[1]

    with nc.Block(f"fwht1_{pb._pctr}") as block:
        @block.gpsimd
        def _(g):
            g.iota(w1[:, :], [[1, 128]], channel_multiplier=0)
            g.iota(w1b[:, :], [[1, 128]], channel_multiplier=128)
            g.drain().then_inc(eP[0], 1)
            g.dma_start(out=surow[:, :], in_=su_dram.rearrange("(r p) -> r p", p=128)
                        ).then_inc(qG[0], 16)
            for c in range(NCD):
                g.dma_start(out=xn[:, c * S:(c + 1) * S], in_=xn_fm_dram[c, :, :]
                            ).then_inc(qG[0], 16)

        @block.tensor
        def _(te):
            te.wait_ge(qG[0], qG0 + 16)
            te.transpose(pst[:, 0:NCD], surow[0:NCD, :], pb.ident[0:NCD, 0:NCD]).then_inc(eT[0], 1)
            te.wait_ge(eV[0], eV0 + 2)
            te.wait_ge(qG[0], qG0 + 16 * (1 + NCD))
            for st in range(4):
                for c in range(NCD):
                    k = st * NCD + c
                    if k >= 2:
                        te.wait_ge(eVq[0], eVq0 + k - 1)
                    te.matmul(psb[k % 2][:, :], hsu[:, c * 128:(c + 1) * 128],
                              xn[:, c * S + st * 512: c * S + (st + 1) * 512],
                              start=True, stop=True).then_inc(eT[0], 1)

        @block.vector
        def _(v):
            v.wait_ge(eT[0], eT0 + 1)
            v.tensor_copy(sucol[:, :], pst[:, 0:NCD])
            v.drain().then_inc(eV[0], 1)
            v.wait_ge(eP[0], eP[1] + 1)
            _emit_h128(pb, v, h128, w1, w1b, tmp1, tmp2)
            for c in range(NCD):
                v.tensor_scalar(hsu[:, c * 128:(c + 1) * 128], h128[:, :],
                                sucol[:, c:c + 1], None, ALU.mult)
            v.drain().then_inc(eV[0], 1)
            for st in range(4):
                for c in range(NCD):
                    k = st * NCD + c
                    v.wait_ge(eT[0], eT0 + 1 + k + 1)
                    v.tensor_copy(ya[:, c * 512:(c + 1) * 512],
                                  psb[k % 2][:, :]).then_inc(eVq[0], 1)
                src, dst = ya, yb
                for stage in range(4):
                    hstep = 1 << stage
                    for pair in range(8):
                        blk = (pair // hstep) * 2 * hstep
                        off = pair % hstep
                        ca, cb = blk + off, blk + off + hstep
                        v.tensor_tensor(dst[:, ca * 512:(ca + 1) * 512],
                                        src[:, ca * 512:(ca + 1) * 512],
                                        src[:, cb * 512:(cb + 1) * 512], ALU.add)
                        v.tensor_tensor(dst[:, cb * 512:(cb + 1) * 512],
                                        src[:, ca * 512:(ca + 1) * 512],
                                        src[:, cb * 512:(cb + 1) * 512], ALU.subtract)
                    src, dst = dst, src
                if st >= 1:
                    v.wait_ge(qC[0], qC0 + 16 * NCD * st)
                v.tensor_copy(yst[:, :], src[:, :])
                v.drain().then_inc(eV[0], 1)

        @block.scalar
        def _(sc):
            for st in range(4):
                sc.wait_ge(eV[0], eV0 + 2 + (st + 1))
                for c in range(NCD):
                    sc.dma_start(out=xr_fm_dram[c, :, st * 512:(st + 1) * 512],
                                 in_=yst[:, c * 512:(c + 1) * 512]).then_inc(qC[0], 16)
            sc.wait_ge(qC[0], qC0 + 16 * NCD * 4)

    qG[1] += 16 * (1 + NCD)
    eP[1] += 1
    eT[1] += 1 + 64
    eV[1] += 2 + 4
    eVq[1] += 64
    qC[1] += 16 * NCD * 4
    pb.end_phase()


def emit_ffn1a(pb, xr_fm_dram, wup_bf, wgate_bf, wsc_dram, sudp_dram, y_dram):
    nc = pb.nc
    pb.begin_phase()
    qS = pb.sem("qS")
    qG = pb.sem("qG")
    qC = pb.sem("qC")
    eT = pb.sem("f1_eT")
    eTv = pb.sem("f1_eTv")
    eSg = pb.sem("f1_eSg")
    eV = pb.sem("f1_eV")
    eVn = pb.sem("f1_eVn")
    eVc = pb.sem("f1_eVc")
    eWf = pb.sem("f1_eWf")
    NCH = INTER1 // 128
    xr = pb.sbuf("f1_xr", [128, NCD * S], BF16)
    wg = pb.sbuf("f1_wg", [128, 2 * NCD * 512], BF16, td=True)
    wu = pb.sbuf("f1_wu", [128, 2 * NCD * 512], BF16, td=True)
    wscrow = pb.sbuf("f1_wr", [128, 128], BF16)
    sdrow = pb.sbuf("f1_sr", [64, 128], BF16)
    ucol = pb.sbuf("f1_uc", [128, NCH], F32)
    gcol = pb.sbuf("f1_gc", [128, NCH], F32)
    gsb = pb.sbuf("f1_g", [128, 512], F32)
    usb = pb.sbuf("f1_u", [128, 512], F32)
    ibuf = pb.sbuf("f1_i", [128, 2 * S], BF16)
    pst = pb.psum("f1_pt", [128, 1024], BF16)
    ps = [pb.psum(f"f1_ps{i}", [128, 512]) for i in range(4)]
    qS0, qG0, qC0 = qS[1], qG[1], qC[1]
    eT0, eTv0, eSg0, eV0, eVn0, eVc0 = eT[1], eTv[1], eSg[1], eV[1], eVn[1], eVc[1]
    eWf0 = eWf[1]
    NG = NCH // 4

    with nc.Block(f"ffn1a_{pb._pctr}") as block:
        @block.gpsimd
        def _(g):
            g.dma_start(out=wscrow[:, :], in_=wsc_dram.rearrange("(r p) -> r p", p=128)
                        ).then_inc(qG[0], 16)
            g.dma_start(out=sdrow[:, :], in_=sudp_dram.rearrange("(r p) -> r p", p=128)
                        ).then_inc(qG[0], 16)
            for c in range(NCD):
                g.dma_start(out=xr[:, c * S:(c + 1) * S], in_=xr_fm_dram[c, :, :]
                            ).then_inc(qG[0], 16)

        @block.tensor
        def _(te):
            te.wait_ge(qG[0], qG0 + 32)
            te.transpose(pst[:, 0:128], wscrow[:, :], pb.ident[:, :]).then_inc(eTv[0], 1)
            te.wait_ge(eVc[0], eVc0 + 1)
            te.transpose(pst[:, 128:192], sdrow[0:64, :], pb.ident[0:64, 0:64]).then_inc(eTv[0], 1)
            te.wait_ge(qG[0], qG0 + 16 * (2 + NCD))
            te.wait_ge(eVc[0], eVc0 + 2)
            for g in range(NG):
                b = g % 2
                te.wait_ge(eWf[0], eWf0 + g + 1)
                for ncg in range(4):
                    for st in range(4):
                        idx = (g * 4 + ncg) * 4 + st
                        if idx >= 2:
                            te.wait_ge(eV[0], eV0 + idx - 1)
                        bg = (st % 2) * 2
                        for d in range(NCD):
                            te.matmul(ps[bg][:, :],
                                      wg[:, (b * NCD + d) * 512 + ncg * 128:
                                         (b * NCD + d) * 512 + (ncg + 1) * 128],
                                      xr[:, d * S + st * 512: d * S + (st + 1) * 512],
                                      start=(d == 0), stop=(d == NCD - 1))
                        for d in range(NCD):
                            ins = te.matmul(ps[bg + 1][:, :],
                                            wu[:, (b * NCD + d) * 512 + ncg * 128:
                                               (b * NCD + d) * 512 + (ncg + 1) * 128],
                                            xr[:, d * S + st * 512: d * S + (st + 1) * 512],
                                            start=(d == 0), stop=(d == NCD - 1))
                        ins.then_inc(eT[0], 1)

        @block.sync
        def _(s):
            for g in range(NG):
                b = g % 2
                if g >= 2:
                    s.wait_ge(eT[0], eT0 + 16 * (g - 1))
                for d in range(NCD):
                    s.dma_start(out=wg[:, (b * NCD + d) * 512:(b * NCD + d + 1) * 512],
                                in_=wgate_bf[g * 512:(g + 1) * 512, d * 128:(d + 1) * 128],
                                transpose=True).then_inc(qS[0], 16)
                    s.dma_start(out=wu[:, (b * NCD + d) * 512:(b * NCD + d + 1) * 512],
                                in_=wup_bf[g * 512:(g + 1) * 512, d * 128:(d + 1) * 128],
                                transpose=True).then_inc(qS[0], 16)
                s.wait_ge(qS[0], qS0 + 16 * 32 * (g + 1))
                s.sem_inc(eWf[0], 1)

        @block.vector
        def _(v):
            isq = float(1.0 / np.sqrt(2048.0))
            v.wait_ge(eTv[0], eTv0 + 1)
            v.tensor_scalar(ucol[:, :], pst[:, 0:64], float(isq), None, ALU.mult)
            v.tensor_scalar(gcol[:, :], pst[:, 64:128], float(isq), None, ALU.mult)
            v.drain().then_inc(eVc[0], 1)
            v.wait_ge(eTv[0], eTv0 + 2)
            v.tensor_tensor(ucol[:, :], ucol[:, :], pst[:, 128:192], ALU.mult)
            v.drain().then_inc(eVc[0], 1)
            for nc_i in range(NCH):
                nb = nc_i % 2
                for st in range(4):
                    idx = nc_i * 4 + st
                    v.wait_ge(eSg[0], eSg0 + idx + 1)
                    if nc_i >= 2 and st == 0:
                        v.wait_ge(qC[0], qC0 + 16 * (nc_i - 1))
                    bg = (st % 2) * 2
                    v.tensor_scalar(usb[:, :], ps[bg + 1][:, :], ucol[:, nc_i:nc_i + 1],
                                    None, ALU.mult)
                    v.tensor_tensor(ibuf[:, nb * S + st * 512: nb * S + (st + 1) * 512],
                                    gsb[:, :], usb[:, :], ALU.mult).then_inc(eV[0], 1)
                v.drain().then_inc(eVn[0], 1)

        @block.scalar
        def _(sc):
            for nc_i in range(NCH):
                for st in range(4):
                    idx = nc_i * 4 + st
                    sc.wait_ge(eT[0], eT0 + idx + 1)
                    if idx >= 1:
                        sc.wait_ge(eV[0], eV0 + idx)
                    bg = (st % 2) * 2
                    sc.activation(gsb[:, :], ps[bg][:, :], AF.Silu,
                                  scale=mst_scale(gcol, nc_i))
                    sc.drain().then_inc(eSg[0], 1)
                if nc_i >= 1:
                    sc.wait_ge(eVn[0], eVn0 + nc_i)
                    pb_ = (nc_i - 1) % 2
                    sc.dma_start(out=y_dram[nc_i - 1, :, :],
                                 in_=ibuf[:, pb_ * S:(pb_ + 1) * S]).then_inc(qC[0], 16)
            sc.wait_ge(eVn[0], eVn0 + NCH)
            sc.dma_start(out=y_dram[NCH - 1, :, :], in_=ibuf[:, S:2 * S]
                         ).then_inc(qC[0], 16)
            sc.wait_ge(qC[0], qC0 + 16 * NCH)

    qG[1] += 16 * (2 + NCD)
    qS[1] += 16 * 32 * NG
    eWf[1] += NG
    eTv[1] += 2
    eVc[1] += 2
    eT[1] += NCH * 4
    eSg[1] += NCH * 4
    eV[1] += NCH * 4
    eVn[1] += NCH
    qC[1] += 16 * NCH
    pb.end_phase()


def mst_scale(col, i):
    return col[:, i:i + 1]


def emit_ffn1b(pb, y_dram, wd_bf, g1_dram):
    nc = pb.nc
    pb.begin_phase()
    qS = pb.sem("qS")
    qG = pb.sem("qG")
    qC = pb.sem("qC")
    eTh = pb.sem("g1_eTh")
    eTk = pb.sem("g1_eTk")
    eVh = pb.sem("g1_eVh")
    eVz = pb.sem("g1_eVz")
    eS = pb.sem("g1_eS")
    eP = pb.sem("g1_eP")
    eWd1 = pb.sem("g1_eWd")
    NCH = INTER1 // 128
    w1 = pb.sbuf("g1_w1", [128, 128], I32)
    w1b = pb.sbuf("g1_w1b", [128, 128], I32)
    tmp1 = pb.sbuf("g1_t1", [128, 128], I32)
    tmp2 = pb.sbuf("g1_t2", [128, 128], I32)
    h128 = pb.sbuf("g1_h", [128, 128], BF16)
    ya = pb.sbuf("g1_ya", [128, NCH * 512], BF16)
    yb = pb.sbuf("g1_yb", [128, NCH * 512], BF16)
    RING = 8
    wd = pb.sbuf("g1_wd", [128, RING * 1024], BF16, td=True)
    stage = pb.sbuf("g1_st", [128, 2 * 4096], F32)
    psd = [pb.psum(f"g1_pd{i}", [128, 512]) for i in range(8)]
    psb = [psd[0], psd[1]]
    qS0, qG0, qC0 = qS[1], qG[1], qC[1]
    eTh0, eTk0, eVh0, eVz0, eS0 = eTh[1], eTk[1], eVh[1], eVz[1], eS[1]
    eWd10 = eWd1[1]

    with nc.Block(f"ffn1b_{pb._pctr}") as block:
        @block.gpsimd
        def _(g):
            g.iota(w1[:, :], [[1, 128]], channel_multiplier=0)
            g.iota(w1b[:, :], [[1, 128]], channel_multiplier=128)
            g.drain().then_inc(eP[0], 1)
            for sg in range(4):
                if sg >= 1:
                    g.wait_ge(eVz[0], eVz0 + sg)
                for c in range(NCH):
                    g.dma_start(out=ya[:, c * 512:(c + 1) * 512],
                                in_=y_dram[c, :, sg * 512:(sg + 1) * 512]
                                ).then_inc(qG[0], 16)

        @block.tensor
        def _(te):
            te.wait_ge(eVh[0], eVh0 + 1)
            for sg in range(4):
                te.wait_ge(qG[0], qG0 + 16 * NCH * (sg + 1))
                if sg >= 1:
                    te.wait_ge(eS[0], eS0 + 2 * sg)
                for c in range(NCH):
                    k = sg * NCH + c
                    if k >= 2:
                        te.wait_ge(eVh[0], eVh0 + 1 + k - 1)
                    te.matmul(psb[k % 2][:, :], h128[:, :],
                              ya[:, c * 512:(c + 1) * 512],
                              start=True, stop=True).then_inc(eTh[0], 1)
                te.wait_ge(eVz[0], eVz0 + sg + 1)
                for mbp in range(2):
                    gi = sg * 2 + mbp
                    if gi >= 1:
                        te.wait_ge(eS[0], eS0 + gi)
                    for nc_i in range(NCH):
                        k2 = gi * NCH + nc_i
                        te.wait_ge(eWd1[0], eWd10 + k2 + 1)
                        for mb2 in range(2):
                            for ss in range(4):
                                ins = te.matmul(psd[mb2 * 4 + ss][:, :],
                                                yb[:, nc_i * 512 + ss * 128:
                                                   nc_i * 512 + (ss + 1) * 128],
                                                wd[:, (k2 % RING) * 1024 + mb2 * 512:
                                                   (k2 % RING) * 1024 + (mb2 + 1) * 512],
                                                start=(nc_i == 0), stop=(nc_i == NCH - 1))
                        ins.then_inc(eTk[0], 1)

        @block.vector
        def _(v):
            v.wait_ge(eP[0], eP[1] + 1)
            _emit_h128(pb, v, h128, w1, w1b, tmp1, tmp2)
            v.drain().then_inc(eVh[0], 1)
            for sg in range(4):
                for c in range(NCH):
                    k = sg * NCH + c
                    v.wait_ge(eTh[0], eTh0 + k + 1)
                    v.tensor_copy(ya[:, c * 512:(c + 1) * 512],
                                  psb[k % 2][:, :]).then_inc(eVh[0], 1)
                if sg >= 1:
                    v.wait_ge(eTk[0], eTk0 + sg * 2 * NCH)
                src, dst = ya, yb
                for stage_i in range(6):
                    hstep = 1 << stage_i
                    for pair in range(32):
                        blk = (pair // hstep) * 2 * hstep
                        off = pair % hstep
                        ca, cb = blk + off, blk + off + hstep
                        v.tensor_tensor(dst[:, ca * 512:(ca + 1) * 512],
                                        src[:, ca * 512:(ca + 1) * 512],
                                        src[:, cb * 512:(cb + 1) * 512], ALU.add)
                        v.tensor_tensor(dst[:, cb * 512:(cb + 1) * 512],
                                        src[:, ca * 512:(ca + 1) * 512],
                                        src[:, cb * 512:(cb + 1) * 512], ALU.subtract)
                    src, dst = dst, src
                if src is not yb:
                    v.tensor_copy(yb[:, :], src[:, :])
                v.drain().then_inc(eVz[0], 1)

        @block.sync
        def _(s):
            for sg in range(4):
                for mbp in range(2):
                    for nc_i in range(NCH):
                        k2 = (sg * 2 + mbp) * NCH + nc_i
                        if k2 >= RING:
                            s.wait_ge(eTk[0], eTk0 + k2 - RING + 1)
                        s.dma_start(out=wd[:, (k2 % RING) * 1024:(k2 % RING + 1) * 1024],
                                    in_=wd_bf[mbp * 1024:(mbp + 1) * 1024,
                                              nc_i * 128:(nc_i + 1) * 128],
                                    transpose=True).then_inc(qS[0], 16)
                        if k2 % 4 == 3:
                            s.wait_ge(qS[0], qS0 + 16 * (k2 + 1))
                            s.sem_inc(eWd1[0], 4)

        @block.scalar
        def _(sc):
            for sg in range(4):
                for mbp in range(2):
                    gi = sg * 2 + mbp
                    sc.wait_ge(eTk[0], eTk0 + (gi + 1) * NCH)
                    if gi >= 2:
                        sc.wait_ge(qC[0], qC0 + 16 * 8 * (gi - 1))
                    bs = (gi % 2) * 4096
                    for mb2 in range(2):
                        for ss in range(4):
                            sc.activation(stage[:, bs + mb2 * 2048 + ss * 512:
                                                bs + mb2 * 2048 + (ss + 1) * 512],
                                          psd[mb2 * 4 + ss][:, :], AF.Silu)
                    sc.drain().then_inc(eS[0], 1)
                    for mb2 in range(2):
                        for ss in range(4):
                            sc.dma_start(out=g1_dram[sg * 512 + ss * 128:
                                                     sg * 512 + (ss + 1) * 128,
                                                     mbp * 1024 + mb2 * 512:
                                                     mbp * 1024 + (mb2 + 1) * 512],
                                         in_=stage[:, bs + mb2 * 2048 + ss * 512:
                                                   bs + mb2 * 2048 + (ss + 1) * 512]
                                         ).then_inc(qC[0], 16)
            sc.wait_ge(qC[0], qC0 + 16 * 64)

    qG[1] += 16 * NCH * 4
    qS[1] += 16 * 8 * NCH
    eWd1[1] += 8 * NCH
    eP[1] += 1
    eTh[1] += 4 * NCH
    eTk[1] += 8 * NCH
    eVh[1] += 1 + 4 * NCH
    eVz[1] += 4
    eS[1] += 16
    qC[1] += 16 * 64
    pb.end_phase()


def emit_l1a(pb, xn_fm_dram, wq1_bf, sq_fm_dram):
    nc = pb.nc
    pb.begin_phase()
    qS = pb.sem("qS")
    qG = pb.sem("qG")
    qC = pb.sem("qC")
    eT = pb.sem("la_eT")
    eV1 = pb.sem("la_eV1")
    eS = pb.sem("la_eS")
    eV2 = pb.sem("la_eV2")
    xn = pb.sbuf("la_xn", [128, NCD * S], BF16)
    wq = pb.sbuf("la_wq", [128, NCD * 4 * 512], BF16, td=True)
    t1 = pb.sbuf("la_t1", [128, 512], F32)
    esb = pb.sbuf("la_e", [128, 512], F32)
    ost = pb.sbuf("la_o", [128, 2 * S], BF16)
    ps = [pb.psum(f"la_ps{i}", [128, 512]) for i in range(2)]
    qS0, qG0, qC0, eT0, eV10, eS0, eV20 = qS[1], qG[1], qC[1], eT[1], eV1[1], eS[1], eV2[1]

    with nc.Block(f"l1a_{pb._pctr}") as block:
        @block.gpsimd
        def _(g):
            for c in range(NCD):
                g.dma_start(out=xn[:, c * S:(c + 1) * S], in_=xn_fm_dram[c, :, :]
                            ).then_inc(qG[0], 16)

        @block.sync
        def _(s):
            for ng in range(4):
                for d in range(NCD):
                    s.dma_start(out=wq[:, (ng * NCD + d) * 512:(ng * NCD + d + 1) * 512],
                                in_=wq1_bf[ng * 512:(ng + 1) * 512, d * 128:(d + 1) * 128],
                                transpose=True).then_inc(qS[0], 16)

        @block.tensor
        def _(te):
            te.wait_ge(qG[0], qG0 + 16 * NCD)
            te.wait_ge(qS[0], qS0 + 16 * 64)
            for n in range(NCD):
                ng, ns = n // 4, n % 4
                for st in range(4):
                    idx = n * 4 + st
                    if idx >= 2:
                        te.wait_ge(eV2[0], eV20 + idx - 1)
                    for d in range(NCD):
                        ins = te.matmul(ps[idx % 2][:, :],
                                        wq[:, (ng * NCD + d) * 512 + ns * 128:
                                           (ng * NCD + d) * 512 + (ns + 1) * 128],
                                        xn[:, d * S + st * 512: d * S + (st + 1) * 512],
                                        start=(d == 0), stop=(d == NCD - 1))
                    ins.then_inc(eT[0], 1)

        @block.vector
        def _(v):
            for n in range(NCD):
                nb = n % 2
                for st in range(4):
                    idx = n * 4 + st
                    v.wait_ge(eT[0], eT0 + idx + 1)
                    if st == 0 and n >= 2:
                        v.wait_ge(qC[0], qC0 + 16 * (n - 1))
                    v.tensor_scalar(t1[:, :], ps[idx % 2][:, :], 0.0, None, ALU.min)
                    v.drain().then_inc(eV1[0], 1)
                    v.wait_ge(eS[0], eS0 + idx + 1)
                    v.scalar_tensor_tensor(ost[:, nb * S + st * 512: nb * S + (st + 1) * 512],
                                           ps[idx % 2][:, :], 0.0, esb[:, :],
                                           ALU.max, ALU.add)
                    v.drain().then_inc(eV2[0], 1)

        @block.scalar
        def _(sc):
            for n in range(NCD):
                for st in range(4):
                    idx = n * 4 + st
                    sc.wait_ge(eV1[0], eV10 + idx + 1)
                    sc.activation(esb[:, :], t1[:, :], AF.Exp)
                    sc.drain().then_inc(eS[0], 1)
                if n >= 1:
                    sc.wait_ge(eV2[0], eV20 + n * 4)
                    sc.wait_ge(qC[0], qC0 + 16 * max(0, n - 1))
                    pb_ = (n - 1) % 2
                    sc.dma_start(out=sq_fm_dram[n - 1, :, :],
                                 in_=ost[:, pb_ * S:(pb_ + 1) * S]).then_inc(qC[0], 16)
            sc.wait_ge(eV2[0], eV20 + 64)
            sc.dma_start(out=sq_fm_dram[NCD - 1, :, :], in_=ost[:, S:2 * S]
                         ).then_inc(qC[0], 16)
            sc.wait_ge(qC[0], qC0 + 16 * NCD)

    qG[1] += 16 * NCD
    qS[1] += 16 * 64
    eT[1] += 64
    eV1[1] += 64
    eS[1] += 64
    eV2[1] += 64
    qC[1] += 16 * NCD
    pb.end_phase()


def emit_l1b(pb, sq_fm_dram, mem_dram, mn_dram, bw_dram, comb_fm_dram):
    nc = pb.nc
    pb.begin_phase()
    qS = pb.sem("qS")
    qG = pb.sem("qG")
    qC = pb.sem("qC")
    eT = pb.sem("lb_eT")
    eTt = pb.sem("lb_eTt")
    eTp = pb.sem("lb_eTp")
    eV = pb.sem("lb_eV")
    eVe = pb.sem("lb_eVe")
    eVp = pb.sem("lb_eVp")
    eSp = pb.sem("lb_eSp")
    sq = pb.sbuf("lb_sq", [128, NCD * S], BF16)
    memn = pb.sbuf("lb_m", [128, 64 * 132], BF16)
    mnrow = pb.sbuf("lb_mnr", [64, 128], BF16)
    mncol = pb.sbuf("lb_mnc", [128, 64], BF16)
    bwrow = pb.sbuf("lb_bw", [1, 64], F32)
    mnfrow = pb.sbuf("lb_mnf", [1, 8192], F32)
    prep = pb.sbuf("lb_pr", [1, 64], F32)
    prep2 = pb.sbuf("lb_pr2", [1, 16], F32)
    wrow = pb.sbuf("lb_wr", [1, 64], BF16)
    wrep = pb.sbuf("lb_wrep", [128, 64], F32)
    rcol = pb.sbuf("lb_rc", [128, 4], F32)
    acc = pb.sbuf("lb_acc", [128, 128], F32)
    accb = pb.sbuf("lb_accb", [128, 128], BF16)
    cst = pb.sbuf("lb_cst", [128, 2 * 128], BF16)
    psA = [pb.psum(f"lb_ps{i}", [128, 512]) for i in range(4)]
    pst = pb.psum("lb_pt", [128, 1024], BF16)
    pstT = [pb.psum(f"lb_pt2{i}", [128, 1024], BF16) for i in range(2)]
    psw = pb.psum("lb_pw", [128, 64])
    qS0, qG0, qC0 = qS[1], qG[1], qC[1]
    eT0, eTt0, eTp0, eV0, eVe0, eVp0, eSp0 = (
        eT[1], eTt[1], eTp[1], eV[1], eVe[1], eVp[1], eSp[1])

    with nc.Block(f"l1b_{pb._pctr}") as block:
        @block.gpsimd
        def _(g):
            g.dma_start(out=mnrow[:, :], in_=mn_dram.rearrange("n h d -> (n h) d")
                        ).then_inc(qG[0], 16)
            for n in range(4):
                for h in range(H):
                    g.dma_start(out=memn[:, (n * 16 + h) * 132:(n * 16 + h) * 132 + 128],
                                in_=mem_dram[n, h, :, :]).then_inc(qG[0], 16)
            for c in range(NCD):
                g.dma_start(out=sq[:, c * S:(c + 1) * S], in_=sq_fm_dram[c, :, :]
                            ).then_inc(qG[0], 16)

        @block.sync
        def _(s):
            s.dma_start(out=bwrow[0:1, :], in_=bw_dram.rearrange("h n -> (h n)")[None, :]
                        ).then_inc(qS[0], 16)
            s.dma_start(out=mnfrow[0:1, :], in_=mn_dram.rearrange("n h d -> (n h d)")[None, :]
                        ).then_inc(qS[0], 16)

        @block.vector
        def _(v):
            v.wait_ge(qS[0], qS0 + 32)
            bw3 = bwrow[0:1, :].rearrange("o (h n) -> o h n", n=4)
            v.tensor_reduce(prep2[0:1, 0:16].unsqueeze(-1),
                            bw3, axis=AXX, op=ALU.max)
            v.drain()
            mx3 = prep2[0:1, 0:16].unsqueeze(-1).broadcast_to((1, 16, 4))
            v.tensor_tensor(prep[0:1, :].rearrange("o (h n) -> o h n", n=4), bw3, mx3,
                            ALU.subtract)
            v.drain().then_inc(eVp[0], 1)
            v.wait_ge(eSp[0], eSp0 + 1)
            pr3 = bwrow[0:1, :].rearrange("o (h n) -> o h n", n=4)
            v.tensor_reduce(prep2[0:1, 0:16].unsqueeze(-1),
                            pr3, axis=AXX, op=ALU.add)
            v.drain()
            v.reciprocal(prep2[0:1, 0:16], prep2[0:1, 0:16])
            v.drain()
            rc3 = prep2[0:1, 0:16].unsqueeze(-1).broadcast_to((1, 16, 4))
            v.tensor_tensor(pr3, pr3, rc3, ALU.mult)
            v.drain()
            mn3 = mnfrow[0:1, :].rearrange("o (n r) -> o n r", n=4)
            v.tensor_reduce(prep2[0:1, 0:4].unsqueeze(-1),
                            mn3, axis=AXX, op=ALU.add)
            v.drain()
            v.tensor_scalar(prep2[0:1, 0:4], prep2[0:1, 0:4], float(EPS_L1), None,
                            ALU.is_ge)
            v.drain()
            act3 = prep2[0:1, 0:4].unsqueeze(1).broadcast_to((1, 16, 4))
            v.tensor_tensor(pr3, pr3, act3, ALU.mult)
            v.drain()
            v.tensor_copy(wrow[0:1, :], bwrow[0:1, :])
            v.drain().then_inc(eVp[0], 1)
            v.wait_ge(eTp[0], eTp0 + 2)
            v.tensor_copy(wrep[:, :], psw[:, :])
            v.tensor_copy(mncol[:, :], pst[:, 0:64])
            v.drain().then_inc(eVp[0], 1)
            for n in range(4):
                for h in range(H):
                    v.tensor_copy(memn[:, (n * 16 + h) * 132 + 128:(n * 16 + h) * 132 + 129],
                                  mncol[:, n * 16 + h:n * 16 + h + 1])
            v.drain().then_inc(eVp[0], 1)
            for h in range(H):
                for st in range(NTS):
                    k = h * NTS + st
                    v.wait_ge(eT[0], eT0 + k + 1)
                    for n in range(4):
                        v.tensor_scalar(rcol[:, n:n + 1], psA[n][:, 128:129],
                                        float(EPS_L1), None, ALU.max)
                    v.drain()
                    v.reciprocal(rcol[:, 0:4], rcol[:, 0:4])
                    v.drain()
                    v.tensor_tensor(rcol[:, 0:4], rcol[:, 0:4],
                                    wrep[:, h * 4:h * 4 + 4], ALU.mult)
                    v.drain()
                    v.tensor_scalar(acc[:, :], psA[0][:, 0:128],
                                    rcol[:, 0:1], None, ALU.mult)
                    for n in range(1, 3):
                        v.scalar_tensor_tensor(acc[:, :], psA[n][:, 0:128],
                                               rcol[:, n:n + 1], acc[:, :],
                                               ALU.mult, ALU.add)
                    v.scalar_tensor_tensor(accb[:, :], psA[3][:, 0:128],
                                           rcol[:, 3:4], acc[:, :],
                                           ALU.mult, ALU.add)
                    v.drain().then_inc(eV[0], 1)
                    v.wait_ge(eTt[0], eTt0 + k + 1)
                    if k >= 2:
                        v.wait_ge(qC[0], qC0 + 16 * (k - 1))
                    v.tensor_copy(cst[:, (k % 2) * 128:(k % 2 + 1) * 128],
                                  pstT[k % 2][:, 0:128]).then_inc(eVe[0], 1)

        @block.scalar
        def _(sc):
            sc.wait_ge(eVp[0], eVp0 + 1)
            sc.activation(bwrow[0:1, :], prep[0:1, :], AF.Exp)
            sc.drain().then_inc(eSp[0], 1)
            for h in range(H):
                for st in range(NTS):
                    k = h * NTS + st
                    sc.wait_ge(eVe[0], eVe0 + k + 1)
                    sc.dma_start(out=comb_fm_dram[h, :, st * 128:(st + 1) * 128],
                                 in_=cst[:, (k % 2) * 128:(k % 2 + 1) * 128]
                                 ).then_inc(qC[0], 16)
            sc.wait_ge(qC[0], qC0 + 16 * 256)

        @block.tensor
        def _(te):
            te.wait_ge(eVp[0], eVp0 + 2)
            te.matmul(psw[:, :], pb.ones1[0:1, 0:128], wrow[0:1, :],
                      start=True, stop=True).then_inc(eTp[0], 1)
            te.wait_ge(qG[0], qG0 + 16)
            te.transpose(pst[:, 0:64], mnrow[0:64, :], pb.ident[0:64, 0:64]).then_inc(eTp[0], 1)
            te.wait_ge(eVp[0], eVp0 + 4)
            te.wait_ge(qG[0], qG0 + 16 * (1 + 64 + NCD))
            for h in range(H):
                for st in range(NTS):
                    k = h * NTS + st
                    if k >= 1:
                        te.wait_ge(eV[0], eV0 + k)
                    for n in range(4):
                        ins = te.matmul(psA[n][:, 0:129],
                                        sq[:, h * S + st * 128:h * S + (st + 1) * 128],
                                        memn[:, (n * 16 + h) * 132:(n * 16 + h) * 132 + 129],
                                        start=True, stop=True)
                    ins.then_inc(eT[0], 1)
                    te.wait_ge(eV[0], eV0 + k + 1)
                    if k >= 2:
                        te.wait_ge(eVe[0], eVe0 + k - 1)
                    te.transpose(pstT[k % 2][:, 0:128],
                                 accb[:, :], pb.ident[:, :]).then_inc(eTt[0], 1)

    qG[1] += 16 * (1 + 64 + NCD)
    qS[1] += 32
    eVp[1] += 4
    eSp[1] += 1
    eTp[1] += 2
    eT[1] += 256
    eTt[1] += 256
    eV[1] += 256
    eVe[1] += 256
    qC[1] += 16 * 256
    pb.end_phase()


def emit_l1o(pb, comb_fm_dram, wo1_bf, s1_dram):
    nc = pb.nc
    pb.begin_phase()
    qS = pb.sem("qS")
    qG = pb.sem("qG")
    qC = pb.sem("qC")
    eT = pb.sem("lo_eT")
    eV = pb.sem("lo_eV")
    comb = pb.sbuf("lo_c", [128, NCD * S], BF16)
    wt = pb.sbuf("lo_w", [128, NCD * 4 * 512], BF16, td=True)
    stage = pb.sbuf("lo_st", [128, 2 * D], F32)
    ps = [pb.psum(f"lo_ps{i}", [128, 512]) for i in range(8)]
    qS0, qG0, qC0, eT0, eV0 = qS[1], qG[1], qC[1], eT[1], eV[1]

    with nc.Block(f"l1o_{pb._pctr}") as block:
        @block.gpsimd
        def _(g):
            for c in range(NCD):
                g.dma_start(out=comb[:, c * S:(c + 1) * S], in_=comb_fm_dram[c, :, :]
                            ).then_inc(qG[0], 16)

        @block.sync
        def _(s):
            for mb in range(4):
                for d in range(NCD):
                    s.dma_start(out=wt[:, (mb * NCD + d) * 512:(mb * NCD + d + 1) * 512],
                                in_=wo1_bf[mb * 512:(mb + 1) * 512, d * 128:(d + 1) * 128],
                                transpose=True).then_inc(qS[0], 16)

        @block.tensor
        def _(te):
            te.wait_ge(qG[0], qG0 + 16 * NCD)
            te.wait_ge(qS[0], qS0 + 16 * 64)
            for t in range(NTS):
                if t >= 2:
                    te.wait_ge(eV[0], eV0 + t - 1)
                bank = (t % 2) * 4
                for mb in range(4):
                    for d in range(NCD):
                        ins = te.matmul(ps[bank + mb][:, :],
                                        comb[:, d * S + t * 128: d * S + (t + 1) * 128],
                                        wt[:, (mb * NCD + d) * 512:(mb * NCD + d + 1) * 512],
                                        start=(d == 0), stop=(d == NCD - 1))
                ins.then_inc(eT[0], 1)

        @block.vector
        def _(v):
            for t in range(NTS):
                v.wait_ge(eT[0], eT0 + t + 1)
                if t >= 2:
                    v.wait_ge(qC[0], qC0 + 16 * (t - 1))
                b = (t % 2) * D
                bank = (t % 2) * 4
                for mb in range(4):
                    v.tensor_copy(stage[:, b + mb * 512: b + (mb + 1) * 512],
                                  ps[bank + mb][:, :])
                v.drain().then_inc(eV[0], 1)

        @block.scalar
        def _(sc):
            for t in range(NTS):
                sc.wait_ge(eV[0], eV0 + t + 1)
                sc.dma_start(out=s1_dram[t * 128:(t + 1) * 128, :],
                             in_=stage[:, (t % 2) * D:(t % 2 + 1) * D]).then_inc(qC[0], 16)
            sc.wait_ge(qC[0], qC0 + 16 * NTS)

    qG[1] += 16 * NCD
    qS[1] += 16 * 64
    eT[1] += NTS
    eV[1] += NTS
    qC[1] += 16 * NTS
    pb.end_phase()


def emit_res(pb, x_dram, s_dram, g_dram, out_dram):
    nc = pb.nc
    pb.begin_phase()
    qS = pb.sem("qS")
    qG = pb.sem("qG")
    eV = pb.sem("rs_eV")
    xt = pb.sbuf("rs_x", [128, 2 * D], F32)
    st_t = pb.sbuf("rs_s", [128, 2 * D], F32)
    gt = pb.sbuf("rs_g", [128, 2 * D], F32)
    qS0, qG0, eV0 = qS[1], qG[1], eV[1]

    with nc.Block(f"res_{pb._pctr}") as block:
        @block.sync
        def _(s):
            for t in range(NTS):
                b = t % 2
                if t >= 2:
                    s.wait_ge(qG[0], qG0 + 16 * (t - 1))
                s.dma_start(out=xt[:, b * D:(b + 1) * D],
                            in_=x_dram[t * 128:(t + 1) * 128, :]).then_inc(qS[0], 16)
                s.dma_start(out=st_t[:, b * D:(b + 1) * D],
                            in_=s_dram[t * 128:(t + 1) * 128, :]).then_inc(qS[0], 16)
                s.dma_start(out=gt[:, b * D:(b + 1) * D],
                            in_=g_dram[t * 128:(t + 1) * 128, :]).then_inc(qS[0], 16)

        @block.vector
        def _(v):
            for t in range(NTS):
                b = t % 2
                sl = slice(b * D, (b + 1) * D)
                v.wait_ge(qS[0], qS0 + 48 * (t + 1))
                v.tensor_tensor(st_t[:, sl], st_t[:, sl], gt[:, sl], ALU.mult)
                v.tensor_tensor(xt[:, sl], xt[:, sl], st_t[:, sl], ALU.add)
                v.drain().then_inc(eV[0], 1)

        @block.gpsimd
        def _(g):
            for t in range(NTS):
                b = t % 2
                g.wait_ge(eV[0], eV0 + t + 1)
                g.dma_start(out=out_dram[t * 128:(t + 1) * 128, :],
                            in_=xt[:, b * D:(b + 1) * D]).then_inc(qG[0], 16)
            g.wait_ge(qG[0], qG0 + 16 * NTS)

    qS[1] += 48 * NTS
    eV[1] += NTS
    qG[1] += 16 * NTS
    pb.end_phase()


def build_program():
    pb = PB()
    nc = pb.nc
    ein = lambda n, sh, dt=F32: nc.dram_tensor(n, sh, dt, kind="ExternalInput")
    x0 = ein("x0", [S, D])
    wq0 = ein("L0_Wq", [D, D]); bq0 = ein("L0_bq", [D])
    wk0 = ein("L0_Wk", [D, D]); bk0 = ein("L0_bk", [D])
    wv0 = ein("L0_Wv", [D, D]); bv0 = ein("L0_bv", [D])
    wo0 = ein("L0_Wo", [D, D]); bo0 = ein("L0_bo", [D])
    gWp = ein("gWp", [MLP0, 1024], I32); gSc = ein("gSc", [MLP0, 64])
    uWp = ein("uWp", [MLP0, 1024], I32); uSc = ein("uSc", [MLP0, 64])
    dWp = ein("dWp", [D, 8192], I32); dSc = ein("dSc", [D, 512])
    wq1 = ein("L1_Wq", [D, D]); wo1 = ein("L1_Wo", [D, D])
    bw1 = ein("L1_bw", [16, 4])
    mem1 = ein("L1_mem", [4, 16, 128, 128]); mn1 = ein("L1_mn", [4, 16, 128])
    su_ug = ein("su_ug", [D]); su_dp = ein("su_dp", [INTER1])
    wsc_ug = ein("wsc_ug", [2 * INTER1]); wsc_dp = ein("wsc_dp", [D])
    w_up = ein("w_up", [INTER1, D]); w_gate = ein("w_gate", [INTER1, D])
    w_down = ein("w_down", [D, INTER1])
    out = nc.dram_tensor("out", [S, D], F32, kind="ExternalOutput")
    dt = lambda n, sh, d=BF16: nc.dram_tensor(n, sh, d)
    xn0 = dt("xn0", [NCD, 128, S])
    q_fm = dt("q_fm", [H, 128, S]); k_fm = dt("k_fm", [H, 128, S])
    v_tm = dt("v_tm", [NTS, 128, D]); ctx_fm = dt("ctx_fm", [H, 128, S])
    s0 = dt("s0", [S, D], F32)
    wg_bf = dt("wg_bf", [MLP0, D]); wu_bf = dt("wu_bf", [MLP0, D])
    wd0_bf = dt("wd0_bf", [D, MLP0])
    inter = dt("inter", [128, 128, S])
    g0 = dt("g0", [S, D], F32)
    x1 = dt("x1", [S, D], F32)
    xn1 = dt("xn1", [NCD, 128, S])
    wq1_bf = dt("wq1_bf", [D, D]); wo1_bf = dt("wo1_bf", [D, D])
    wup_bf = dt("wup_bf", [INTER1, D]); wg1_bf = dt("wg1_bf", [INTER1, D])
    wd1_bf = dt("wd1_bf", [D, INTER1])
    sq_fm = dt("sq_fm", [NCD, 128, S]); comb_fm = dt("comb_fm", [NCD, 128, S])
    s1 = dt("s1", [S, D], F32)
    xr_fm = dt("xr_fm", [NCD, 128, S])
    y1 = dt("y1", [64, 128, S])
    g1 = dt("g1", [S, D], F32)

    pb.consts()
    emit_ln(pb, x0, xn0)
    emit_qkv(pb, xn0, wq0, bq0, wk0, bk0, wv0, bv0, q_fm, k_fm, v_tm)
    emit_att0(pb, q_fm, k_fm, v_tm, ctx_fm)
    emit_wo0(pb, ctx_fm, wo0, bo0, s0)
    emit_ffn0_fused(pb, xn0, gWp, gSc, uWp, uSc, dWp, dSc,
                    wg_bf, wu_bf, wd0_bf, inter)
    emit_ffn0b(pb, inter, wd0_bf, g0, nchunks=128, act="silu")
    emit_ln(pb, x0, xn1, s_in=s0, g_in=g0, x_out=x1)
    emit_casts(pb, [(wq1, wq1_bf, D, D), (wo1, wo1_bf, D, D),
                    (w_up, wup_bf, INTER1, D), (w_gate, wg1_bf, INTER1, D)])
    emit_scaled_cast(pb, w_down, wsc_dp, wd1_bf, D, INTER1, 1.0 / np.sqrt(float(INTER1)))
    emit_l1a(pb, xn1, wq1_bf, sq_fm)
    emit_l1b(pb, sq_fm, mem1, mn1, bw1, comb_fm)
    emit_l1o(pb, comb_fm, wo1_bf, s1)
    emit_fwht1(pb, xn1, su_ug, xr_fm)
    emit_ffn1a(pb, xr_fm, wup_bf, wg1_bf, wsc_ug, su_dp, y1)
    emit_ffn1b(pb, y1, wd1_bf, g1)
    emit_res(pb, x1, s1, g1, out)
    return pb


_PROG_CACHE = {}


def kernel(**inputs):
    if "prog" not in _PROG_CACHE:
        _PROG_CACHE["prog"] = build_program()
    pb = _PROG_CACHE["prog"]
    x = np.ascontiguousarray(inputs["L0_attn_input_sequences"])
    _ = inputs["L1_attn_Wk"], inputs["L1_attn_Wv"]
    common = {
        "L0_Wq": inputs["L0_attn_Wq"], "L0_bq": inputs["L0_attn_bq"],
        "L0_Wk": inputs["L0_attn_Wk"], "L0_bk": inputs["L0_attn_bk"],
        "L0_Wv": inputs["L0_attn_Wv"], "L0_bv": inputs["L0_attn_bv"],
        "L0_Wo": inputs["L0_attn_Wo"], "L0_bo": inputs["L0_attn_bo"],
        "gWp": inputs["L0_ffn_gate_W_packed"], "gSc": inputs["L0_ffn_gate_scales"],
        "uWp": inputs["L0_ffn_up_W_packed"], "uSc": inputs["L0_ffn_up_scales"],
        "dWp": inputs["L0_ffn_down_W_packed"], "dSc": inputs["L0_ffn_down_scales"],
        "L1_Wq": inputs["L1_attn_Wq"], "L1_Wo": inputs["L1_attn_Wo"],
        "L1_bw": inputs["L1_attn_bank_weights"],
        "L1_mem": inputs["L1_attn_memories"], "L1_mn": inputs["L1_attn_memory_norms"],
        "su_ug": inputs["L1_ffn_SU_ug"], "su_dp": inputs["L1_ffn_SU_dp"],
        "wsc_ug": inputs["L1_ffn_Wscale_ug"], "wsc_dp": inputs["L1_ffn_Wscale_dp"],
        "w_up": inputs["L1_ffn_W_up"], "w_gate": inputs["L1_ffn_W_gate"],
        "w_down": inputs["L1_ffn_W_down"],
    }
    common = {k: np.ascontiguousarray(v) for k, v in common.items()}
    in_maps = [dict(common, x0=x[i]) for i in range(8)]
    res = run_bass_kernel_spmd(pb.nc, in_maps, core_ids=list(range(8)))
    return np.stack([np.asarray(res.results[i]["out"]) for i in range(8)], axis=0)


def emit_ffn0_fused(pb, xn_fm_dram, gWp, gSc, uWp, uSc, dWp, dSc,
                    wg_bf, wu_bf, wd0_bf, inter_dram):
    nc = pb.nc
    pb.begin_phase()
    KP, KO, NB, R = 512, 1024, 32, 5
    qGp = pb.sem("fz_qGp")
    qGc = pb.sem("fz_qGc")
    qGs = pb.sem("fz_qGs")
    eLd = pb.sem("fz_eLd")
    eSt = pb.sem("fz_eSt")
    eWg = pb.sem("fz_eWg")
    qS = pb.sem("qS")
    qC = pb.sem("qC")
    dV0 = pb.sem("fz_dV0")
    dV1 = pb.sem("fz_dV1")
    dV2 = pb.sem("fz_dV2")
    dS1 = pb.sem("fz_dS1")
    dS2 = pb.sem("fz_dS2")
    dP = pb.sem("fz_dP")
    fT = pb.sem("fz_fT")
    fSg = pb.sem("fz_fSg")
    fV = pb.sem("fz_fV")
    fVn = pb.sem("fz_fVn")
    wt = pb.sbuf("fz_w", [128, R * KP], I32)
    sct = pb.sbuf("fz_sc", [128, R * NB], F32)
    nibD = pb.sbuf("fz_nD", [128, R * KO], I32)
    Df = nibD[:, :].bitcast(F32)
    A = pb.sbuf("fz_A", [128, R * KO], F32)
    B = pb.sbuf("fz_B", [128, R * KO], F32)
    C = pb.sbuf("fz_C", [128, R * KO], F32)
    obv = None
    ob = B[:, :].bitcast(BF16)
    xn = pb.sbuf("fz_xn", [128, NCD * S], BF16)
    wg = pb.sbuf("fz_wg", [128, NCD * 512], BF16, td=True)
    wu = pb.sbuf("fz_wu", [128, NCD * 512], BF16, td=True)
    gsb = pb.sbuf("fz_g", [128, 512], F32)
    ibuf = pb.sbuf("fz_i", [128, 2 * S], BF16)
    ps = [pb.psum(f"fz_ps{i}", [128, 512]) for i in range(4)]

    TILES = []
    for g in range(32):
        for r in range(4 * g, 4 * g + 4):
            for kc in range(2):
                TILES.append((gWp, gSc, wg_bf, r, kc))
        for r in range(4 * g, 4 * g + 4):
            for kc in range(2):
                TILES.append((uWp, uSc, wu_bf, r, kc))
    for r in range(16):
        for kc in range(16):
            TILES.append((dWp, dSc, wd0_bf, r, kc))
    T = len(TILES)
    NCHUNK = T // 16

    qGp0, qGc0, qGs0, qS0, qC0 = qGp[1], qGc[1], qGs[1], qS[1], qC[1]
    eLd0, eSt0, eWg0 = eLd[1], eSt[1], eWg[1]
    dV00, dV10, dV20, dS10, dS20, dP0 = dV0[1], dV1[1], dV2[1], dS1[1], dS2[1], dP[1]
    fT0, fSg0, fV0, fVn0 = fT[1], fSg[1], fV[1], fVn[1]

    def dq_src(t):
        pk, sc_, dst, r, kc = TILES[t]
        return (pk[r * 128:(r + 1) * 128, kc * KP:(kc + 1) * KP],
                sc_[r * 128:(r + 1) * 128, kc * NB:(kc + 1) * NB],
                dst[r * 128:(r + 1) * 128, kc * KO:(kc + 1) * KO])

    def sl(t, n):
        s_ = t % R
        return slice(s_ * n, (s_ + 1) * n)

    LEAD = 24
    def events():
        out = []
        tnext = 0
        for j in range(512):
            target = min(T, LEAD + ((j + 1) * (T - LEAD)) // 512)
            while tnext < target:
                out.append(("dq", tnext))
                tnext += 1
            out.append(("ev", j))
        while tnext < T:
            out.append(("dq", tnext))
            tnext += 1
        for t in range(T, T + 6):
            out.append(("dq", t))
        return out

    with nc.Block(f"ffz_{pb._pctr}") as block:
        @block.gpsimd
        def _(g_):
            for t in range(T + 6):
                if t < T:
                    pk, sc_, _d = dq_src(t)
                    if t >= R:
                        g_.wait_ge(dV0[0], dV00 + t - R + 1)
                        g_.wait_ge(dV2[0], dV20 + t - R + 1)
                    g_.dma_start(out=wt[:, sl(t, KP)], in_=pk).then_inc(qGp[0], 16)
                    g_.dma_start(out=sct[:, sl(t, NB)], in_=sc_).then_inc(qGc[0], 16)
                    if t % 2 == 1:
                        g_.wait_ge(qGp[0], qGp0 + 16 * (t + 1))
                        g_.wait_ge(qGc[0], qGc0 + 16 * (t + 1))
                        g_.sem_inc(eLd[0], 2)
                if t >= R and t - R < T:
                    t3 = t - R
                    g_.wait_ge(dV2[0], dV20 + t3 + 1)
                    _pk, _sc, dst = dq_src(t3)
                    s3 = (t3 % R) * KO * 2
                    g_.dma_start(out=dst, in_=ob[:, s3:s3 + KO]).then_inc(qGs[0], 16)
                    if t3 % 16 == 15:
                        g_.wait_ge(qGs[0], qGs0 + 16 * (t3 + 1))
                        g_.sem_inc(eSt[0], 1)
            g_.wait_ge(qGs[0], qGs0 + 16 * T)

        @block.vector
        def _(v):
            for kind, t in events():
                if kind == "dq":
                    if t >= 4 and t - 4 < T:
                        t2 = t - 4
                        v.wait_ge(dS2[0], dS20 + t2 + 1)
                        v.tensor_tensor(A[:, sl(t2, KO)], A[:, sl(t2, KO)],
                                        Df[:, sl(t2, KO)], ALU.add)
                        v.scalar_tensor_tensor(C[:, sl(t2, KO)], C[:, sl(t2, KO)], 0.5,
                                               A[:, sl(t2, KO)], ALU.mult, ALU.add)
                        v.scalar_tensor_tensor(C[:, sl(t2, KO)], C[:, sl(t2, KO)], -2.0,
                                               B[:, sl(t2, KO)], ALU.add, ALU.mult)
                        s2o = (t2 % R) * KO * 2
                        cv = C[:, sl(t2, KO)].rearrange("p (bk j) -> p bk j", j=32)
                        ov = ob[:, s2o:s2o + KO].rearrange("p (bk j) -> p bk j", j=32)
                        scb = sct[:, sl(t2, NB)].unsqueeze(-1).broadcast_to((128, NB, 32))
                        v.tensor_tensor(ov, cv, scb, ALU.mult)
                        v.drain().then_inc(dV2[0], 1)
                    if t >= 2 and t - 2 < T:
                        t1 = t - 2
                        v.wait_ge(dS1[0], dS10 + t1 + 1)
                        v.scalar_tensor_tensor(C[:, sl(t1, KO)], B[:, sl(t1, KO)], 4.0,
                                               A[:, sl(t1, KO)], ALU.mult, ALU.add)
                        v.drain().then_inc(dV1[0], 1)
                    if t < T:
                        v.wait_ge(eLd[0], eLd0 + t + 1)
                        if t >= R:
                            v.wait_ge(dS1[0], dS10 + t - R + 1)
                            v.wait_ge(dS2[0], dS20 + t - R + 1)
                        nv = nibD[:, sl(t, KO)].rearrange("p (k two) -> p k two", two=2)
                        wv = wt[:, sl(t, KP)].unsqueeze(-1)
                        v.tensor_scalar(nv[:, :, 0:1], wv, 15, None, ALU.bitwise_and)
                        v.tensor_scalar(nv[:, :, 1:2], wv, 4, 15,
                                        ALU.logical_shift_right, ALU.bitwise_and)
                        v.drain().then_inc(dV0[0], 1)
                else:
                    j = t
                    nci, st = j // 4, j % 4
                    v.wait_ge(fSg[0], fSg0 + j + 1)
                    if st == 0 and nci >= 2:
                        v.wait_ge(qC[0], qC0 + 16 * (nci - 1))
                    bg = st % 2
                    ins = v.tensor_tensor(
                        ibuf[:, (nci % 2) * S + st * 512:(nci % 2) * S + (st + 1) * 512],
                        gsb[:, :], ps[bg * 2 + 1][:, :], ALU.mult)
                    ins.then_inc(fV[0], 1)
                    if st == 3:
                        v.drain().then_inc(fVn[0], 1)

        @block.scalar
        def _(sc_):
            for kind, t in events():
                if kind == "dq":
                    if t >= 1 and t - 1 < T:
                        t0 = t - 1
                        sc_.wait_ge(dV0[0], dV00 + t0 + 1)
                        if t0 >= R:
                            sc_.wait_ge(dV2[0], dV20 + t0 - R + 1)
                            sc_.wait_ge(qGs[0], qGs0 + 16 * (t0 - R + 1))
                        sc_.activation(A[:, sl(t0, KO)], nibD[:, sl(t0, KO)], AF.Copy,
                                       bias=0.0)
                        sc_.activation(B[:, sl(t0, KO)], A[:, sl(t0, KO)], AF.Sign,
                                       bias=f32(7.5), scale=f32(-1.0))
                        sc_.drain().then_inc(dS1[0], 1)
                    if t >= 3 and t - 3 < T:
                        t1 = t - 3
                        sc_.wait_ge(dV1[0], dV10 + t1 + 1)
                        sc_.activation(A[:, sl(t1, KO)], C[:, sl(t1, KO)], AF.Relu,
                                       bias=f32(-4.0), scale=f32(0.5))
                        sc_.activation(Df[:, sl(t1, KO)], C[:, sl(t1, KO)], AF.Relu,
                                       bias=f32(-10.0))
                        sc_.drain().then_inc(dS2[0], 1)
                else:
                    j = t
                    nci, st = j // 4, j % 4
                    sc_.wait_ge(fT[0], fT0 + j + 1)
                    if j >= 1:
                        sc_.wait_ge(fV[0], fV0 + j)
                    bg = st % 2
                    sc_.activation(gsb[:, :], ps[bg * 2][:, :], AF.Gelu_apprx_tanh)
                    sc_.drain().then_inc(fSg[0], 1)
                    if st == 3 and nci >= 1:
                        sc_.wait_ge(fVn[0], fVn0 + nci)
                        sc_.dma_start(out=inter_dram[nci - 1, :, :],
                                      in_=ibuf[:, ((nci - 1) % 2) * S:
                                               ((nci - 1) % 2 + 1) * S]
                                      ).then_inc(qC[0], 16)
            sc_.wait_ge(fVn[0], fVn0 + 128)
            sc_.dma_start(out=inter_dram[127, :, :], in_=ibuf[:, S:2 * S]
                          ).then_inc(qC[0], 16)
            sc_.wait_ge(qC[0], qC0 + 16 * 128)

        @block.sync
        def _(s):
            for cc in range(NCD):
                s.dma_start(out=xn[:, cc * S:(cc + 1) * S], in_=xn_fm_dram[cc, :, :]
                            ).then_inc(qS[0], 16)
            for g in range(32):
                s.wait_ge(eSt[0], eSt0 + g + 1)
                if g >= 1:
                    s.wait_ge(fT[0], fT0 + 16 * g)
                for d in range(NCD):
                    s.dma_start(out=wg[:, d * 512:(d + 1) * 512],
                                in_=wg_bf[g * 512:(g + 1) * 512, d * 128:(d + 1) * 128],
                                transpose=True).then_inc(qS[0], 16)
                for d in range(NCD):
                    s.dma_start(out=wu[:, d * 512:(d + 1) * 512],
                                in_=wu_bf[g * 512:(g + 1) * 512, d * 128:(d + 1) * 128],
                                transpose=True).then_inc(qS[0], 16)
                s.wait_ge(qS[0], qS0 + 16 * (NCD + 32 * (g + 1)))
                s.sem_inc(eWg[0], 1)

        @block.tensor
        def _(te):
            te.wait_ge(qS[0], qS0 + 16 * NCD)
            for g in range(32):
                te.wait_ge(eWg[0], eWg0 + g + 1)
                for ncg in range(4):
                    for st in range(4):
                        idx = (g * 4 + ncg) * 4 + st
                        if idx >= 2:
                            te.wait_ge(fV[0], fV0 + idx - 1)
                        bg = st % 2
                        for d in range(NCD):
                            te.matmul(ps[bg * 2][:, :],
                                      wg[:, d * 512 + ncg * 128: d * 512 + (ncg + 1) * 128],
                                      xn[:, d * S + st * 512: d * S + (st + 1) * 512],
                                      start=(d == 0), stop=(d == NCD - 1))
                        for d in range(NCD):
                            ins = te.matmul(ps[bg * 2 + 1][:, :],
                                            wu[:, d * 512 + ncg * 128: d * 512 + (ncg + 1) * 128],
                                            xn[:, d * S + st * 512: d * S + (st + 1) * 512],
                                            start=(d == 0), stop=(d == NCD - 1))
                        ins.then_inc(fT[0], 1)

    qGp[1] += 16 * T
    qGc[1] += 16 * T
    qGs[1] += 16 * T
    qS[1] += 16 * (NCD + 32 * 32)
    qC[1] += 16 * 128
    eLd[1] += T
    eSt[1] += 48
    eWg[1] += 32
    dV0[1] += T
    dV1[1] += T
    dV2[1] += T
    dS1[1] += T
    dS2[1] += T
    fT[1] += 512
    fSg[1] += 512
    fV[1] += 512
    fVn[1] += 128
    pb.end_phase()


# version 16
# speedup vs baseline: 1.0404x; 1.0148x over previous
import numpy as np
import ml_dtypes
from contextlib import ExitStack
import concourse.bass as bass
import concourse.mybir as mybir
from concourse.bass_utils import run_bass_kernel_spmd

F32 = mybir.dt.float32
BF16 = mybir.dt.bfloat16
I32 = mybir.dt.int32
AF = mybir.ActivationFunctionType
ALU = mybir.AluOpType
AXX = mybir.AxisListType.X
f32 = np.float32

D = 2048
S = 2048
H = 16
NCD = D // 128
NTS = S // 128
MLP0 = 16384
INTER1 = 8192
EPS_LN = 1e-5
EPS_L1 = 1e-6
ISQ_HD = float(1.0 / np.sqrt(128.0))
BF = np.dtype(ml_dtypes.bfloat16)


class PB:

    def __init__(self):
        self.nc = bass.Bass("TRN2", target_bir_lowering=False)
        self.stack = ExitStack()
        self.cur = self.stack
        self.sems = {}
        self._consts = False
        self._pctr = 0

    def begin_phase(self):
        self._pctr += 1
        self.cur = ExitStack()
        return self._pctr

    def end_phase(self):
        self.cur.close()
        self.cur = self.stack

    def sem(self, name):
        if name not in self.sems:
            h = self.stack.enter_context(self.nc.semaphore(name))
            self.sems[name] = [h, 0]
        return self.sems[name]

    def sbuf(self, name, shape, dt, td=False, global_=False):
        st = self.stack if global_ else self.cur
        return st.enter_context(
            self.nc.sbuf_tensor(f"{name}_{self._pctr}", list(shape), dt,
                                is_transpose_dest=td))

    def psum(self, name, shape, dt=F32):
        return self.cur.enter_context(
            self.nc.psum_tensor(f"{name}_{self._pctr}", list(shape), dt))

    def consts(self):
        if self._consts:
            return
        self._consts = True
        nc = self.nc
        self.ident = self.sbuf("c_ident", [128, 128], BF16, global_=True)
        self.trimask = self.sbuf("c_tri", [128, 128], BF16, global_=True)
        self.utri = self.sbuf("c_utri", [128, 128], I32, global_=True)
        self.neg30 = self.sbuf("c_n30", [128, 128], F32, global_=True)
        self.ones1 = self.sbuf("c_ones1", [1, 512], BF16, global_=True)
        w1 = self.sbuf("c_iw1", [128, 128], I32)
        w1b = self.sbuf("c_iw1b", [128, 128], I32)
        w2 = self.sbuf("c_iw2", [128, 128], I32)
        with nc.Block("consts") as block:
            @block.gpsimd
            def _(g):
                g.iota(w1[:, :], [[1, 128]], channel_multiplier=0)
                g.iota(w1b[:, :], [[1, 128]], channel_multiplier=128)
            @block.vector
            def _(v):
                v.memset(self.ones1[:, :], 1.0)
        with nc.Block("consts2") as block:
            @block.vector
            def _(v):
                v.tensor_scalar(w2[:, :], w1[:, :], 129, None, ALU.mult)
                v.tensor_tensor(w2[:, :], w2[:, :], w1b[:, :], ALU.is_equal)
                v.tensor_copy(self.ident[:, :], w2[:, :])
                v.tensor_tensor(w2[:, :], w1b[:, :], w1[:, :], ALU.subtract)
                v.tensor_scalar(w2[:, :], w2[:, :], 7, None, ALU.logical_shift_right)
                v.tensor_tensor(w2[:, :], w1[:, :], w2[:, :], ALU.is_le)
                v.tensor_copy(self.trimask[:, :], w2[:, :])
                v.tensor_scalar(self.utri[:, :], w2[:, :], -1, 1, ALU.mult, ALU.add)
                v.memset(self.neg30[:, :], -1e30)


def emit_ln(pb, x_dram, xn_fm_dram, s_in=None, g_in=None, x_out=None):
    nc = pb.nc
    pb.begin_phase()
    qS = pb.sem("qS")
    qG = pb.sem("qG")
    eV = pb.sem("eV")
    eS = pb.sem("eS")
    eT = pb.sem("eT")
    fused = s_in is not None
    nper = 3 if fused else 1
    nv = 3 if fused else 2
    xt = pb.sbuf("ln_x", [128, 2 * D], F32)
    st_t = pb.sbuf("ln_s", [128, 2 * D], F32) if fused else None
    gt = pb.sbuf("ln_g", [128, 2 * D], F32) if fused else None
    xsq = pb.sbuf("ln_xsq", [128, D], F32)
    stat = pb.sbuf("ln_stat", [128, 8], F32)
    xn_tm = pb.sbuf("ln_xntm", [128, NTS * D], BF16)
    qS0, qG0, eV0, eS0, eT0 = qS[1], qG[1], eV[1], eS[1], eT[1]

    with nc.Block(f"ln_a_{pb._pctr}") as block:
        @block.sync
        def _(s):
            for t in range(NTS):
                b = t % 2
                if t >= 2:
                    s.wait_ge(eV[0], eV0 + nv * (t - 1))
                    if fused:
                        s.wait_ge(qG[0], qG0 + 16 * (t - 1))
                s.dma_start(out=xt[:, b * D:(b + 1) * D],
                            in_=x_dram[t * 128:(t + 1) * 128, :]).then_inc(qS[0], 16)
                if fused:
                    s.dma_start(out=st_t[:, b * D:(b + 1) * D],
                                in_=s_in[t * 128:(t + 1) * 128, :]).then_inc(qS[0], 16)
                    s.dma_start(out=gt[:, b * D:(b + 1) * D],
                                in_=g_in[t * 128:(t + 1) * 128, :]).then_inc(qS[0], 16)

        @block.vector
        def _(v):
            for t in range(NTS):
                b = t % 2
                sl = slice(b * D, (b + 1) * D)
                v.wait_ge(qS[0], qS0 + 16 * nper * (t + 1))
                if fused:
                    v.tensor_tensor(st_t[:, sl], st_t[:, sl], gt[:, sl], ALU.mult)
                    v.tensor_tensor(xt[:, sl], xt[:, sl], st_t[:, sl], ALU.add)
                    v.drain().then_inc(eV[0], 1)
                v.tensor_reduce(stat[:, 0:1], xt[:, sl], axis=AXX, op=ALU.add)
                v.wait_ge(eS[0], eS0 + 2 * t + 1)
                v.tensor_reduce(stat[:, 1:2], xsq[:, :], axis=AXX, op=ALU.add)
                v.drain()
                v.tensor_scalar(stat[:, 2:3], stat[:, 0:1], float(1.0 / D), None, ALU.mult)
                v.drain()
                v.tensor_tensor(stat[:, 3:4], stat[:, 2:3], stat[:, 2:3], ALU.mult)
                v.drain()
                v.scalar_tensor_tensor(stat[:, 3:4], stat[:, 1:2], float(1.0 / D),
                                       stat[:, 3:4], ALU.mult, ALU.subtract)
                v.drain()
                v.tensor_scalar(stat[:, 3:4], stat[:, 3:4], float(EPS_LN), None, ALU.add)
                v.drain().then_inc(eV[0], 1)
                v.wait_ge(eS[0], eS0 + 2 * t + 2)
                v.reciprocal(stat[:, 5:6], stat[:, 4:5])
                v.drain()
                v.scalar_tensor_tensor(stat[:, 6:7], stat[:, 2:3], float(-1.0),
                                       stat[:, 5:6], ALU.mult, ALU.mult)
                v.drain()
                v.tensor_scalar(xn_tm[:, t * D:(t + 1) * D], xt[:, sl],
                                stat[:, 5:6], stat[:, 6:7], ALU.mult, ALU.add)
                v.drain().then_inc(eV[0], 1)

        @block.scalar
        def _(sc):
            for t in range(NTS):
                b = t % 2
                sc.wait_ge(qS[0], qS0 + 16 * nper * (t + 1))
                if fused:
                    sc.wait_ge(eV[0], eV0 + nv * t + 1)
                if t >= 1:
                    sc.wait_ge(eV[0], eV0 + nv * (t - 1) + nv - 1)
                sc.activation(xsq[:, :], xt[:, b * D:(b + 1) * D], AF.Square)
                sc.drain().then_inc(eS[0], 1)
                sc.wait_ge(eV[0], eV0 + nv * t + (nv - 1))
                sc.activation(stat[:, 4:5], stat[:, 3:4], AF.Sqrt)
                sc.drain().then_inc(eS[0], 1)

        if fused:
            @block.gpsimd
            def _(g):
                for t in range(NTS):
                    b = t % 2
                    g.wait_ge(eV[0], eV0 + nv * t + 1)
                    g.dma_start(out=x_out[t * 128:(t + 1) * 128, :],
                                in_=xt[:, b * D:(b + 1) * D]).then_inc(qG[0], 16)
                g.wait_ge(qG[0], qG0 + 16 * NTS)
    qS[1] += 16 * nper * NTS
    eV[1] += nv * NTS
    eS[1] += 2 * NTS
    if fused:
        qG[1] += 16 * NTS

    ptb = [pb.psum(f"ln_pt{i}", [128, 1024], BF16) for i in range(4)]
    stage = pb.sbuf("ln_stage", [128, 2 * D], BF16)
    eT0, eV0, qG0 = eT[1], eV[1], qG[1]
    with nc.Block(f"ln_b_{pb._pctr}") as block:
        @block.tensor
        def _(te):
            for t in range(NTS):
                for c in range(NCD):
                    k = t * NCD + c
                    if k >= 4:
                        te.wait_ge(eV[0], eV0 + k - 3)
                    te.transpose(ptb[k % 4][:, 0:128],
                                 xn_tm[:, t * D + c * 128: t * D + (c + 1) * 128],
                                 pb.ident[:, :]).then_inc(eT[0], 1)

        @block.vector
        def _(v):
            for t in range(NTS):
                b = t % 2
                if t >= 2:
                    v.wait_ge(qG[0], qG0 + 16 * NCD * (t - 1))
                for c in range(NCD):
                    k = t * NCD + c
                    v.wait_ge(eT[0], eT0 + k + 1)
                    v.tensor_copy(stage[:, b * D + c * 128: b * D + (c + 1) * 128],
                                  ptb[k % 4][:, 0:128])
                    v.drain().then_inc(eV[0], 1)

        @block.gpsimd
        def _(g):
            for t in range(NTS):
                b = t % 2
                g.wait_ge(eV[0], eV0 + NCD * (t + 1))
                for c in range(NCD):
                    g.dma_start(out=xn_fm_dram[c, :, t * 128:(t + 1) * 128],
                                in_=stage[:, b * D + c * 128: b * D + (c + 1) * 128]
                                ).then_inc(qG[0], 16)
            g.wait_ge(qG[0], qG0 + 16 * NCD * NTS)
    eT[1] += NTS * NCD
    eV[1] += NTS * NCD
    qG[1] += 16 * NCD * NTS
    pb.end_phase()


def emit_qkv(pb, xn_fm_dram, wq, bq, wk, bk, wv, bv, q_fm_dram, k_fm_dram, v_tm_dram):
    nc = pb.nc
    pb.begin_phase()
    qS = pb.sem("qS")
    qG = pb.sem("qG")
    qC = pb.sem("qC")
    eT = pb.sem("eT")
    eV = pb.sem("eV")
    eWq = pb.sem("qk_eWq")
    xn = pb.sbuf("qk_xn", [128, NCD * S], BF16)
    wt = pb.sbuf("qk_w", [128, NCD * D], BF16)
    brow = pb.sbuf("qk_b", [1, 3 * D], BF16)
    outst = pb.sbuf("qk_o", [128, 2 * D], BF16)
    ps = [pb.psum(f"qk_ps{i}", [128, 512], F32) for i in range(8)]
    qS0, qG0, qC0, eT0, eV0 = qS[1], qG[1], qC[1], eT[1], eV[1]
    eWq0 = eWq[1]

    with nc.Block(f"qkv_{pb._pctr}") as block:
        @block.sync
        def _(s):
            for c in range(NCD):
                s.dma_start(out=xn[:, c * S:(c + 1) * S], in_=xn_fm_dram[c, :, :]
                            ).then_inc(qS[0], 16)

        @block.gpsimd
        def _(g):
            for i, b in enumerate((bq, bk, bv)):
                g.dma_start(out=brow[0:1, i * D:(i + 1) * D], in_=b[None, :]).then_inc(qG[0], 16)
            for wi, w in enumerate((wq, wk, wv)):
                if wi > 0:
                    g.wait_ge(eT[0], eT0 + 16 * wi)
                for c in range(NCD):
                    g.dma_start(out=wt[:, c * D:(c + 1) * D],
                                in_=w[c * 128:(c + 1) * 128, :]).then_inc(qG[0], 16)
                g.wait_ge(qG[0], qG0 + 16 * (3 + NCD * (wi + 1)))
                g.sem_inc(eWq[0], 1)

        @block.tensor
        def _(te):
            te.wait_ge(qS[0], qS0 + 16 * NCD)
            for wi in range(2):
                te.wait_ge(eWq[0], eWq0 + wi + 1)
                for n in range(NCD):
                    grp = wi * 16 + n
                    if grp >= 2:
                        te.wait_ge(eV[0], eV0 + 4 * (grp - 1))
                    bank = (grp % 2) * 4
                    for d in range(NCD):
                        for t in range(4):
                            te.matmul(ps[bank + t][:, :],
                                      wt[:, d * D + n * 128: d * D + (n + 1) * 128],
                                      xn[:, d * S + t * 512: d * S + (t + 1) * 512],
                                      start=(d == 0), stop=False)
                    for t in range(4):
                        ins = te.matmul(ps[bank + t][:, :],
                                        brow[0:1, wi * D + n * 128: wi * D + (n + 1) * 128],
                                        pb.ones1[0:1, 0:512], start=False, stop=True)
                    ins.then_inc(eT[0], 1)
            te.wait_ge(eWq[0], eWq0 + 3)
            for t in range(NTS):
                grp = 32 + t
                if grp >= 2:
                    te.wait_ge(eV[0], eV0 + 4 * (grp - 1))
                bank = (grp % 2) * 4
                for d in range(NCD):
                    for nb in range(4):
                        te.matmul(ps[bank + nb][:, :],
                                  xn[:, d * S + t * 128: d * S + (t + 1) * 128],
                                  wt[:, d * D + nb * 512: d * D + (nb + 1) * 512],
                                  start=(d == 0), stop=False)
                for nb in range(4):
                    ins = te.matmul(ps[bank + nb][:, :],
                                    pb.ones1[0:1, 0:128],
                                    brow[0:1, 2 * D + nb * 512: 2 * D + (nb + 1) * 512],
                                    start=False, stop=True)
                ins.then_inc(eT[0], 1)

        @block.vector
        def _(v):
            for grp in range(48):
                v.wait_ge(eT[0], eT0 + grp + 1)
                if grp >= 2:
                    v.wait_ge(qC[0], qC0 + 16 * (grp - 1))
                b = (grp % 2) * D
                bank = (grp % 2) * 4
                for t in range(4):
                    v.tensor_copy(outst[:, b + t * 512: b + (t + 1) * 512],
                                  ps[bank + t][:, :]).then_inc(eV[0], 1)

        @block.scalar
        def _(sc):
            for grp in range(48):
                sc.wait_ge(eV[0], eV0 + 4 * (grp + 1))
                b = (grp % 2) * D
                if grp < 16:
                    dst = q_fm_dram[grp, :, :]
                elif grp < 32:
                    dst = k_fm_dram[grp - 16, :, :]
                else:
                    dst = v_tm_dram[grp - 32, :, :]
                sc.dma_start(out=dst, in_=outst[:, b:b + D]).then_inc(qC[0], 16)
            sc.wait_ge(qC[0], qC0 + 16 * 48)

    qS[1] += 16 * NCD
    qG[1] += 16 * (3 + 3 * NCD)
    eWq[1] += 3
    eT[1] += 48
    eV[1] += 192
    qC[1] += 16 * 48
    pb.end_phase()


def emit_att0(pb, q_fm_dram, k_fm_dram, v_tm_dram, ctx_fm_dram,
              dq=None, r0=0, r1=0):
    nc = pb.nc
    pb.begin_phase()
    qS = pb.sem("qS")
    qG = pb.sem("qG")
    eHd = pb.sem("at_eHd")
    eT = pb.sem("at_eT")
    eTt = pb.sem("at_eTt")
    eTc = pb.sem("at_eTc")
    eVm = pb.sem("at_eVm")
    eS = pb.sem("at_eS")
    eVp = pb.sem("at_eVp")
    eVe = pb.sem("at_eVe")
    eSe = pb.sem("at_eSe")
    eVz = pb.sem("at_eVz")
    eVc = pb.sem("at_eVc")
    qh = pb.sbuf("at_q", [128, 2 * S], BF16)
    kh = pb.sbuf("at_k", [128, 2 * S], BF16)
    vh = pb.sbuf("at_v", [128, 2 * S], BF16)
    E = pb.sbuf("at_E", [128, S], BF16)
    P = pb.sbuf("at_P", [128, S], BF16)
    PT = pb.sbuf("at_PT", [128, 16 * 512], BF16)
    cst = pb.sbuf("at_cst", [128, 2 * 512], BF16)
    mst = pb.sbuf("at_mst", [128, 16], F32)
    psc = [pb.psum(f"at_sc{i}", [128, 512]) for i in range(4)]
    pst = [pb.psum(f"at_tp{i}", [128, 1024], BF16) for i in range(2)]
    pctx = pb.psum("at_ctx", [128, 512])
    qS0, qG0 = qS[1], qG[1]
    (eHd0, eT0, eTt0, eTc0, eVm0, eS0, eVp0, eVe0, eSe0, eVz0, eVc0) = (
        eHd[1], eT[1], eTt[1], eTc[1], eVm[1], eS[1], eVp[1], eVe[1], eSe[1],
        eVz[1], eVc[1])

    def tr_bf(h, i):
        return h * 136 + i * (i + 1) // 2

    def grp_bf(h, i):
        return h * 40 + sum((ii + 4) // 4 for ii in range(i))

    NS = H * 17
    bounds = [r0 + s * (r1 - r0) // NS for s in range(NS + 4)]
    for s in range(NS, NS + 4):
        bounds[s] = r1

    def v_iters(s):
        return range(bounds[s], bounds[s + 1])

    def s_iters(s):
        if s == 0:
            return range(bounds[0], bounds[2])
        return range(bounds[s + 1], bounds[s + 2])

    def p_iters(s):
        if s == 0:
            return range(bounds[0], bounds[3])
        return range(bounds[s + 2], bounds[s + 3])

    def z_bf(h, i):
        perh = sum(4 * (ii // 4 + 1) - (ii + 1) for ii in range(16))
        return h * perh + sum(4 * (ii // 4 + 1) - (ii + 1) for ii in range(i))

    ZPH = sum(4 * (ii // 4 + 1) - (ii + 1) for ii in range(16))

    with nc.Block(f"att0_{pb._pctr}") as block:
        @block.sync
        def _(s):
            for h in range(H):
                b = h % 2
                if h >= 2:
                    s.wait_ge(eT[0], eT0 + (h - 1) * 16)
                    s.wait_ge(eTc[0], eTc0 + (h - 1) * 4)
                s.dma_start(out=qh[:, b * S:(b + 1) * S], in_=q_fm_dram[h, :, :]
                            ).then_inc(qS[0], 16)
                s.dma_start(out=kh[:, b * S:(b + 1) * S], in_=k_fm_dram[h, :, :]
                            ).then_inc(qS[0], 16)
                for t in range(NTS):
                    s.dma_start(out=vh[:, b * S + t * 128: b * S + (t + 1) * 128],
                                in_=v_tm_dram[t, :, h * 128:(h + 1) * 128]
                                ).then_inc(qS[0], 16)
                s.wait_ge(qS[0], qS0 + 16 * 18 * (h + 1))
                s.sem_inc(eHd[0], 1)

        @block.tensor
        def _(te):
            for h in range(H):
                b = h % 2
                te.wait_ge(eHd[0], eHd0 + h + 1)
                for it in range(17):
                    if it < 16:
                        i = it
                        k = h * 16 + i
                        jbn = i // 4 + 1
                        if k >= 1:
                            te.wait_ge(eS[0], eS0 + k)
                        for jb in range(jbn):
                            ins = te.matmul(psc[jb][:, :],
                                            qh[:, b * S + i * 128: b * S + (i + 1) * 128],
                                            kh[:, b * S + jb * 512: b * S + (jb + 1) * 512],
                                            start=True, stop=True)
                        ins.then_inc(eT[0], 1)
                    if it >= 1:
                        i = it - 1
                        k = h * 16 + i
                        te.wait_ge(eVp[0], eVp0 + k + 1)
                        for gi2 in range((i + 4) // 4):
                            j0 = gi2 * 4
                            j1 = min(i + 1, j0 + 4)
                            gidx = grp_bf(h, i) + gi2
                            if gidx >= 2:
                                te.wait_ge(eVe[0], eVe0 + gidx - 1)
                            for jt in range(j0, j1):
                                te.transpose(pst[gidx % 2][:, (jt - j0) * 128:
                                                           (jt - j0 + 1) * 128],
                                             P[:, jt * 128:(jt + 1) * 128],
                                             pb.ident[:, :]).then_inc(eTt[0], 1)
                        if i % 4 == 3:
                            g = h * 4 + i // 4
                            if g >= 1:
                                te.wait_ge(eVc[0], eVc0 + g)
                            te.wait_ge(eVe[0], eVe0 + grp_bf(h, i + 1))
                            te.wait_ge(eVz[0], eVz0 + z_bf(h, i + 1))
                            njt = 4 * (i // 4 + 1)
                            for jt in range(njt):
                                ins = te.matmul(pctx[:, :],
                                                vh[:, b * S + jt * 128: b * S + (jt + 1) * 128],
                                                PT[:, jt * 512:(jt + 1) * 512],
                                                start=(jt == 0), stop=(jt == njt - 1))
                            ins.then_inc(eTc[0], 1)

        @block.vector
        def _(v):
            for h in range(H):
                for it in range(17):
                    if dq is not None:
                        for t_dq in v_iters(h * 17 + it):
                            dq.v_step(v, t_dq)
                    if it >= 1:
                        i = it - 1
                        ii = i % 4
                        for gi2 in range((i + 4) // 4):
                            j0 = gi2 * 4
                            j1 = min(i + 1, j0 + 4)
                            gidx = grp_bf(h, i) + gi2
                            v.wait_ge(eTt[0], eTt0 + tr_bf(h, i) + j1)
                            dst = bass.AP(PT.tensor if hasattr(PT, "tensor") else PT,
                                          0, [[0, 0]])
                            ptv = PT[:, :].rearrange("p (jt c) -> p jt c", c=512)
                            v.tensor_copy(ptv[:, j0:j1, ii * 128:(ii + 1) * 128],
                                          pst[gidx % 2][:, 0:(j1 - j0) * 128]
                                          .rearrange("p (jt c) -> p jt c", c=128)
                                          ).then_inc(eVe[0], 1)
                        for jt in range(i + 1, 4 * (i // 4 + 1)):
                            v.memset(PT[:, jt * 512 + ii * 128:
                                        jt * 512 + (ii + 1) * 128], 0.0)
                            v.drain().then_inc(eVz[0], 1)
                        if i % 4 == 3:
                            g = h * 4 + i // 4
                            v.wait_ge(eTc[0], eTc0 + g + 1)
                            if g >= 2:
                                v.wait_ge(qG[0], qG0 + 16 * (g - 1))
                            v.tensor_copy(cst[:, (g % 2) * 512:(g % 2 + 1) * 512],
                                          pctx[:, :]).then_inc(eVc[0], 1)
                    if it < 16:
                        i = it
                        k = h * 16 + i
                        jbn = i // 4 + 1
                        v.wait_ge(eT[0], eT0 + k + 1)
                        v.copy_predicated(psc[i // 4][:, (i % 4) * 128:(i % 4 + 1) * 128],
                                          pb.utri[:, :], pb.neg30[:, :])
                        lo = (i + 1) * 128 - (jbn - 1) * 512
                        if lo < 512:
                            v.memset(psc[jbn - 1][:, lo:512], -1e30)
                        v.drain()
                        for jb in range(jbn):
                            v.reduce_max(mst[:, jb:jb + 1], psc[jb][:, :], axis=AXX)
                        v.drain()
                        v.reduce_max(mst[:, 4:5], mst[:, 0:jbn], axis=AXX)
                        v.drain()
                        v.tensor_scalar(mst[:, 5:6], mst[:, 4:5], float(-ISQ_HD), None,
                                        ALU.mult)
                        v.drain().then_inc(eVm[0], 1)
                        v.wait_ge(eS[0], eS0 + k + 1)
                        v.tensor_reduce(mst[:, 6:7], mst[:, 8:8 + jbn], axis=AXX,
                                        op=ALU.add)
                        v.drain()
                        v.reciprocal(mst[:, 7:8], mst[:, 6:7])
                        v.drain()
                        v.wait_ge(eTt[0], eTt0 + tr_bf(h, i))
                        v.tensor_scalar(P[:, 0:jbn * 512], E[:, 0:jbn * 512],
                                        mst[:, 7:8], None, ALU.mult)
                        v.drain().then_inc(eVp[0], 1)
        @block.scalar
        def _(sc):
            for h in range(H):
                for it in range(17):
                    if dq is not None:
                        for t_dq in s_iters(h * 17 + it):
                            dq.s_step(sc, t_dq)

                    if it < 16:
                        i = it
                        k = h * 16 + i
                        jbn = i // 4 + 1
                        sc.wait_ge(eVm[0], eVm0 + k + 1)
                        if k >= 1:
                            sc.wait_ge(eVp[0], eVp0 + k)
                        for jb in range(jbn):
                            sc.activation(E[:, jb * 512:(jb + 1) * 512], psc[jb][:, :],
                                          AF.Exp, bias=mst[:, 5:6], scale=f32(ISQ_HD),
                                          accum_out=mst[:, 8 + jb:9 + jb])
                        sc.drain().then_inc(eS[0], 1)
        @block.gpsimd
        def _(g_):
            for h in range(H):
                for it in range(17):
                    if dq is not None:
                        for t_dq in p_iters(h * 17 + it):
                            dq.pool_step(g_, t_dq)
                    if it >= 4 and it % 4 == 0:
                        ig = it // 4 - 1
                        g = h * 4 + ig
                        g_.wait_ge(eVc[0], eVc0 + g + 1)
                        g_.dma_start(out=ctx_fm_dram[h, :, ig * 512:(ig + 1) * 512],
                                     in_=cst[:, (g % 2) * 512:(g % 2 + 1) * 512]
                                     ).then_inc(qG[0], 16)
            g_.wait_ge(qG[0], qG0 + 16 * 64)

    qS[1] += 16 * 18 * H
    eHd[1] += H
    eT[1] += 16 * H
    eTt[1] += 136 * H
    eTc[1] += 4 * H
    eVm[1] += 16 * H
    eS[1] += 16 * H
    eVp[1] += 16 * H
    eVe[1] += 40 * H
    eVz[1] += ZPH * H
    eVc[1] += 4 * H
    qG[1] += 16 * 64
    pb.end_phase()


def emit_wo0(pb, ctx_fm_dram, wo, bo, s0_dram):
    nc = pb.nc
    pb.begin_phase()
    qS = pb.sem("qS")
    qG = pb.sem("qG")
    qC = pb.sem("qC")
    eT = pb.sem("wo_eT")
    eV = pb.sem("wo_eV")
    ctx = pb.sbuf("wo_ctx", [128, NCD * S], BF16)
    wt = pb.sbuf("wo_w", [128, NCD * D], BF16)
    brow = pb.sbuf("wo_b", [1, D], BF16)
    stage = pb.sbuf("wo_st", [128, 2 * D], F32)
    ps = [pb.psum(f"wo_ps{i}", [128, 512]) for i in range(8)]
    qS0, qG0, qC0, eT0, eV0 = qS[1], qG[1], qC[1], eT[1], eV[1]

    with nc.Block(f"wo0_{pb._pctr}") as block:
        @block.sync
        def _(s):
            for c in range(NCD):
                s.dma_start(out=ctx[:, c * S:(c + 1) * S], in_=ctx_fm_dram[c, :, :]
                            ).then_inc(qS[0], 16)

        @block.gpsimd
        def _(g):
            g.dma_start(out=brow[0:1, :], in_=bo[None, :]).then_inc(qG[0], 16)
            for c in range(NCD):
                g.dma_start(out=wt[:, c * D:(c + 1) * D],
                            in_=wo[c * 128:(c + 1) * 128, :]).then_inc(qG[0], 16)

        @block.tensor
        def _(te):
            te.wait_ge(qS[0], qS0 + 16 * NCD)
            te.wait_ge(qG[0], qG0 + 16 * (NCD + 1))
            for t in range(NTS):
                if t >= 2:
                    te.wait_ge(eV[0], eV0 + t - 1)
                bank = (t % 2) * 4
                for nb in range(4):
                    for d in range(NCD):
                        te.matmul(ps[bank + nb][:, :],
                                  ctx[:, d * S + t * 128: d * S + (t + 1) * 128],
                                  wt[:, d * D + nb * 512: d * D + (nb + 1) * 512],
                                  start=(d == 0), stop=False)
                    ins = te.matmul(ps[bank + nb][:, :], pb.ones1[0:1, 0:128],
                                    brow[0:1, nb * 512:(nb + 1) * 512],
                                    start=False, stop=True)
                ins.then_inc(eT[0], 1)

        @block.vector
        def _(v):
            for t in range(NTS):
                v.wait_ge(eT[0], eT0 + t + 1)
                if t >= 2:
                    v.wait_ge(qC[0], qC0 + 16 * (t - 1))
                b = (t % 2) * D
                bank = (t % 2) * 4
                for nb in range(4):
                    v.tensor_copy(stage[:, b + nb * 512: b + (nb + 1) * 512],
                                  ps[bank + nb][:, :])
                v.drain().then_inc(eV[0], 1)

        @block.scalar
        def _(sc):
            for t in range(NTS):
                sc.wait_ge(eV[0], eV0 + t + 1)
                sc.dma_start(out=s0_dram[t * 128:(t + 1) * 128, :],
                             in_=stage[:, (t % 2) * D:(t % 2 + 1) * D]).then_inc(qC[0], 16)
            sc.wait_ge(qC[0], qC0 + 16 * NTS)

    qS[1] += 16 * NCD
    qG[1] += 16 * (NCD + 1)
    eT[1] += NTS
    eV[1] += NTS
    qC[1] += 16 * NTS
    pb.end_phase()


def emit_dq(pb, packed_dram, scales_dram, out_dram, NR, KPT):
    nc = pb.nc
    pb.begin_phase()
    qS = pb.sem("qS")
    qG = pb.sem("qG")
    eP = pb.sem("dq_eP")
    eV0 = pb.sem("dq_eV0")
    eS1 = pb.sem("dq_eS1")
    eS2 = pb.sem("dq_eS2")
    eV1 = pb.sem("dq_eV1")
    eV2 = pb.sem("dq_eV2")
    KP = 1024
    KO = 2 * KP
    NB = KO // 32
    nrt = NR // 128
    nkc = KPT // KP
    ntile = nrt * nkc
    wt = pb.sbuf("dq_w", [128, 2 * KP], I32)
    sct = pb.sbuf("dq_sc", [128, 2 * NB], F32)
    nib = pb.sbuf("dq_nib", [128, 2 * KO], I32)
    A = pb.sbuf("dq_A", [128, 2 * KO], F32)
    B = pb.sbuf("dq_B", [128, 2 * KO], F32)
    C = pb.sbuf("dq_C", [128, 2 * KO], F32)
    Dd = pb.sbuf("dq_D", [128, 2 * KO], F32)
    ob = pb.sbuf("dq_o", [128, 2 * KO], BF16)
    qS0, qG0 = qS[1], qG[1]
    eP0, eS10, eS20, eV10, eV20 = eP[1], eS1[1], eS2[1], eV1[1], eV2[1]
    eV00 = eV0[1]

    def tiles():
        for r in range(nrt):
            for kc in range(nkc):
                yield r, kc

    with nc.Block(f"dq_{pb._pctr}") as block:
        @block.sync
        def _(s):
            for k, (r, kc) in enumerate(tiles()):
                b = k % 2
                if k >= 2:
                    s.wait_ge(eV0[0], eV00 + k - 1)
                    s.wait_ge(eV2[0], eV20 + k - 1)
                s.dma_start(out=wt[:, b * KP:(b + 1) * KP],
                            in_=packed_dram[r * 128:(r + 1) * 128, kc * KP:(kc + 1) * KP]
                            ).then_inc(qS[0], 16)
                s.dma_start(out=sct[:, b * NB:(b + 1) * NB],
                            in_=scales_dram[r * 128:(r + 1) * 128, kc * NB:(kc + 1) * NB]
                            ).then_inc(qS[0], 16)

        @block.gpsimd
        def _(g):
            for k in range(ntile):
                b = k % 2
                sl = slice(b * KO, (b + 1) * KO)
                g.wait_ge(eS2[0], eS20 + k + 1)
                g.tensor_tensor(A[:, sl], A[:, sl], Dd[:, sl], ALU.add)
                g.drain().then_inc(eP[0], 1)
                if k >= 1:
                    g.wait_ge(eV2[0], eV20 + k)
                    bb = (k - 1) % 2
                    rr, kk = (k - 1) // nkc, (k - 1) % nkc
                    g.dma_start(out=out_dram[rr * 128:(rr + 1) * 128,
                                             kk * KO:(kk + 1) * KO],
                                in_=ob[:, bb * KO:(bb + 1) * KO]).then_inc(qG[0], 16)
            g.wait_ge(eV2[0], eV20 + ntile)
            bb = (ntile - 1) % 2
            rr, kk = (ntile - 1) // nkc, (ntile - 1) % nkc
            g.dma_start(out=out_dram[rr * 128:(rr + 1) * 128, kk * KO:(kk + 1) * KO],
                        in_=ob[:, bb * KO:(bb + 1) * KO]).then_inc(qG[0], 16)
            g.wait_ge(qG[0], qG0 + 16 * ntile)

        @block.scalar
        def _(sc):
            for k in range(ntile):
                b = k % 2
                sl = slice(b * KO, (b + 1) * KO)
                sc.wait_ge(eV0[0], eV00 + k + 1)
                if k >= 2:
                    sc.wait_ge(eP[0], eP0 + k - 1)
                    sc.wait_ge(eV2[0], eV20 + k - 1)
                sc.activation(A[:, sl], nib[:, sl], AF.Copy, bias=0.0)
                sc.activation(B[:, sl], A[:, sl], AF.Sign, bias=f32(7.5), scale=f32(-1.0))
                sc.drain().then_inc(eS1[0], 1)
                sc.wait_ge(eV1[0], eV10 + k + 1)
                sc.activation(A[:, sl], C[:, sl], AF.Relu, bias=f32(-4.0), scale=f32(0.5))
                sc.activation(Dd[:, sl], C[:, sl], AF.Relu, bias=f32(-10.0))
                sc.drain().then_inc(eS2[0], 1)

        @block.vector
        def _(v):
            for k in range(ntile):
                b = k % 2
                sl = slice(b * KO, (b + 1) * KO)
                v.wait_ge(qS[0], qS0 + 32 * k + 16)
                if k >= 2:
                    v.wait_ge(eS1[0], eS10 + k - 1)
                nv = nib[:, sl].rearrange("p (k two) -> p k two", two=2)
                wv = wt[:, b * KP:(b + 1) * KP].unsqueeze(-1)
                v.tensor_scalar(nv[:, :, 0:1], wv, 15, None, ALU.bitwise_and)
                v.tensor_scalar(nv[:, :, 1:2], wv, 4, 15,
                                ALU.logical_shift_right, ALU.bitwise_and)
                v.drain().then_inc(eV0[0], 1)
                v.wait_ge(eS1[0], eS10 + k + 1)
                v.scalar_tensor_tensor(C[:, sl], B[:, sl], 4.0, A[:, sl],
                                       ALU.mult, ALU.add)
                v.drain().then_inc(eV1[0], 1)
                v.wait_ge(eP[0], eP0 + k + 1)
                v.scalar_tensor_tensor(C[:, sl], C[:, sl], 0.5, A[:, sl],
                                       ALU.mult, ALU.add)
                v.scalar_tensor_tensor(C[:, sl], C[:, sl], -2.0, B[:, sl],
                                       ALU.add, ALU.mult)
                if k >= 2:
                    v.wait_ge(qG[0], qG0 + 16 * (k - 1))
                v.wait_ge(qS[0], qS0 + 32 * (k + 1))
                cv = C[:, sl].rearrange("p (bk j) -> p bk j", j=32)
                ov = ob[:, sl].rearrange("p (bk j) -> p bk j", j=32)
                for hb in range(NB // 32):
                    scv = sct[:, b * NB + hb * 32: b * NB + (hb + 1) * 32]
                    scb = scv.unsqueeze(-1).broadcast_to((128, 32, 32))
                    v.tensor_tensor(ov[:, hb * 32:(hb + 1) * 32, :],
                                    cv[:, hb * 32:(hb + 1) * 32, :], scb, ALU.mult)
                v.drain().then_inc(eV2[0], 1)

    qS[1] += 32 * ntile
    qG[1] += 16 * ntile
    eP[1] += ntile
    eV0[1] += ntile
    eS1[1] += ntile
    eS2[1] += ntile
    eV1[1] += ntile
    eV2[1] += ntile
    pb.end_phase()


def emit_ffn0a(pb, xn_fm_dram, wg_bf, wu_bf, inter_dram):
    nc = pb.nc
    pb.begin_phase()
    qS = pb.sem("qS")
    qG = pb.sem("qG")
    qC = pb.sem("qC")
    eT = pb.sem("fa_eT")
    eSg = pb.sem("fa_eSg")
    eV = pb.sem("fa_eV")
    eVn = pb.sem("fa_eVn")
    xn = pb.sbuf("fa_xn", [128, NCD * S], BF16)
    wg = pb.sbuf("fa_wg", [128, 2 * NCD * 512], BF16, td=True)
    wu = pb.sbuf("fa_wu", [128, 2 * NCD * 512], BF16, td=True)
    gsb = pb.sbuf("fa_g", [128, 512], F32)
    ibuf = pb.sbuf("fa_i", [128, 2 * S], BF16)
    ps = [pb.psum(f"fa_ps{i}", [128, 512]) for i in range(4)]
    qS0, qG0, qC0 = qS[1], qG[1], qC[1]
    eT0, eSg0, eV0, eVn0 = eT[1], eSg[1], eV[1], eVn[1]
    NG = 32

    with nc.Block(f"ffn0a_{pb._pctr}") as block:
        @block.gpsimd
        def _(g):
            for c in range(NCD):
                g.dma_start(out=xn[:, c * S:(c + 1) * S], in_=xn_fm_dram[c, :, :]
                            ).then_inc(qG[0], 16)

        @block.sync
        def _(s):
            for g in range(NG):
                b = g % 2
                if g >= 2:
                    s.wait_ge(eT[0], eT0 + 16 * (g - 1))
                for d in range(NCD):
                    s.dma_start(out=wg[:, (b * NCD + d) * 512:(b * NCD + d + 1) * 512],
                                in_=wg_bf[g * 512:(g + 1) * 512, d * 128:(d + 1) * 128],
                                transpose=True).then_inc(qS[0], 16)
                    s.dma_start(out=wu[:, (b * NCD + d) * 512:(b * NCD + d + 1) * 512],
                                in_=wu_bf[g * 512:(g + 1) * 512, d * 128:(d + 1) * 128],
                                transpose=True).then_inc(qS[0], 16)

        @block.tensor
        def _(te):
            te.wait_ge(qG[0], qG0 + 16 * NCD)
            for g in range(NG):
                b = g % 2
                te.wait_ge(eWf[0], eWf0 + g + 1)
                for ncg in range(4):
                    for st in range(4):
                        idx = (g * 4 + ncg) * 4 + st
                        if idx >= 2:
                            te.wait_ge(eV[0], eV0 + idx - 1)
                        bg = (st % 2) * 2
                        for d in range(NCD):
                            te.matmul(ps[bg][:, :],
                                      wg[:, (b * NCD + d) * 512 + ncg * 128:
                                         (b * NCD + d) * 512 + (ncg + 1) * 128],
                                      xn[:, d * S + st * 512: d * S + (st + 1) * 512],
                                      start=(d == 0), stop=(d == NCD - 1))
                        for d in range(NCD):
                            ins = te.matmul(ps[bg + 1][:, :],
                                            wu[:, (b * NCD + d) * 512 + ncg * 128:
                                               (b * NCD + d) * 512 + (ncg + 1) * 128],
                                            xn[:, d * S + st * 512: d * S + (st + 1) * 512],
                                            start=(d == 0), stop=(d == NCD - 1))
                        ins.then_inc(eT[0], 1)

        @block.scalar
        def _(sc):
            for nc_i in range(128):
                for st in range(4):
                    idx = nc_i * 4 + st
                    sc.wait_ge(eT[0], eT0 + idx + 1)
                    if idx >= 1:
                        sc.wait_ge(eV[0], eV0 + idx)
                    bg = (st % 2) * 2
                    sc.activation(gsb[:, :], ps[bg][:, :], AF.Gelu_apprx_tanh)
                    sc.drain().then_inc(eSg[0], 1)
                if nc_i >= 1:
                    sc.wait_ge(eVn[0], eVn0 + nc_i)
                    pb_ = (nc_i - 1) % 2
                    sc.dma_start(out=inter_dram[nc_i - 1, :, :],
                                 in_=ibuf[:, pb_ * S:(pb_ + 1) * S]).then_inc(qC[0], 16)
            sc.wait_ge(eVn[0], eVn0 + 128)
            sc.dma_start(out=inter_dram[127, :, :], in_=ibuf[:, S:2 * S]
                         ).then_inc(qC[0], 16)
            sc.wait_ge(qC[0], qC0 + 16 * 128)

        @block.vector
        def _(v):
            for nc_i in range(128):
                nb = nc_i % 2
                for st in range(4):
                    idx = nc_i * 4 + st
                    v.wait_ge(eSg[0], eSg0 + idx + 1)
                    if nc_i >= 2 and st == 0:
                        v.wait_ge(qC[0], qC0 + 16 * (nc_i - 1))
                    bg = (st % 2) * 2
                    v.tensor_tensor(ibuf[:, nb * S + st * 512: nb * S + (st + 1) * 512],
                                    gsb[:, :], ps[bg + 1][:, :],
                                    ALU.mult).then_inc(eV[0], 1)
                v.drain().then_inc(eVn[0], 1)

    qG[1] += 16 * NCD
    qS[1] += 16 * 32 * NG
    eT[1] += 512
    eSg[1] += 512
    eV[1] += 512
    eVn[1] += 128
    qC[1] += 16 * 128
    pb.end_phase()


def emit_ffn0b(pb, inter_dram, wd_bf, g0_dram, nchunks=128, act="silu"):
    nc = pb.nc
    pb.begin_phase()
    qS = pb.sem("qS")
    qG = pb.sem("qG")
    qGs2 = pb.sem("fb_qGs2")
    qC = pb.sem("qC")
    eTk = pb.sem("fb_eTk")
    eS = pb.sem("fb_eS")
    eWdS = pb.sem("fb_eWdS")
    eWdC = pb.sem("fb_eWdC")
    RING = 8
    ibig = pb.sbuf("fb_i", [128, nchunks * 512], BF16)
    wd = pb.sbuf("fb_wd", [128, RING * 1024], BF16, td=True)
    stage = pb.sbuf("fb_st", [128, 2 * 4096], F32)
    ps = [pb.psum(f"fb_ps{i}", [128, 512]) for i in range(8)]
    qS0, qG0, qGs20, qC0 = qS[1], qG[1], qGs2[1], qC[1]
    eTk0, eS0, eWdS0, eWdC0 = eTk[1], eS[1], eWdS[1], eWdC[1]
    afunc = AF.Silu if act == "silu" else AF.Copy
    NGI = 8

    with nc.Block(f"ffn0b_{pb._pctr}") as block:
        @block.gpsimd
        def _(g):
            for sg in range(4):
                if sg > 0:
                    g.wait_ge(eTk[0], eTk0 + sg * 2 * nchunks)
                for nc_i in range(nchunks):
                    g.dma_start(out=ibig[:, nc_i * 512:(nc_i + 1) * 512],
                                in_=inter_dram[nc_i, :, sg * 512:(sg + 1) * 512]
                                ).then_inc(qG[0], 16)
                if sg > 0:
                    for mbp in range(2):
                        gi = (sg - 1) * 2 + mbp
                        g.wait_ge(eS[0], eS0 + gi + 1)
                        for mb2 in range(2):
                            for ss in range(4):
                                g.dma_start(
                                    out=g0_dram[(sg - 1) * 512 + ss * 128:
                                                (sg - 1) * 512 + (ss + 1) * 128,
                                                mbp * 1024 + mb2 * 512:
                                                mbp * 1024 + (mb2 + 1) * 512],
                                    in_=stage[:, (gi % 2) * 4096 + mb2 * 2048 + ss * 512:
                                              (gi % 2) * 4096 + mb2 * 2048 + (ss + 1) * 512]
                                    ).then_inc(qGs2[0], 16)
            for mbp in range(2):
                gi = 6 + mbp
                g.wait_ge(eS[0], eS0 + gi + 1)
                for mb2 in range(2):
                    for ss in range(4):
                        g.dma_start(
                            out=g0_dram[3 * 512 + ss * 128: 3 * 512 + (ss + 1) * 128,
                                        mbp * 1024 + mb2 * 512:
                                        mbp * 1024 + (mb2 + 1) * 512],
                            in_=stage[:, (gi % 2) * 4096 + mb2 * 2048 + ss * 512:
                                      (gi % 2) * 4096 + mb2 * 2048 + (ss + 1) * 512]
                            ).then_inc(qGs2[0], 16)
            g.wait_ge(qGs2[0], qGs20 + 16 * 64)

        @block.sync
        def _(s):
            nS = 0
            for k in range(NGI * nchunks):
                gi, nc_i = divmod(k, nchunks)
                mbp = gi % 2
                sg = gi // 2
                if k >= RING:
                    s.wait_ge(eTk[0], eTk0 + k - RING + 1)
                s.dma_start(out=wd[:, (k % RING) * 1024:(k % RING + 1) * 1024],
                            in_=wd_bf[mbp * 1024:(mbp + 1) * 1024,
                                      nc_i * 128:(nc_i + 1) * 128],
                            transpose=True).then_inc(qS[0], 16)
                nS += 1
                if k % 4 == 3:
                    s.wait_ge(qS[0], qS0 + 16 * nS)
                    s.sem_inc(eWdS[0], 4)

        @block.scalar
        def _(sc):
            nC = 0
            for gi in range(NGI):
                sg, mbp2 = divmod(gi, 2)
                if gi >= 1:
                    gp = gi - 1
                    sc.wait_ge(eTk[0], eTk0 + gi * nchunks)
                    if gp >= 2:
                        sc.wait_ge(qGs2[0], qGs20 + 16 * 8 * (gp - 1))
                    bs = (gp % 2) * 4096
                    for mb2 in range(2):
                        for ss in range(4):
                            sc.activation(
                                stage[:, bs + mb2 * 2048 + ss * 512:
                                      bs + mb2 * 2048 + (ss + 1) * 512],
                                ps[mb2 * 4 + ss][:, :], afunc)
                    sc.drain().then_inc(eS[0], 1)

            gp = NGI - 1
            sc.wait_ge(eTk[0], eTk0 + NGI * nchunks)
            sc.wait_ge(qGs2[0], qGs20 + 16 * 8 * (gp - 1))
            bs = (gp % 2) * 4096
            for mb2 in range(2):
                for ss in range(4):
                    sc.activation(stage[:, bs + mb2 * 2048 + ss * 512:
                                        bs + mb2 * 2048 + (ss + 1) * 512],
                                  ps[mb2 * 4 + ss][:, :], afunc)
            sc.drain().then_inc(eS[0], 1)

        @block.tensor
        def _(te):
            for gi in range(NGI):
                sg = gi // 2
                te.wait_ge(qG[0], qG0 + 16 * nchunks * (sg + 1))
                if gi >= 1:
                    te.wait_ge(eS[0], eS0 + gi)
                for nc_i in range(nchunks):
                    k = gi * nchunks + nc_i
                    te.wait_ge(eWdS[0], eWdS0 + k + 1)
                    for mb2 in range(2):
                        for ss in range(4):
                            ins = te.matmul(
                                ps[mb2 * 4 + ss][:, :],
                                ibig[:, nc_i * 512 + ss * 128:
                                     nc_i * 512 + (ss + 1) * 128],
                                wd[:, (k % RING) * 1024 + mb2 * 512:
                                   (k % RING) * 1024 + (mb2 + 1) * 512],
                                start=(nc_i == 0), stop=(nc_i == nchunks - 1))
                    ins.then_inc(eTk[0], 1)

    qG[1] += 16 * nchunks * 4
    qGs2[1] += 16 * 64
    qS[1] += 16 * NGI * nchunks
    eWdS[1] += NGI * nchunks
    eTk[1] += NGI * nchunks
    eS[1] += NGI
    pb.end_phase()


def emit_casts(pb, jobs):
    nc = pb.nc
    pb.begin_phase()
    qG = pb.sem("qG")
    qG0 = qG[1]
    n = 0
    with nc.Block(f"casts_{pb._pctr}") as block:
        @block.gpsimd
        def _(g):
            nonlocal n
            for src, dst, nr, ncol in jobs:
                step = max(1, (4 << 20) // (ncol * 4))
                for r0 in range(0, nr, step):
                    r1 = min(nr, r0 + step)
                    g.dma_start(out=dst[r0:r1, :], in_=src[r0:r1, :]).then_inc(qG[0], 16)
                    n += 1
            g.wait_ge(qG[0], qG0 + 16 * n)
    qG[1] += 16 * n
    pb.end_phase()


def emit_scaled_cast(pb, w_dram, scale_vec_dram, out_dram, NRr, NCc, imm):
    nc = pb.nc
    pb.begin_phase()
    qS = pb.sem("qS")
    qG = pb.sem("qG")
    qC = pb.sem("qC")
    eT = pb.sem("sc_eT")
    eV = pb.sem("sc_eV")
    nrt = NRr // 128
    wt = pb.sbuf("scst_w", [128, 2 * NCc], F32)
    ob = pb.sbuf("scst_o", [128, 2 * NCc], BF16)
    svrow = pb.sbuf("scst_sv", [nrt, 128], BF16)
    scol = pb.sbuf("scst_c", [128, nrt], F32)
    pst = pb.psum("scst_p", [128, 1024], BF16)
    qS0, qG0, qC0, eT0, eV0 = qS[1], qG[1], qC[1], eT[1], eV[1]

    with nc.Block(f"scast_{pb._pctr}") as block:
        @block.gpsimd
        def _(g):
            g.dma_start(out=svrow[:, :], in_=scale_vec_dram.rearrange("(r p) -> r p", p=128)
                        ).then_inc(qG[0], 16)
        @block.tensor
        def _(te):
            te.wait_ge(qG[0], qG0 + 16)
            te.transpose(pst[:, 0:nrt], svrow[0:nrt, :], pb.ident[0:nrt, 0:nrt]).then_inc(eT[0], 1)
        @block.vector
        def _(v):
            v.wait_ge(eT[0], eT0 + 1)
            v.tensor_copy(scol[:, :], pst[:, 0:nrt])
            v.drain().then_inc(eV[0], 1)
            for r in range(nrt):
                b = r % 2
                v.wait_ge(qS[0], qS0 + 16 * (r + 1))
                if r >= 2:
                    v.wait_ge(qC[0], qC0 + 16 * (r - 1))
                v.tensor_scalar(ob[:, b * NCc:(b + 1) * NCc], wt[:, b * NCc:(b + 1) * NCc],
                                scol[:, r:r + 1], float(imm), ALU.mult, ALU.mult)
                v.drain().then_inc(eV[0], 1)
        @block.sync
        def _(s):
            for r in range(nrt):
                b = r % 2
                if r >= 2:
                    s.wait_ge(eV[0], eV0 + r)
                s.dma_start(out=wt[:, b * NCc:(b + 1) * NCc],
                            in_=w_dram[r * 128:(r + 1) * 128, :]).then_inc(qS[0], 16)
        @block.scalar
        def _(sc):
            for r in range(nrt):
                b = r % 2
                sc.wait_ge(eV[0], eV0 + r + 2)
                sc.dma_start(out=out_dram[r * 128:(r + 1) * 128, :],
                             in_=ob[:, b * NCc:(b + 1) * NCc]).then_inc(qC[0], 16)
            sc.wait_ge(qC[0], qC0 + 16 * nrt)
    qG[1] += 16
    eT[1] += 1
    eV[1] += 1 + nrt
    qS[1] += 16 * nrt
    qC[1] += 16 * nrt
    pb.end_phase()


def _emit_h128(pb, v_eng, h128, w1, w1b, tmp1, tmp2):
    ALUv = ALU
    v = v_eng
    v.tensor_scalar(tmp1[:, :], w1b[:, :], 7, None, ALUv.logical_shift_right)
    v.drain()
    v.tensor_tensor(tmp1[:, :], tmp1[:, :], w1[:, :], ALUv.bitwise_and)
    v.drain()
    v.tensor_scalar(tmp2[:, :], tmp1[:, :], 1, None, ALUv.bitwise_and)
    v.drain()
    for b in range(1, 7):
        v.tensor_scalar(tmp1[:, :], tmp1[:, :], 1, None, ALUv.logical_shift_right)
        v.drain()
        v.tensor_tensor(tmp2[:, :], tmp2[:, :],
                        tmp1[:, :], ALUv.bitwise_xor)
        v.drain()
    v.tensor_scalar(tmp2[:, :], tmp2[:, :], 1, None, ALUv.bitwise_and)
    v.drain()
    v.tensor_scalar(tmp2[:, :], tmp2[:, :], -2, 1, ALUv.mult, ALUv.add)
    v.drain()
    v.tensor_copy(h128[:, :], tmp2[:, :])
    v.drain()


def emit_fwht1(pb, xn_fm_dram, su_dram, xr_fm_dram):
    nc = pb.nc
    pb.begin_phase()
    qS = pb.sem("qS")
    qG = pb.sem("qG")
    qC = pb.sem("qC")
    eT = pb.sem("fw_eT")
    eV = pb.sem("fw_eV")
    eVq = pb.sem("fw_eVq")
    eP = pb.sem("fw_eP")
    xn = pb.sbuf("fw_xn", [128, NCD * S], BF16)
    hsu = pb.sbuf("fw_h", [128, NCD * 128], BF16)
    surow = pb.sbuf("fw_su", [NCD, 128], BF16)
    sucol = pb.sbuf("fw_suc", [128, NCD], F32)
    w1 = pb.sbuf("fw_w1", [128, 128], I32)
    w1b = pb.sbuf("fw_w1b", [128, 128], I32)
    tmp1 = pb.sbuf("fw_t1", [128, 128], I32)
    tmp2 = pb.sbuf("fw_t2", [128, 128], I32)
    h128 = pb.sbuf("fw_h128", [128, 128], BF16)
    ya = pb.sbuf("fw_ya", [128, NCD * 512], BF16)
    yb = pb.sbuf("fw_yb", [128, NCD * 512], BF16)
    yst = pb.sbuf("fw_yst", [128, NCD * 512], BF16)
    pst = pb.psum("fw_pt", [128, 1024], BF16)
    psb = [pb.psum(f"fw_ps{i}", [128, 512]) for i in range(2)]
    qS0, qG0, qC0, eT0, eV0, eVq0 = qS[1], qG[1], qC[1], eT[1], eV[1], eVq[1]

    with nc.Block(f"fwht1_{pb._pctr}") as block:
        @block.gpsimd
        def _(g):
            g.iota(w1[:, :], [[1, 128]], channel_multiplier=0)
            g.iota(w1b[:, :], [[1, 128]], channel_multiplier=128)
            g.drain().then_inc(eP[0], 1)
            g.dma_start(out=surow[:, :], in_=su_dram.rearrange("(r p) -> r p", p=128)
                        ).then_inc(qG[0], 16)
            for c in range(NCD):
                g.dma_start(out=xn[:, c * S:(c + 1) * S], in_=xn_fm_dram[c, :, :]
                            ).then_inc(qG[0], 16)

        @block.tensor
        def _(te):
            te.wait_ge(qG[0], qG0 + 16)
            te.transpose(pst[:, 0:NCD], surow[0:NCD, :], pb.ident[0:NCD, 0:NCD]).then_inc(eT[0], 1)
            te.wait_ge(eV[0], eV0 + 2)
            te.wait_ge(qG[0], qG0 + 16 * (1 + NCD))
            for st in range(4):
                for c in range(NCD):
                    k = st * NCD + c
                    if k >= 2:
                        te.wait_ge(eVq[0], eVq0 + k - 1)
                    te.matmul(psb[k % 2][:, :], hsu[:, c * 128:(c + 1) * 128],
                              xn[:, c * S + st * 512: c * S + (st + 1) * 512],
                              start=True, stop=True).then_inc(eT[0], 1)

        @block.vector
        def _(v):
            v.wait_ge(eT[0], eT0 + 1)
            v.tensor_copy(sucol[:, :], pst[:, 0:NCD])
            v.drain().then_inc(eV[0], 1)
            v.wait_ge(eP[0], eP[1] + 1)
            _emit_h128(pb, v, h128, w1, w1b, tmp1, tmp2)
            for c in range(NCD):
                v.tensor_scalar(hsu[:, c * 128:(c + 1) * 128], h128[:, :],
                                sucol[:, c:c + 1], None, ALU.mult)
            v.drain().then_inc(eV[0], 1)
            for st in range(4):
                for c in range(NCD):
                    k = st * NCD + c
                    v.wait_ge(eT[0], eT0 + 1 + k + 1)
                    v.tensor_copy(ya[:, c * 512:(c + 1) * 512],
                                  psb[k % 2][:, :]).then_inc(eVq[0], 1)
                src, dst = ya, yb
                for stage in range(4):
                    hstep = 1 << stage
                    for pair in range(8):
                        blk = (pair // hstep) * 2 * hstep
                        off = pair % hstep
                        ca, cb = blk + off, blk + off + hstep
                        v.tensor_tensor(dst[:, ca * 512:(ca + 1) * 512],
                                        src[:, ca * 512:(ca + 1) * 512],
                                        src[:, cb * 512:(cb + 1) * 512], ALU.add)
                        v.tensor_tensor(dst[:, cb * 512:(cb + 1) * 512],
                                        src[:, ca * 512:(ca + 1) * 512],
                                        src[:, cb * 512:(cb + 1) * 512], ALU.subtract)
                    src, dst = dst, src
                if st >= 1:
                    v.wait_ge(qC[0], qC0 + 16 * NCD * st)
                v.tensor_copy(yst[:, :], src[:, :])
                v.drain().then_inc(eV[0], 1)

        @block.scalar
        def _(sc):
            for st in range(4):
                sc.wait_ge(eV[0], eV0 + 2 + (st + 1))
                for c in range(NCD):
                    sc.dma_start(out=xr_fm_dram[c, :, st * 512:(st + 1) * 512],
                                 in_=yst[:, c * 512:(c + 1) * 512]).then_inc(qC[0], 16)
            sc.wait_ge(qC[0], qC0 + 16 * NCD * 4)

    qG[1] += 16 * (1 + NCD)
    eP[1] += 1
    eT[1] += 1 + 64
    eV[1] += 2 + 4
    eVq[1] += 64
    qC[1] += 16 * NCD * 4
    pb.end_phase()


def emit_ffn1a(pb, xr_fm_dram, wup_bf, wgate_bf, wsc_dram, sudp_dram, y_dram):
    nc = pb.nc
    pb.begin_phase()
    qS = pb.sem("qS")
    qG = pb.sem("qG")
    qC = pb.sem("qC")
    eT = pb.sem("f1_eT")
    eTv = pb.sem("f1_eTv")
    eSg = pb.sem("f1_eSg")
    eV = pb.sem("f1_eV")
    eVn = pb.sem("f1_eVn")
    eVc = pb.sem("f1_eVc")
    eWf = pb.sem("f1_eWf")
    NCH = INTER1 // 128
    xr = pb.sbuf("f1_xr", [128, NCD * S], BF16)
    wg = pb.sbuf("f1_wg", [128, 2 * NCD * 512], BF16, td=True)
    wu = pb.sbuf("f1_wu", [128, 2 * NCD * 512], BF16, td=True)
    wscrow = pb.sbuf("f1_wr", [128, 128], BF16)
    sdrow = pb.sbuf("f1_sr", [64, 128], BF16)
    ucol = pb.sbuf("f1_uc", [128, NCH], F32)
    gcol = pb.sbuf("f1_gc", [128, NCH], F32)
    gsb = pb.sbuf("f1_g", [128, 512], F32)
    usb = pb.sbuf("f1_u", [128, 512], F32)
    ibuf = pb.sbuf("f1_i", [128, 2 * S], BF16)
    pst = pb.psum("f1_pt", [128, 1024], BF16)
    ps = [pb.psum(f"f1_ps{i}", [128, 512]) for i in range(4)]
    qS0, qG0, qC0 = qS[1], qG[1], qC[1]
    eT0, eTv0, eSg0, eV0, eVn0, eVc0 = eT[1], eTv[1], eSg[1], eV[1], eVn[1], eVc[1]
    eWf0 = eWf[1]
    NG = NCH // 4

    with nc.Block(f"ffn1a_{pb._pctr}") as block:
        @block.gpsimd
        def _(g):
            g.dma_start(out=wscrow[:, :], in_=wsc_dram.rearrange("(r p) -> r p", p=128)
                        ).then_inc(qG[0], 16)
            g.dma_start(out=sdrow[:, :], in_=sudp_dram.rearrange("(r p) -> r p", p=128)
                        ).then_inc(qG[0], 16)
            for c in range(NCD):
                g.dma_start(out=xr[:, c * S:(c + 1) * S], in_=xr_fm_dram[c, :, :]
                            ).then_inc(qG[0], 16)

        @block.tensor
        def _(te):
            te.wait_ge(qG[0], qG0 + 32)
            te.transpose(pst[:, 0:128], wscrow[:, :], pb.ident[:, :]).then_inc(eTv[0], 1)
            te.wait_ge(eVc[0], eVc0 + 1)
            te.transpose(pst[:, 128:192], sdrow[0:64, :], pb.ident[0:64, 0:64]).then_inc(eTv[0], 1)
            te.wait_ge(qG[0], qG0 + 16 * (2 + NCD))
            te.wait_ge(eVc[0], eVc0 + 2)
            for g in range(NG):
                b = g % 2
                te.wait_ge(eWf[0], eWf0 + g + 1)
                for ncg in range(4):
                    for st in range(4):
                        idx = (g * 4 + ncg) * 4 + st
                        if idx >= 2:
                            te.wait_ge(eV[0], eV0 + idx - 1)
                        bg = (st % 2) * 2
                        for d in range(NCD):
                            te.matmul(ps[bg][:, :],
                                      wg[:, (b * NCD + d) * 512 + ncg * 128:
                                         (b * NCD + d) * 512 + (ncg + 1) * 128],
                                      xr[:, d * S + st * 512: d * S + (st + 1) * 512],
                                      start=(d == 0), stop=(d == NCD - 1))
                        for d in range(NCD):
                            ins = te.matmul(ps[bg + 1][:, :],
                                            wu[:, (b * NCD + d) * 512 + ncg * 128:
                                               (b * NCD + d) * 512 + (ncg + 1) * 128],
                                            xr[:, d * S + st * 512: d * S + (st + 1) * 512],
                                            start=(d == 0), stop=(d == NCD - 1))
                        ins.then_inc(eT[0], 1)

        @block.sync
        def _(s):
            for g in range(NG):
                b = g % 2
                if g >= 2:
                    s.wait_ge(eT[0], eT0 + 16 * (g - 1))
                for d in range(NCD):
                    s.dma_start(out=wg[:, (b * NCD + d) * 512:(b * NCD + d + 1) * 512],
                                in_=wgate_bf[g * 512:(g + 1) * 512, d * 128:(d + 1) * 128],
                                transpose=True).then_inc(qS[0], 16)
                    s.dma_start(out=wu[:, (b * NCD + d) * 512:(b * NCD + d + 1) * 512],
                                in_=wup_bf[g * 512:(g + 1) * 512, d * 128:(d + 1) * 128],
                                transpose=True).then_inc(qS[0], 16)
                s.wait_ge(qS[0], qS0 + 16 * 32 * (g + 1))
                s.sem_inc(eWf[0], 1)

        @block.vector
        def _(v):
            isq = float(1.0 / np.sqrt(2048.0))
            v.wait_ge(eTv[0], eTv0 + 1)
            v.tensor_scalar(ucol[:, :], pst[:, 0:64], float(isq), None, ALU.mult)
            v.tensor_scalar(gcol[:, :], pst[:, 64:128], float(isq), None, ALU.mult)
            v.drain().then_inc(eVc[0], 1)
            v.wait_ge(eTv[0], eTv0 + 2)
            v.tensor_tensor(ucol[:, :], ucol[:, :], pst[:, 128:192], ALU.mult)
            v.drain().then_inc(eVc[0], 1)
            for nc_i in range(NCH):
                nb = nc_i % 2
                for st in range(4):
                    idx = nc_i * 4 + st
                    v.wait_ge(eSg[0], eSg0 + idx + 1)
                    if nc_i >= 2 and st == 0:
                        v.wait_ge(qC[0], qC0 + 16 * (nc_i - 1))
                    bg = (st % 2) * 2
                    v.tensor_scalar(usb[:, :], ps[bg + 1][:, :], ucol[:, nc_i:nc_i + 1],
                                    None, ALU.mult)
                    v.tensor_tensor(ibuf[:, nb * S + st * 512: nb * S + (st + 1) * 512],
                                    gsb[:, :], usb[:, :], ALU.mult).then_inc(eV[0], 1)
                v.drain().then_inc(eVn[0], 1)

        @block.scalar
        def _(sc):
            for nc_i in range(NCH):
                for st in range(4):
                    idx = nc_i * 4 + st
                    sc.wait_ge(eT[0], eT0 + idx + 1)
                    if idx >= 1:
                        sc.wait_ge(eV[0], eV0 + idx)
                    bg = (st % 2) * 2
                    sc.activation(gsb[:, :], ps[bg][:, :], AF.Silu,
                                  scale=mst_scale(gcol, nc_i))
                    sc.drain().then_inc(eSg[0], 1)
                if nc_i >= 1:
                    sc.wait_ge(eVn[0], eVn0 + nc_i)
                    pb_ = (nc_i - 1) % 2
                    sc.dma_start(out=y_dram[nc_i - 1, :, :],
                                 in_=ibuf[:, pb_ * S:(pb_ + 1) * S]).then_inc(qC[0], 16)
            sc.wait_ge(eVn[0], eVn0 + NCH)
            sc.dma_start(out=y_dram[NCH - 1, :, :], in_=ibuf[:, S:2 * S]
                         ).then_inc(qC[0], 16)
            sc.wait_ge(qC[0], qC0 + 16 * NCH)

    qG[1] += 16 * (2 + NCD)
    qS[1] += 16 * 32 * NG
    eWf[1] += NG
    eTv[1] += 2
    eVc[1] += 2
    eT[1] += NCH * 4
    eSg[1] += NCH * 4
    eV[1] += NCH * 4
    eVn[1] += NCH
    qC[1] += 16 * NCH
    pb.end_phase()


def mst_scale(col, i):
    return col[:, i:i + 1]


def emit_ffn1b(pb, y_dram, wd_bf, g1_dram):
    nc = pb.nc
    pb.begin_phase()
    qS = pb.sem("qS")
    qG = pb.sem("qG")
    qC = pb.sem("qC")
    eTh = pb.sem("g1_eTh")
    eTk = pb.sem("g1_eTk")
    eVh = pb.sem("g1_eVh")
    eVz = pb.sem("g1_eVz")
    eS = pb.sem("g1_eS")
    eP = pb.sem("g1_eP")
    eWd1 = pb.sem("g1_eWd")
    NCH = INTER1 // 128
    w1 = pb.sbuf("g1_w1", [128, 128], I32)
    w1b = pb.sbuf("g1_w1b", [128, 128], I32)
    tmp1 = pb.sbuf("g1_t1", [128, 128], I32)
    tmp2 = pb.sbuf("g1_t2", [128, 128], I32)
    h128 = pb.sbuf("g1_h", [128, 128], BF16)
    ya = pb.sbuf("g1_ya", [128, NCH * 512], BF16)
    yb = pb.sbuf("g1_yb", [128, NCH * 512], BF16)
    RING = 8
    wd = pb.sbuf("g1_wd", [128, RING * 1024], BF16, td=True)
    stage = pb.sbuf("g1_st", [128, 2 * 4096], F32)
    psd = [pb.psum(f"g1_pd{i}", [128, 512]) for i in range(8)]
    psb = [psd[0], psd[1]]
    qS0, qG0, qC0 = qS[1], qG[1], qC[1]
    eTh0, eTk0, eVh0, eVz0, eS0 = eTh[1], eTk[1], eVh[1], eVz[1], eS[1]
    eWd10 = eWd1[1]

    with nc.Block(f"ffn1b_{pb._pctr}") as block:
        @block.gpsimd
        def _(g):
            g.iota(w1[:, :], [[1, 128]], channel_multiplier=0)
            g.iota(w1b[:, :], [[1, 128]], channel_multiplier=128)
            g.drain().then_inc(eP[0], 1)
            for sg in range(4):
                if sg >= 1:
                    g.wait_ge(eVz[0], eVz0 + sg)
                for c in range(NCH):
                    g.dma_start(out=ya[:, c * 512:(c + 1) * 512],
                                in_=y_dram[c, :, sg * 512:(sg + 1) * 512]
                                ).then_inc(qG[0], 16)

        @block.tensor
        def _(te):
            te.wait_ge(eVh[0], eVh0 + 1)
            for sg in range(4):
                te.wait_ge(qG[0], qG0 + 16 * NCH * (sg + 1))
                if sg >= 1:
                    te.wait_ge(eS[0], eS0 + 2 * sg)
                for c in range(NCH):
                    k = sg * NCH + c
                    if k >= 2:
                        te.wait_ge(eVh[0], eVh0 + 1 + k - 1)
                    te.matmul(psb[k % 2][:, :], h128[:, :],
                              ya[:, c * 512:(c + 1) * 512],
                              start=True, stop=True).then_inc(eTh[0], 1)
                te.wait_ge(eVz[0], eVz0 + sg + 1)
                for mbp in range(2):
                    gi = sg * 2 + mbp
                    if gi >= 1:
                        te.wait_ge(eS[0], eS0 + gi)
                    for nc_i in range(NCH):
                        k2 = gi * NCH + nc_i
                        te.wait_ge(eWd1[0], eWd10 + k2 + 1)
                        for mb2 in range(2):
                            for ss in range(4):
                                ins = te.matmul(psd[mb2 * 4 + ss][:, :],
                                                yb[:, nc_i * 512 + ss * 128:
                                                   nc_i * 512 + (ss + 1) * 128],
                                                wd[:, (k2 % RING) * 1024 + mb2 * 512:
                                                   (k2 % RING) * 1024 + (mb2 + 1) * 512],
                                                start=(nc_i == 0), stop=(nc_i == NCH - 1))
                        ins.then_inc(eTk[0], 1)

        @block.vector
        def _(v):
            v.wait_ge(eP[0], eP[1] + 1)
            _emit_h128(pb, v, h128, w1, w1b, tmp1, tmp2)
            v.drain().then_inc(eVh[0], 1)
            for sg in range(4):
                for c in range(NCH):
                    k = sg * NCH + c
                    v.wait_ge(eTh[0], eTh0 + k + 1)
                    v.tensor_copy(ya[:, c * 512:(c + 1) * 512],
                                  psb[k % 2][:, :]).then_inc(eVh[0], 1)
                if sg >= 1:
                    v.wait_ge(eTk[0], eTk0 + sg * 2 * NCH)
                src, dst = ya, yb
                for stage_i in range(6):
                    hstep = 1 << stage_i
                    for pair in range(32):
                        blk = (pair // hstep) * 2 * hstep
                        off = pair % hstep
                        ca, cb = blk + off, blk + off + hstep
                        v.tensor_tensor(dst[:, ca * 512:(ca + 1) * 512],
                                        src[:, ca * 512:(ca + 1) * 512],
                                        src[:, cb * 512:(cb + 1) * 512], ALU.add)
                        v.tensor_tensor(dst[:, cb * 512:(cb + 1) * 512],
                                        src[:, ca * 512:(ca + 1) * 512],
                                        src[:, cb * 512:(cb + 1) * 512], ALU.subtract)
                    src, dst = dst, src
                if src is not yb:
                    v.tensor_copy(yb[:, :], src[:, :])
                v.drain().then_inc(eVz[0], 1)

        @block.sync
        def _(s):
            for sg in range(4):
                for mbp in range(2):
                    for nc_i in range(NCH):
                        k2 = (sg * 2 + mbp) * NCH + nc_i
                        if k2 >= RING:
                            s.wait_ge(eTk[0], eTk0 + k2 - RING + 1)
                        s.dma_start(out=wd[:, (k2 % RING) * 1024:(k2 % RING + 1) * 1024],
                                    in_=wd_bf[mbp * 1024:(mbp + 1) * 1024,
                                              nc_i * 128:(nc_i + 1) * 128],
                                    transpose=True).then_inc(qS[0], 16)
                        if k2 % 4 == 3:
                            s.wait_ge(qS[0], qS0 + 16 * (k2 + 1))
                            s.sem_inc(eWd1[0], 4)

        @block.scalar
        def _(sc):
            for sg in range(4):
                for mbp in range(2):
                    gi = sg * 2 + mbp
                    sc.wait_ge(eTk[0], eTk0 + (gi + 1) * NCH)
                    if gi >= 2:
                        sc.wait_ge(qC[0], qC0 + 16 * 8 * (gi - 1))
                    bs = (gi % 2) * 4096
                    for mb2 in range(2):
                        for ss in range(4):
                            sc.activation(stage[:, bs + mb2 * 2048 + ss * 512:
                                                bs + mb2 * 2048 + (ss + 1) * 512],
                                          psd[mb2 * 4 + ss][:, :], AF.Silu)
                    sc.drain().then_inc(eS[0], 1)
                    for mb2 in range(2):
                        for ss in range(4):
                            sc.dma_start(out=g1_dram[sg * 512 + ss * 128:
                                                     sg * 512 + (ss + 1) * 128,
                                                     mbp * 1024 + mb2 * 512:
                                                     mbp * 1024 + (mb2 + 1) * 512],
                                         in_=stage[:, bs + mb2 * 2048 + ss * 512:
                                                   bs + mb2 * 2048 + (ss + 1) * 512]
                                         ).then_inc(qC[0], 16)
            sc.wait_ge(qC[0], qC0 + 16 * 64)

    qG[1] += 16 * NCH * 4
    qS[1] += 16 * 8 * NCH
    eWd1[1] += 8 * NCH
    eP[1] += 1
    eTh[1] += 4 * NCH
    eTk[1] += 8 * NCH
    eVh[1] += 1 + 4 * NCH
    eVz[1] += 4
    eS[1] += 16
    qC[1] += 16 * 64
    pb.end_phase()


def emit_l1a(pb, xn_fm_dram, wq1_bf, sq_fm_dram):
    nc = pb.nc
    pb.begin_phase()
    qS = pb.sem("qS")
    qG = pb.sem("qG")
    qC = pb.sem("qC")
    eT = pb.sem("la_eT")
    eV1 = pb.sem("la_eV1")
    eS = pb.sem("la_eS")
    eV2 = pb.sem("la_eV2")
    xn = pb.sbuf("la_xn", [128, NCD * S], BF16)
    wq = pb.sbuf("la_wq", [128, NCD * 4 * 512], BF16, td=True)
    t1 = pb.sbuf("la_t1", [128, 512], F32)
    esb = pb.sbuf("la_e", [128, 512], F32)
    ost = pb.sbuf("la_o", [128, 2 * S], BF16)
    ps = [pb.psum(f"la_ps{i}", [128, 512]) for i in range(2)]
    qS0, qG0, qC0, eT0, eV10, eS0, eV20 = qS[1], qG[1], qC[1], eT[1], eV1[1], eS[1], eV2[1]

    with nc.Block(f"l1a_{pb._pctr}") as block:
        @block.gpsimd
        def _(g):
            for c in range(NCD):
                g.dma_start(out=xn[:, c * S:(c + 1) * S], in_=xn_fm_dram[c, :, :]
                            ).then_inc(qG[0], 16)

        @block.sync
        def _(s):
            for ng in range(4):
                for d in range(NCD):
                    s.dma_start(out=wq[:, (ng * NCD + d) * 512:(ng * NCD + d + 1) * 512],
                                in_=wq1_bf[ng * 512:(ng + 1) * 512, d * 128:(d + 1) * 128],
                                transpose=True).then_inc(qS[0], 16)

        @block.tensor
        def _(te):
            te.wait_ge(qG[0], qG0 + 16 * NCD)
            te.wait_ge(qS[0], qS0 + 16 * 64)
            for n in range(NCD):
                ng, ns = n // 4, n % 4
                for st in range(4):
                    idx = n * 4 + st
                    if idx >= 2:
                        te.wait_ge(eV2[0], eV20 + idx - 1)
                    for d in range(NCD):
                        ins = te.matmul(ps[idx % 2][:, :],
                                        wq[:, (ng * NCD + d) * 512 + ns * 128:
                                           (ng * NCD + d) * 512 + (ns + 1) * 128],
                                        xn[:, d * S + st * 512: d * S + (st + 1) * 512],
                                        start=(d == 0), stop=(d == NCD - 1))
                    ins.then_inc(eT[0], 1)

        @block.vector
        def _(v):
            for n in range(NCD):
                nb = n % 2
                for st in range(4):
                    idx = n * 4 + st
                    v.wait_ge(eT[0], eT0 + idx + 1)
                    if st == 0 and n >= 2:
                        v.wait_ge(qC[0], qC0 + 16 * (n - 1))
                    v.tensor_scalar(t1[:, :], ps[idx % 2][:, :], 0.0, None, ALU.min)
                    v.drain().then_inc(eV1[0], 1)
                    v.wait_ge(eS[0], eS0 + idx + 1)
                    v.scalar_tensor_tensor(ost[:, nb * S + st * 512: nb * S + (st + 1) * 512],
                                           ps[idx % 2][:, :], 0.0, esb[:, :],
                                           ALU.max, ALU.add)
                    v.drain().then_inc(eV2[0], 1)

        @block.scalar
        def _(sc):
            for n in range(NCD):
                for st in range(4):
                    idx = n * 4 + st
                    sc.wait_ge(eV1[0], eV10 + idx + 1)
                    sc.activation(esb[:, :], t1[:, :], AF.Exp)
                    sc.drain().then_inc(eS[0], 1)
                if n >= 1:
                    sc.wait_ge(eV2[0], eV20 + n * 4)
                    sc.wait_ge(qC[0], qC0 + 16 * max(0, n - 1))
                    pb_ = (n - 1) % 2
                    sc.dma_start(out=sq_fm_dram[n - 1, :, :],
                                 in_=ost[:, pb_ * S:(pb_ + 1) * S]).then_inc(qC[0], 16)
            sc.wait_ge(eV2[0], eV20 + 64)
            sc.dma_start(out=sq_fm_dram[NCD - 1, :, :], in_=ost[:, S:2 * S]
                         ).then_inc(qC[0], 16)
            sc.wait_ge(qC[0], qC0 + 16 * NCD)

    qG[1] += 16 * NCD
    qS[1] += 16 * 64
    eT[1] += 64
    eV1[1] += 64
    eS[1] += 64
    eV2[1] += 64
    qC[1] += 16 * NCD
    pb.end_phase()


def emit_l1b(pb, sq_fm_dram, mem_dram, mn_dram, bw_dram, comb_fm_dram):
    nc = pb.nc
    pb.begin_phase()
    qS = pb.sem("qS")
    qG = pb.sem("qG")
    qC = pb.sem("qC")
    eT = pb.sem("lb_eT")
    eTt = pb.sem("lb_eTt")
    eTp = pb.sem("lb_eTp")
    eV = pb.sem("lb_eV")
    eVe = pb.sem("lb_eVe")
    eVp = pb.sem("lb_eVp")
    eSp = pb.sem("lb_eSp")
    sq = pb.sbuf("lb_sq", [128, NCD * S], BF16)
    memn = pb.sbuf("lb_m", [128, 64 * 132], BF16)
    mnrow = pb.sbuf("lb_mnr", [64, 128], BF16)
    mncol = pb.sbuf("lb_mnc", [128, 64], BF16)
    bwrow = pb.sbuf("lb_bw", [1, 64], F32)
    mnfrow = pb.sbuf("lb_mnf", [1, 8192], F32)
    prep = pb.sbuf("lb_pr", [1, 64], F32)
    prep2 = pb.sbuf("lb_pr2", [1, 16], F32)
    wrow = pb.sbuf("lb_wr", [1, 64], BF16)
    wrep = pb.sbuf("lb_wrep", [128, 64], F32)
    rcol = pb.sbuf("lb_rc", [128, 4], F32)
    acc = pb.sbuf("lb_acc", [128, 128], F32)
    accb = pb.sbuf("lb_accb", [128, 128], BF16)
    cst = pb.sbuf("lb_cst", [128, 2 * 128], BF16)
    psA = [pb.psum(f"lb_ps{i}", [128, 512]) for i in range(4)]
    pst = pb.psum("lb_pt", [128, 1024], BF16)
    pstT = [pb.psum(f"lb_pt2{i}", [128, 1024], BF16) for i in range(2)]
    psw = pb.psum("lb_pw", [128, 64])
    qS0, qG0, qC0 = qS[1], qG[1], qC[1]
    eT0, eTt0, eTp0, eV0, eVe0, eVp0, eSp0 = (
        eT[1], eTt[1], eTp[1], eV[1], eVe[1], eVp[1], eSp[1])

    with nc.Block(f"l1b_{pb._pctr}") as block:
        @block.gpsimd
        def _(g):
            g.dma_start(out=mnrow[:, :], in_=mn_dram.rearrange("n h d -> (n h) d")
                        ).then_inc(qG[0], 16)
            for n in range(4):
                for h in range(H):
                    g.dma_start(out=memn[:, (n * 16 + h) * 132:(n * 16 + h) * 132 + 128],
                                in_=mem_dram[n, h, :, :]).then_inc(qG[0], 16)
            for c in range(NCD):
                g.dma_start(out=sq[:, c * S:(c + 1) * S], in_=sq_fm_dram[c, :, :]
                            ).then_inc(qG[0], 16)

        @block.sync
        def _(s):
            s.dma_start(out=bwrow[0:1, :], in_=bw_dram.rearrange("h n -> (h n)")[None, :]
                        ).then_inc(qS[0], 16)
            s.dma_start(out=mnfrow[0:1, :], in_=mn_dram.rearrange("n h d -> (n h d)")[None, :]
                        ).then_inc(qS[0], 16)

        @block.vector
        def _(v):
            v.wait_ge(qS[0], qS0 + 32)
            bw3 = bwrow[0:1, :].rearrange("o (h n) -> o h n", n=4)
            v.tensor_reduce(prep2[0:1, 0:16].unsqueeze(-1),
                            bw3, axis=AXX, op=ALU.max)
            v.drain()
            mx3 = prep2[0:1, 0:16].unsqueeze(-1).broadcast_to((1, 16, 4))
            v.tensor_tensor(prep[0:1, :].rearrange("o (h n) -> o h n", n=4), bw3, mx3,
                            ALU.subtract)
            v.drain().then_inc(eVp[0], 1)
            v.wait_ge(eSp[0], eSp0 + 1)
            pr3 = bwrow[0:1, :].rearrange("o (h n) -> o h n", n=4)
            v.tensor_reduce(prep2[0:1, 0:16].unsqueeze(-1),
                            pr3, axis=AXX, op=ALU.add)
            v.drain()
            v.reciprocal(prep2[0:1, 0:16], prep2[0:1, 0:16])
            v.drain()
            rc3 = prep2[0:1, 0:16].unsqueeze(-1).broadcast_to((1, 16, 4))
            v.tensor_tensor(pr3, pr3, rc3, ALU.mult)
            v.drain()
            mn3 = mnfrow[0:1, :].rearrange("o (n r) -> o n r", n=4)
            v.tensor_reduce(prep2[0:1, 0:4].unsqueeze(-1),
                            mn3, axis=AXX, op=ALU.add)
            v.drain()
            v.tensor_scalar(prep2[0:1, 0:4], prep2[0:1, 0:4], float(EPS_L1), None,
                            ALU.is_ge)
            v.drain()
            act3 = prep2[0:1, 0:4].unsqueeze(1).broadcast_to((1, 16, 4))
            v.tensor_tensor(pr3, pr3, act3, ALU.mult)
            v.drain()
            v.tensor_copy(wrow[0:1, :], bwrow[0:1, :])
            v.drain().then_inc(eVp[0], 1)
            v.wait_ge(eTp[0], eTp0 + 2)
            v.tensor_copy(wrep[:, :], psw[:, :])
            v.tensor_copy(mncol[:, :], pst[:, 0:64])
            v.drain().then_inc(eVp[0], 1)
            for n in range(4):
                for h in range(H):
                    v.tensor_copy(memn[:, (n * 16 + h) * 132 + 128:(n * 16 + h) * 132 + 129],
                                  mncol[:, n * 16 + h:n * 16 + h + 1])
            v.drain().then_inc(eVp[0], 1)
            for h in range(H):
                for st in range(NTS):
                    k = h * NTS + st
                    v.wait_ge(eT[0], eT0 + k + 1)
                    for n in range(4):
                        v.tensor_scalar(rcol[:, n:n + 1], psA[n][:, 128:129],
                                        float(EPS_L1), None, ALU.max)
                    v.drain()
                    v.reciprocal(rcol[:, 0:4], rcol[:, 0:4])
                    v.drain()
                    v.tensor_tensor(rcol[:, 0:4], rcol[:, 0:4],
                                    wrep[:, h * 4:h * 4 + 4], ALU.mult)
                    v.drain()
                    v.tensor_scalar(acc[:, :], psA[0][:, 0:128],
                                    rcol[:, 0:1], None, ALU.mult)
                    for n in range(1, 3):
                        v.scalar_tensor_tensor(acc[:, :], psA[n][:, 0:128],
                                               rcol[:, n:n + 1], acc[:, :],
                                               ALU.mult, ALU.add)
                    v.scalar_tensor_tensor(accb[:, :], psA[3][:, 0:128],
                                           rcol[:, 3:4], acc[:, :],
                                           ALU.mult, ALU.add)
                    v.drain().then_inc(eV[0], 1)
                    v.wait_ge(eTt[0], eTt0 + k + 1)
                    if k >= 2:
                        v.wait_ge(qC[0], qC0 + 16 * (k - 1))
                    v.tensor_copy(cst[:, (k % 2) * 128:(k % 2 + 1) * 128],
                                  pstT[k % 2][:, 0:128]).then_inc(eVe[0], 1)

        @block.scalar
        def _(sc):
            sc.wait_ge(eVp[0], eVp0 + 1)
            sc.activation(bwrow[0:1, :], prep[0:1, :], AF.Exp)
            sc.drain().then_inc(eSp[0], 1)
            for h in range(H):
                for st in range(NTS):
                    k = h * NTS + st
                    sc.wait_ge(eVe[0], eVe0 + k + 1)
                    sc.dma_start(out=comb_fm_dram[h, :, st * 128:(st + 1) * 128],
                                 in_=cst[:, (k % 2) * 128:(k % 2 + 1) * 128]
                                 ).then_inc(qC[0], 16)
            sc.wait_ge(qC[0], qC0 + 16 * 256)

        @block.tensor
        def _(te):
            te.wait_ge(eVp[0], eVp0 + 2)
            te.matmul(psw[:, :], pb.ones1[0:1, 0:128], wrow[0:1, :],
                      start=True, stop=True).then_inc(eTp[0], 1)
            te.wait_ge(qG[0], qG0 + 16)
            te.transpose(pst[:, 0:64], mnrow[0:64, :], pb.ident[0:64, 0:64]).then_inc(eTp[0], 1)
            te.wait_ge(eVp[0], eVp0 + 4)
            te.wait_ge(qG[0], qG0 + 16 * (1 + 64 + NCD))
            for h in range(H):
                for st in range(NTS):
                    k = h * NTS + st
                    if k >= 1:
                        te.wait_ge(eV[0], eV0 + k)
                    for n in range(4):
                        ins = te.matmul(psA[n][:, 0:129],
                                        sq[:, h * S + st * 128:h * S + (st + 1) * 128],
                                        memn[:, (n * 16 + h) * 132:(n * 16 + h) * 132 + 129],
                                        start=True, stop=True)
                    ins.then_inc(eT[0], 1)
                    te.wait_ge(eV[0], eV0 + k + 1)
                    if k >= 2:
                        te.wait_ge(eVe[0], eVe0 + k - 1)
                    te.transpose(pstT[k % 2][:, 0:128],
                                 accb[:, :], pb.ident[:, :]).then_inc(eTt[0], 1)

    qG[1] += 16 * (1 + 64 + NCD)
    qS[1] += 32
    eVp[1] += 4
    eSp[1] += 1
    eTp[1] += 2
    eT[1] += 256
    eTt[1] += 256
    eV[1] += 256
    eVe[1] += 256
    qC[1] += 16 * 256
    pb.end_phase()


def emit_l1o(pb, comb_fm_dram, wo1_bf, s1_dram):
    nc = pb.nc
    pb.begin_phase()
    qS = pb.sem("qS")
    qG = pb.sem("qG")
    qC = pb.sem("qC")
    eT = pb.sem("lo_eT")
    eV = pb.sem("lo_eV")
    comb = pb.sbuf("lo_c", [128, NCD * S], BF16)
    wt = pb.sbuf("lo_w", [128, NCD * 4 * 512], BF16, td=True)
    stage = pb.sbuf("lo_st", [128, 2 * D], F32)
    ps = [pb.psum(f"lo_ps{i}", [128, 512]) for i in range(8)]
    qS0, qG0, qC0, eT0, eV0 = qS[1], qG[1], qC[1], eT[1], eV[1]

    with nc.Block(f"l1o_{pb._pctr}") as block:
        @block.gpsimd
        def _(g):
            for c in range(NCD):
                g.dma_start(out=comb[:, c * S:(c + 1) * S], in_=comb_fm_dram[c, :, :]
                            ).then_inc(qG[0], 16)

        @block.sync
        def _(s):
            for mb in range(4):
                for d in range(NCD):
                    s.dma_start(out=wt[:, (mb * NCD + d) * 512:(mb * NCD + d + 1) * 512],
                                in_=wo1_bf[mb * 512:(mb + 1) * 512, d * 128:(d + 1) * 128],
                                transpose=True).then_inc(qS[0], 16)

        @block.tensor
        def _(te):
            te.wait_ge(qG[0], qG0 + 16 * NCD)
            te.wait_ge(qS[0], qS0 + 16 * 64)
            for t in range(NTS):
                if t >= 2:
                    te.wait_ge(eV[0], eV0 + t - 1)
                bank = (t % 2) * 4
                for mb in range(4):
                    for d in range(NCD):
                        ins = te.matmul(ps[bank + mb][:, :],
                                        comb[:, d * S + t * 128: d * S + (t + 1) * 128],
                                        wt[:, (mb * NCD + d) * 512:(mb * NCD + d + 1) * 512],
                                        start=(d == 0), stop=(d == NCD - 1))
                ins.then_inc(eT[0], 1)

        @block.vector
        def _(v):
            for t in range(NTS):
                v.wait_ge(eT[0], eT0 + t + 1)
                if t >= 2:
                    v.wait_ge(qC[0], qC0 + 16 * (t - 1))
                b = (t % 2) * D
                bank = (t % 2) * 4
                for mb in range(4):
                    v.tensor_copy(stage[:, b + mb * 512: b + (mb + 1) * 512],
                                  ps[bank + mb][:, :])
                v.drain().then_inc(eV[0], 1)

        @block.scalar
        def _(sc):
            for t in range(NTS):
                sc.wait_ge(eV[0], eV0 + t + 1)
                sc.dma_start(out=s1_dram[t * 128:(t + 1) * 128, :],
                             in_=stage[:, (t % 2) * D:(t % 2 + 1) * D]).then_inc(qC[0], 16)
            sc.wait_ge(qC[0], qC0 + 16 * NTS)

    qG[1] += 16 * NCD
    qS[1] += 16 * 64
    eT[1] += NTS
    eV[1] += NTS
    qC[1] += 16 * NTS
    pb.end_phase()


def emit_res(pb, x_dram, s_dram, g_dram, out_dram):
    nc = pb.nc
    pb.begin_phase()
    qS = pb.sem("qS")
    qG = pb.sem("qG")
    eV = pb.sem("rs_eV")
    xt = pb.sbuf("rs_x", [128, 2 * D], F32)
    st_t = pb.sbuf("rs_s", [128, 2 * D], F32)
    gt = pb.sbuf("rs_g", [128, 2 * D], F32)
    qS0, qG0, eV0 = qS[1], qG[1], eV[1]

    with nc.Block(f"res_{pb._pctr}") as block:
        @block.sync
        def _(s):
            for t in range(NTS):
                b = t % 2
                if t >= 2:
                    s.wait_ge(qG[0], qG0 + 16 * (t - 1))
                s.dma_start(out=xt[:, b * D:(b + 1) * D],
                            in_=x_dram[t * 128:(t + 1) * 128, :]).then_inc(qS[0], 16)
                s.dma_start(out=st_t[:, b * D:(b + 1) * D],
                            in_=s_dram[t * 128:(t + 1) * 128, :]).then_inc(qS[0], 16)
                s.dma_start(out=gt[:, b * D:(b + 1) * D],
                            in_=g_dram[t * 128:(t + 1) * 128, :]).then_inc(qS[0], 16)

        @block.vector
        def _(v):
            for t in range(NTS):
                b = t % 2
                sl = slice(b * D, (b + 1) * D)
                v.wait_ge(qS[0], qS0 + 48 * (t + 1))
                v.tensor_tensor(st_t[:, sl], st_t[:, sl], gt[:, sl], ALU.mult)
                v.tensor_tensor(xt[:, sl], xt[:, sl], st_t[:, sl], ALU.add)
                v.drain().then_inc(eV[0], 1)

        @block.gpsimd
        def _(g):
            for t in range(NTS):
                b = t % 2
                g.wait_ge(eV[0], eV0 + t + 1)
                g.dma_start(out=out_dram[t * 128:(t + 1) * 128, :],
                            in_=xt[:, b * D:(b + 1) * D]).then_inc(qG[0], 16)
            g.wait_ge(qG[0], qG0 + 16 * NTS)

    qS[1] += 48 * NTS
    eV[1] += NTS
    qG[1] += 16 * NTS
    pb.end_phase()


def build_program():
    pb = PB()
    nc = pb.nc
    ein = lambda n, sh, dt=F32: nc.dram_tensor(n, sh, dt, kind="ExternalInput")
    x0 = ein("x0", [S, D])
    wq0 = ein("L0_Wq", [D, D]); bq0 = ein("L0_bq", [D])
    wk0 = ein("L0_Wk", [D, D]); bk0 = ein("L0_bk", [D])
    wv0 = ein("L0_Wv", [D, D]); bv0 = ein("L0_bv", [D])
    wo0 = ein("L0_Wo", [D, D]); bo0 = ein("L0_bo", [D])
    gWp = ein("gWp", [MLP0, 1024], I32); gSc = ein("gSc", [MLP0, 64])
    uWp = ein("uWp", [MLP0, 1024], I32); uSc = ein("uSc", [MLP0, 64])
    dWp = ein("dWp", [D, 8192], I32); dSc = ein("dSc", [D, 512])
    wq1 = ein("L1_Wq", [D, D]); wo1 = ein("L1_Wo", [D, D])
    bw1 = ein("L1_bw", [16, 4])
    mem1 = ein("L1_mem", [4, 16, 128, 128]); mn1 = ein("L1_mn", [4, 16, 128])
    su_ug = ein("su_ug", [D]); su_dp = ein("su_dp", [INTER1])
    wsc_ug = ein("wsc_ug", [2 * INTER1]); wsc_dp = ein("wsc_dp", [D])
    w_up = ein("w_up", [INTER1, D]); w_gate = ein("w_gate", [INTER1, D])
    w_down = ein("w_down", [D, INTER1])
    out = nc.dram_tensor("out", [S, D], F32, kind="ExternalOutput")
    dt = lambda n, sh, d=BF16: nc.dram_tensor(n, sh, d)
    xn0 = dt("xn0", [NCD, 128, S])
    q_fm = dt("q_fm", [H, 128, S]); k_fm = dt("k_fm", [H, 128, S])
    v_tm = dt("v_tm", [NTS, 128, D]); ctx_fm = dt("ctx_fm", [H, 128, S])
    s0 = dt("s0", [S, D], F32)
    wg_bf = dt("wg_bf", [MLP0, D]); wu_bf = dt("wu_bf", [MLP0, D])
    wd0_bf = dt("wd0_bf", [D, MLP0])
    inter = dt("inter", [128, 128, S])
    g0 = dt("g0", [S, D], F32)
    x1 = dt("x1", [S, D], F32)
    xn1 = dt("xn1", [NCD, 128, S])
    wq1_bf = dt("wq1_bf", [D, D]); wo1_bf = dt("wo1_bf", [D, D])
    wup_bf = dt("wup_bf", [INTER1, D]); wg1_bf = dt("wg1_bf", [INTER1, D])
    wd1_bf = dt("wd1_bf", [D, INTER1])
    sq_fm = dt("sq_fm", [NCD, 128, S]); comb_fm = dt("comb_fm", [NCD, 128, S])
    s1 = dt("s1", [S, D], F32)
    xr_fm = dt("xr_fm", [NCD, 128, S])
    y1 = dt("y1", [64, 128, S])
    g1 = dt("g1", [S, D], F32)

    pb.consts()
    emit_ln(pb, x0, xn0)
    emit_qkv(pb, xn0, wq0, bq0, wk0, bk0, wv0, bv0, q_fm, k_fm, v_tm)
    dq = DQ(pb, gWp, gSc, uWp, uSc, dWp, dSc, wg_bf, wu_bf, wd0_bf)
    N1 = 224
    emit_att0(pb, q_fm, k_fm, v_tm, ctx_fm, dq=dq, r0=0, r1=N1)
    emit_ffz(pb, dq, N1, xn0, wg_bf, wu_bf, inter)
    dq.close()
    emit_ffn0b(pb, inter, wd0_bf, g0, nchunks=128, act="silu")
    emit_wo0(pb, ctx_fm, wo0, bo0, s0)
    emit_ln(pb, x0, xn1, s_in=s0, g_in=g0, x_out=x1)
    emit_casts(pb, [(wq1, wq1_bf, D, D), (wo1, wo1_bf, D, D),
                    (w_up, wup_bf, INTER1, D), (w_gate, wg1_bf, INTER1, D)])
    emit_scaled_cast(pb, w_down, wsc_dp, wd1_bf, D, INTER1, 1.0 / np.sqrt(float(INTER1)))
    emit_l1a(pb, xn1, wq1_bf, sq_fm)
    emit_l1b(pb, sq_fm, mem1, mn1, bw1, comb_fm)
    emit_l1o(pb, comb_fm, wo1_bf, s1)
    emit_fwht1(pb, xn1, su_ug, xr_fm)
    emit_ffn1a(pb, xr_fm, wup_bf, wg1_bf, wsc_ug, su_dp, y1)
    emit_ffn1b(pb, y1, wd1_bf, g1)
    emit_res(pb, x1, s1, g1, out)
    return pb


_PROG_CACHE = {}


def kernel(**inputs):
    if "prog" not in _PROG_CACHE:
        _PROG_CACHE["prog"] = build_program()
    pb = _PROG_CACHE["prog"]
    x = np.ascontiguousarray(inputs["L0_attn_input_sequences"])
    _ = inputs["L1_attn_Wk"], inputs["L1_attn_Wv"]
    common = {
        "L0_Wq": inputs["L0_attn_Wq"], "L0_bq": inputs["L0_attn_bq"],
        "L0_Wk": inputs["L0_attn_Wk"], "L0_bk": inputs["L0_attn_bk"],
        "L0_Wv": inputs["L0_attn_Wv"], "L0_bv": inputs["L0_attn_bv"],
        "L0_Wo": inputs["L0_attn_Wo"], "L0_bo": inputs["L0_attn_bo"],
        "gWp": inputs["L0_ffn_gate_W_packed"], "gSc": inputs["L0_ffn_gate_scales"],
        "uWp": inputs["L0_ffn_up_W_packed"], "uSc": inputs["L0_ffn_up_scales"],
        "dWp": inputs["L0_ffn_down_W_packed"], "dSc": inputs["L0_ffn_down_scales"],
        "L1_Wq": inputs["L1_attn_Wq"], "L1_Wo": inputs["L1_attn_Wo"],
        "L1_bw": inputs["L1_attn_bank_weights"],
        "L1_mem": inputs["L1_attn_memories"], "L1_mn": inputs["L1_attn_memory_norms"],
        "su_ug": inputs["L1_ffn_SU_ug"], "su_dp": inputs["L1_ffn_SU_dp"],
        "wsc_ug": inputs["L1_ffn_Wscale_ug"], "wsc_dp": inputs["L1_ffn_Wscale_dp"],
        "w_up": inputs["L1_ffn_W_up"], "w_gate": inputs["L1_ffn_W_gate"],
        "w_down": inputs["L1_ffn_W_down"],
    }
    common = {k: np.ascontiguousarray(v) for k, v in common.items()}
    in_maps = [dict(common, x0=x[i]) for i in range(8)]
    res = run_bass_kernel_spmd(pb.nc, in_maps, core_ids=list(range(8)))
    return np.stack([np.asarray(res.results[i]["out"]) for i in range(8)], axis=0)


class DQ:
    R = 5
    KP, KO, NB = 512, 1024, 32

    def __init__(self, pb, gWp, gSc, uWp, uSc, dWp, dSc, wg_bf, wu_bf, wd0_bf):
        import contextlib
        self.pb = pb
        self.scope = contextlib.ExitStack()
        old = pb.cur
        pb.cur = self.scope
        KP, KO, NB, R = self.KP, self.KO, self.NB, self.R
        self.wt = pb.sbuf("dqx_w", [128, R * KP], I32)
        self.sct = pb.sbuf("dqx_sc", [128, R * NB], F32)
        self.nibD = pb.sbuf("dqx_nD", [128, R * KO], I32)
        self.Df = self.nibD[:, :].bitcast(F32)
        self.A = pb.sbuf("dqx_A", [128, R * KO], F32)
        self.B = pb.sbuf("dqx_B", [128, R * KO], F32)
        self.C = pb.sbuf("dqx_C", [128, R * KO], F32)
        self.ob = self.B[:, :].bitcast(BF16)
        pb.cur = old
        self.qGp = pb.sem("fz_qGp")
        self.qGc = pb.sem("fz_qGc")
        self.qGs = pb.sem("fz_qGs")
        self.eLd = pb.sem("fz_eLd")
        self.eSt = pb.sem("fz_eSt")
        self.dV0 = pb.sem("fz_dV0")
        self.dV1 = pb.sem("fz_dV1")
        self.dV2 = pb.sem("fz_dV2")
        self.dS1 = pb.sem("fz_dS1")
        self.dS2 = pb.sem("fz_dS2")
        self.z = {n: s[1] for n, s in pb.sems.items() if n.startswith("fz_")}
        TILES = []
        for g in range(32):
            for r in range(4 * g, 4 * g + 4):
                for kc in range(2):
                    TILES.append((gWp, gSc, wg_bf, r, kc))
            for r in range(4 * g, 4 * g + 4):
                for kc in range(2):
                    TILES.append((uWp, uSc, wu_bf, r, kc))
        for r in range(16):
            for kc in range(16):
                TILES.append((dWp, dSc, wd0_bf, r, kc))
        self.TILES = TILES
        self.T = len(TILES)

    def close(self):
        self.scope.close()

    def finalize_counts(self):
        pb = self.pb
        T = self.T
        pb.sems["fz_qGp"][1] = self.z["fz_qGp"] + 16 * T
        pb.sems["fz_qGc"][1] = self.z["fz_qGc"] + 16 * T
        pb.sems["fz_qGs"][1] = self.z["fz_qGs"] + 16 * T
        pb.sems["fz_eLd"][1] = self.z["fz_eLd"] + T
        pb.sems["fz_eSt"][1] = self.z["fz_eSt"] + 48
        pb.sems["fz_dV0"][1] = self.z["fz_dV0"] + T
        pb.sems["fz_dV1"][1] = self.z["fz_dV1"] + T
        pb.sems["fz_dV2"][1] = self.z["fz_dV2"] + T
        pb.sems["fz_dS1"][1] = self.z["fz_dS1"] + T
        pb.sems["fz_dS2"][1] = self.z["fz_dS2"] + T

    def src(self, t):
        pk, sc_, dst, r, kc = self.TILES[t]
        KP, KO, NB = self.KP, self.KO, self.NB
        return (pk[r * 128:(r + 1) * 128, kc * KP:(kc + 1) * KP],
                sc_[r * 128:(r + 1) * 128, kc * NB:(kc + 1) * NB],
                dst[r * 128:(r + 1) * 128, kc * KO:(kc + 1) * KO])

    def sl(self, t, n):
        return slice((t % self.R) * n, (t % self.R + 1) * n)

    def pool_step(self, g_, t):
        T, R, KP, KO, NB = self.T, self.R, self.KP, self.KO, self.NB
        z = self.z
        if t < T:
            pk, sc_, _d = self.src(t)
            if t >= R:
                g_.wait_ge(self.dV0[0], z["fz_dV0"] + t - R + 1)
                g_.wait_ge(self.dV2[0], z["fz_dV2"] + t - R + 1)
            g_.dma_start(out=self.wt[:, self.sl(t, KP)], in_=pk).then_inc(self.qGp[0], 16)
            g_.dma_start(out=self.sct[:, self.sl(t, NB)], in_=sc_).then_inc(self.qGc[0], 16)
            if t % 2 == 1:
                g_.wait_ge(self.qGp[0], z["fz_qGp"] + 16 * (t + 1))
                g_.wait_ge(self.qGc[0], z["fz_qGc"] + 16 * (t + 1))
                g_.sem_inc(self.eLd[0], 2)
        if t >= R and t - R < T:
            t3 = t - R
            g_.wait_ge(self.dV2[0], z["fz_dV2"] + t3 + 1)
            _pk, _sc, dst = self.src(t3)
            s3 = (t3 % R) * KO * 2
            g_.dma_start(out=dst, in_=self.ob[:, s3:s3 + KO]).then_inc(self.qGs[0], 16)
            if t3 % 16 == 15:
                g_.wait_ge(self.qGs[0], z["fz_qGs"] + 16 * (t3 + 1))
                g_.sem_inc(self.eSt[0], 1)

    def v_step(self, v, t):
        T, R, KP, KO, NB = self.T, self.R, self.KP, self.KO, self.NB
        z = self.z
        A, B, C, Df = self.A, self.B, self.C, self.Df
        sl = self.sl
        if t >= 4 and t - 4 < T:
            t2 = t - 4
            v.wait_ge(self.dS2[0], z["fz_dS2"] + t2 + 1)
            v.tensor_tensor(A[:, sl(t2, KO)], A[:, sl(t2, KO)], Df[:, sl(t2, KO)], ALU.add)
            v.scalar_tensor_tensor(C[:, sl(t2, KO)], C[:, sl(t2, KO)], 0.5,
                                   A[:, sl(t2, KO)], ALU.mult, ALU.add)
            v.scalar_tensor_tensor(C[:, sl(t2, KO)], C[:, sl(t2, KO)], -2.0,
                                   B[:, sl(t2, KO)], ALU.add, ALU.mult)
            s2o = (t2 % R) * KO * 2
            cv = C[:, sl(t2, KO)].rearrange("p (bk j) -> p bk j", j=32)
            ov = self.ob[:, s2o:s2o + KO].rearrange("p (bk j) -> p bk j", j=32)
            scb = self.sct[:, sl(t2, NB)].unsqueeze(-1).broadcast_to((128, NB, 32))
            v.tensor_tensor(ov, cv, scb, ALU.mult)
            v.drain().then_inc(self.dV2[0], 1)
        if t >= 2 and t - 2 < T:
            t1 = t - 2
            v.wait_ge(self.dS1[0], z["fz_dS1"] + t1 + 1)
            v.scalar_tensor_tensor(C[:, sl(t1, KO)], B[:, sl(t1, KO)], 4.0,
                                   A[:, sl(t1, KO)], ALU.mult, ALU.add)
            v.drain().then_inc(self.dV1[0], 1)
        if t < T:
            v.wait_ge(self.eLd[0], z["fz_eLd"] + t + 1)
            if t >= R:
                v.wait_ge(self.dS1[0], z["fz_dS1"] + t - R + 1)
                v.wait_ge(self.dS2[0], z["fz_dS2"] + t - R + 1)
            nv = self.nibD[:, sl(t, KO)].rearrange("p (k two) -> p k two", two=2)
            wv = self.wt[:, sl(t, KP)].unsqueeze(-1)
            v.tensor_scalar(nv[:, :, 0:1], wv, 15, None, ALU.bitwise_and)
            v.tensor_scalar(nv[:, :, 1:2], wv, 4, 15,
                            ALU.logical_shift_right, ALU.bitwise_and)
            v.drain().then_inc(self.dV0[0], 1)

    def s_step(self, sc_, t):
        T, R, KO = self.T, self.R, self.KO
        z = self.z
        A, B, C, Df = self.A, self.B, self.C, self.Df
        sl = self.sl
        if t >= 1 and t - 1 < T:
            t0 = t - 1
            sc_.wait_ge(self.dV0[0], z["fz_dV0"] + t0 + 1)
            if t0 >= R:
                sc_.wait_ge(self.dV2[0], z["fz_dV2"] + t0 - R + 1)
                sc_.wait_ge(self.qGs[0], z["fz_qGs"] + 16 * (t0 - R + 1))
            sc_.activation(A[:, sl(t0, KO)], self.nibD[:, sl(t0, KO)], AF.Copy, bias=0.0)
            sc_.activation(B[:, sl(t0, KO)], A[:, sl(t0, KO)], AF.Sign,
                           bias=f32(7.5), scale=f32(-1.0))
            sc_.drain().then_inc(self.dS1[0], 1)
        if t >= 3 and t - 3 < T:
            t1 = t - 3
            sc_.wait_ge(self.dV1[0], z["fz_dV1"] + t1 + 1)
            sc_.activation(A[:, sl(t1, KO)], C[:, sl(t1, KO)], AF.Relu,
                           bias=f32(-4.0), scale=f32(0.5))
            sc_.activation(Df[:, sl(t1, KO)], C[:, sl(t1, KO)], AF.Relu,
                           bias=f32(-10.0))
            sc_.drain().then_inc(self.dS2[0], 1)


def emit_ffz(pb, dq, n1, xn_fm_dram, wg_bf, wu_bf, inter_dram):
    nc = pb.nc
    pb.begin_phase()
    qS = pb.sem("qS")
    qC = pb.sem("qC")
    fT = pb.sem("fz_fT")
    fSg = pb.sem("fz_fSg")
    fV = pb.sem("fz_fV")
    fVn = pb.sem("fz_fVn")
    eWg = pb.sem("fz_eWg")
    xn = pb.sbuf("fz_xn", [128, NCD * S], BF16)
    wg = pb.sbuf("fz_wg", [128, NCD * 512], BF16, td=True)
    wu = pb.sbuf("fz_wu", [128, NCD * 512], BF16, td=True)
    gsb = pb.sbuf("fz_g", [128, 512], F32)
    ibuf = pb.sbuf("fz_i", [128, 2 * S], BF16)
    ps = [pb.psum(f"fz_ps{i}", [128, 512]) for i in range(4)]
    qS0, qC0 = qS[1], qC[1]
    fT0, fSg0, fV0, fVn0, eWg0 = fT[1], fSg[1], fV[1], fVn[1], eWg[1]
    T6 = dq.T + 6
    z = dq.z

    def events():
        out = []
        tnext = n1
        for j in range(512):
            target = min(T6, n1 + ((j + 1) * (T6 - n1)) // 512)
            while tnext < target:
                out.append(("dq", tnext))
                tnext += 1
            out.append(("ev", j))
        while tnext < T6:
            out.append(("dq", tnext))
            tnext += 1
        return out

    with nc.Block(f"ffz_{pb._pctr}") as block:
        @block.gpsimd
        def _(g_):
            for t in range(n1, T6):
                dq.pool_step(g_, t)
            g_.wait_ge(dq.qGs[0], z["fz_qGs"] + 16 * dq.T)

        @block.vector
        def _(v):
            for kind, t in events():
                if kind == "dq":
                    dq.v_step(v, t)
                else:
                    j = t
                    nci, st = j // 4, j % 4
                    v.wait_ge(fSg[0], fSg0 + j + 1)
                    if st == 0 and nci >= 2:
                        v.wait_ge(qC[0], qC0 + 16 * (nci - 1))
                    bg = st % 2
                    ins = v.tensor_tensor(
                        ibuf[:, (nci % 2) * S + st * 512:(nci % 2) * S + (st + 1) * 512],
                        gsb[:, :], ps[bg * 2 + 1][:, :], ALU.mult)
                    ins.then_inc(fV[0], 1)
                    if st == 3:
                        v.drain().then_inc(fVn[0], 1)

        @block.scalar
        def _(sc_):
            for kind, t in events():
                if kind == "dq":
                    dq.s_step(sc_, t)
                else:
                    j = t
                    nci, st = j // 4, j % 4
                    sc_.wait_ge(fT[0], fT0 + j + 1)
                    if j >= 1:
                        sc_.wait_ge(fV[0], fV0 + j)
                    bg = st % 2
                    sc_.activation(gsb[:, :], ps[bg * 2][:, :], AF.Gelu_apprx_tanh)
                    sc_.drain().then_inc(fSg[0], 1)
                    if st == 3 and nci >= 1:
                        sc_.wait_ge(fVn[0], fVn0 + nci)
                        sc_.dma_start(out=inter_dram[nci - 1, :, :],
                                      in_=ibuf[:, ((nci - 1) % 2) * S:
                                               ((nci - 1) % 2 + 1) * S]
                                      ).then_inc(qC[0], 16)
            sc_.wait_ge(fVn[0], fVn0 + 128)
            sc_.dma_start(out=inter_dram[127, :, :], in_=ibuf[:, S:2 * S]
                          ).then_inc(qC[0], 16)
            sc_.wait_ge(qC[0], qC0 + 16 * 128)

        @block.sync
        def _(s):
            for cc in range(NCD):
                s.dma_start(out=xn[:, cc * S:(cc + 1) * S], in_=xn_fm_dram[cc, :, :]
                            ).then_inc(qS[0], 16)
            for g in range(32):
                s.wait_ge(dq.eSt[0], z["fz_eSt"] + g + 1)
                if g >= 1:
                    s.wait_ge(fT[0], fT0 + 16 * g)
                for d in range(NCD):
                    s.dma_start(out=wg[:, d * 512:(d + 1) * 512],
                                in_=wg_bf[g * 512:(g + 1) * 512, d * 128:(d + 1) * 128],
                                transpose=True).then_inc(qS[0], 16)
                for d in range(NCD):
                    s.dma_start(out=wu[:, d * 512:(d + 1) * 512],
                                in_=wu_bf[g * 512:(g + 1) * 512, d * 128:(d + 1) * 128],
                                transpose=True).then_inc(qS[0], 16)
                s.wait_ge(qS[0], qS0 + 16 * (NCD + 32 * (g + 1)))
                s.sem_inc(eWg[0], 1)

        @block.tensor
        def _(te):
            te.wait_ge(qS[0], qS0 + 16 * NCD)
            for g in range(32):
                te.wait_ge(eWg[0], eWg0 + g + 1)
                for ncg in range(4):
                    for st in range(4):
                        idx = (g * 4 + ncg) * 4 + st
                        if idx >= 2:
                            te.wait_ge(fV[0], fV0 + idx - 1)
                        bg = st % 2
                        for d in range(NCD):
                            te.matmul(ps[bg * 2][:, :],
                                      wg[:, d * 512 + ncg * 128: d * 512 + (ncg + 1) * 128],
                                      xn[:, d * S + st * 512: d * S + (st + 1) * 512],
                                      start=(d == 0), stop=(d == NCD - 1))
                        for d in range(NCD):
                            ins = te.matmul(ps[bg * 2 + 1][:, :],
                                            wu[:, d * 512 + ncg * 128: d * 512 + (ncg + 1) * 128],
                                            xn[:, d * S + st * 512: d * S + (st + 1) * 512],
                                            start=(d == 0), stop=(d == NCD - 1))
                        ins.then_inc(fT[0], 1)

    dq.finalize_counts()
    qS[1] += 16 * (NCD + 32 * 32)
    eWg[1] += 32
    qC[1] += 16 * 128
    fT[1] += 512
    fSg[1] += 512
    fV[1] += 512
    fVn[1] += 128
    pb.end_phase()
